# version 01
# baseline (speedup 1.0000x reference)
import numpy as np
from contextlib import ExitStack

import concourse.bass as bass
import concourse.mybir as mybir
from concourse.bass_utils import run_bass_kernel_spmd

F32 = mybir.dt.float32
BF16 = mybir.dt.bfloat16
F16 = mybir.dt.float16
I32 = mybir.dt.int32
U8 = mybir.dt.uint8
AF = mybir.ActivationFunctionType
ALU = mybir.AluOpType
AX = mybir.AxisListType.X

NCORES = 8
D = 2048
TB = 512
NTILES = 4
NT = 16384
H0, K0 = 32, 64
H1, K1 = 16, 128
I0 = 5632
GU = 2 * I0
I1 = 5632
LN2 = 0.6931471805599453
SCL1 = 0.088388347648318447

CONSTS = [0.0, 1.0, -1.0, 0.5, -0.5, 2.0, -2.0, 8.0, -8.0,
          LN2, -2.0 * LN2, 1.0 / 2048.0, 1.0 / 64.0,
          1e-5, 0.00064, 1e-6, 1e-8, SCL1, -16383.0]
CIDX = {v: i for i, v in enumerate(CONSTS)}

SEG_ROWS = {"l0": 1280, "l1": 1024, "gul": 128, "guh": 128,
            "dnl": 352, "dnh": 352, "w1q": 256, "w2q": 256, "w3q": 704}
SEG_W = {"l0": D, "l1": D, "gul": GU, "guh": GU, "dnl": D, "dnh": D,
         "w1q": I1, "w2q": I1, "w3q": D}

DBG = {"taps": [], "parts": {"casts", "gamma", "tern", "fp4"},
       "ntiles": NTILES,
       "stages": {"l0attn", "l0ffn", "l1attn", "l1out", "l1ffn"}}


def _sigma_perm():
    out = []
    for c in range(I0 // 256):
        out.extend(range(256 * c, 256 * c + 256, 2))
        out.extend(range(256 * c + 1, 256 * c + 256, 2))
    return np.array(out, dtype=np.int64)


def _col16(v):
    return np.ascontiguousarray(v.reshape(16, 128).T)


def host_prep(inp):
    import ml_dtypes
    x = np.asarray(inp["L0_attn_x"], np.float32).reshape(NT, D)
    state = np.asarray(inp["L0_attn_state"], np.float32).reshape(1, D)

    wrT = np.ascontiguousarray(np.asarray(inp["L0_attn_w_r"], np.float32).T)
    wkT = np.ascontiguousarray(np.asarray(inp["L0_attn_w_k"], np.float32).T)
    wvT = np.ascontiguousarray(np.asarray(inp["L0_attn_w_v"], np.float32).T)
    wgT = np.ascontiguousarray(np.asarray(inp["L0_attn_w_g"], np.float32).T)
    woT = np.ascontiguousarray(np.asarray(inp["L0_attn_w_o"], np.float32).T)
    l1q = np.asarray(inp["L1_attn_Wq"], np.float32)
    l1k = np.asarray(inp["L1_attn_Wk"], np.float32)
    l1i = np.asarray(inp["L1_attn_Wi"], np.float32)
    l1o = np.asarray(inp["L1_attn_Wo"], np.float32)

    sig = _sigma_perm()
    cols_perm = np.concatenate([sig, I0 + sig])
    gu_t = np.ascontiguousarray(
        np.asarray(inp["L0_ffn_gate_up_packed"], np.int32).view(np.uint8).T)
    gu_tp = np.ascontiguousarray(gu_t[:, cols_perm])
    gsc_p = np.ascontiguousarray(
        np.asarray(inp["L0_ffn_gate_up_scales"])[:, cols_perm])
    dn_t = np.ascontiguousarray(
        np.asarray(inp["L0_ffn_down_packed"], np.int32).view(np.uint8).T)
    dsc = np.asarray(inp["L0_ffn_down_scales"])

    w1T = np.ascontiguousarray(np.asarray(inp["L1_ffn_w1"], np.float32).T)
    w2T = np.ascontiguousarray(np.asarray(inp["L1_ffn_w2"], np.float32).T)
    w3T = np.ascontiguousarray(np.asarray(inp["L1_ffn_w3"], np.float32).T)

    tmix = np.concatenate(
        [_col16(np.asarray(inp["L0_attn_time_mix_" + s], np.float32).reshape(D))
         for s in ("r", "k", "v", "g")], axis=1)
    smallp = np.concatenate(
        [_col16(np.asarray(inp[n], np.float32).reshape(D)) for n in
         ("L0_attn_time_decay", "L0_attn_time_faaaa", "L0_attn_ln_w",
          "L0_attn_ln_b", "L1_attn_g_weight")], axis=1)
    wkv_l = np.ascontiguousarray(
        np.asarray(inp["L0_attn_wkvstate"], np.float32).transpose(1, 0, 2).reshape(64, D))

    ccol = np.broadcast_to(np.array(CONSTS, np.float32), (128, len(CONSTS))).copy()
    rng_p = np.arange(128)
    ident = np.eye(128, dtype=np.float32)
    m_lt = (rng_p[:, None] < rng_p[None, :]).astype(np.float32)
    le64 = (np.arange(64)[:, None] <= np.arange(64)[None, :]).astype(np.float32)
    mask_l1 = np.zeros((128, 128), np.float32)
    mask_l1[0:64, 0:64] = le64
    mask_l1[0:64, 64:128] = 1.0
    mask_l1[64:128, 64:128] = le64
    msk_bf = np.concatenate([ident, m_lt, mask_l1], axis=1).astype(ml_dtypes.bfloat16)
    m_diag_i = np.eye(128, dtype=np.int32)
    o642 = np.zeros((128, 2), np.float32)
    o642[0:64, 0] = 1.0
    o642[64:128, 1] = 1.0
    aux = np.concatenate([o642, np.ones((128, 1), np.float32)], axis=1)
    aux_bf = aux.astype(ml_dtypes.bfloat16)
    e2_bf = np.zeros((2, 128), np.float32)
    e2_bf[0, 0:64] = 1.0
    e2_bf[1, 64:128] = 1.0
    e2_bf = e2_bf.astype(ml_dtypes.bfloat16)
    ones1_bf = np.ones((1, 128), np.float32).astype(ml_dtypes.bfloat16)

    in_maps = []
    for i in range(NCORES):
        m = {}
        for r in range(NTILES):
            blk = 8 * r + i
            rows = x[blk * TB: blk * TB + TB]
            bnd = state if blk == 0 else x[blk * TB - 1: blk * TB]
            m[f"x{r}"] = np.concatenate([bnd, rows], axis=0)
            m[f"iota{r}"] = np.broadcast_to(
                np.arange(blk * TB, blk * TB + TB, dtype=np.float32)[None, :],
                (128, TB)).copy()
        fl = np.zeros((128, 2 * NTILES), np.float32)
        for r in range(NTILES):
            f = 0.0 if (8 * r + i) == 0 else 1.0
            fl[:, 2 * r] = f
            fl[:, 2 * r + 1] = 1.0 - f
        m["flags"] = fl
        pm = np.broadcast_to((np.arange(8) < i).astype(np.float32), (128, 8)).copy()
        m["pmask"] = pm
        sh = slice(256 * i, 256 * (i + 1))
        m["l0w_sh"] = np.ascontiguousarray(
            np.concatenate([wrT[sh], wkT[sh], wvT[sh], wgT[sh], woT[sh]], axis=0))
        m["l1w_sh"] = np.ascontiguousarray(
            np.concatenate([l1q[sh], l1k[sh], l1i[sh], l1o[sh]], axis=0))
        m["gu_sh"] = np.ascontiguousarray(gu_tp[128 * i: 128 * (i + 1)])
        m["gu_sc"] = np.ascontiguousarray(
            np.repeat(gsc_p[[2 * i, 2 * i + 1]], 64, axis=0))
        m["dn_sh"] = np.ascontiguousarray(dn_t[352 * i: 352 * (i + 1)])
        m["dn_sc"] = np.ascontiguousarray(dsc[(np.arange(352) + 352 * i) // 64])
        m["w1_sh"] = np.ascontiguousarray(w1T[sh])
        m["w2_sh"] = np.ascontiguousarray(w2T[sh])
        m["w3_sh"] = np.ascontiguousarray(w3T[704 * i: 704 * (i + 1)])
        m["tmix"] = tmix
        m["smallp"] = smallp
        m["wkv0"] = wkv_l
        m["ccol"] = ccol
        m["msk_bf"] = np.asarray(msk_bf)
        m["m_diag_i"] = m_diag_i
        m["aux_bf"] = np.asarray(aux_bf)
        m["e2_bf"] = np.asarray(e2_bf)
        m["ones1_bf"] = np.asarray(ones1_bf)
        m["ident_f32"] = ident
        m["onesf"] = np.ones((128, 16), np.float32)
        in_maps.append(m)
    return in_maps



class Sem:
    def __init__(self, handle):
        self.h = handle
        self.n = 0


class K:
    ENG_SEM = {"vector": "V", "scalar": "S", "gpsimd": "G", "tensor": "T"}

    def __init__(self, nc):
        self.nc = nc
        self.sems = {}
        self.cur = None
        self.seen = {}
        self.tensors = {}

    def new_pass(self, eng):
        self.cur = eng
        self.seen = {}
        for s in self.sems.values():
            s.n = 0

    def op(self, eng, fn, wait=(), inc=None, inc_n=1, ss=True):
        auto = self.ENG_SEM.get(eng)
        if eng in ("vector", "scalar", "gpsimd") and inc is None:
            inc, inc_n = auto, 1
        waits = list(wait)
        if ss and eng in ("vector", "scalar", "gpsimd"):
            pre = self.sems[auto].n
            if pre > 0:
                waits.append((auto, pre))
        if self.cur == eng:
            e = getattr(self.nc, eng)
            for sname, val in waits:
                if val <= 0:
                    continue
                key = (eng, sname)
                if self.seen.get(key, -1) < val:
                    e.wait_ge(self.sems[sname].h, val)
                    self.seen[key] = val
            inst = fn(e)
        else:
            inst = None
        if inc is not None:
            s = self.sems[inc]
            s.n += inc_n
            if inst is not None:
                inst.then_inc(s.h, inc_n)
            return (inc, s.n)
        return None

    def dma(self, eng, out, in_, wait=(), inc="LD", ss=False, **kw):
        return self.op(eng, lambda e: e.dma_start(out=out, in_=in_, **kw),
                       wait=wait, inc=inc, inc_n=16, ss=ss)

    def mm(self, out, lhsT, rhs, start, stop, wait=(), inc=None):
        return self.op("tensor",
                       lambda e: e.matmul(out, lhsT, rhs, start=start, stop=stop),
                       wait=wait, inc=inc, inc_n=1)

    def tr(self, out, in_, ident, wait=(), inc=None):
        return self.op("tensor",
                       lambda e: e.transpose(out, in_, ident),
                       wait=wait, inc=inc, inc_n=1)

    def ev(self, sname):
        return (sname, self.sems[sname].n)


def cc_(k, val):
    return k.tensors["ccol"][:, CIDX[val]:CIDX[val] + 1]


def ccp(k, val, p0, p1):
    return k.tensors["ccol"][p0:p1, CIDX[val]:CIDX[val] + 1]



def build_program():
    nc = bass.Bass("TRN2", target_bir_lowering=False, debug=False,
                   detect_race_conditions=False)
    k = K(nc)
    t = k.tensors
    k.din = {}

    def dram_in(name, shape, dt):
        k.din[name] = nc.dram_tensor(name, shape, dt, kind="ExternalInput").ap()

    for r in range(NTILES):
        dram_in(f"x{r}", [TB + 1, D], F32)
        dram_in(f"iota{r}", [128, TB], F32)
    dram_in("flags", [128, 2 * NTILES], F32)
    dram_in("pmask", [128, 8], F32)
    dram_in("l0w_sh", [5 * 256, D], F32)
    dram_in("l1w_sh", [4 * 256, D], F32)
    dram_in("gu_sh", [128, GU], U8)
    dram_in("gu_sc", [128, GU], F16)
    dram_in("dn_sh", [352, D], U8)
    dram_in("dn_sc", [352, D], F16)
    dram_in("w1_sh", [256, I1], F32)
    dram_in("w2_sh", [256, I1], F32)
    dram_in("w3_sh", [704, D], F32)
    dram_in("tmix", [128, 64], F32)
    dram_in("smallp", [128, 80], F32)
    dram_in("wkv0", [64, D], F32)
    dram_in("ccol", [128, len(CONSTS)], F32)
    dram_in("msk_bf", [128, 384], BF16)
    dram_in("m_diag_i", [128, 128], I32)
    dram_in("aux_bf", [128, 3], BF16)
    dram_in("e2_bf", [2, 128], BF16)
    dram_in("ones1_bf", [1, 128], BF16)
    dram_in("ident_f32", [128, 128], F32)
    dram_in("onesf", [128, 16], F32)

    k.out_hbm = nc.dram_tensor("out", [NTILES * TB, D], F32,
                               kind="ExternalOutput").ap()
    k.dtaps = {}
    for name, shape, dt in DBG["taps"]:
        k.dtaps[name] = nc.dram_tensor("tap " + name, shape, dt,
                                       kind="ExternalOutput").ap()

    woff = {}
    off = 0
    for name in ("l0", "l1", "gul", "guh", "dnl", "dnh", "w1q", "w2q", "w3q"):
        woff[name] = off
        off += SEG_ROWS[name] // (5 if name == "l0" else 4 if name == "l1" else 1) \
            * SEG_W[name] * (5 if name == "l0" else 4 if name == "l1" else 1)
    k.woff = woff
    k.WSH = off
    k.wag_in = nc.dram_tensor("wag_in", [k.WSH], BF16)
    k.wag_out = nc.dram_tensor("wag_out", [NCORES * k.WSH], BF16)
    k.gsum_in = nc.dram_tensor("gsum_in", [1, 8], F32)
    k.gsum_out = nc.dram_tensor("gsum_out", [1, 8], F32)
    k.gtmp = nc.dram_tensor("gtmp", [128, 4], F32)
    k.sag = {}
    for r in range(NTILES):
        k.sag[f"l0i{r}"] = nc.dram_tensor(f"sag_l0i{r}", [64, H0 * K0], F32)
        k.sag[f"l0o{r}"] = nc.dram_tensor(f"sag_l0o{r}", [NCORES * 64, H0 * K0], F32)
        k.sag[f"l1i{r}"] = nc.dram_tensor(f"sag_l1i{r}", [128, H1 * K1 + 16], F32)
        k.sag[f"l1o{r}"] = nc.dram_tensor(f"sag_l1o{r}",
                                          [NCORES * 128, H1 * K1 + 16], F32)
    k.x1scr = nc.dram_tensor("x1scr", [TB, D], F32)
    k.sscr = nc.dram_tensor("sscr", [16, 128, TB], F32)
    k.xnscr = nc.dram_tensor("xnscr", [16, 128, TB + 4], BF16)

    with ExitStack() as ctx:
        for sname in ("LD", "LW", "GD", "V", "S", "T", "G", "CC", "OUT"):
            k.sems[sname] = Sem(ctx.enter_context(nc.semaphore(sname)))
        alloc_tensors(nc, k)
        for v in CONSTS:
            if v not in (0.0, 1.0):
                nc.const_aps.register(F32, v, t["ccol"][:, CIDX[v]:CIDX[v] + 1])
        block = ctx.enter_context(nc.Block())

        def run(eng):
            def f(_):
                k.new_pass(eng)
                walk(k)
            return f
        block.sync(run("sync"))
        block.gpsimd(run("gpsimd"))
        block.vector(run("vector"))
        block.scalar(run("scalar"))
        block.tensor(run("tensor"))
    return nc, k


def alloc_tensors(nc, k):
    t = k.tensors

    def P(name, shape, dt):
        t[name] = nc.alloc_sbuf_tensor("sb_" + name, shape, dt).ap()

    P("ccol", [128, len(CONSTS)], F32)
    P("tmix", [128, 64], F32)
    P("smallp", [128, 80], F32)
    P("flags", [128, 2 * NTILES], F32)
    P("pmask", [128, 8], F32)
    P("msk_bf", [128, 384], BF16)
    P("m_diag_i", [128, 128], I32)
    P("aux_bf", [128, 3], BF16)
    P("e2", [2, 128], BF16)
    P("ones1", [1, 128], BF16)
    P("ident_f32", [128, 128], F32)
    P("onesf", [128, 16], F32)
    P("iota", [128, TB], F32)
    P("etd", [128, 16], F32)
    P("netd", [128, 16], F32)
    P("bwb", [128, 16], F32)
    P("wkv0b", [64, D], BF16)
    P("s0f", [64, D], F32)
    P("s0b", [64, D], BF16)
    P("s0run", [64, D], F32)
    P("s1f", [128, H1 * K1], F32)
    P("s1b", [128, H1 * K1], BF16)
    P("s1run", [128, H1 * K1], F32)
    P("s1lg", [128, 16], F32)
    P("tcol", [128, 64], F32)
    P("trow", [1, 64], F32)
    P("trow_bf", [1, 64], BF16)
    P("ginv", [128, 4], F32)
    P("hbuf", [128, 64], F32)

    k.psum = []
    for b in range(8):
        if b in (4, 5):
            h = nc.alloc_psum_tensor(f"ps{b}", [128, 1024], BF16)
        else:
            h = nc.alloc_psum_tensor(f"ps{b}", [128, 512], F32)
        k.psum.append(h.ap())

    rem = nc.sbuf_bytes_remaining
    base = nc.SBUF_PARTITION_SIZE_BYTES - rem + 128
    base = (base + 31) // 32 * 32
    k.arena_base = base
    k.arena_size = nc.SBUF_PARTITION_SIZE_BYTES - base - 256

    def A(name, shape, dt, offset):
        nbytes = int(np.prod(shape[1:])) * mybir.dt.size(dt)
        assert offset + nbytes <= k.arena_size, (name, offset, nbytes, k.arena_size)
        t[name] = nc.alloc_sbuf_tensor_at("sb_" + name, shape, dt,
                                          offset=base + offset).ap()
        return offset + ((nbytes + 31) // 32 * 32)

    def CL(name, n, shape, dt, offset):
        o = offset
        lst = []
        for i in range(n):
            o = A(f"{name}{i}", shape, dt, o)
            lst.append(t[f"{name}{i}"])
        t[name] = lst
        return o

    R1 = 0
    R2 = CL("xn", 16, [128, TB + 4], BF16, R1)
    R3 = CL("mixy", 16, [128, TB], BF16, R2)
    o = R3
    for nm in ("rf", "kw", "ku", "rw"):
        o = CL(nm, 16, [128, TB], BF16, o)
    R4 = o
    R5 = CL("gsl", 16, [128, TB], BF16, R4)
    R6 = CL("vt", TB // 128, [128, D], BF16, R5)
    R7 = CL("wsl", 16, [128, 512], BF16, R6)
    o = R7
    for nm in ("rrh", "kwh", "kuh", "rwh"):
        o = A(nm, [64, TB], BF16, o)
    for nm in ("attsb", "q2b", "ks2b", "kst"):
        o = A(nm, [128, 128], BF16, o)
    stg_off = o
    o = A("stg", [128, TB], F32, o)
    o = A("stg2", [128, TB], F32, o)
    R8 = o
    o = A("tB", [128, TB], F32, o)
    o = A("tD", [128, 16], F32, o)
    o = A("lgpre", [128, 128], F32, o)
    arena_hi = o
    A("tC", [128, TB], F32, stg_off)
    assert arena_hi <= k.arena_size, (arena_hi, k.arena_size)

    o = 0
    for nm, shape, dt in (("wp_u8", [128, 2816], U8), ("wp_sc", [128, 2816], F16),
                          ("wp_iA", [128, I1], I32), ("wp_iB", [128, 2816], I32),
                          ("wp_iC", [128, 2816], I32), ("wp_fA", [128, I1], F32),
                          ("wp_fB", [128, 2816], F32), ("wp_fC", [128, 2816], F32),
                          ("wp_fD", [128, 2816], F32), ("wp_f32", [128, I1], F32),
                          ("wp_bf", [128, I1], BF16)):
        o = A(nm, shape, dt, o)
    o = A("wkv0f", [64, D], F32, o)
    assert o <= k.arena_size
    o = A("xtm", [128, D], F32, R5)
    A("xsq", [128, D], F32, o)
    A("xntm", [128, D], BF16, R7 + 4096 + 1024)
    CL("xnil", 16, [128, TB], BF16, R2)
    CL("hh", 44, [128, TB], BF16, R3)
    o = CL("dd", 8, [128, TB], F32, R4)
    CL("ddx", 8, [128, TB], F32, o)
    t["dd"] = t["dd"] + t["ddx"]
    CL("wsl2", 16, [128, 512], BF16, R3 + 44 * 1024)
    o = R3
    for nm in ("qt", "kd", "ks"):
        o = CL(nm, 16, [128, TB], BF16, o)
    for nm in ("kraw", "sgb", "gbv", "abuf"):
        o = A(nm, [128, TB], F32, o)
    assert o <= R4, (o, R4)
    CL("hid", 44, [128, TB], BF16, R3)
    o = CL("fo", 8, [128, TB], F32, R4)
    CL("fox", 8, [128, TB], F32, o)
    t["fo"] = t["fo"] + t["fox"]
    A("cxa", [128, D], F32, R2)
    A("cxb", [128, D], F32, R2 + 8192)
    A("csq", [128, D], F32, R3)
    A("s0in", [64, D], F32, R6)
    A("s0inb", [64, D], BF16, R6 + 8192)
    A("s1in", [128, H1 * K1], F32, R6)
    A("s1inb", [128, H1 * K1], BF16, R6 + 8192)
    CL("kwt", TB // 128, [128, D], BF16, R6)
    A("gnhl", [2, 4 * 512], BF16, R7)
    A("rmshl", [1, 2 * 512], BF16, R7)
    k.regions = dict(R1=R1, R2=R2, R3=R3, R4=R4, R5=R5, R6=R6, R7=R7, R8=R8,
                     hi=arena_hi)


def wag_seg(k, elt0, p, w):
    return k.wag_in.ap()[elt0: elt0 + p * w].rearrange("(p w) -> p w", w=w)


def wago(k, core, segoff, p, w):
    elt0 = core * k.WSH + segoff
    return k.wag_out.ap()[elt0: elt0 + p * w].rearrange("(p w) -> p w", w=w)


def load_wslab(k, dst, seg, row0, nrows, col0, width, wait=(), inc="LW"):
    rpc = SEG_ROWS[seg]
    w = SEG_W[seg]
    ev = None
    done = 0
    while done < nrows:
        row = row0 + done
        core = row // rpc
        rin = row % rpc
        take = min(nrows - done, rpc - rin)
        src = wago(k, core, k.woff[seg] + rin * w, take, w)[:, col0:col0 + width]
        ev = k.dma("sync", dst[done:done + take, 0:width], src,
                   wait=wait if done == 0 else (), inc=inc)
        done += take
    return ev



def weight_prep(k):
    nc, t, din, woff = k.nc, k.tensors, k.din, k.woff
    zout = k.ev("OUT")
    prev_cv = zout

    if "casts" in DBG["parts"]:
        for mi in range(9):
            src = din["l0w_sh"] if mi < 5 else din["l1w_sh"]
            row0 = (mi if mi < 5 else mi - 5) * 256
            segoff = (woff["l0"] + mi * 256 * D) if mi < 5 else (
                woff["l1"] + (mi - 5) * 256 * D)
            for chn in range(2):
                ld = k.dma("sync", t["wp_f32"][:, 0:D],
                           src[row0 + 128 * chn: row0 + 128 * (chn + 1), :],
                           wait=[prev_cv], inc="LW")
                cv = k.op("vector",
                          lambda e: e.tensor_copy(out=t["wp_bf"][:, 0:D],
                                                  in_=t["wp_f32"][:, 0:D]),
                          wait=[ld, k.ev("GD")])
                prev_cv = cv
                k.dma("gpsimd", wag_seg(k, segoff + 128 * chn * D, 128, D),
                      t["wp_bf"][:, 0:D], wait=[cv], inc="GD")

    if "gamma" in DBG["parts"]:
        for g, (nm, rows, cols) in enumerate(
                (("w1_sh", 256, I1), ("w2_sh", 256, I1), ("w3_sh", 704, D))):
            nch = (rows + 127) // 128
            for chn in range(nch):
                p = min(128, rows - 128 * chn)
                ld = k.dma("sync", t["wp_f32"][0:p, 0:cols],
                           din[nm][128 * chn: 128 * chn + p, :],
                           wait=[prev_cv, k.ev("V")], inc="LW")
                nseg = (cols + 511) // 512
                for sg in range(nseg):
                    s0, s1 = 512 * sg, min(cols, 512 * (sg + 1))
                    k.op("vector",
                         lambda e, p=p, s0=s0, s1=s1, sg=sg:
                         e.reduce_sum(out=t["tcol"][0:p, sg:sg + 1],
                                      in_=t["wp_f32"][0:p, s0:s1], axis=AX,
                                      apply_absolute_value=True),
                         wait=[ld])
                dst = 16 + g if chn == 0 else 24 + g
                prev_cv = k.op("vector",
                               lambda e, p=p, nseg=nseg, dst=dst:
                               e.reduce_sum(out=t["tcol"][0:p, dst:dst + 1],
                                            in_=t["tcol"][0:p, 0:nseg], axis=AX))
                if chn > 0:
                    prev_cv = k.op("vector",
                                   lambda e, p=p, g=g:
                                   e.tensor_tensor(
                                       out=t["tcol"][0:p, 16 + g:17 + g],
                                       in0=t["tcol"][0:p, 16 + g:17 + g],
                                       in1=t["tcol"][0:p, 24 + g:25 + g],
                                       op=ALU.add))
        gv = k.ev("V")
        gd = k.dma("gpsimd", k.gtmp.ap()[:, 0:3], t["tcol"][:, 16:19], wait=[gv],
                   inc="GD")
        gl = None
        for g in range(3):
            gl = k.dma("sync", t["wp_fA"][0:1, 128 * g:128 * (g + 1)],
                       k.gtmp.ap()[:, g:g + 1], wait=[gd], inc="LW",
                       allow_slow_non_contiguous=True)
        for g in range(3):
            k.op("vector",
                 lambda e, g=g: e.reduce_sum(
                     out=t["trow"][0:1, g:g + 1],
                     in_=t["wp_fA"][0:1, 128 * g:128 * (g + 1)], axis=AX),
                 wait=[gl])
        gv2 = k.ev("V")
        gd2 = k.dma("gpsimd", k.gsum_in.ap()[0:1, 0:8], t["trow"][0:1, 0:8],
                    wait=[gv2], inc="GD")
        ccr = k.op("gpsimd",
                   lambda e: e.collective_compute(
                       "AllReduce", ALU.add, replica_groups=[list(range(NCORES))],
                       ins=[k.gsum_in.ap().opt()], outs=[k.gsum_out.ap().opt()]),
                   wait=[gd2], inc="CC")
        gl2 = k.dma("sync", t["trow"][0:1, 8:11], k.gsum_out.ap()[0:1, 0:3],
                    wait=[ccr], inc="LW")
        k.op("vector",
             lambda e: e.tensor_scalar(out=t["trow"][0:1, 12:15],
                                       in0=t["trow"][0:1, 8:11],
                                       scalar1=1.0 / (I1 * D), scalar2=None,
                                       op0=ALU.mult),
             wait=[gl2])
        k.op("vector", lambda e: e.tensor_copy(out=t["trow_bf"][0:1, 0:3],
                                               in_=t["trow"][0:1, 12:15]))
        k.op("vector", lambda e: e.tensor_copy(out=t["trow"][0:1, 16:19],
                                               in_=t["trow_bf"][0:1, 0:3]))
        k.op("vector", lambda e: e.tensor_tensor(out=t["trow"][0:1, 20:23],
                                                 in0=t["trow"][0:1, 12:15],
                                                 in1=t["trow"][0:1, 16:19],
                                                 op=ALU.subtract))
        hv = k.op("vector", lambda e: e.tensor_copy(out=t["trow_bf"][0:1, 8:11],
                                                    in_=t["trow"][0:1, 20:23]))
        k.mm(k.psum[6][:, 0:3], t["ones1"][0:1, :], t["trow_bf"][0:1, 0:3],
             True, False, wait=[hv, k.ld_small])
        mm2 = k.mm(k.psum[6][:, 0:3], t["ones1"][0:1, :], t["trow_bf"][0:1, 8:11],
                   False, True, inc="T")
        ad = k.op("vector",
                  lambda e: e.tensor_scalar(out=t["ginv"][:, 0:3],
                                            in0=k.psum[6][:, 0:3],
                                            scalar1=1e-8, scalar2=None,
                                            op0=ALU.add),
                  wait=[mm2])
        k.ginv_ev = k.op("vector",
                         lambda e: e.reciprocal(out=t["ginv"][:, 0:3],
                                                in_=t["ginv"][:, 0:3]),
                         wait=[ad])

    if "tern" in DBG["parts"]:
        gi = k.ginv_ev
        for g, (nm, rows, cols, seg) in enumerate(
                (("w1_sh", 256, I1, "w1q"), ("w2_sh", 256, I1, "w2q"),
                 ("w3_sh", 704, D, "w3q"))):
            nch = (rows + 127) // 128
            for chn in range(nch):
                p = min(128, rows - 128 * chn)
                ld = k.dma("sync", t["wp_f32"][0:p, 0:cols],
                           din[nm][128 * chn: 128 * chn + p, :],
                           wait=[k.ev("S"), k.ev("V")], inc="LW")
                s1 = k.op("scalar",
                          lambda e, p=p, cols=cols, g=g:
                          e.mul(out=t["wp_fA"][0:p, 0:cols],
                                in_=t["wp_f32"][0:p, 0:cols],
                                mul=t["ginv"][0:p, g:g + 1]),
                          wait=[ld, gi])
                k.op("vector",
                     lambda e, p=p, cols=cols:
                     e.tensor_scalar(out=t["wp_f32"][0:p, 0:cols],
                                     in0=t["wp_fA"][0:p, 0:cols],
                                     scalar1=0.5, scalar2=None,
                                     op0=ALU.is_gt),
                     wait=[s1, k.ev("GD")])
                k.op("vector",
                     lambda e, p=p, cols=cols:
                     e.tensor_scalar(out=t["wp_fA"][0:p, 0:cols],
                                     in0=t["wp_fA"][0:p, 0:cols],
                                     scalar1=-0.5, scalar2=None,
                                     op0=ALU.is_lt))
                r3 = k.op("vector",
                          lambda e, p=p, cols=cols:
                          e.tensor_tensor(out=t["wp_bf"][0:p, 0:cols],
                                          in0=t["wp_f32"][0:p, 0:cols],
                                          in1=t["wp_fA"][0:p, 0:cols],
                                          op=ALU.subtract))
                k.dma("gpsimd",
                      wag_seg(k, woff[seg] + 128 * chn * cols, p, cols),
                      t["wp_bf"][0:p, 0:cols], wait=[r3], inc="GD")

    if "fp4" in DBG["parts"]:
        fp4_decode(k, "gu_sh", "gu_sc", 128, GU, 2816, ("gul", "guh"))
        fp4_decode(k, "dn_sh", "dn_sc", 352, D, 2048, ("dnl", "dnh"))

    aw = k.ev("GD")
    k.wag_done = k.op("gpsimd",
                      lambda e: e.collective_compute(
                          "AllGather", ALU.bypass,
                          replica_groups=[list(range(NCORES))],
                          ins=[k.wag_in.ap().opt()],
                          outs=[k.wag_out.ap().opt()]),
                      wait=[aw], inc="CC")


def fp4_decode(k, bname, scname, rows, cols, W, segs):
    t, din, woff = k.tensors, k.din, k.woff
    nch = (rows + 127) // 128
    for chn in range(nch):
        p = min(128, rows - 128 * chn)
        for c0 in range(0, cols, W):
            w = min(W, cols - c0)
            ldb = k.dma("sync", t["wp_u8"][0:p, 0:w],
                        din[bname][128 * chn: 128 * chn + p, c0:c0 + w],
                        wait=[k.ev("V"), k.ev("S")], inc="LW")
            lds = k.dma("sync", t["wp_sc"][0:p, 0:w],
                        din[scname][128 * chn: 128 * chn + p, c0:c0 + w], inc="LW")
            k.op("vector",
                 lambda e, p=p, w=w: e.tensor_copy(
                     out=t["wp_iA"][0:p, 0:w], in_=t["wp_u8"][0:p, 0:w]),
                 wait=[ldb, lds])
            for par, seg in enumerate(segs):
                if par == 0:
                    k.op("vector",
                         lambda e, p=p, w=w: e.tensor_scalar(
                             out=t["wp_iB"][0:p, 0:w], in0=t["wp_iA"][0:p, 0:w],
                             scalar1=15, scalar2=None, op0=ALU.bitwise_and))
                else:
                    k.op("vector",
                         lambda e, p=p, w=w: e.tensor_scalar(
                             out=t["wp_iB"][0:p, 0:w], in0=t["wp_iA"][0:p, 0:w],
                             scalar1=4, scalar2=None,
                             op0=ALU.logical_shift_right))
                k.op("vector", lambda e, p=p, w=w: e.tensor_scalar(
                    out=t["wp_iC"][0:p, 0:w], in0=t["wp_iB"][0:p, 0:w],
                    scalar1=1, scalar2=3, op0=ALU.logical_shift_right,
                    op1=ALU.bitwise_and))
                ev_ = k.op("vector", lambda e, p=p, w=w: e.tensor_copy(
                    out=t["wp_fA"][0:p, 0:w], in_=t["wp_iC"][0:p, 0:w]))
                t2 = k.op("scalar", lambda e, p=p, w=w: e.activation(
                    out=t["wp_fB"][0:p, 0:w], in_=t["wp_fA"][0:p, 0:w],
                    func=AF.Exp, scale=LN2, bias=-2.0 * LN2),
                    wait=[ev_])
                k.op("vector", lambda e, p=p, w=w: e.tensor_scalar(
                    out=t["wp_iC"][0:p, 0:w], in0=t["wp_iB"][0:p, 0:w],
                    scalar1=1, scalar2=None, op0=ALU.bitwise_and))
                k.op("vector", lambda e, p=p, w=w: e.tensor_copy(
                    out=t["wp_fC"][0:p, 0:w], in_=t["wp_iC"][0:p, 0:w]))
                k.op("vector", lambda e, p=p, w=w: e.scalar_tensor_tensor(
                    out=t["wp_fD"][0:p, 0:w], in0=t["wp_fC"][0:p, 0:w],
                    scalar=2.0, in1=t["wp_fB"][0:p, 0:w],
                    op0=ALU.add, op1=ALU.mult), wait=[t2])
                k.op("vector", lambda e, p=p, w=w: e.tensor_scalar(
                    out=t["wp_fA"][0:p, 0:w], in0=t["wp_fA"][0:p, 0:w],
                    scalar1=8.0, scalar2=None, op0=ALU.mult))
                k.op("vector", lambda e, p=p, w=w: e.scalar_tensor_tensor(
                    out=t["wp_fA"][0:p, 0:w], in0=t["wp_fC"][0:p, 0:w],
                    scalar=0.5, in1=t["wp_fA"][0:p, 0:w],
                    op0=ALU.mult, op1=ALU.add))
                k.op("vector", lambda e, p=p, w=w: e.tensor_tensor(
                    out=t["wp_fD"][0:p, 0:w], in0=t["wp_fD"][0:p, 0:w],
                    in1=t["wp_fA"][0:p, 0:w], op=ALU.min))
                k.op("vector", lambda e, p=p, w=w: e.tensor_scalar(
                    out=t["wp_iC"][0:p, 0:w], in0=t["wp_iB"][0:p, 0:w],
                    scalar1=3, scalar2=None, op0=ALU.logical_shift_right))
                k.op("vector", lambda e, p=p, w=w: e.tensor_copy(
                    out=t["wp_fC"][0:p, 0:w], in_=t["wp_iC"][0:p, 0:w]))
                k.op("vector", lambda e, p=p, w=w: e.tensor_scalar(
                    out=t["wp_fC"][0:p, 0:w], in0=t["wp_fC"][0:p, 0:w],
                    scalar1=-2.0, scalar2=1.0, op0=ALU.mult, op1=ALU.add))
                k.op("vector", lambda e, p=p, w=w: e.tensor_tensor(
                    out=t["wp_fD"][0:p, 0:w], in0=t["wp_fD"][0:p, 0:w],
                    in1=t["wp_fC"][0:p, 0:w], op=ALU.mult))
                sv = k.op("vector", lambda e, p=p, w=w: e.tensor_tensor(
                    out=t["wp_bf"][0:p, 0:w], in0=t["wp_fD"][0:p, 0:w],
                    in1=t["wp_sc"][0:p, 0:w], op=ALU.mult),
                    wait=[k.ev("GD")])
                k.dma("gpsimd",
                      wag_seg(k, woff[seg] + (128 * chn) * cols, p,
                              cols)[:, c0:c0 + w],
                      t["wp_bf"][0:p, 0:w], wait=[sv], inc="GD")



def walk(k):
    t, din = k.tensors, k.din

    for name, src in [("ccol", "ccol"), ("tmix", "tmix"), ("smallp", "smallp"),
                      ("flags", "flags"), ("pmask", "pmask"),
                      ("msk_bf", "msk_bf"), ("m_diag_i", "m_diag_i"),
                      ("aux_bf", "aux_bf"), ("e2", "e2_bf"),
                      ("ones1", "ones1_bf"), ("ident_f32", "ident_f32"),
                      ("onesf", "onesf"), ("wkv0f", "wkv0")]:
        k.dma("sync", t[name][:, :], din[src][:, :])
    k.ld_small = k.ev("LD")

    for b in range(8):
        k.op("scalar", lambda e, b=b: e.memzero(k.psum[b][:, :]))
    k.op("scalar", lambda e: e.activation(out=t["etd"][:, :],
                                          in_=t["smallp"][:, 0:16], func=AF.Exp),
         wait=[k.ld_small])
    se = k.ev("S")
    k.op("vector", lambda e: e.tensor_scalar(out=t["netd"][:, :],
                                             in0=t["etd"][:, :], scalar1=-1.0,
                                             scalar2=None, op0=ALU.mult),
         wait=[se])
    k.op("vector", lambda e: e.tensor_scalar(out=t["bwb"][:, :],
                                             in0=t["etd"][:, :], scalar1=-16383.0,
                                             scalar2=None, op0=ALU.mult))
    k.op("vector", lambda e: e.tensor_copy(out=t["wkv0b"][:, :],
                                           in_=t["wkv0f"][:, :]),
         wait=[k.ld_small])
    k.op("vector", lambda e: e.memzero(t["s0run"][:, :]))
    k.op("vector", lambda e: e.memzero(t["s1run"][:, :]))
    k.op("vector", lambda e: e.memzero(t["trow"][0:1, 0:16]))

    weight_prep(k)

    for r in range(DBG["ntiles"]):
        tile(k, r)

    emit_taps(k)

    fin = k.ev("OUT")
    k.op("gpsimd", lambda e: e.nop(), wait=[fin, k.wag_done], inc="G")


def emit_taps(k):
    t = k.tensors
    allq = [k.ev("V"), k.ev("S"), k.ev("T"), k.ev("G"), k.wag_done,
            k.ev("GD"), k.ev("LD"), k.ev("LW"), k.ev("CC")]
    for name, ap in k.dtaps.items():
        parts = name.split(":")
        if parts[0] == "sbuf":
            _, tname, p, w = parts
            p, w = int(p), int(w)
            k.dma("gpsimd", ap[:, :], t[tname][0:p, 0:w], wait=allq, inc="OUT")
        elif parts[0] == "psum":
            _, idx, p, w = parts
            idx, p, w = int(idx), int(p), int(w)
            cpe = k.op("vector", lambda e, idx=idx, p=p, w=w: e.tensor_copy(
                out=t["xtm"][0:p, 0:w], in_=k.psum[idx][0:p, 0:w]),
                wait=allq + [k.ev("OUT")])
            k.dma("gpsimd", ap[:, :], t["xtm"][0:p, 0:w], wait=[cpe],
                  inc="OUT")
        elif parts[0] == "wag":
            _, seg, core, row0, p, w = parts
            core, row0, p, w = int(core), int(row0), int(p), int(w)
            src = wago(k, core, k.woff[seg] + row0 * w, p, w)
            k.dma("gpsimd", ap[:, :], src, wait=[k.wag_done], inc="OUT")
        elif parts[0] == "hbm":
            _, tn, row0, p, w = parts
            row0, p, w = int(row0), int(p), int(w)
            src = getattr(k, tn)
            if callable(getattr(src, "ap", None)):
                src = src.ap()
            src = src.rearrange("a b c -> (a b) c") if len(src.shape) == 3 else src
            k.dma("gpsimd", ap[:, :], src[row0:row0 + p, 0:w], wait=allq,
                  inc="OUT")





def qslab(k, dst, seg, row0, col0, width, nrows=128, wait=(), inc="LW"):
    rpc = SEG_ROWS[seg]
    w = SEG_W[seg]
    ev = None
    done = 0
    while done < nrows:
        row = row0 + done
        core, rin = divmod(row, rpc)
        take = min(nrows - done, rpc - rin)
        src = wago(k, core, k.woff[seg] + rin * w, take, w)
        ev = k.dma("sync", dst[done:done + take, 0:width],
                   src[:, col0:col0 + width],
                   wait=wait if done == 0 else (), inc=inc)
        done += take
    return ev


def attn_slab(k, dst, layer, mi, kk, col0, width, wait=(), inc="LW"):
    row = 128 * kk
    core, rin = divmod(row, 256)
    src = wago(k, core, k.woff[layer] + (mi * 256 + rin) * D, 128, D)
    return k.dma("sync", dst[:, 0:width], src[:, col0:col0 + width],
                 wait=wait, inc=inc)


def barrier(k):
    evs = [k.ev(s) for s in ("V", "S", "T", "G", "LD", "LW", "GD", "OUT", "CC")]
    for eng in ("sync", "gpsimd", "vector", "scalar", "tensor"):
        k.op(eng, lambda e: e.nop(), wait=evs, ss=False)
    return evs


def proj2048(k, layer, mi, rhs_chunks, out_cb, rhs_ev=None, n_out=16):
    t = k.tensors
    for mg in range((n_out + 3) // 4):
        evs = []
        for kk in range(16):
            w8 = [k.slab_free] if kk == 0 else ()
            evs.append(attn_slab(k, t["wsl"][kk], layer, mi, kk, 512 * mg, 512,
                                 wait=w8))
        evall = evs[-1]
        last = None
        for m in range(4):
            oc = 4 * mg + m
            if oc >= n_out:
                break
            bank = k.psum[m]
            for kk in range(16):
                w = [evall] if kk == 0 else []
                if kk == 0:
                    w.append(k.bank_ev[m])
                    if rhs_ev is not None:
                        w.append(rhs_ev)
                last = k.mm(bank[:, 0:TB], t["wsl"][kk][:, 128 * m:128 * m + 128],
                            rhs_chunks[kk][:, 0:TB],
                            start=(kk == 0), stop=(kk == 15),
                            wait=w, inc="T" if kk == 15 else None)
            out_cb(oc, bank, last)
        if last is not None:
            k.slab_free = last


def ln_row_chunk(k, r, tc, src_rows, dst_writer):
    t, din = k.tensors, k.din
    ld = k.dma("sync", t["xtm"][:, :], din[f"x{r}"][src_rows:src_rows + 128, :],
               wait=[k.ev("S"), k.ev("V")])
    k.op("scalar", lambda e: e.activation(out=t["xsq"][:, :], in_=t["xtm"][:, :],
                                          func=AF.Square), wait=[ld])
    sq = k.ev("S")
    k.op("vector", lambda e: e.reduce_sum(out=t["hbuf"][:, 0:1], in_=t["xtm"][:, :],
                                          axis=AX), wait=[ld])
    k.op("vector", lambda e: e.reduce_sum(out=t["hbuf"][:, 1:2], in_=t["xsq"][:, :],
                                          axis=AX), wait=[sq])
    k.op("vector", lambda e: e.tensor_scalar(out=t["hbuf"][:, 2:3],
                                             in0=t["hbuf"][:, 0:1],
                                             scalar1=1.0 / 2048.0, scalar2=None,
                                             op0=ALU.mult))
    k.op("vector", lambda e: e.tensor_scalar(out=t["hbuf"][:, 3:4],
                                             in0=t["hbuf"][:, 1:2],
                                             scalar1=1.0 / 2048.0, scalar2=None,
                                             op0=ALU.mult))
    k.op("vector", lambda e: e.tensor_tensor(out=t["hbuf"][:, 4:5],
                                             in0=t["hbuf"][:, 2:3],
                                             in1=t["hbuf"][:, 2:3], op=ALU.mult))
    vv = k.op("vector", lambda e: e.tensor_tensor(out=t["hbuf"][:, 4:5],
                                                  in0=t["hbuf"][:, 3:4],
                                                  in1=t["hbuf"][:, 4:5],
                                                  op=ALU.subtract))
    k.op("scalar", lambda e: e.activation(out=t["hbuf"][:, 5:6],
                                          in_=t["hbuf"][:, 4:5], func=AF.Ln,
                                          bias=1e-5), wait=[vv])
    k.op("scalar", lambda e: e.activation(out=t["hbuf"][:, 5:6],
                                          in_=t["hbuf"][:, 5:6], func=AF.Exp,
                                          scale=-0.5))
    sr = k.ev("S")
    k.op("vector", lambda e: e.tensor_tensor(out=t["hbuf"][:, 6:7],
                                             in0=t["hbuf"][:, 2:3],
                                             in1=t["hbuf"][:, 5:6], op=ALU.mult),
         wait=[sr])
    nb = k.op("vector", lambda e: e.tensor_scalar(out=t["hbuf"][:, 6:7],
                                                  in0=t["hbuf"][:, 6:7],
                                                  scalar1=-1.0, scalar2=None,
                                                  op0=ALU.mult))
    xe = k.op("scalar", lambda e: e.activation(out=t["xntm"][:, :],
                                               in_=t["xtm"][:, :],
                                               func=AF.Identity,
                                               scale=t["hbuf"][:, 5:6],
                                               bias=t["hbuf"][:, 6:7]),
              wait=[nb, k.ev("T")])
    dst_writer(tc, xe)


def transpose_to_fm(k, src_ev, src_ap_fn, dst_fn):
    t = k.tensors
    for half in range(2):
        bank = k.psum[4 + half]
        tev = None
        for j in range(8):
            fc = 8 * half + j
            w = [k.tp_free[half]] if j == 0 else []
            if j == 0 and src_ev is not None:
                w.append(src_ev)
            tev = k.tr(bank[:, 128 * j:128 * j + 128],
                       src_ap_fn(fc), t["msk_bf"][:, 0:128],
                       wait=w, inc="T" if j == 7 else None)
        ev = None
        for j in range(8):
            fc = 8 * half + j
            ev = dst_fn(fc, bank[:, 128 * j:128 * j + 128],
                        [tev] if j == 0 else ())
        k.tp_free[half] = ev





def l0_attn(k, r, xn_ev):
    t = k.tensors
    cut = DBG.get("l0cut", 99)
    if cut <= 0:
        return

    def build_mix(mi, wait0):
        ev = None
        for fc in range(16):
            k.op("vector", lambda e, fc=fc: e.tensor_tensor(
                out=t["tB"][:, 0:TB], in0=t["xn"][fc][:, 1:1 + TB],
                in1=t["xn"][fc][:, 0:TB], op=ALU.subtract),
                wait=list(wait0) if fc == 0 else ())
            ev = k.op("vector", lambda e, fc=fc, mi=mi: e.scalar_tensor_tensor(
                out=t["mixy"][fc][:, 0:TB], in0=t["tB"][:, 0:TB],
                scalar=t["tmix"][:, 16 * mi + fc:16 * mi + fc + 1],
                in1=t["xn"][fc][:, 0:TB], op0=ALU.mult, op1=ALU.add))
        return ev

    iota_ev = k.iota_ev
    mev = build_mix(0, [xn_ev])
    if cut <= 0.2:
        return
    prev_read = [k.ev("V")]

    def cb_r(oc, bank, mmev):
        k.op("vector", lambda e, oc=oc, bank=bank: e.tensor_copy(
            out=t["rf"][oc][:, 0:TB], in_=bank[:, 0:TB]), wait=[mmev])
        wf = k.op("scalar", lambda e, oc=oc: e.activation(
            out=t["tC"][:, 0:TB], in_=t["iota"][:, 0:TB], func=AF.Exp,
            scale=t["netd"][:, oc:oc + 1]),
            wait=[iota_ev, prev_read[0]])
        ev = k.op("vector", lambda e, bank=bank, oc=oc: e.tensor_tensor(
            out=t["rw"][oc][:, 0:TB], in0=bank[:, 0:TB],
            in1=t["tC"][:, 0:TB], op=ALU.mult), wait=[wf])
        prev_read[0] = ev
        k.bank_ev[oc % 4] = ev
    proj2048(k, "l0", 0, t["mixy"], cb_r, rhs_ev=mev)
    if cut <= 0.4:
        return

    mev = build_mix(1, [k.ev("T")])
    if cut <= 0.5:
        return
    kvar = DBG.get("kvar", 0)

    def cb_k(oc, bank, mmev):
        if kvar == 1:
            ev = k.op("vector", lambda e, oc=oc, bank=bank: e.tensor_copy(
                out=t["kw"][oc][:, 0:TB], in_=bank[:, 0:TB]), wait=[mmev])
            k.bank_ev[oc % 4] = ev
            return
        if kvar == 3:
            k.op("vector", lambda e, oc=oc, bank=bank: e.tensor_copy(
                out=t["kw"][oc][:, 0:TB], in_=bank[:, 0:TB]), wait=[mmev])
            ev = k.op("scalar", lambda e, bank=bank, oc=oc: e.mul(
                out=t["ku"][oc][:, 0:TB], in_=bank[:, 0:TB],
                mul=t["smallp"][:, 16 + oc:17 + oc]),
                wait=[mmev])
            k.bank_ev[oc % 4] = ev
            return
        wb = k.op("scalar", lambda e, oc=oc: e.activation(
            out=t["tC"][:, 0:TB], in_=t["iota"][:, 0:TB], func=AF.Exp,
            scale=t["etd"][:, oc:oc + 1], bias=t["bwb"][:, oc:oc + 1]),
            wait=[prev_read[0]])
        tte = k.op("vector", lambda e, bank=bank, oc=oc: e.tensor_tensor(
            out=t["kw"][oc][:, 0:TB], in0=bank[:, 0:TB],
            in1=t["tC"][:, 0:TB], op=ALU.mult), wait=[wb, mmev])
        prev_read[0] = tte
        if kvar == 2:
            k.bank_ev[oc % 4] = tte
            return
        ev = k.op("scalar", lambda e, bank=bank, oc=oc: e.mul(
            out=t["ku"][oc][:, 0:TB], in_=bank[:, 0:TB],
            mul=t["smallp"][:, 16 + oc:17 + oc]),
            wait=[mmev, tte])
        k.bank_ev[oc % 4] = ev
    proj2048(k, "l0", 1, t["mixy"], cb_k, rhs_ev=mev)
    if cut <= 0.6:
        return

    mev = build_mix(2, [k.ev("T")])
    for mg in range(4):
        evs = []
        for kk in range(16):
            w8 = [k.slab_free] if kk == 0 else ()
            evs.append(attn_slab(k, t["wsl"][kk], "l0", 2, kk, 512 * mg, 512,
                                 wait=w8))
        evall = evs[-1]
        last = None
        for tc in range(TB // 128):
            bank = k.psum[tc]
            for kk in range(16):
                w = [evall] if kk == 0 else []
                if kk == 0:
                    w.append(k.bank_ev[tc])
                    w.append(mev)
                last = k.mm(bank[:, 0:512],
                            t["mixy"][kk][:, 128 * tc:128 * tc + 128],
                            t["wsl"][kk][:, 0:512],
                            start=(kk == 0), stop=(kk == 15), wait=w,
                            inc="T" if kk == 15 else None)
            ev = k.op("vector", lambda e, tc=tc, mg=mg, bank=bank: e.tensor_copy(
                out=t["vt"][tc][:, 512 * mg:512 * mg + 512], in_=bank[:, 0:512]),
                wait=[last])
            k.bank_ev[tc] = ev
        k.slab_free = last
    if cut <= 0.8:
        return

    mev = build_mix(3, [k.ev("T")])

    def cb_g(oc, bank, mmev):
        ev = k.op("scalar", lambda e, oc=oc, bank=bank: e.activation(
            out=t["gsl"][oc][:, 0:TB], in_=bank[:, 0:TB], func=AF.Silu),
            wait=[mmev])
        k.bank_ev[oc % 4] = ev
    proj2048(k, "l0", 3, t["mixy"], cb_g, rhs_ev=mev)

    scan_t0 = k.ev("T")
    for tc in range(TB // 128):
        def dst(fc, psrc, w, tc=tc):
            return k.op("vector",
                        lambda e, fc=fc: e.tensor_copy(
                            out=t["kwt"][tc][:, 128 * fc:128 * fc + 128],
                            in_=psrc), wait=w)
        transpose_to_fm(k, scan_t0 if tc == 0 else None,
                        lambda fc, tc=tc: t["kw"][fc][:, 128 * tc:128 * tc + 128],
                        dst)

    k.op("vector", lambda e: e.memzero(t["s0f"][:, :]))
    z0 = k.op("vector", lambda e: e.memzero(t["s0b"][:, :]))
    if DBG.get("l0cut", 99) <= 1:
        return

    NCH = TB // 128
    att_free = k.ev("V")
    ob_free = [k.ev("V"), k.ev("V")]
    kv_free = k.ev("V")
    head_bufs_free = k.ev("T")
    k.attsb_free = k.ev("T")
    s0b_ev = z0
    for h in range(H0):
        hc, hp = h // 2, h % 2
        k.op("vector", lambda e, hc=hc, hp=hp: e.tensor_copy(
            out=t["rrh"][:, 0:TB], in_=t["rf"][hc][64 * hp:64 * hp + 64, 0:TB]),
            wait=[head_bufs_free])
        k.op("vector", lambda e, hc=hc, hp=hp: e.tensor_copy(
            out=t["kwh"][:, 0:TB], in_=t["kw"][hc][64 * hp:64 * hp + 64, 0:TB]))
        k.op("vector", lambda e, hc=hc, hp=hp: e.tensor_copy(
            out=t["kuh"][:, 0:TB], in_=t["ku"][hc][64 * hp:64 * hp + 64, 0:TB]))
        cpe = k.op("vector", lambda e, hc=hc, hp=hp: e.tensor_copy(
            out=t["rwh"][:, 0:TB], in_=t["rw"][hc][64 * hp:64 * hp + 64, 0:TB]))
        ob = k.psum[h % 2]
        olast = None
        for c in range(NCH):
            sl = slice(128 * c, 128 * c + 128)
            k.mm(k.psum[6][:, 0:128], t["kwh"][:, sl], t["rrh"][:, sl],
                 True, True, wait=[cpe, att_free])
            a2 = k.mm(k.psum[6][:, 128:256], t["kuh"][:, sl], t["rrh"][:, sl],
                      True, True, inc="T")
            k.op("vector", lambda e: e.tensor_tensor(
                out=t["attsb"][:, :], in0=k.psum[6][:, 0:128],
                in1=t["msk_bf"][:, 128:256], op=ALU.mult),
                wait=[a2, k.attsb_free])
            m2 = k.op("vector", lambda e: e.copy_predicated(
                out=t["attsb"][:, :], mask=t["m_diag_i"][:, :],
                data=k.psum[6][:, 128:256]))
            att_free = m2
            w1 = [m2]
            if c == 0:
                w1.append(ob_free[h % 2])
            k.attsb_free = k.mm(ob[0:64, sl], t["vt"][c][:, 64 * h:64 * h + 64],
                                t["attsb"][:, :], True, False, wait=w1, inc="T")
            k.mm(ob[0:64, sl], t["s0b"][:, 64 * h:64 * h + 64], t["rrh"][:, sl],
                 False, False, wait=[s0b_ev])
            olast = k.mm(ob[0:64, sl], t["wkv0b"][:, 64 * h:64 * h + 64],
                         t["rwh"][:, sl], False, True,
                         inc="T" if c == NCH - 1 else None)
            kv = k.mm(k.psum[7][0:64, 0:64], t["kwt"][c][:, 64 * h:64 * h + 64],
                      t["vt"][c][:, 64 * h:64 * h + 64], True, True,
                      wait=[kv_free], inc="T")
            k.op("vector", lambda e, h=h: e.tensor_tensor(
                out=t["s0f"][:, 64 * h:64 * h + 64],
                in0=t["s0f"][:, 64 * h:64 * h + 64],
                in1=k.psum[7][0:64, 0:64], op=ALU.add), wait=[kv])
            s0b_ev = k.op("vector", lambda e, h=h: e.tensor_copy(
                out=t["s0b"][:, 64 * h:64 * h + 64],
                in_=t["s0f"][:, 64 * h:64 * h + 64]))
            kv_free = s0b_ev
        yv = k.op("vector", lambda e, hc=hc, hp=hp, ob=ob: e.tensor_copy(
            out=t["mixy"][hc][64 * hp:64 * hp + 64, 0:TB], in_=ob[0:64, 0:TB]),
            wait=[olast])
        ob_free[h % 2] = yv
        head_bufs_free = olast

    if DBG.get("l0cut", 99) <= 2:
        return
    sx = k.ev("V")
    d1 = k.dma("gpsimd", k.sag[f"l0i{r}"].ap()[:, :], t["s0f"][:, :],
               wait=[sx], inc="GD")
    ag = k.op("gpsimd",
              lambda e: e.collective_compute(
                  "AllGather", ALU.bypass, replica_groups=[list(range(NCORES))],
                  ins=[k.sag[f"l0i{r}"].ap().opt()],
                  outs=[k.sag[f"l0o{r}"].ap().opt()]),
              wait=[d1], inc="CC")
    k.op("vector", lambda e: e.tensor_copy(out=t["s0in"][:, :],
                                           in_=t["s0run"][:, :]),
         wait=[k.ev("T")])
    for j in range(NCORES):
        ld = k.dma("sync", t["xtm"][0:64, 0:D],
                   k.sag[f"l0o{r}"].ap()[64 * j:64 * j + 64, :],
                   wait=[ag, k.ev("V")])
        k.op("vector", lambda e: e.tensor_tensor(
            out=t["s0run"][:, :], in0=t["s0run"][:, :], in1=t["xtm"][0:64, 0:D],
            op=ALU.add), wait=[ld])
        k.op("vector", lambda e, j=j: e.scalar_tensor_tensor(
            out=t["s0in"][:, :], in0=t["xtm"][0:64, 0:D],
            scalar=t["pmask"][0:64, j:j + 1], in1=t["s0in"][:, :],
            op0=ALU.mult, op1=ALU.add))
    sbi = k.op("vector", lambda e: e.tensor_copy(out=t["s0inb"][:, :],
                                                 in_=t["s0in"][:, :]))
    corr_free = [k.ev("V"), k.ev("V")]
    for h in range(H0):
        hc, hp = h // 2, h % 2
        cpe = k.op("vector", lambda e, hc=hc, hp=hp: e.tensor_copy(
            out=t["rrh"][:, 0:TB], in_=t["rf"][hc][64 * hp:64 * hp + 64, 0:TB]),
            wait=[k.ev("T")])
        cm = k.mm(k.psum[2 + h % 2][0:64, 0:TB], t["s0inb"][:, 64 * h:64 * h + 64],
                  t["rrh"][:, 0:TB], True, True,
                  wait=[sbi, cpe, corr_free[h % 2]], inc="T")
        ya = k.op("vector", lambda e, hc=hc, hp=hp, h=h: e.tensor_tensor(
            out=t["mixy"][hc][64 * hp:64 * hp + 64, 0:TB],
            in0=t["mixy"][hc][64 * hp:64 * hp + 64, 0:TB],
            in1=k.psum[2 + h % 2][0:64, 0:TB], op=ALU.add), wait=[cm])
        corr_free[h % 2] = ya

    if DBG.get("l0cut", 99) <= 3:
        return
    st_free = [k.ev("V"), k.ev("V")]
    bc_free = [k.ev("V"), k.ev("V")]
    ya_prev = k.ev("S")
    for hc in range(16):
        y2 = k.op("vector", lambda e, hc=hc: e.tensor_tensor(
            out=t["xntm"][:, 0:TB], in0=t["mixy"][hc][:, 0:TB],
            in1=t["mixy"][hc][:, 0:TB], op=ALU.mult),
            wait=[k.ev("T"), ya_prev])
        s1 = k.mm(k.psum[6][0:2, 0:TB], t["aux_bf"][:, 0:2],
                  t["mixy"][hc][:, 0:TB], True, True,
                  wait=[k.ev("V"), st_free[0]], inc="T")
        s2 = k.mm(k.psum[7][0:2, 0:TB], t["aux_bf"][:, 0:2],
                  t["xntm"][:, 0:TB], True, True,
                  wait=[y2, st_free[1]], inc="T")
        k.op("vector", lambda e: e.tensor_scalar(
            out=t["tB"][0:2, 0:TB], in0=k.psum[6][0:2, 0:TB],
            scalar1=1.0 / 64.0, scalar2=None, op0=ALU.mult), wait=[s1])
        st_free[0] = k.ev("V")
        k.op("vector", lambda e: e.tensor_scalar(
            out=t["tC"][0:2, 0:TB], in0=k.psum[7][0:2, 0:TB],
            scalar1=1.0 / 64.0, scalar2=None, op0=ALU.mult),
            wait=[s2, ya_prev])
        st_free[1] = k.ev("V")
        k.op("vector", lambda e: e.tensor_tensor(
            out=t["stg2"][0:2, 0:TB], in0=t["tB"][0:2, 0:TB],
            in1=t["tB"][0:2, 0:TB], op=ALU.mult))
        vv = k.op("vector", lambda e: e.tensor_tensor(
            out=t["stg2"][0:2, 0:TB], in0=t["tC"][0:2, 0:TB],
            in1=t["stg2"][0:2, 0:TB], op=ALU.subtract))
        k.op("scalar", lambda e: e.activation(
            out=t["stg2"][0:2, 0:TB], in_=t["stg2"][0:2, 0:TB], func=AF.Ln,
            bias=0.00064), wait=[vv])
        rse = k.op("scalar", lambda e: e.activation(
            out=t["stg2"][0:2, 0:TB], in_=t["stg2"][0:2, 0:TB], func=AF.Exp,
            scale=-0.5))
        k.op("vector", lambda e: e.tensor_tensor(
            out=t["tB"][0:2, 0:TB], in0=t["tB"][0:2, 0:TB],
            in1=t["stg2"][0:2, 0:TB], op=ALU.mult), wait=[rse])
        k.op("vector", lambda e: e.tensor_scalar(
            out=t["tB"][0:2, 0:TB], in0=t["tB"][0:2, 0:TB],
            scalar1=-1.0, scalar2=None, op0=ALU.mult))
        k.op("vector", lambda e: e.tensor_copy(out=t["gnhl"][:, 0:TB],
                                               in_=t["stg2"][0:2, 0:TB]))
        k.op("vector", lambda e: e.tensor_copy(out=t["tC"][0:2, 0:TB],
                                               in_=t["gnhl"][:, 0:TB]))
        k.op("vector", lambda e: e.tensor_tensor(
            out=t["tC"][0:2, 0:TB], in0=t["stg2"][0:2, 0:TB],
            in1=t["tC"][0:2, 0:TB], op=ALU.subtract))
        k.op("vector", lambda e: e.tensor_copy(out=t["gnhl"][:, TB:2 * TB],
                                               in_=t["tC"][0:2, 0:TB]))
        k.op("vector", lambda e: e.tensor_copy(out=t["gnhl"][:, 2 * TB:3 * TB],
                                               in_=t["tB"][0:2, 0:TB]))
        k.op("vector", lambda e: e.tensor_copy(out=t["tC"][0:2, 0:TB],
                                               in_=t["gnhl"][:, 2 * TB:3 * TB]))
        k.op("vector", lambda e: e.tensor_tensor(
            out=t["tC"][0:2, 0:TB], in0=t["tB"][0:2, 0:TB],
            in1=t["tC"][0:2, 0:TB], op=ALU.subtract))
        hl = k.op("vector", lambda e: e.tensor_copy(out=t["gnhl"][:, 3 * TB:4 * TB],
                                                    in_=t["tC"][0:2, 0:TB]))
        k.mm(k.psum[2][:, 0:TB], t["e2"][0:2, :], t["gnhl"][:, 0:TB],
             True, False, wait=[hl, bc_free[0]])
        b1 = k.mm(k.psum[2][:, 0:TB], t["e2"][0:2, :], t["gnhl"][:, TB:2 * TB],
                  False, True, inc="T")
        k.mm(k.psum[3][:, 0:TB], t["e2"][0:2, :], t["gnhl"][:, 2 * TB:3 * TB],
             True, False, wait=[bc_free[1]])
        b2 = k.mm(k.psum[3][:, 0:TB], t["e2"][0:2, :], t["gnhl"][:, 3 * TB:4 * TB],
                  False, True, inc="T")
        k.op("vector", lambda e, hc=hc: e.tensor_tensor(
            out=t["tC"][:, 0:TB], in0=t["mixy"][hc][:, 0:TB],
            in1=k.psum[2][:, 0:TB], op=ALU.mult), wait=[b1, b2])
        bc_free[0] = k.ev("V")
        tt2 = k.op("vector", lambda e: e.tensor_tensor(
            out=t["tC"][:, 0:TB], in0=t["tC"][:, 0:TB],
            in1=k.psum[3][:, 0:TB], op=ALU.add))
        bc_free[1] = tt2
        ya = k.op("scalar", lambda e, hc=hc: e.activation(
            out=t["tB"][:, 0:TB], in_=t["tC"][:, 0:TB], func=AF.Identity,
            scale=t["smallp"][:, 32 + hc:33 + hc],
            bias=t["smallp"][:, 48 + hc:49 + hc]), wait=[tt2])
        ya_prev = ya
        k.op("vector", lambda e, hc=hc: e.tensor_tensor(
            out=t["mixy"][hc][:, 0:TB], in0=t["tB"][:, 0:TB],
            in1=t["gsl"][hc][:, 0:TB], op=ALU.mult), wait=[ya])

    if DBG.get("l0cut", 99) <= 4:
        return
    if DBG.get("wobar", 0):
        barrier(k)
    k.slab_free = k.ev("T")
    k.bank_ev = [k.ev("V")] * 4
    stg_free = [k.ev("GD"), k.ev("GD")]

    def cb_wo(oc, bank, mmev):
        sb = t["stg"] if oc % 2 == 0 else t["stg2"]
        cpv = k.op("vector", lambda e, bank=bank, sb=sb: e.tensor_copy(
            out=sb[:, 0:TB], in_=bank[:, 0:TB]),
            wait=[mmev, stg_free[oc % 2]])
        k.bank_ev[oc % 4] = cpv
        stg_free[oc % 2] = k.dma("gpsimd",
                                 k.sscr.ap()[oc, :, :], sb[:, 0:TB],
                                 wait=[cpv], inc="GD")
    proj2048(k, "l0", 4, t["mixy"], cb_wo, rhs_ev=k.ev("V"))
    k.s_done = k.ev("GD")



def dslab(k, kk):
    t = k.tensors
    if kk < 32:
        return t["wsl"][kk // 2][:, 256 * (kk % 2):256 * (kk % 2) + 256]
    return t["wsl2"][(kk - 32) // 2][:, 256 * (kk % 2):256 * (kk % 2) + 256]


def l0_ffn(k, r, xn_ev):
    t = k.tensors
    war0 = [k.ev("T"), k.ev("GD")]
    il_ev = None
    for kb in range(8):
        for tc in range(TB // 128):
            csl = slice(1 + 128 * tc, 129 + 128 * tc)
            k.tr(k.psum[4][:, 0:128], t["xn"][2 * kb][:, csl],
                 t["msk_bf"][:, 0:128], wait=[k.tp_free[0]])
            t1 = k.tr(k.psum[4][:, 128:256], t["xn"][2 * kb + 1][:, csl],
                      t["msk_bf"][:, 0:128], inc="T")
            tm = k.op("vector", lambda e: e.tensor_copy(
                out=t["xntm"][:, 0:256], in_=k.psum[4][:, 0:256]),
                wait=[t1] + (war0 if kb == 0 and tc == 0 else []))
            k.tp_free[0] = tm
            k.tr(k.psum[5][:, 0:128], t["xntm"][:, 0:256:2],
                 t["msk_bf"][:, 0:128], wait=[tm, k.tp_free[1]])
            t2 = k.tr(k.psum[5][:, 128:256], t["xntm"][:, 1:256:2],
                      t["msk_bf"][:, 0:128], inc="T")
            k.op("vector", lambda e, kb=kb, tc=tc: e.tensor_copy(
                out=t["xnil"][2 * kb][:, 128 * tc:128 * tc + 128],
                in_=k.psum[5][:, 0:128]), wait=[t2])
            il_ev = k.op("vector", lambda e, kb=kb, tc=tc: e.tensor_copy(
                out=t["xnil"][2 * kb + 1][:, 128 * tc:128 * tc + 128],
                in_=k.psum[5][:, 128:256]))
            k.tp_free[1] = il_ev

    k.slab_free = k.ev("T")
    bank_ev = [k.ev("V")] * 4
    for jg in range(22):
        evg, evu = [], []
        for kk in range(16):
            kb, par = kk // 2, kk % 2
            seg = "gul" if par == 0 else "guh"
            w8 = [k.slab_free] if kk == 0 else ()
            evg.append(qslab(k, t["wsl"][kk], seg, 128 * kb, 256 * jg, 256,
                             wait=w8))
            evu.append(qslab(k, t["wsl2"][kk], seg, 128 * kb, I0 + 256 * jg, 256))
        evall = evu[-1]
        last = None
        for jj in range(2):
            for kk in range(16):
                w = [evall] if kk == 0 else []
                if kk == 0:
                    w += [bank_ev[jj], il_ev]
                k.mm(k.psum[jj][:, 0:TB],
                     t["wsl"][kk][:, 128 * jj:128 * jj + 128],
                     t["xnil"][kk][:, 0:TB], start=(kk == 0), stop=(kk == 15),
                     wait=w, inc="T" if kk == 15 else None)
            for kk in range(16):
                w = [evall] if kk == 0 else []
                if kk == 0:
                    w.append(bank_ev[2 + jj])
                last = k.mm(k.psum[2 + jj][:, 0:TB],
                            t["wsl2"][kk][:, 128 * jj:128 * jj + 128],
                            t["xnil"][kk][:, 0:TB],
                            start=(kk == 0), stop=(kk == 15),
                            wait=w, inc="T" if kk == 15 else None)
            sg = k.op("scalar", lambda e, jj=jj: e.activation(
                out=t["xntm"][:, 0:TB], in_=k.psum[jj][:, 0:TB], func=AF.Silu),
                wait=[last, k.ev("V")])
            bank_ev[jj] = sg
            hv = k.op("vector", lambda e, jg=jg, jj=jj: e.tensor_tensor(
                out=t["hh"][2 * jg + jj][:, 0:TB], in0=t["xntm"][:, 0:TB],
                in1=k.psum[2 + jj][:, 0:TB], op=ALU.mult), wait=[sg])
            bank_ev[2 + jj] = hv
        k.slab_free = last

    hh_ev = k.ev("V")
    k.slab_free = k.ev("T")
    bank_ev = [k.ev("V")] * 2
    for og in range(8):
        evs = []
        for kk in range(44):
            kb, par = kk // 2, kk % 2
            seg = "dnl" if par == 0 else "dnh"
            w8 = [k.slab_free] if kk == 0 else ()
            evs.append(qslab(k, dslab(k, kk), seg, 128 * kb, 256 * og, 256,
                             wait=w8))
        evall = evs[-1]
        last = None
        for m in range(2):
            oc = 2 * og + m
            for kk in range(44):
                w = [evall] if kk == 0 else []
                if kk == 0:
                    w += [bank_ev[m], hh_ev]
                last = k.mm(k.psum[m][:, 0:TB],
                            dslab(k, kk)[:, 128 * m:128 * m + 128],
                            t["hh"][kk][:, 0:TB], start=(kk == 0),
                            stop=(kk == 43), wait=w,
                            inc="T" if kk == 43 else None)
            ev = k.op("vector", lambda e, oc=oc, m=m: e.tensor_copy(
                out=t["dd"][oc][:, 0:TB], in_=k.psum[m][:, 0:TB]), wait=[last])
            bank_ev[m] = ev
        k.slab_free = last

    for oc in range(16):
        y2 = k.op("vector", lambda e, oc=oc: e.tensor_tensor(
            out=t["xntm"][:, 0:TB], in0=t["dd"][oc][:, 0:TB],
            in1=t["dd"][oc][:, 0:TB], op=ALU.mult), wait=[k.ev("T")])
        k.mm(k.psum[6][0:1, 0:TB], t["aux_bf"][:, 2:3], t["xntm"][:, 0:TB],
             start=(oc == 0), stop=(oc == 15), wait=[y2], inc="T")
    sq = k.ev("T")
    k.op("scalar", lambda e: e.activation(
        out=t["tB"][0:1, 0:TB], in_=k.psum[6][0:1, 0:TB], func=AF.Ln,
        scale=1.0 / 2048.0, bias=1e-6), wait=[sq])
    rse = k.op("scalar", lambda e: e.activation(
        out=t["tB"][0:1, 0:TB], in_=t["tB"][0:1, 0:TB], func=AF.Exp, scale=-0.5))
    k.op("vector", lambda e: e.tensor_copy(out=t["rmshl"][:, 0:TB],
                                           in_=t["tB"][0:1, 0:TB]), wait=[rse])
    k.op("vector", lambda e: e.tensor_copy(out=t["tC"][0:1, 0:TB],
                                           in_=t["rmshl"][:, 0:TB]))
    k.op("vector", lambda e: e.tensor_tensor(
        out=t["tC"][0:1, 0:TB], in0=t["tB"][0:1, 0:TB], in1=t["tC"][0:1, 0:TB],
        op=ALU.subtract))
    hl = k.op("vector", lambda e: e.tensor_copy(out=t["rmshl"][:, TB:2 * TB],
                                                in_=t["tC"][0:1, 0:TB]))
    k.mm(k.psum[2][:, 0:TB], t["ones1"][0:1, :], t["rmshl"][:, 0:TB],
         True, False, wait=[hl, k.ev("V")])
    b1 = k.mm(k.psum[2][:, 0:TB], t["ones1"][0:1, :], t["rmshl"][:, TB:2 * TB],
              False, True, inc="T")
    for oc in range(16):
        k.op("vector", lambda e, oc=oc: e.tensor_tensor(
            out=t["dd"][oc][:, 0:TB], in0=t["dd"][oc][:, 0:TB],
            in1=k.psum[2][:, 0:TB], op=ALU.mult), wait=[b1])
        k.op("scalar", lambda e, oc=oc: e.activation(
            out=t["dd"][oc][:, 0:TB], in_=t["dd"][oc][:, 0:TB], func=AF.Silu),
            wait=[k.ev("V")])
    k.g1_ev = k.ev("S")



def combine0(k, r):
    t, din = k.tensors, k.din
    stg_free = k.ev("V")
    for oc in range(16):
        ld = k.dma("sync", t["stg"][:, 0:TB], k.sscr.ap()[oc, :, :],
                   wait=[k.s_done if oc == 0 else stg_free, k.g1_ev])
        sg = k.op("vector", lambda e, oc=oc: e.tensor_tensor(
            out=t["dd"][oc][:, 0:TB], in0=t["stg"][:, 0:TB],
            in1=t["dd"][oc][:, 0:TB], op=ALU.mult), wait=[ld])
        stg_free = sg
    sg_ev = k.ev("V")
    tp_free = k.ev("V")
    for tc in range(TB // 128):
        ld = k.dma("sync", t["cxa"][:, :],
                   din[f"x{r}"][1 + 128 * tc:129 + 128 * tc, :],
                   wait=[k.ev("V"), k.ev("S")])
        tl = None
        for oc in range(16):
            b = oc // 4
            w = [sg_ev, tp_free] if oc == 0 else ()
            tl = k.tr(k.psum[b][:, 128 * (oc % 4):128 * (oc % 4) + 128],
                      t["dd"][oc][:, 128 * tc:128 * tc + 128],
                      t["ident_f32"][:, :], wait=w,
                      inc="T" if oc % 4 == 3 else None)
        for b in range(4):
            k.op("vector", lambda e, b=b: e.tensor_tensor(
                out=t["cxb"][:, 512 * b:512 * b + 512],
                in0=t["cxa"][:, 512 * b:512 * b + 512],
                in1=k.psum[b][:, 0:512], op=ALU.add),
                wait=[ld, tl, k.ev("GD")] if b == 0 else ())
        tp_free = k.ev("V")
        x1e = k.ev("V")
        k.dma("gpsimd", k.x1scr.ap()[128 * tc:128 * tc + 128, :],
              t["cxb"][:, :], wait=[x1e], inc="GD")
        ln_sbuf_chunk(k, t["cxb"], tc, eps_reg=1e-5)
    k.xn1_ev = k.ev("V")


def ln_sbuf_chunk(k, src, tc, eps_reg):
    t = k.tensors
    k.op("scalar", lambda e: e.activation(out=t["csq"][:, :], in_=src[:, :],
                                          func=AF.Square), wait=[k.ev("V")])
    sq = k.ev("S")
    k.op("vector", lambda e: e.reduce_sum(out=t["hbuf"][:, 0:1], in_=src[:, :],
                                          axis=AX))
    k.op("vector", lambda e: e.reduce_sum(out=t["hbuf"][:, 1:2], in_=t["csq"][:, :],
                                          axis=AX), wait=[sq])
    k.op("vector", lambda e: e.tensor_scalar(out=t["hbuf"][:, 2:3],
                                             in0=t["hbuf"][:, 0:1],
                                             scalar1=1.0 / 2048.0, scalar2=None,
                                             op0=ALU.mult))
    k.op("vector", lambda e: e.tensor_scalar(out=t["hbuf"][:, 3:4],
                                             in0=t["hbuf"][:, 1:2],
                                             scalar1=1.0 / 2048.0, scalar2=None,
                                             op0=ALU.mult))
    k.op("vector", lambda e: e.tensor_tensor(out=t["hbuf"][:, 4:5],
                                             in0=t["hbuf"][:, 2:3],
                                             in1=t["hbuf"][:, 2:3], op=ALU.mult))
    vv = k.op("vector", lambda e: e.tensor_tensor(out=t["hbuf"][:, 4:5],
                                                  in0=t["hbuf"][:, 3:4],
                                                  in1=t["hbuf"][:, 4:5],
                                                  op=ALU.subtract))
    k.op("scalar", lambda e: e.activation(out=t["hbuf"][:, 5:6],
                                          in_=t["hbuf"][:, 4:5], func=AF.Ln,
                                          bias=eps_reg), wait=[vv])
    k.op("scalar", lambda e: e.activation(out=t["hbuf"][:, 5:6],
                                          in_=t["hbuf"][:, 5:6], func=AF.Exp,
                                          scale=-0.5))
    sr = k.ev("S")
    k.op("vector", lambda e: e.tensor_tensor(out=t["hbuf"][:, 6:7],
                                             in0=t["hbuf"][:, 2:3],
                                             in1=t["hbuf"][:, 5:6], op=ALU.mult),
         wait=[sr])
    nb = k.op("vector", lambda e: e.tensor_scalar(out=t["hbuf"][:, 6:7],
                                                  in0=t["hbuf"][:, 6:7],
                                                  scalar1=-1.0, scalar2=None,
                                                  op0=ALU.mult))
    xe = k.op("scalar", lambda e: e.activation(out=t["xntm"][:, 0:D],
                                               in_=src[:, :], func=AF.Identity,
                                               scale=t["hbuf"][:, 5:6],
                                               bias=t["hbuf"][:, 6:7]),
              wait=[nb, k.ev("T")])

    def dst(fc, psrc, w):
        return k.op("vector",
                    lambda e, fc=fc: e.tensor_copy(
                        out=k.tensors["xn"][fc][:, 1 + 128 * tc:129 + 128 * tc],
                        in_=psrc), wait=w)
    transpose_to_fm(k, xe, lambda fc: t["xntm"][:, 128 * fc:128 * fc + 128], dst)





def l1_attn(k, r):
    t = k.tensors
    xn1 = k.xn1_ev
    rhs = [t["xn"][kk][:, 1:1 + TB] for kk in range(16)]
    k.slab_free = k.ev("T")
    k.bank_ev = [k.ev("V")] * 4

    def cb_q(oc, bank, mmev):
        ev = k.op("scalar", lambda e, oc=oc, bank=bank: e.activation(
            out=t["qt"][oc][:, 0:TB], in_=bank[:, 0:TB], func=AF.Relu,
            scale=SCL1), wait=[mmev])
        k.bank_ev[oc % 4] = ev
    proj2048(k, "l1", 0, rhs, cb_q, rhs_ev=xn1)

    qt_ev = k.ev("S")

    def cb_k(oc, bank, mmev):
        h = oc
        sg = k.op("scalar", lambda e, bank=bank: e.activation(
            out=t["sgb"][:, 0:TB], in_=bank[:, 0:TB], func=AF.Sigmoid),
            wait=[mmev, k.ev("V")])
        k.bank_ev[oc % 4] = sg
        gb = k.op("scalar", lambda e: e.activation(
            out=t["gbv"][:, 0:TB], in_=t["sgb"][:, 0:TB], func=AF.Ln,
            scale=-1.0, bias=1.0))
        sc = None
        for c in range(TB // 64):
            sc = k.op("vector", lambda e, c=c: e.tensor_tensor_scan(
                out=t["abuf"][:, 64 * c:64 * c + 64],
                data0=t["gbv"][:, 64 * c:64 * c + 64],
                data1=t["gbv"][:, 64 * c:64 * c + 64],
                initial=0.0, op0=ALU.add, op1=ALU.bypass),
                wait=[gb] if c == 0 else ())
        e1 = k.op("scalar", lambda e: e.activation(
            out=t["tB"][:, 0:TB], in_=t["abuf"][:, 0:TB], func=AF.Exp),
            wait=[sc])
        k.op("vector", lambda e, h=h: e.tensor_tensor(
            out=t["qt"][h][:, 0:TB], in0=t["qt"][h][:, 0:TB],
            in1=t["tB"][:, 0:TB], op=ALU.mult), wait=[e1, qt_ev])
        ed = k.op("scalar", lambda e: e.activation(
            out=t["tB"][:, 0:TB], in_=t["abuf"][:, 0:TB], func=AF.Exp,
            scale=-1.0), wait=[k.ev("V")])
        k.op("vector", lambda e, h=h: e.tensor_tensor(
            out=t["kd"][h][:, 0:TB], in0=t["sgb"][:, 0:TB],
            in1=t["tB"][:, 0:TB], op=ALU.mult), wait=[ed])
        e3 = None
        for c in range(TB // 64):
            e3 = k.op("scalar", lambda e, c=c: e.activation(
                out=t["tB"][:, 64 * c:64 * c + 64],
                in_=t["abuf"][:, 64 * c:64 * c + 64], func=AF.Exp, scale=-1.0,
                bias=t["abuf"][:, 64 * c + 63:64 * c + 64]),
                wait=[k.ev("V")] if c == 0 else ())
        k.op("vector", lambda e, h=h: e.tensor_tensor(
            out=t["ks"][h][:, 0:TB], in0=t["sgb"][:, 0:TB],
            in1=t["tB"][:, 0:TB], op=ALU.mult), wait=[e3])
        k.op("vector", lambda e, h=h: e.tensor_copy(
            out=t["lgpre"][:, 8 * h:8 * h + 8],
            in_=t["abuf"][:, 63:TB:64]))
    proj2048(k, "l1", 1, rhs, cb_k, rhs_ev=xn1)

    for mg in range(4):
        evs = []
        for kk in range(16):
            w8 = [k.slab_free] if kk == 0 else ()
            evs.append(attn_slab(k, t["wsl"][kk], "l1", 2, kk, 512 * mg, 512,
                                 wait=w8))
        evall = evs[-1]
        last = None
        for tc in range(TB // 128):
            bank = k.psum[tc]
            for kk in range(16):
                w = [evall] if kk == 0 else []
                if kk == 0:
                    w += [k.bank_ev[tc], xn1]
                last = k.mm(bank[:, 0:512], rhs[kk][:, 128 * tc:128 * tc + 128],
                            t["wsl"][kk][:, 0:512],
                            start=(kk == 0), stop=(kk == 15), wait=w,
                            inc="T" if kk == 15 else None)
            ev = k.op("vector", lambda e, tc=tc, mg=mg, bank=bank: e.tensor_copy(
                out=t["vt"][tc][:, 512 * mg:512 * mg + 512], in_=bank[:, 0:512]),
                wait=[last])
            k.bank_ev[tc] = ev
        k.slab_free = last

    k.op("vector", lambda e: e.memzero(t["s1f"][:, :]))
    z1 = k.op("vector", lambda e: e.memzero(t["s1b"][:, :]))
    s1b_ev = z1
    att_free = k.ev("V")
    ob_free = [k.ev("V"), k.ev("V")]
    kv_free = k.ev("V")
    kst_free = k.ev("V")
    k.attsb_free = k.ev("T")
    k.q2b_free = k.ev("T")
    k.ks2b_free = k.ev("T")
    k.kst_rd = k.ev("T")
    NP = TB // 128
    prep_ev = k.ev("V")
    for h in range(H1):
        olast = None
        ob = k.psum[h % 2]
        for p in range(NP):
            c0 = slice(128 * p, 128 * p + 64)
            c1 = slice(128 * p + 64, 128 * p + 128)
            pr = slice(128 * p, 128 * p + 128)
            d0 = k.op("scalar", lambda e, h=h, p=p: e.activation(
                out=t["hbuf"][:, 8:9],
                in_=t["lgpre"][:, 8 * h + 2 * p:8 * h + 2 * p + 1],
                func=AF.Exp), wait=[prep_ev, k.ev("V")])
            d1 = k.op("scalar", lambda e, h=h, p=p: e.activation(
                out=t["hbuf"][:, 9:10],
                in_=t["lgpre"][:, 8 * h + 2 * p + 1:8 * h + 2 * p + 2],
                func=AF.Exp))
            k.op("vector", lambda e, h=h, p=p: e.tensor_tensor(
                out=t["hbuf"][:, 10:11],
                in0=t["lgpre"][:, 8 * h + 2 * p:8 * h + 2 * p + 1],
                in1=t["lgpre"][:, 8 * h + 2 * p + 1:8 * h + 2 * p + 2],
                op=ALU.add))
            dp = k.op("scalar", lambda e: e.activation(
                out=t["hbuf"][:, 11:12], in_=t["hbuf"][:, 10:11], func=AF.Exp),
                wait=[k.ev("V")])
            k.op("vector", lambda e, h=h, p=p: e.tensor_copy(
                out=t["q2b"][:, 0:64], in_=t["qt"][h][:, c0]),
                wait=[k.q2b_free])
            q2 = k.op("scalar", lambda e, h=h, p=p: e.activation(
                out=t["q2b"][:, 64:128], in_=t["qt"][h][:, c1],
                func=AF.Identity, scale=t["hbuf"][:, 8:9]), wait=[d0, k.ev("V")])
            k.op("vector", lambda e, h=h, p=p: e.tensor_copy(
                out=t["ks2b"][:, 64:128], in_=t["ks"][h][:, c1]),
                wait=[k.ks2b_free])
            ks2 = k.op("scalar", lambda e, h=h, p=p: e.activation(
                out=t["ks2b"][:, 0:64], in_=t["ks"][h][:, c0],
                func=AF.Identity, scale=t["hbuf"][:, 9:10]), wait=[d1, k.ev("V")])
            tke = k.tr(k.psum[4][:, 0:128], t["ks2b"][:, :], t["msk_bf"][:, 0:128],
                       wait=[ks2, kst_free], inc="T")
            k.ks2b_free = tke
            kste = k.op("vector", lambda e: e.tensor_copy(
                out=t["kst"][:, :], in_=k.psum[4][:, 0:128]),
                wait=[tke, k.kst_rd])
            kst_free = kste
            k.mm(k.psum[6][0:64, 0:64],
                 t["kd"][h][:, c0], t["qt"][h][:, c0], True, True,
                 wait=[att_free])
            k.mm(k.psum[6][0:64, 64:128], t["ks"][h][:, c0], t["qt"][h][:, c1],
                 True, True)
            a3 = k.mm(k.psum[7][0:64, 64:128], t["kd"][h][:, c1], t["qt"][h][:, c1],
                      True, True, inc="T")
            k.op("vector", lambda e: e.tensor_tensor(
                out=t["attsb"][0:64, :], in0=k.psum[6][0:64, 0:128],
                in1=t["msk_bf"][0:64, 256:384], op=ALU.mult),
                wait=[a3, k.attsb_free])
            ae = k.op("vector", lambda e: e.tensor_tensor(
                out=t["attsb"][64:128, :], in0=k.psum[7][0:64, 0:128],
                in1=t["msk_bf"][64:128, 256:384], op=ALU.mult))
            att_free = ae
            w1 = [ae]
            if p == 0:
                w1.append(ob_free[h % 2])
            k.attsb_free = k.mm(ob[:, pr], t["vt"][p][:, 128 * h:128 * h + 128],
                                t["attsb"][:, :], True, False, wait=w1, inc="T")
            olast = k.mm(ob[:, pr], t["s1b"][:, 128 * h:128 * h + 128],
                         t["q2b"][:, :], False, True,
                         wait=[s1b_ev, q2], inc="T")
            kv = k.mm(k.psum[7][:, 128:256], t["kst"][:, :],
                      t["vt"][p][:, 128 * h:128 * h + 128], True, True,
                      wait=[kste, kv_free], inc="T")
            k.kst_rd = kv
            k.q2b_free = olast
            k.op("vector", lambda e, h=h: e.scalar_tensor_tensor(
                out=t["s1f"][:, 128 * h:128 * h + 128],
                in0=t["s1f"][:, 128 * h:128 * h + 128],
                scalar=t["hbuf"][:, 11:12], in1=k.psum[7][:, 128:256],
                op0=ALU.mult, op1=ALU.add), wait=[kv, dp])
            s1b_ev = k.op("vector", lambda e, h=h: e.tensor_copy(
                out=t["s1b"][:, 128 * h:128 * h + 128],
                in_=t["s1f"][:, 128 * h:128 * h + 128]))
            kv_free = s1b_ev
        ov = k.op("vector", lambda e, h=h, ob=ob: e.tensor_copy(
            out=t["gsl"][h][:, 0:TB], in_=ob[:, 0:TB]), wait=[olast])
        ob_free[h % 2] = ov
        prep_ev = k.ev("S")

    for h in range(H1):
        k.op("vector", lambda e, h=h: e.reduce_sum(
            out=t["s1lg"][:, h:h + 1], in_=t["lgpre"][:, 8 * h:8 * h + 8],
            axis=AX))
    de = k.op("scalar", lambda e: e.activation(
        out=t["hbuf"][:, 16:32], in_=t["s1lg"][:, 0:16], func=AF.Exp),
        wait=[k.ev("V")])
    sx = k.ev("V")
    k.dma("gpsimd", k.sag[f"l1i{r}"].ap()[:, 0:D], t["s1f"][:, :],
          wait=[sx], inc="GD")
    d2 = k.dma("gpsimd", k.sag[f"l1i{r}"].ap()[:, D:D + 16], t["hbuf"][:, 16:32],
               wait=[de], inc="GD")
    ag = k.op("gpsimd",
              lambda e: e.collective_compute(
                  "AllGather", ALU.bypass, replica_groups=[list(range(NCORES))],
                  ins=[k.sag[f"l1i{r}"].ap().opt()],
                  outs=[k.sag[f"l1o{r}"].ap().opt()]),
              wait=[d2], inc="CC")
    k.op("vector", lambda e: e.tensor_copy(out=t["s1in"][:, :],
                                           in_=t["s1run"][:, :]),
         wait=[k.ev("T")])
    for j in range(NCORES):
        ld = k.dma("sync", t["xtm"][:, 0:D],
                   k.sag[f"l1o{r}"].ap()[128 * j:128 * j + 128, 0:D],
                   wait=[ag, k.ev("V")])
        ld2 = k.dma("sync", t["tcol"][:, 0:16],
                    k.sag[f"l1o{r}"].ap()[128 * j:128 * j + 128, D:D + 16])
        for h in range(H1):
            k.op("vector", lambda e, h=h: e.scalar_tensor_tensor(
                out=t["s1run"][:, 128 * h:128 * h + 128],
                in0=t["s1run"][:, 128 * h:128 * h + 128],
                scalar=t["tcol"][:, h:h + 1],
                in1=t["xtm"][:, 128 * h:128 * h + 128],
                op0=ALU.mult, op1=ALU.add),
                wait=[ld, ld2] if h == 0 else ())
        k.op("vector", lambda e, j=j: e.scalar_tensor_tensor(
            out=t["xtm"][:, 0:D], in0=t["xtm"][:, 0:D],
            scalar=t["pmask"][:, j:j + 1], in1=t["xtm"][:, 0:D],
            op0=ALU.mult, op1=ALU.bypass))
        k.op("scalar", lambda e, j=j: e.activation(
            out=t["tcol"][:, 16:32], in_=t["onesf"][:, 0:16], func=AF.Identity,
            scale=t["tcol"][:, 32 + j:33 + j]), wait=[k.ev("V")])
        k.op("vector", lambda e, j=j: e.scalar_tensor_tensor(
            out=t["tcol"][:, 0:16], in0=t["tcol"][:, 0:16],
            scalar=t["pmask"][:, j:j + 1], in1=t["tcol"][:, 16:32],
            op0=ALU.mult, op1=ALU.add), wait=[k.ev("S")])
        for h in range(H1):
            k.op("vector", lambda e, h=h: e.scalar_tensor_tensor(
                out=t["s1in"][:, 128 * h:128 * h + 128],
                in0=t["s1in"][:, 128 * h:128 * h + 128],
                scalar=t["tcol"][:, h:h + 1],
                in1=t["xtm"][:, 128 * h:128 * h + 128],
                op0=ALU.mult, op1=ALU.add))
    sbi = k.op("vector", lambda e: e.tensor_copy(out=t["s1inb"][:, :],
                                                 in_=t["s1in"][:, :]))

    corr_free = [k.ev("V"), k.ev("V")]
    for h in range(H1):
        pf = k.op("vector", lambda e, h=h: e.tensor_tensor_scan(
            out=t["tD"][:, 0:8], data0=t["lgpre"][:, 8 * h:8 * h + 8],
            data1=t["lgpre"][:, 8 * h:8 * h + 8], initial=0.0,
            op0=ALU.add, op1=ALU.bypass))
        pe = k.op("scalar", lambda e: e.activation(
            out=t["tD"][:, 8:16], in_=t["tD"][:, 0:8], func=AF.Exp), wait=[pf])
        k.op("vector", lambda e, h=h: e.tensor_copy(
            out=t["mixy"][h][:, 0:64], in_=t["qt"][h][:, 0:64]))
        qg = None
        for c in range(1, TB // 64):
            qg = k.op("scalar", lambda e, h=h, c=c: e.activation(
                out=t["mixy"][h][:, 64 * c:64 * c + 64],
                in_=t["qt"][h][:, 64 * c:64 * c + 64], func=AF.Identity,
                scale=t["tD"][:, 7 + c:8 + c]), wait=[pe] if c == 1 else ())
        cm = k.mm(k.psum[2 + h % 2][:, 0:TB], t["s1inb"][:, 128 * h:128 * h + 128],
                  t["mixy"][h][:, 0:TB], True, True,
                  wait=[sbi, qg, k.ev("V"), corr_free[h % 2]], inc="T")
        ya = k.op("vector", lambda e, h=h: e.tensor_tensor(
            out=t["gsl"][h][:, 0:TB], in0=t["gsl"][h][:, 0:TB],
            in1=k.psum[2 + h % 2][:, 0:TB], op=ALU.add), wait=[cm])
        corr_free[h % 2] = ya

    for h in range(H1):
        y2 = k.op("vector", lambda e, h=h: e.tensor_tensor(
            out=t["xntm"][:, 0:TB], in0=t["gsl"][h][:, 0:TB],
            in1=t["gsl"][h][:, 0:TB], op=ALU.mult), wait=[k.ev("T")])
        k.mm(k.psum[6][0:1, 0:TB], t["aux_bf"][:, 2:3], t["xntm"][:, 0:TB],
             start=(h == 0), stop=(h == 15), wait=[y2], inc="T")
    sq = k.ev("T")
    k.op("scalar", lambda e: e.activation(
        out=t["tB"][0:1, 0:TB], in_=k.psum[6][0:1, 0:TB], func=AF.Ln,
        scale=1.0 / 2048.0, bias=1e-5), wait=[sq])
    rse = k.op("scalar", lambda e: e.activation(
        out=t["tB"][0:1, 0:TB], in_=t["tB"][0:1, 0:TB], func=AF.Exp, scale=-0.5))
    k.op("vector", lambda e: e.tensor_copy(out=t["rmshl"][:, 0:TB],
                                           in_=t["tB"][0:1, 0:TB]), wait=[rse])
    k.op("vector", lambda e: e.tensor_copy(out=t["tC"][0:1, 0:TB],
                                           in_=t["rmshl"][:, 0:TB]))
    k.op("vector", lambda e: e.tensor_tensor(
        out=t["tC"][0:1, 0:TB], in0=t["tB"][0:1, 0:TB], in1=t["tC"][0:1, 0:TB],
        op=ALU.subtract))
    hl = k.op("vector", lambda e: e.tensor_copy(out=t["rmshl"][:, TB:2 * TB],
                                                in_=t["tC"][0:1, 0:TB]))
    k.mm(k.psum[2][:, 0:TB], t["ones1"][0:1, :], t["rmshl"][:, 0:TB],
         True, False, wait=[hl, k.ev("V")])
    b1 = k.mm(k.psum[2][:, 0:TB], t["ones1"][0:1, :], t["rmshl"][:, TB:2 * TB],
              False, True, inc="T")
    act_prev = k.ev("S")
    for h in range(H1):
        k.op("vector", lambda e, h=h: e.tensor_tensor(
            out=t["tC"][:, 0:TB], in0=t["gsl"][h][:, 0:TB],
            in1=k.psum[2][:, 0:TB], op=ALU.mult), wait=[b1, act_prev])
        act_prev = k.op("scalar", lambda e, h=h: e.activation(
            out=t["mixy"][h][:, 0:TB], in_=t["tC"][:, 0:TB], func=AF.Identity,
            scale=t["smallp"][:, 64 + h:65 + h]), wait=[k.ev("V")])

    k.slab_free = k.ev("T")
    k.bank_ev = [k.ev("V")] * 4
    og_ev = k.ev("S")
    stg_free = [k.ev("GD"), k.ev("GD")]

    def cb_wo(oc, bank, mmev):
        sb = t["stg"] if oc % 2 == 0 else t["stg2"]
        cpv = k.op("vector", lambda e, bank=bank, sb=sb: e.tensor_copy(
            out=sb[:, 0:TB], in_=bank[:, 0:TB]),
            wait=[mmev, stg_free[oc % 2]])
        k.bank_ev[oc % 4] = cpv
        stg_free[oc % 2] = k.dma("gpsimd",
                                 k.sscr.ap()[oc, :, :], sb[:, 0:TB],
                                 wait=[cpv], inc="GD")
    proj2048(k, "l1", 3, t["mixy"], cb_wo, rhs_ev=og_ev)
    k.s_done = k.ev("GD")



def l1_ffn(k, r):
    t, din = k.tensors, k.din
    rhs = [t["xn"][kk][:, 1:1 + TB] for kk in range(16)]
    k.slab_free = k.ev("T")
    bank_ev = [k.ev("V")] * 4
    if "l1ffn" in DBG["stages"]:
        for jg in range(22):
            ev1, ev2 = [], []
            for kk in range(16):
                w8 = [k.slab_free] if kk == 0 else ()
                ev1.append(qslab(k, t["wsl"][kk], "w1q", 128 * kk, 256 * jg, 256,
                                 wait=w8))
                ev2.append(qslab(k, t["wsl2"][kk], "w2q", 128 * kk, 256 * jg, 256))
            evall = ev2[-1]
            last = None
            for jj in range(2):
                for kk in range(16):
                    w = [evall] if kk == 0 else []
                    if kk == 0:
                        w += [bank_ev[jj], k.xn1_ev]
                    k.mm(k.psum[jj][:, 0:TB],
                         t["wsl"][kk][:, 128 * jj:128 * jj + 128],
                         rhs[kk], start=(kk == 0), stop=(kk == 15),
                         wait=w, inc="T" if kk == 15 else None)
                for kk in range(16):
                    w = [evall] if kk == 0 else []
                    if kk == 0:
                        w.append(bank_ev[2 + jj])
                    last = k.mm(k.psum[2 + jj][:, 0:TB],
                                t["wsl2"][kk][:, 128 * jj:128 * jj + 128],
                                rhs[kk], start=(kk == 0), stop=(kk == 15),
                                wait=w, inc="T" if kk == 15 else None)
                sg = k.op("scalar", lambda e, jj=jj: e.activation(
                    out=t["xntm"][:, 0:TB], in_=k.psum[jj][:, 0:TB],
                    func=AF.Sigmoid), wait=[last, k.ev("V")])
                bank_ev[jj] = sg
                hvar = DBG.get("hvar", 0)
                if hvar == 1:
                    hv = k.op("vector", lambda e, jg=jg, jj=jj: e.tensor_copy(
                        out=t["hid"][2 * jg + jj][:, 0:TB],
                        in_=k.psum[2 + jj][:, 0:TB]), wait=[sg])
                elif hvar == 2:
                    hv = k.op("vector", lambda e, jg=jg, jj=jj: e.tensor_copy(
                        out=t["hid"][2 * jg + jj][:, 0:TB],
                        in_=t["xntm"][:, 0:TB]), wait=[sg])
                else:
                    hv = k.op("vector", lambda e, jg=jg, jj=jj: e.tensor_tensor(
                        out=t["hid"][2 * jg + jj][:, 0:TB],
                        in0=t["xntm"][:, 0:TB],
                        in1=k.psum[2 + jj][:, 0:TB], op=ALU.mult), wait=[sg])
                bank_ev[2 + jj] = hv
            k.slab_free = last
            if DBG.get("l1cut", 0):
                return

        hid_ev = k.ev("V")
        k.slab_free = k.ev("T")
        bank_ev = [k.ev("V")] * 2
        for og in range(8):
            evs = []
            for kk in range(44):
                w8 = [k.slab_free] if kk == 0 else ()
                evs.append(qslab(k, dslab(k, kk), "w3q", 128 * kk, 256 * og, 256,
                                 wait=w8))
            evall = evs[-1]
            last = None
            for m in range(2):
                oc = 2 * og + m
                for kk in range(44):
                    w = [evall] if kk == 0 else []
                    if kk == 0:
                        w += [bank_ev[m], hid_ev]
                    last = k.mm(k.psum[m][:, 0:TB],
                                dslab(k, kk)[:, 128 * m:128 * m + 128],
                                t["hid"][kk][:, 0:TB], start=(kk == 0),
                                stop=(kk == 43), wait=w,
                                inc="T" if kk == 43 else None)
                ev = k.op("vector", lambda e, oc=oc, m=m: e.tensor_copy(
                    out=t["fo"][oc][:, 0:TB], in_=k.psum[m][:, 0:TB]),
                    wait=[last])
                bank_ev[m] = ev
            k.slab_free = last
    else:
        for oc in range(16):
            k.op("vector", lambda e, oc=oc: e.memzero(t["fo"][oc][:, 0:TB]))

    stg_free = k.ev("V")
    for oc in range(16):
        sl = k.op("scalar", lambda e, oc=oc: e.activation(
            out=t["fo"][oc][:, 0:TB], in_=t["fo"][oc][:, 0:TB], func=AF.Silu),
            wait=[k.ev("V")])
        ld = k.dma("sync", t["stg"][:, 0:TB], k.sscr.ap()[oc, :, :],
                   wait=[k.s_done if oc == 0 else stg_free])
        sg = k.op("vector", lambda e, oc=oc: e.tensor_tensor(
            out=t["fo"][oc][:, 0:TB], in0=t["stg"][:, 0:TB],
            in1=t["fo"][oc][:, 0:TB], op=ALU.mult), wait=[ld, sl])
        stg_free = sg
    sg_ev = k.ev("V")
    tp_free = k.ev("V")
    for tc in range(TB // 128):
        ld = k.dma("sync", t["cxa"][:, :],
                   k.x1scr.ap()[128 * tc:128 * tc + 128, :],
                   wait=[k.ev("V"), k.ev("S")])
        tl = None
        for oc in range(16):
            b = oc // 4
            w = [sg_ev, tp_free] if oc == 0 else ()
            tl = k.tr(k.psum[b][:, 128 * (oc % 4):128 * (oc % 4) + 128],
                      t["fo"][oc][:, 128 * tc:128 * tc + 128],
                      t["ident_f32"][:, :], wait=w,
                      inc="T" if oc % 4 == 3 else None)
        oe = None
        for b in range(4):
            oe = k.op("vector", lambda e, b=b: e.tensor_tensor(
                out=t["cxb"][:, 512 * b:512 * b + 512],
                in0=t["cxa"][:, 512 * b:512 * b + 512],
                in1=k.psum[b][:, 0:512], op=ALU.add),
                wait=[ld, tl, k.ev("OUT")] if b == 0 else ())
        tp_free = oe
        k.dma("gpsimd", k.out_hbm[TB * r + 128 * tc: TB * r + 128 * tc + 128, :],
              t["cxb"][:, :], wait=[oe], inc="OUT")


def tile(k, r):
    t, din = k.tensors, k.din
    barrier(k)
    k.iota_ev = k.dma("sync", t["iota"][:, :], din[f"iota{r}"][:, :])
    k.slab_free = k.ev("T")
    k.bank_ev = [k.ev("T")] * 4
    k.tp_free = [k.ev("V"), k.ev("V")]
    k.op("vector", lambda e: e.tensor_scalar(out=t["tcol"][:, 32:40],
                                             in0=t["pmask"][:, 0:8],
                                             scalar1=-1.0, scalar2=1.0,
                                             op0=ALU.mult, op1=ALU.add),
         wait=[k.ld_small])

    def write_xn(tc, xe):
        def dst(fc, psrc, w):
            return k.op("vector",
                        lambda e: e.tensor_copy(
                            out=t["xn"][fc][:, 1 + 128 * tc:129 + 128 * tc],
                            in_=psrc),
                        wait=w)
        transpose_to_fm(k, xe, lambda fc: t["xntm"][:, 128 * fc:128 * fc + 128],
                        dst)
    for tc in range(TB // 128):
        ln_row_chunk(k, r, tc, 1 + 128 * tc, write_xn)
    ld = k.dma("sync", t["xtm"][0:1, :], din[f"x{r}"][0:1, :],
               wait=[k.ev("S"), k.ev("V")])
    k.op("scalar", lambda e: e.activation(out=t["xsq"][0:1, :], in_=t["xtm"][0:1, :],
                                          func=AF.Square), wait=[ld])
    sq = k.ev("S")
    k.op("vector", lambda e: e.reduce_sum(out=t["hbuf"][0:1, 0:1],
                                          in_=t["xtm"][0:1, :], axis=AX), wait=[ld])
    k.op("vector", lambda e: e.reduce_sum(out=t["hbuf"][0:1, 1:2],
                                          in_=t["xsq"][0:1, :], axis=AX), wait=[sq])
    k.op("vector", lambda e: e.tensor_scalar(out=t["hbuf"][0:1, 2:3],
                                             in0=t["hbuf"][0:1, 0:1],
                                             scalar1=1.0 / 2048.0, scalar2=None,
                                             op0=ALU.mult))
    k.op("vector", lambda e: e.tensor_scalar(out=t["hbuf"][0:1, 3:4],
                                             in0=t["hbuf"][0:1, 1:2],
                                             scalar1=1.0 / 2048.0, scalar2=None,
                                             op0=ALU.mult))
    k.op("vector", lambda e: e.tensor_tensor(out=t["hbuf"][0:1, 4:5],
                                             in0=t["hbuf"][0:1, 2:3],
                                             in1=t["hbuf"][0:1, 2:3], op=ALU.mult))
    vv = k.op("vector", lambda e: e.tensor_tensor(out=t["hbuf"][0:1, 4:5],
                                                  in0=t["hbuf"][0:1, 3:4],
                                                  in1=t["hbuf"][0:1, 4:5],
                                                  op=ALU.subtract))
    k.op("scalar", lambda e: e.activation(out=t["hbuf"][0:1, 5:6],
                                          in_=t["hbuf"][0:1, 4:5], func=AF.Ln,
                                          bias=1e-5), wait=[vv])
    k.op("scalar", lambda e: e.activation(out=t["hbuf"][0:1, 5:6],
                                          in_=t["hbuf"][0:1, 5:6], func=AF.Exp,
                                          scale=-0.5))
    sr = k.ev("S")
    k.op("vector", lambda e: e.tensor_tensor(out=t["hbuf"][0:1, 6:7],
                                             in0=t["hbuf"][0:1, 2:3],
                                             in1=t["hbuf"][0:1, 5:6], op=ALU.mult),
         wait=[sr])
    nbb = k.op("vector", lambda e: e.tensor_scalar(out=t["hbuf"][0:1, 6:7],
                                                   in0=t["hbuf"][0:1, 6:7],
                                                   scalar1=-1.0, scalar2=None,
                                                   op0=ALU.mult))
    bl = k.op("scalar", lambda e: e.activation(out=t["xsq"][0:1, :],
                                               in_=t["xtm"][0:1, :],
                                               func=AF.Identity,
                                               scale=t["hbuf"][0:1, 5:6],
                                               bias=t["hbuf"][0:1, 6:7]),
              wait=[nbb])
    k.op("vector", lambda e: e.scalar_tensor_tensor(
        out=t["xsq"][0:1, :], in0=t["xsq"][0:1, :],
        scalar=t["flags"][0:1, 2 * r:2 * r + 1], in1=t["xsq"][0:1, :],
        op0=ALU.mult, op1=ALU.bypass), wait=[bl])
    k.op("vector", lambda e: e.scalar_tensor_tensor(
        out=t["xsq"][0:1, :], in0=t["xtm"][0:1, :],
        scalar=t["flags"][0:1, 2 * r + 1:2 * r + 2], in1=t["xsq"][0:1, :],
        op0=ALU.mult, op1=ALU.add))
    bb = k.op("vector", lambda e: e.tensor_copy(out=t["xntm"][0:1, :],
                                                in_=t["xsq"][0:1, :]),
              wait=[k.ev("T")])
    for fc in range(16):
        tev = k.tr(k.psum[4][0:128, 0:1],
                   t["xntm"][0:1, 128 * fc:128 * fc + 128], t["ones1"][0:1, 0:1],
                   wait=[bb, k.tp_free[0]], inc="T")
        k.tp_free[0] = k.op("vector",
                            lambda e, fc=fc: e.tensor_copy(
                                out=t["xn"][fc][:, 0:1],
                                in_=k.psum[4][0:128, 0:1]),
                            wait=[tev])
    xn_ev = k.ev("V")

    st = DBG["stages"]
    if "l0attn" in st:
        l0_attn(k, r, xn_ev)
    if "l0ffn" in st:
        l0_ffn(k, r, xn_ev)
    if "l0attn" in st and "l0ffn" in st:
        combine0(k, r)
    else:
        return
    if "l1attn" in st:
        l1_attn(k, r)
        if "l1out" in st:
            l1_ffn(k, r)


def kernel(**inputs):
    in_maps = host_prep(inputs)
    nc, k = build_program()
    res = run_bass_kernel_spmd(nc, in_maps, core_ids=list(range(NCORES)))
    out = np.zeros((NT, D), np.float32)
    for i in range(NCORES):
        o = res.results[i]["out"]
        for r in range(NTILES):
            blk = 8 * r + i
            out[blk * TB:(blk + 1) * TB] = o[r * TB:(r + 1) * TB]
    return out.reshape(1, NT, D)


# version 04
# speedup vs baseline: 1.1905x; 1.1905x over previous
import numpy as np
from contextlib import ExitStack

import concourse.bass as bass
import concourse.mybir as mybir
from concourse.bass_utils import run_bass_kernel_spmd

F32 = mybir.dt.float32
BF16 = mybir.dt.bfloat16
F16 = mybir.dt.float16
I32 = mybir.dt.int32
U8 = mybir.dt.uint8
AF = mybir.ActivationFunctionType
ALU = mybir.AluOpType
AX = mybir.AxisListType.X

NCORES = 8
D = 2048
TB = 512
NTILES = 4
NT = 16384
H0, K0 = 32, 64
H1, K1 = 16, 128
I0 = 5632
GU = 2 * I0
I1 = 5632
LN2 = 0.6931471805599453
SCL1 = 0.088388347648318447

CONSTS = [0.0, 1.0, -1.0, 0.5, -0.5, 2.0, -2.0, 8.0, -8.0,
          LN2, -2.0 * LN2, 1.0 / 2048.0, 1.0 / 64.0,
          1e-5, 0.00064, 1e-6, 1e-8, SCL1, -16383.0]
CIDX = {v: i for i, v in enumerate(CONSTS)}

SEG_ROWS = {"l0": 1280, "l1": 1024, "gul": 128, "guh": 128,
            "dnl": 352, "dnh": 352, "w1q": 256, "w2q": 256, "w3q": 704}
SEG_W = {"l0": D, "l1": D, "gul": GU, "guh": GU, "dnl": D, "dnh": D,
         "w1q": I1, "w2q": I1, "w3q": D}

DBG = {"taps": [], "parts": {"casts", "gamma", "tern", "fp4"},
       "ntiles": NTILES,
       "stages": {"l0attn", "l0ffn", "l1attn", "l1out", "l1ffn"}}


def _sigma_perm():
    out = []
    for c in range(I0 // 256):
        out.extend(range(256 * c, 256 * c + 256, 2))
        out.extend(range(256 * c + 1, 256 * c + 256, 2))
    return np.array(out, dtype=np.int64)


def _col16(v):
    return np.ascontiguousarray(v.reshape(16, 128).T)


def host_prep(inp):
    import ml_dtypes
    x = np.asarray(inp["L0_attn_x"], np.float32).reshape(NT, D)
    state = np.asarray(inp["L0_attn_state"], np.float32).reshape(1, D)

    wrT = np.ascontiguousarray(np.asarray(inp["L0_attn_w_r"], np.float32).T)
    wkT = np.ascontiguousarray(np.asarray(inp["L0_attn_w_k"], np.float32).T)
    wvT = np.ascontiguousarray(np.asarray(inp["L0_attn_w_v"], np.float32).T)
    wgT = np.ascontiguousarray(np.asarray(inp["L0_attn_w_g"], np.float32).T)
    woT = np.ascontiguousarray(np.asarray(inp["L0_attn_w_o"], np.float32).T)
    l1q = np.asarray(inp["L1_attn_Wq"], np.float32)
    l1k = np.asarray(inp["L1_attn_Wk"], np.float32)
    l1i = np.asarray(inp["L1_attn_Wi"], np.float32)
    l1o = np.asarray(inp["L1_attn_Wo"], np.float32)

    sig = _sigma_perm()
    cols_perm = np.concatenate([sig, I0 + sig])
    gu_t = np.ascontiguousarray(
        np.asarray(inp["L0_ffn_gate_up_packed"], np.int32).view(np.uint8).T)
    gu_tp = np.ascontiguousarray(gu_t[:, cols_perm])
    gsc_p = np.ascontiguousarray(
        np.asarray(inp["L0_ffn_gate_up_scales"])[:, cols_perm])
    dn_t = np.ascontiguousarray(
        np.asarray(inp["L0_ffn_down_packed"], np.int32).view(np.uint8).T)
    dsc = np.asarray(inp["L0_ffn_down_scales"])

    w1T = np.ascontiguousarray(np.asarray(inp["L1_ffn_w1"], np.float32).T)
    w2T = np.ascontiguousarray(np.asarray(inp["L1_ffn_w2"], np.float32).T)
    w3T = np.ascontiguousarray(np.asarray(inp["L1_ffn_w3"], np.float32).T)

    tmix = np.concatenate(
        [_col16(np.asarray(inp["L0_attn_time_mix_" + s], np.float32).reshape(D))
         for s in ("r", "k", "v", "g")], axis=1)
    smallp = np.concatenate(
        [_col16(np.asarray(inp[n], np.float32).reshape(D)) for n in
         ("L0_attn_time_decay", "L0_attn_time_faaaa", "L0_attn_ln_w",
          "L0_attn_ln_b", "L1_attn_g_weight")], axis=1)
    wkv_l = np.ascontiguousarray(
        np.asarray(inp["L0_attn_wkvstate"], np.float32).transpose(1, 0, 2).reshape(64, D))

    ccol = np.broadcast_to(np.array(CONSTS, np.float32), (128, len(CONSTS))).copy()
    rng_p = np.arange(128)
    ident = np.eye(128, dtype=np.float32)
    m_lt = (rng_p[:, None] < rng_p[None, :]).astype(np.float32)
    le64 = (np.arange(64)[:, None] <= np.arange(64)[None, :]).astype(np.float32)
    mask_l1 = np.zeros((128, 128), np.float32)
    mask_l1[0:64, 0:64] = le64
    mask_l1[0:64, 64:128] = 1.0
    mask_l1[64:128, 64:128] = le64
    msk_bf = np.concatenate([ident, m_lt, mask_l1], axis=1).astype(ml_dtypes.bfloat16)
    m_diag_i = np.eye(128, dtype=np.int32)
    o642 = np.zeros((128, 2), np.float32)
    o642[0:64, 0] = 1.0
    o642[64:128, 1] = 1.0
    aux = np.concatenate([o642, np.ones((128, 1), np.float32)], axis=1)
    aux_bf = aux.astype(ml_dtypes.bfloat16)
    e2_bf = np.zeros((2, 128), np.float32)
    e2_bf[0, 0:64] = 1.0
    e2_bf[1, 64:128] = 1.0
    e2_bf = e2_bf.astype(ml_dtypes.bfloat16)
    ones1_bf = np.ones((1, 128), np.float32).astype(ml_dtypes.bfloat16)

    in_maps = []
    for i in range(NCORES):
        m = {}
        for r in range(NTILES):
            blk = 8 * r + i
            rows = x[blk * TB: blk * TB + TB]
            bnd = state if blk == 0 else x[blk * TB - 1: blk * TB]
            m[f"x{r}"] = np.concatenate([bnd, rows], axis=0)
            m[f"iota{r}"] = np.broadcast_to(
                np.arange(blk * TB, blk * TB + TB, dtype=np.float32)[None, :],
                (128, TB)).copy()
        fl = np.zeros((128, 2 * NTILES), np.float32)
        for r in range(NTILES):
            f = 0.0 if (8 * r + i) == 0 else 1.0
            fl[:, 2 * r] = f
            fl[:, 2 * r + 1] = 1.0 - f
        m["flags"] = fl
        pm = np.broadcast_to((np.arange(8) < i).astype(np.float32), (128, 8)).copy()
        m["pmask"] = pm
        sh = slice(256 * i, 256 * (i + 1))
        m["l0w_sh"] = np.ascontiguousarray(
            np.concatenate([wrT[sh], wkT[sh], wvT[sh], wgT[sh], woT[sh]], axis=0))
        m["l1w_sh"] = np.ascontiguousarray(
            np.concatenate([l1q[sh], l1k[sh], l1i[sh], l1o[sh]], axis=0))
        m["gu_sh"] = np.ascontiguousarray(gu_tp[128 * i: 128 * (i + 1)])
        m["gu_sc"] = np.ascontiguousarray(
            np.repeat(gsc_p[[2 * i, 2 * i + 1]], 64, axis=0))
        m["dn_sh"] = np.ascontiguousarray(dn_t[352 * i: 352 * (i + 1)])
        m["dn_sc"] = np.ascontiguousarray(dsc[(np.arange(352) + 352 * i) // 64])
        m["w1_sh"] = np.ascontiguousarray(w1T[sh])
        m["w2_sh"] = np.ascontiguousarray(w2T[sh])
        m["w3_sh"] = np.ascontiguousarray(w3T[704 * i: 704 * (i + 1)])
        m["tmix"] = tmix
        m["smallp"] = smallp
        m["wkv0"] = wkv_l
        m["ccol"] = ccol
        m["msk_bf"] = np.asarray(msk_bf)
        m["m_diag_i"] = m_diag_i
        m["aux_bf"] = np.asarray(aux_bf)
        m["e2_bf"] = np.asarray(e2_bf)
        m["ones1_bf"] = np.asarray(ones1_bf)
        m["ident_f32"] = ident
        m["onesf"] = np.ones((128, 16), np.float32)
        in_maps.append(m)
    return in_maps



class Sem:
    def __init__(self, handle):
        self.h = handle
        self.n = 0


class K:
    ENG_SEM = {"vector": "V", "scalar": "S", "gpsimd": "G", "tensor": "T"}

    def __init__(self, nc):
        self.nc = nc
        self.sems = {}
        self.cur = None
        self.seen = {}
        self.tensors = {}

    def new_pass(self, eng):
        self.cur = eng
        self.seen = {}
        for s in self.sems.values():
            s.n = 0

    def op(self, eng, fn, wait=(), inc=None, inc_n=1, ss=True):
        auto = self.ENG_SEM.get(eng)
        if eng in ("vector", "scalar", "gpsimd") and inc is None:
            inc, inc_n = auto, 1
        waits = list(wait)
        if ss and eng in ("vector", "scalar", "gpsimd"):
            pre = self.sems[auto].n
            if pre > 0:
                waits.append((auto, pre))
        if self.cur == eng:
            e = getattr(self.nc, eng)
            for sname, val in waits:
                if val <= 0:
                    continue
                key = (eng, sname)
                if self.seen.get(key, -1) < val:
                    e.wait_ge(self.sems[sname].h, val)
                    self.seen[key] = val
            inst = fn(e)
        else:
            inst = None
        if inc is not None:
            s = self.sems[inc]
            s.n += inc_n
            if inst is not None:
                inst.then_inc(s.h, inc_n)
            return (inc, s.n)
        return None

    def dma(self, eng, out, in_, wait=(), inc="LD", ss=False, **kw):
        return self.op(eng, lambda e: e.dma_start(out=out, in_=in_, **kw),
                       wait=wait, inc=inc, inc_n=16, ss=ss)

    def mm(self, out, lhsT, rhs, start, stop, wait=(), inc=None):
        return self.op("tensor",
                       lambda e: e.matmul(out, lhsT, rhs, start=start, stop=stop),
                       wait=wait, inc=inc, inc_n=1)

    def tr(self, out, in_, ident, wait=(), inc=None):
        return self.op("tensor",
                       lambda e: e.transpose(out, in_, ident),
                       wait=wait, inc=inc, inc_n=1)

    def ev(self, sname):
        return (sname, self.sems[sname].n)


def cc_(k, val):
    return k.tensors["ccol"][:, CIDX[val]:CIDX[val] + 1]


def ccp(k, val, p0, p1):
    return k.tensors["ccol"][p0:p1, CIDX[val]:CIDX[val] + 1]



def build_program():
    nc = bass.Bass("TRN2", target_bir_lowering=False, debug=False,
                   detect_race_conditions=False)
    k = K(nc)
    t = k.tensors
    k.din = {}

    def dram_in(name, shape, dt):
        k.din[name] = nc.dram_tensor(name, shape, dt, kind="ExternalInput").ap()

    for r in range(NTILES):
        dram_in(f"x{r}", [TB + 1, D], F32)
        dram_in(f"iota{r}", [128, TB], F32)
    dram_in("flags", [128, 2 * NTILES], F32)
    dram_in("pmask", [128, 8], F32)
    dram_in("l0w_sh", [5 * 256, D], F32)
    dram_in("l1w_sh", [4 * 256, D], F32)
    dram_in("gu_sh", [128, GU], U8)
    dram_in("gu_sc", [128, GU], F16)
    dram_in("dn_sh", [352, D], U8)
    dram_in("dn_sc", [352, D], F16)
    dram_in("w1_sh", [256, I1], F32)
    dram_in("w2_sh", [256, I1], F32)
    dram_in("w3_sh", [704, D], F32)
    dram_in("tmix", [128, 64], F32)
    dram_in("smallp", [128, 80], F32)
    dram_in("wkv0", [64, D], F32)
    dram_in("ccol", [128, len(CONSTS)], F32)
    dram_in("msk_bf", [128, 384], BF16)
    dram_in("m_diag_i", [128, 128], I32)
    dram_in("aux_bf", [128, 3], BF16)
    dram_in("e2_bf", [2, 128], BF16)
    dram_in("ones1_bf", [1, 128], BF16)
    dram_in("ident_f32", [128, 128], F32)
    dram_in("onesf", [128, 16], F32)

    k.out_hbm = nc.dram_tensor("out", [NTILES * TB, D], F32,
                               kind="ExternalOutput").ap()
    k.dtaps = {}
    for name, shape, dt in DBG["taps"]:
        k.dtaps[name] = nc.dram_tensor("tap " + name, shape, dt,
                                       kind="ExternalOutput").ap()

    woff = {}
    off = 0
    for name in ("l0", "l1", "gul", "guh", "dnl", "dnh", "w1q", "w2q", "w3q"):
        woff[name] = off
        off += SEG_ROWS[name] // (5 if name == "l0" else 4 if name == "l1" else 1) \
            * SEG_W[name] * (5 if name == "l0" else 4 if name == "l1" else 1)
    k.woff = woff
    k.WSH = off
    k.wag_in = nc.dram_tensor("wag_in", [k.WSH], BF16)
    k.wag_out = nc.dram_tensor("wag_out", [NCORES * k.WSH], BF16)
    k.gsum_in = nc.dram_tensor("gsum_in", [1, 8], F32)
    k.gsum_out = nc.dram_tensor("gsum_out", [1, 8], F32)
    k.gtmp = nc.dram_tensor("gtmp", [128, 4], F32)
    k.sag = {}
    for r in range(NTILES):
        k.sag[f"l0i{r}"] = nc.dram_tensor(f"sag_l0i{r}", [64, H0 * K0], F32)
        k.sag[f"l0o{r}"] = nc.dram_tensor(f"sag_l0o{r}", [NCORES * 64, H0 * K0], F32)
        k.sag[f"l1i{r}"] = nc.dram_tensor(f"sag_l1i{r}", [128, H1 * K1 + 16], F32)
        k.sag[f"l1o{r}"] = nc.dram_tensor(f"sag_l1o{r}",
                                          [NCORES * 128, H1 * K1 + 16], F32)
    k.x1scr = nc.dram_tensor("x1scr", [TB, D], F32)
    k.sscr = nc.dram_tensor("sscr", [16, 128, TB], F32)
    k.xnscr = nc.dram_tensor("xnscr", [16, 128, TB + 4], BF16)

    with ExitStack() as ctx:
        for sname in ("LD", "LW", "GD", "V", "S", "T", "G", "CC", "OUT"):
            k.sems[sname] = Sem(ctx.enter_context(nc.semaphore(sname)))
        alloc_tensors(nc, k)
        for v in CONSTS:
            if v not in (0.0, 1.0):
                nc.const_aps.register(F32, v, t["ccol"][:, CIDX[v]:CIDX[v] + 1])
        block = ctx.enter_context(nc.Block())

        def run(eng):
            def f(_):
                k.new_pass(eng)
                walk(k)
            return f
        block.sync(run("sync"))
        block.gpsimd(run("gpsimd"))
        block.vector(run("vector"))
        block.scalar(run("scalar"))
        block.tensor(run("tensor"))
    return nc, k


def alloc_tensors(nc, k):
    t = k.tensors

    def P(name, shape, dt):
        t[name] = nc.alloc_sbuf_tensor("sb_" + name, shape, dt).ap()

    P("ccol", [128, len(CONSTS)], F32)
    P("tmix", [128, 64], F32)
    P("smallp", [128, 80], F32)
    P("flags", [128, 2 * NTILES], F32)
    P("pmask", [128, 8], F32)
    P("msk_bf", [128, 384], BF16)
    P("m_diag_i", [128, 128], I32)
    P("aux_bf", [128, 3], BF16)
    P("e2", [2, 128], BF16)
    P("ones1", [1, 128], BF16)
    P("ident_f32", [128, 128], F32)
    P("onesf", [128, 16], F32)
    P("iota", [128, TB], F32)
    P("etd", [128, 16], F32)
    P("netd", [128, 16], F32)
    P("bwb", [128, 16], F32)
    P("wkv0b", [64, D], BF16)
    P("s0f", [64, D], F32)
    P("s0b", [64, D], BF16)
    P("s0run", [64, D], F32)
    P("s1f", [128, H1 * K1], F32)
    P("s1b", [128, H1 * K1], BF16)
    P("s1run", [128, H1 * K1], F32)
    P("s1lg", [128, 16], F32)
    P("tcol", [128, 64], F32)
    P("trow", [1, 64], F32)
    P("trow_bf", [1, 64], BF16)
    P("ginv", [128, 4], F32)
    P("hbuf", [128, 64], F32)

    k.psum = []
    for b in range(8):
        if b in (4, 5):
            h = nc.alloc_psum_tensor(f"ps{b}", [128, 1024], BF16)
        else:
            h = nc.alloc_psum_tensor(f"ps{b}", [128, 512], F32)
        k.psum.append(h.ap())

    rem = nc.sbuf_bytes_remaining
    base = nc.SBUF_PARTITION_SIZE_BYTES - rem + 128
    base = (base + 31) // 32 * 32
    k.arena_base = base
    k.arena_size = nc.SBUF_PARTITION_SIZE_BYTES - base - 256

    def A(name, shape, dt, offset):
        nbytes = int(np.prod(shape[1:])) * mybir.dt.size(dt)
        assert offset + nbytes <= k.arena_size, (name, offset, nbytes, k.arena_size)
        t[name] = nc.alloc_sbuf_tensor_at("sb_" + name, shape, dt,
                                          offset=base + offset).ap()
        return offset + ((nbytes + 31) // 32 * 32)

    def CL(name, n, shape, dt, offset):
        o = offset
        lst = []
        for i in range(n):
            o = A(f"{name}{i}", shape, dt, o)
            lst.append(t[f"{name}{i}"])
        t[name] = lst
        return o

    R1 = 0
    R2 = CL("xn", 16, [128, TB + 4], BF16, R1)
    R3 = CL("mixy", 16, [128, TB], BF16, R2)
    o = R3
    for nm in ("rf", "kw", "ku", "rw"):
        o = CL(nm, 16, [128, TB], BF16, o)
    R4 = o
    R5 = CL("gsl", 16, [128, TB], BF16, R4)
    R6 = CL("vt", TB // 128, [128, D], BF16, R5)
    R7 = CL("wsl", 16, [128, 512], BF16, R6)
    o = R7
    for nm in ("rrh", "kwh", "kuh", "rwh"):
        o = A(nm, [64, TB], BF16, o)
    for nm in ("attsb", "q2b", "ks2b", "kst"):
        o = A(nm, [128, 128], BF16, o)
    stg_off = o
    o = A("stg", [128, TB], F32, o)
    o = A("stg2", [128, TB], F32, o)
    R8 = o
    o = A("tB", [128, TB], F32, o)
    o = A("tD", [128, 16], F32, o)
    o = A("lgpre", [128, 128], F32, o)
    arena_hi = o
    A("tC", [128, TB], F32, stg_off)
    assert arena_hi <= k.arena_size, (arena_hi, k.arena_size)

    o = 0
    for nm, shape, dt in (("wp_u8", [128, 2816], U8), ("wp_sc", [128, 2816], F16),
                          ("wp_iA", [128, I1], I32), ("wp_iB", [128, 2816], I32),
                          ("wp_iC", [128, 2816], I32), ("wp_fA", [128, I1], F32),
                          ("wp_fB", [128, 2816], F32), ("wp_fC", [128, 2816], F32),
                          ("wp_fD", [128, 2816], F32), ("wp_f32", [128, I1], F32),
                          ("wp_bf", [128, I1], BF16)):
        o = A(nm, shape, dt, o)
    o = A("wkv0f", [64, D], F32, o)
    assert o <= k.arena_size
    o = A("xtm", [128, D], F32, R5)
    A("xsq", [128, D], F32, o)
    A("xntm", [128, D], BF16, R7 + 4096 + 1024)
    CL("xnil", 16, [128, TB], BF16, R2)
    CL("hh", 44, [128, TB], BF16, R3)
    o = CL("dd", 8, [128, TB], F32, R4)
    CL("ddx", 8, [128, TB], F32, o)
    t["dd"] = t["dd"] + t["ddx"]
    CL("wsl2", 16, [128, 512], BF16, R3 + 44 * 1024)
    o = R3
    for nm in ("qt", "kd", "ks"):
        o = CL(nm, 16, [128, TB], BF16, o)
    for nm in ("kraw", "sgb", "gbv", "abuf"):
        o = A(nm, [128, TB], F32, o)
    assert o <= R4, (o, R4)
    CL("hid", 44, [128, TB], BF16, R3)
    o = CL("fo", 8, [128, TB], F32, R4)
    CL("fox", 8, [128, TB], F32, o)
    t["fo"] = t["fo"] + t["fox"]
    A("cxa", [128, D], F32, R2)
    A("cxb", [128, D], F32, R2 + 8192)
    A("csq", [128, D], F32, R3)
    A("s0in", [64, D], F32, R6)
    A("s0inb", [64, D], BF16, R6 + 8192)
    A("s1in", [128, H1 * K1], F32, R6)
    A("s1inb", [128, H1 * K1], BF16, R6 + 8192)
    CL("kwt", TB // 128, [128, D], BF16, R6)
    A("gnhl", [2, 4 * 512], BF16, R7)
    A("rmshl", [1, 2 * 512], BF16, R7)
    k.regions = dict(R1=R1, R2=R2, R3=R3, R4=R4, R5=R5, R6=R6, R7=R7, R8=R8,
                     hi=arena_hi)


def wag_seg(k, elt0, p, w):
    return k.wag_in.ap()[elt0: elt0 + p * w].rearrange("(p w) -> p w", w=w)


def wago(k, core, segoff, p, w):
    elt0 = core * k.WSH + segoff
    return k.wag_out.ap()[elt0: elt0 + p * w].rearrange("(p w) -> p w", w=w)


def load_wslab(k, dst, seg, row0, nrows, col0, width, wait=(), inc="LW"):
    rpc = SEG_ROWS[seg]
    w = SEG_W[seg]
    ev = None
    done = 0
    while done < nrows:
        row = row0 + done
        core = row // rpc
        rin = row % rpc
        take = min(nrows - done, rpc - rin)
        src = wago(k, core, k.woff[seg] + rin * w, take, w)[:, col0:col0 + width]
        ev = k.dma("sync", dst[done:done + take, 0:width], src,
                   wait=wait if done == 0 else (), inc=inc)
        done += take
    return ev



def weight_prep(k):
    nc, t, din, woff = k.nc, k.tensors, k.din, k.woff
    zout = k.ev("OUT")
    prev_cv = zout

    if "casts" in DBG["parts"]:
        for mi in range(9):
            src = din["l0w_sh"] if mi < 5 else din["l1w_sh"]
            row0 = (mi if mi < 5 else mi - 5) * 256
            segoff = (woff["l0"] + mi * 256 * D) if mi < 5 else (
                woff["l1"] + (mi - 5) * 256 * D)
            for chn in range(2):
                ld = k.dma("sync", t["wp_f32"][:, 0:D],
                           src[row0 + 128 * chn: row0 + 128 * (chn + 1), :],
                           wait=[prev_cv], inc="LW")
                cv = k.op("vector",
                          lambda e: e.tensor_copy(out=t["wp_bf"][:, 0:D],
                                                  in_=t["wp_f32"][:, 0:D]),
                          wait=[ld, k.ev("GD")])
                prev_cv = cv
                k.dma("gpsimd", wag_seg(k, segoff + 128 * chn * D, 128, D),
                      t["wp_bf"][:, 0:D], wait=[cv], inc="GD")

    if "gamma" in DBG["parts"]:
        for g, (nm, rows, cols) in enumerate(
                (("w1_sh", 256, I1), ("w2_sh", 256, I1), ("w3_sh", 704, D))):
            nch = (rows + 127) // 128
            for chn in range(nch):
                p = min(128, rows - 128 * chn)
                ld = k.dma("sync", t["wp_f32"][0:p, 0:cols],
                           din[nm][128 * chn: 128 * chn + p, :],
                           wait=[prev_cv, k.ev("V")], inc="LW")
                nseg = (cols + 511) // 512
                for sg in range(nseg):
                    s0, s1 = 512 * sg, min(cols, 512 * (sg + 1))
                    k.op("vector",
                         lambda e, p=p, s0=s0, s1=s1, sg=sg:
                         e.reduce_sum(out=t["tcol"][0:p, sg:sg + 1],
                                      in_=t["wp_f32"][0:p, s0:s1], axis=AX,
                                      apply_absolute_value=True),
                         wait=[ld])
                dst = 16 + g if chn == 0 else 24 + g
                prev_cv = k.op("vector",
                               lambda e, p=p, nseg=nseg, dst=dst:
                               e.reduce_sum(out=t["tcol"][0:p, dst:dst + 1],
                                            in_=t["tcol"][0:p, 0:nseg], axis=AX))
                if chn > 0:
                    prev_cv = k.op("vector",
                                   lambda e, p=p, g=g:
                                   e.tensor_tensor(
                                       out=t["tcol"][0:p, 16 + g:17 + g],
                                       in0=t["tcol"][0:p, 16 + g:17 + g],
                                       in1=t["tcol"][0:p, 24 + g:25 + g],
                                       op=ALU.add))
        gv = k.ev("V")
        gd = k.dma("gpsimd", k.gtmp.ap()[:, 0:3], t["tcol"][:, 16:19], wait=[gv],
                   inc="GD")
        gl = None
        for g in range(3):
            gl = k.dma("sync", t["wp_fA"][0:1, 128 * g:128 * (g + 1)],
                       k.gtmp.ap()[:, g:g + 1], wait=[gd], inc="LW",
                       allow_slow_non_contiguous=True)
        for g in range(3):
            k.op("vector",
                 lambda e, g=g: e.reduce_sum(
                     out=t["trow"][0:1, g:g + 1],
                     in_=t["wp_fA"][0:1, 128 * g:128 * (g + 1)], axis=AX),
                 wait=[gl])
        gv2 = k.ev("V")
        gd2 = k.dma("gpsimd", k.gsum_in.ap()[0:1, 0:8], t["trow"][0:1, 0:8],
                    wait=[gv2], inc="GD")
        ccr = k.op("gpsimd",
                   lambda e: e.collective_compute(
                       "AllReduce", ALU.add, replica_groups=[list(range(NCORES))],
                       ins=[k.gsum_in.ap().opt()], outs=[k.gsum_out.ap().opt()]),
                   wait=[gd2], inc="CC")
        gl2 = k.dma("sync", t["trow"][0:1, 8:11], k.gsum_out.ap()[0:1, 0:3],
                    wait=[ccr], inc="LW")
        k.op("vector",
             lambda e: e.tensor_scalar(out=t["trow"][0:1, 12:15],
                                       in0=t["trow"][0:1, 8:11],
                                       scalar1=1.0 / (I1 * D), scalar2=None,
                                       op0=ALU.mult),
             wait=[gl2])
        k.op("vector", lambda e: e.tensor_copy(out=t["trow_bf"][0:1, 0:3],
                                               in_=t["trow"][0:1, 12:15]))
        k.op("vector", lambda e: e.tensor_copy(out=t["trow"][0:1, 16:19],
                                               in_=t["trow_bf"][0:1, 0:3]))
        k.op("vector", lambda e: e.tensor_tensor(out=t["trow"][0:1, 20:23],
                                                 in0=t["trow"][0:1, 12:15],
                                                 in1=t["trow"][0:1, 16:19],
                                                 op=ALU.subtract))
        hv = k.op("vector", lambda e: e.tensor_copy(out=t["trow_bf"][0:1, 8:11],
                                                    in_=t["trow"][0:1, 20:23]))
        k.mm(k.psum[6][:, 0:3], t["ones1"][0:1, :], t["trow_bf"][0:1, 0:3],
             True, False, wait=[hv, k.ld_small])
        mm2 = k.mm(k.psum[6][:, 0:3], t["ones1"][0:1, :], t["trow_bf"][0:1, 8:11],
                   False, True, inc="T")
        ad = k.op("vector",
                  lambda e: e.tensor_scalar(out=t["ginv"][:, 0:3],
                                            in0=k.psum[6][:, 0:3],
                                            scalar1=1e-8, scalar2=None,
                                            op0=ALU.add),
                  wait=[mm2])
        k.ginv_ev = k.op("vector",
                         lambda e: e.reciprocal(out=t["ginv"][:, 0:3],
                                                in_=t["ginv"][:, 0:3]),
                         wait=[ad])

    if "tern" in DBG["parts"]:
        gi = k.ginv_ev
        for g, (nm, rows, cols, seg) in enumerate(
                (("w1_sh", 256, I1, "w1q"), ("w2_sh", 256, I1, "w2q"),
                 ("w3_sh", 704, D, "w3q"))):
            nch = (rows + 127) // 128
            for chn in range(nch):
                p = min(128, rows - 128 * chn)
                ld = k.dma("sync", t["wp_f32"][0:p, 0:cols],
                           din[nm][128 * chn: 128 * chn + p, :],
                           wait=[k.ev("S"), k.ev("V")], inc="LW")
                s1 = k.op("scalar",
                          lambda e, p=p, cols=cols, g=g:
                          e.mul(out=t["wp_fA"][0:p, 0:cols],
                                in_=t["wp_f32"][0:p, 0:cols],
                                mul=t["ginv"][0:p, g:g + 1]),
                          wait=[ld, gi])
                k.op("vector",
                     lambda e, p=p, cols=cols:
                     e.tensor_scalar(out=t["wp_f32"][0:p, 0:cols],
                                     in0=t["wp_fA"][0:p, 0:cols],
                                     scalar1=0.5, scalar2=None,
                                     op0=ALU.is_gt),
                     wait=[s1, k.ev("GD")])
                k.op("vector",
                     lambda e, p=p, cols=cols:
                     e.tensor_scalar(out=t["wp_fA"][0:p, 0:cols],
                                     in0=t["wp_fA"][0:p, 0:cols],
                                     scalar1=-0.5, scalar2=None,
                                     op0=ALU.is_lt))
                r3 = k.op("vector",
                          lambda e, p=p, cols=cols:
                          e.tensor_tensor(out=t["wp_bf"][0:p, 0:cols],
                                          in0=t["wp_f32"][0:p, 0:cols],
                                          in1=t["wp_fA"][0:p, 0:cols],
                                          op=ALU.subtract))
                k.dma("gpsimd",
                      wag_seg(k, woff[seg] + 128 * chn * cols, p, cols),
                      t["wp_bf"][0:p, 0:cols], wait=[r3], inc="GD")

    if "fp4" in DBG["parts"]:
        fp4_decode(k, "gu_sh", "gu_sc", 128, GU, 2816, ("gul", "guh"))
        fp4_decode(k, "dn_sh", "dn_sc", 352, D, 2048, ("dnl", "dnh"))

    aw = k.ev("GD")
    k.wag_done = k.op("gpsimd",
                      lambda e: e.collective_compute(
                          "AllGather", ALU.bypass,
                          replica_groups=[list(range(NCORES))],
                          ins=[k.wag_in.ap().opt()],
                          outs=[k.wag_out.ap().opt()]),
                      wait=[aw], inc="CC")


def fp4_decode(k, bname, scname, rows, cols, W, segs):
    t, din, woff = k.tensors, k.din, k.woff
    nch = (rows + 127) // 128
    for chn in range(nch):
        p = min(128, rows - 128 * chn)
        for c0 in range(0, cols, W):
            w = min(W, cols - c0)
            ldb = k.dma("sync", t["wp_u8"][0:p, 0:w],
                        din[bname][128 * chn: 128 * chn + p, c0:c0 + w],
                        wait=[k.ev("V"), k.ev("S")], inc="LW")
            lds = k.dma("sync", t["wp_sc"][0:p, 0:w],
                        din[scname][128 * chn: 128 * chn + p, c0:c0 + w], inc="LW")
            k.op("vector",
                 lambda e, p=p, w=w: e.tensor_copy(
                     out=t["wp_iA"][0:p, 0:w], in_=t["wp_u8"][0:p, 0:w]),
                 wait=[ldb, lds])
            for par, seg in enumerate(segs):
                if par == 0:
                    k.op("vector",
                         lambda e, p=p, w=w: e.tensor_scalar(
                             out=t["wp_iB"][0:p, 0:w], in0=t["wp_iA"][0:p, 0:w],
                             scalar1=15, scalar2=None, op0=ALU.bitwise_and))
                else:
                    k.op("vector",
                         lambda e, p=p, w=w: e.tensor_scalar(
                             out=t["wp_iB"][0:p, 0:w], in0=t["wp_iA"][0:p, 0:w],
                             scalar1=4, scalar2=None,
                             op0=ALU.logical_shift_right))
                k.op("vector", lambda e, p=p, w=w: e.tensor_scalar(
                    out=t["wp_iC"][0:p, 0:w], in0=t["wp_iB"][0:p, 0:w],
                    scalar1=1, scalar2=3, op0=ALU.logical_shift_right,
                    op1=ALU.bitwise_and))
                ev_ = k.op("vector", lambda e, p=p, w=w: e.tensor_copy(
                    out=t["wp_fA"][0:p, 0:w], in_=t["wp_iC"][0:p, 0:w]))
                t2 = k.op("scalar", lambda e, p=p, w=w: e.activation(
                    out=t["wp_fB"][0:p, 0:w], in_=t["wp_fA"][0:p, 0:w],
                    func=AF.Exp, scale=LN2, bias=-2.0 * LN2),
                    wait=[ev_])
                k.op("vector", lambda e, p=p, w=w: e.tensor_scalar(
                    out=t["wp_iC"][0:p, 0:w], in0=t["wp_iB"][0:p, 0:w],
                    scalar1=1, scalar2=None, op0=ALU.bitwise_and))
                k.op("vector", lambda e, p=p, w=w: e.tensor_copy(
                    out=t["wp_fC"][0:p, 0:w], in_=t["wp_iC"][0:p, 0:w]))
                k.op("vector", lambda e, p=p, w=w: e.scalar_tensor_tensor(
                    out=t["wp_fD"][0:p, 0:w], in0=t["wp_fC"][0:p, 0:w],
                    scalar=2.0, in1=t["wp_fB"][0:p, 0:w],
                    op0=ALU.add, op1=ALU.mult), wait=[t2])
                k.op("vector", lambda e, p=p, w=w: e.tensor_scalar(
                    out=t["wp_fA"][0:p, 0:w], in0=t["wp_fA"][0:p, 0:w],
                    scalar1=8.0, scalar2=None, op0=ALU.mult))
                k.op("vector", lambda e, p=p, w=w: e.scalar_tensor_tensor(
                    out=t["wp_fA"][0:p, 0:w], in0=t["wp_fC"][0:p, 0:w],
                    scalar=0.5, in1=t["wp_fA"][0:p, 0:w],
                    op0=ALU.mult, op1=ALU.add))
                k.op("vector", lambda e, p=p, w=w: e.tensor_tensor(
                    out=t["wp_fD"][0:p, 0:w], in0=t["wp_fD"][0:p, 0:w],
                    in1=t["wp_fA"][0:p, 0:w], op=ALU.min))
                k.op("vector", lambda e, p=p, w=w: e.tensor_scalar(
                    out=t["wp_iC"][0:p, 0:w], in0=t["wp_iB"][0:p, 0:w],
                    scalar1=3, scalar2=None, op0=ALU.logical_shift_right))
                k.op("vector", lambda e, p=p, w=w: e.tensor_copy(
                    out=t["wp_fC"][0:p, 0:w], in_=t["wp_iC"][0:p, 0:w]))
                k.op("vector", lambda e, p=p, w=w: e.tensor_scalar(
                    out=t["wp_fC"][0:p, 0:w], in0=t["wp_fC"][0:p, 0:w],
                    scalar1=-2.0, scalar2=1.0, op0=ALU.mult, op1=ALU.add))
                k.op("vector", lambda e, p=p, w=w: e.tensor_tensor(
                    out=t["wp_fD"][0:p, 0:w], in0=t["wp_fD"][0:p, 0:w],
                    in1=t["wp_fC"][0:p, 0:w], op=ALU.mult))
                sv = k.op("vector", lambda e, p=p, w=w: e.tensor_tensor(
                    out=t["wp_bf"][0:p, 0:w], in0=t["wp_fD"][0:p, 0:w],
                    in1=t["wp_sc"][0:p, 0:w], op=ALU.mult),
                    wait=[k.ev("GD")])
                k.dma("gpsimd",
                      wag_seg(k, woff[seg] + (128 * chn) * cols, p,
                              cols)[:, c0:c0 + w],
                      t["wp_bf"][0:p, 0:w], wait=[sv], inc="GD")



def walk(k):
    t, din = k.tensors, k.din

    for name, src in [("ccol", "ccol"), ("tmix", "tmix"), ("smallp", "smallp"),
                      ("flags", "flags"), ("pmask", "pmask"),
                      ("msk_bf", "msk_bf"), ("m_diag_i", "m_diag_i"),
                      ("aux_bf", "aux_bf"), ("e2", "e2_bf"),
                      ("ones1", "ones1_bf"), ("ident_f32", "ident_f32"),
                      ("onesf", "onesf"), ("wkv0f", "wkv0")]:
        k.dma("sync", t[name][:, :], din[src][:, :])
    k.ld_small = k.ev("LD")

    for b in range(8):
        k.op("scalar", lambda e, b=b: e.memzero(k.psum[b][:, :]))
    k.op("scalar", lambda e: e.activation(out=t["etd"][:, :],
                                          in_=t["smallp"][:, 0:16], func=AF.Exp),
         wait=[k.ld_small])
    se = k.ev("S")
    k.op("vector", lambda e: e.tensor_scalar(out=t["netd"][:, :],
                                             in0=t["etd"][:, :], scalar1=-1.0,
                                             scalar2=None, op0=ALU.mult),
         wait=[se])
    k.op("vector", lambda e: e.tensor_scalar(out=t["bwb"][:, :],
                                             in0=t["etd"][:, :], scalar1=-16383.0,
                                             scalar2=None, op0=ALU.mult))
    k.op("vector", lambda e: e.tensor_copy(out=t["wkv0b"][:, :],
                                           in_=t["wkv0f"][:, :]),
         wait=[k.ld_small])
    k.op("vector", lambda e: e.memzero(t["s0run"][:, :]))
    k.op("vector", lambda e: e.memzero(t["s1run"][:, :]))
    k.op("vector", lambda e: e.memzero(t["trow"][0:1, 0:16]))

    weight_prep(k)

    for r in range(DBG["ntiles"]):
        tile(k, r)

    emit_taps(k)

    fin = k.ev("OUT")
    k.op("gpsimd", lambda e: e.nop(), wait=[fin, k.wag_done], inc="G")


def emit_taps(k):
    t = k.tensors
    allq = [k.ev("V"), k.ev("S"), k.ev("T"), k.ev("G"), k.wag_done,
            k.ev("GD"), k.ev("LD"), k.ev("LW"), k.ev("CC")]
    for name, ap in k.dtaps.items():
        parts = name.split(":")
        if parts[0] == "sbuf":
            _, tname, p, w = parts
            p, w = int(p), int(w)
            k.dma("gpsimd", ap[:, :], t[tname][0:p, 0:w], wait=allq, inc="OUT")
        elif parts[0] == "psum":
            _, idx, p, w = parts
            idx, p, w = int(idx), int(p), int(w)
            cpe = k.op("vector", lambda e, idx=idx, p=p, w=w: e.tensor_copy(
                out=t["xtm"][0:p, 0:w], in_=k.psum[idx][0:p, 0:w]),
                wait=allq + [k.ev("OUT")])
            k.dma("gpsimd", ap[:, :], t["xtm"][0:p, 0:w], wait=[cpe],
                  inc="OUT")
        elif parts[0] == "wag":
            _, seg, core, row0, p, w = parts
            core, row0, p, w = int(core), int(row0), int(p), int(w)
            src = wago(k, core, k.woff[seg] + row0 * w, p, w)
            k.dma("gpsimd", ap[:, :], src, wait=[k.wag_done], inc="OUT")
        elif parts[0] == "hbm":
            _, tn, row0, p, w = parts
            row0, p, w = int(row0), int(p), int(w)
            src = getattr(k, tn)
            if callable(getattr(src, "ap", None)):
                src = src.ap()
            src = src.rearrange("a b c -> (a b) c") if len(src.shape) == 3 else src
            k.dma("gpsimd", ap[:, :], src[row0:row0 + p, 0:w], wait=allq,
                  inc="OUT")





def qslab(k, dst, seg, row0, col0, width, nrows=128, wait=(), inc="LW",
          eng="sync"):
    rpc = SEG_ROWS[seg]
    w = SEG_W[seg]
    ev = None
    done = 0
    while done < nrows:
        row = row0 + done
        core, rin = divmod(row, rpc)
        take = min(nrows - done, rpc - rin)
        src = wago(k, core, k.woff[seg] + rin * w, take, w)
        ev = k.dma(eng, dst[done:done + take, 0:width],
                   src[:, col0:col0 + width],
                   wait=wait if done == 0 else (), inc=inc)
        done += take
    return ev


def attn_slab(k, dst, layer, mi, kk, col0, width, wait=(), inc="LW",
              eng="sync"):
    row = 128 * kk
    core, rin = divmod(row, 256)
    src = wago(k, core, k.woff[layer] + (mi * 256 + rin) * D, 128, D)
    return k.dma(eng, dst[:, 0:width], src[:, col0:col0 + width],
                 wait=wait, inc=inc)


def barrier(k):
    evs = [k.ev(s) for s in ("V", "S", "T", "G", "LD", "LW", "GD", "OUT", "CC")]
    for eng in ("sync", "gpsimd", "vector", "scalar", "tensor"):
        k.op(eng, lambda e: e.nop(), wait=evs, ss=False)
    return evs


def proj2048(k, layer, mi, rhs_chunks, out_cb, rhs_ev=None, n_out=16):
    t = k.tensors
    m3 = None
    for mg in range((n_out + 3) // 4):
        evs = []
        for kk in range(16):
            if m3 is None:
                w8 = [k.slab_free] if kk <= 1 else ()
            else:
                w8 = [m3[kk]]
            evs.append(attn_slab(k, t["wsl"][kk], layer, mi, kk, 512 * mg, 512,
                                 wait=w8, eng="gpsimd" if kk % 2 else "sync",
                                 inc="GD" if kk % 2 else "LW"))
        evall = [evs[-2], evs[-1]]
        new_m3 = [None] * 16
        last = None
        for m in range(4):
            oc = 4 * mg + m
            if oc >= n_out:
                break
            bank = k.psum[m]
            for kk in range(16):
                w = list(evall) if kk == 0 else []
                if kk == 0:
                    w.append(k.bank_ev[m])
                    if rhs_ev is not None:
                        w.append(rhs_ev)
                last = k.mm(bank[:, 0:TB], t["wsl"][kk][:, 128 * m:128 * m + 128],
                            rhs_chunks[kk][:, 0:TB],
                            start=(kk == 0), stop=(kk == 15),
                            wait=w,
                            inc="T" if (kk == 15 or m == 3) else None)
                if m == 3:
                    new_m3[kk] = last
            out_cb(oc, bank, last)
        if new_m3[15] is not None:
            m3 = new_m3
        if last is not None:
            k.slab_free = last


def ln_row_chunk(k, r, tc, src_rows, dst_writer):
    t, din = k.tensors, k.din
    ld = k.dma("sync", t["xtm"][:, :], din[f"x{r}"][src_rows:src_rows + 128, :],
               wait=[k.ev("S"), k.ev("V")])
    k.op("scalar", lambda e: e.activation(out=t["xsq"][:, :], in_=t["xtm"][:, :],
                                          func=AF.Square), wait=[ld])
    sq = k.ev("S")
    k.op("vector", lambda e: e.reduce_sum(out=t["hbuf"][:, 0:1], in_=t["xtm"][:, :],
                                          axis=AX), wait=[ld])
    k.op("vector", lambda e: e.reduce_sum(out=t["hbuf"][:, 1:2], in_=t["xsq"][:, :],
                                          axis=AX), wait=[sq])
    k.op("vector", lambda e: e.tensor_scalar(out=t["hbuf"][:, 2:3],
                                             in0=t["hbuf"][:, 0:1],
                                             scalar1=1.0 / 2048.0, scalar2=None,
                                             op0=ALU.mult))
    k.op("vector", lambda e: e.tensor_scalar(out=t["hbuf"][:, 3:4],
                                             in0=t["hbuf"][:, 1:2],
                                             scalar1=1.0 / 2048.0, scalar2=None,
                                             op0=ALU.mult))
    k.op("vector", lambda e: e.tensor_tensor(out=t["hbuf"][:, 4:5],
                                             in0=t["hbuf"][:, 2:3],
                                             in1=t["hbuf"][:, 2:3], op=ALU.mult))
    vv = k.op("vector", lambda e: e.tensor_tensor(out=t["hbuf"][:, 4:5],
                                                  in0=t["hbuf"][:, 3:4],
                                                  in1=t["hbuf"][:, 4:5],
                                                  op=ALU.subtract))
    k.op("scalar", lambda e: e.activation(out=t["hbuf"][:, 5:6],
                                          in_=t["hbuf"][:, 4:5], func=AF.Ln,
                                          bias=1e-5), wait=[vv])
    k.op("scalar", lambda e: e.activation(out=t["hbuf"][:, 5:6],
                                          in_=t["hbuf"][:, 5:6], func=AF.Exp,
                                          scale=-0.5))
    sr = k.ev("S")
    k.op("vector", lambda e: e.tensor_tensor(out=t["hbuf"][:, 6:7],
                                             in0=t["hbuf"][:, 2:3],
                                             in1=t["hbuf"][:, 5:6], op=ALU.mult),
         wait=[sr])
    nb = k.op("vector", lambda e: e.tensor_scalar(out=t["hbuf"][:, 6:7],
                                                  in0=t["hbuf"][:, 6:7],
                                                  scalar1=-1.0, scalar2=None,
                                                  op0=ALU.mult))
    xe = k.op("scalar", lambda e: e.activation(out=t["xntm"][:, :],
                                               in_=t["xtm"][:, :],
                                               func=AF.Identity,
                                               scale=t["hbuf"][:, 5:6],
                                               bias=t["hbuf"][:, 6:7]),
              wait=[nb, k.ev("T")])
    dst_writer(tc, xe)


def transpose_to_fm(k, src_ev, src_ap_fn, dst_fn):
    t = k.tensors
    for half in range(2):
        bank = k.psum[4 + half]
        tev = None
        for j in range(8):
            fc = 8 * half + j
            w = [k.tp_free[half]] if j == 0 else []
            if j == 0 and src_ev is not None:
                w.append(src_ev)
            tev = k.tr(bank[:, 128 * j:128 * j + 128],
                       src_ap_fn(fc), t["msk_bf"][:, 0:128],
                       wait=w, inc="T" if j == 7 else None)
        ev = None
        for j in range(8):
            fc = 8 * half + j
            ev = dst_fn(fc, bank[:, 128 * j:128 * j + 128],
                        [tev] if j == 0 else ())
        k.tp_free[half] = ev





def l0_attn(k, r, xn_ev):
    t = k.tensors
    cut = DBG.get("l0cut", 99)
    if cut <= 0:
        return

    def build_mix(mi, wait0):
        ev = None
        for fc in range(16):
            k.op("vector", lambda e, fc=fc: e.tensor_tensor(
                out=t["tB"][:, 0:TB], in0=t["xn"][fc][:, 1:1 + TB],
                in1=t["xn"][fc][:, 0:TB], op=ALU.subtract),
                wait=list(wait0) if fc == 0 else ())
            ev = k.op("vector", lambda e, fc=fc, mi=mi: e.scalar_tensor_tensor(
                out=t["mixy"][fc][:, 0:TB], in0=t["tB"][:, 0:TB],
                scalar=t["tmix"][:, 16 * mi + fc:16 * mi + fc + 1],
                in1=t["xn"][fc][:, 0:TB], op0=ALU.mult, op1=ALU.add))
        return ev

    iota_ev = k.iota_ev
    mev = build_mix(0, [xn_ev])
    if cut <= 0.2:
        return
    prev_read = [k.ev("V")]

    def cb_r(oc, bank, mmev):
        k.op("vector", lambda e, oc=oc, bank=bank: e.tensor_copy(
            out=t["rf"][oc][:, 0:TB], in_=bank[:, 0:TB]), wait=[mmev])
        wf = k.op("scalar", lambda e, oc=oc: e.activation(
            out=t["tC"][:, 0:TB], in_=t["iota"][:, 0:TB], func=AF.Exp,
            scale=t["netd"][:, oc:oc + 1]),
            wait=[iota_ev, prev_read[0]])
        ev = k.op("vector", lambda e, bank=bank, oc=oc: e.tensor_tensor(
            out=t["rw"][oc][:, 0:TB], in0=bank[:, 0:TB],
            in1=t["tC"][:, 0:TB], op=ALU.mult), wait=[wf])
        prev_read[0] = ev
        k.bank_ev[oc % 4] = ev
    proj2048(k, "l0", 0, t["mixy"], cb_r, rhs_ev=mev)
    if cut <= 0.4:
        return

    mev = build_mix(1, [k.ev("T")])
    if cut <= 0.5:
        return
    kvar = DBG.get("kvar", 0)

    def cb_k(oc, bank, mmev):
        if kvar == 1:
            ev = k.op("vector", lambda e, oc=oc, bank=bank: e.tensor_copy(
                out=t["kw"][oc][:, 0:TB], in_=bank[:, 0:TB]), wait=[mmev])
            k.bank_ev[oc % 4] = ev
            return
        if kvar == 3:
            k.op("vector", lambda e, oc=oc, bank=bank: e.tensor_copy(
                out=t["kw"][oc][:, 0:TB], in_=bank[:, 0:TB]), wait=[mmev])
            ev = k.op("scalar", lambda e, bank=bank, oc=oc: e.mul(
                out=t["ku"][oc][:, 0:TB], in_=bank[:, 0:TB],
                mul=t["smallp"][:, 16 + oc:17 + oc]),
                wait=[mmev])
            k.bank_ev[oc % 4] = ev
            return
        wb = k.op("scalar", lambda e, oc=oc: e.activation(
            out=t["tC"][:, 0:TB], in_=t["iota"][:, 0:TB], func=AF.Exp,
            scale=t["etd"][:, oc:oc + 1], bias=t["bwb"][:, oc:oc + 1]),
            wait=[prev_read[0]])
        tte = k.op("vector", lambda e, bank=bank, oc=oc: e.tensor_tensor(
            out=t["kw"][oc][:, 0:TB], in0=bank[:, 0:TB],
            in1=t["tC"][:, 0:TB], op=ALU.mult), wait=[wb, mmev])
        prev_read[0] = tte
        if kvar == 2:
            k.bank_ev[oc % 4] = tte
            return
        ev = k.op("scalar", lambda e, bank=bank, oc=oc: e.mul(
            out=t["ku"][oc][:, 0:TB], in_=bank[:, 0:TB],
            mul=t["smallp"][:, 16 + oc:17 + oc]),
            wait=[mmev, tte])
        k.bank_ev[oc % 4] = ev
    proj2048(k, "l0", 1, t["mixy"], cb_k, rhs_ev=mev)
    if cut <= 0.6:
        return

    mev = build_mix(2, [k.ev("T")])
    m3v = None
    for mg in range(4):
        evs = []
        for kk in range(16):
            if m3v is None:
                w8 = [k.slab_free] if kk <= 1 else ()
            else:
                w8 = [m3v[kk]]
            evs.append(attn_slab(k, t["wsl"][kk], "l0", 2, kk, 512 * mg, 512,
                                 wait=w8, eng="gpsimd" if kk % 2 else "sync",
                                 inc="GD" if kk % 2 else "LW"))
        evall = [evs[-2], evs[-1]]
        new_m3 = [None] * 16
        last = None
        for tc in range(TB // 128):
            bank = k.psum[tc]
            for kk in range(16):
                w = list(evall) if kk == 0 else []
                if kk == 0:
                    w.append(k.bank_ev[tc])
                    w.append(mev)
                last = k.mm(bank[:, 0:512],
                            t["mixy"][kk][:, 128 * tc:128 * tc + 128],
                            t["wsl"][kk][:, 0:512],
                            start=(kk == 0), stop=(kk == 15), wait=w,
                            inc="T" if (kk == 15 or tc == 3) else None)
                if tc == 3:
                    new_m3[kk] = last
            ev = k.op("vector", lambda e, tc=tc, mg=mg, bank=bank: e.tensor_copy(
                out=t["vt"][tc][:, 512 * mg:512 * mg + 512], in_=bank[:, 0:512]),
                wait=[last])
            k.bank_ev[tc] = ev
        m3v = new_m3
        k.slab_free = last
    if cut <= 0.8:
        return

    mev = build_mix(3, [k.ev("T")])

    def cb_g(oc, bank, mmev):
        ev = k.op("scalar", lambda e, oc=oc, bank=bank: e.activation(
            out=t["gsl"][oc][:, 0:TB], in_=bank[:, 0:TB], func=AF.Silu),
            wait=[mmev])
        k.bank_ev[oc % 4] = ev
    proj2048(k, "l0", 3, t["mixy"], cb_g, rhs_ev=mev)

    scan_t0 = k.ev("T")
    for tc in range(TB // 128):
        def dst(fc, psrc, w, tc=tc):
            return k.op("vector",
                        lambda e, fc=fc: e.tensor_copy(
                            out=t["kwt"][tc][:, 128 * fc:128 * fc + 128],
                            in_=psrc), wait=w)
        transpose_to_fm(k, scan_t0 if tc == 0 else None,
                        lambda fc, tc=tc: t["kw"][fc][:, 128 * tc:128 * tc + 128],
                        dst)

    k.op("vector", lambda e: e.memzero(t["s0f"][:, :]))
    z0 = k.op("vector", lambda e: e.memzero(t["s0b"][:, :]))
    if DBG.get("l0cut", 99) <= 1:
        return

    NCH = TB // 128
    att_free = k.ev("V")
    ob_free = [k.ev("V"), k.ev("V")]
    kv_free = k.ev("V")
    head_bufs_free = k.ev("T")
    k.attsb_free = k.ev("T")
    s0b_ev = z0
    for h in range(H0):
        hc, hp = h // 2, h % 2
        k.op("vector", lambda e, hc=hc, hp=hp: e.tensor_copy(
            out=t["rrh"][:, 0:TB], in_=t["rf"][hc][64 * hp:64 * hp + 64, 0:TB]),
            wait=[head_bufs_free])
        k.op("vector", lambda e, hc=hc, hp=hp: e.tensor_copy(
            out=t["kwh"][:, 0:TB], in_=t["kw"][hc][64 * hp:64 * hp + 64, 0:TB]))
        k.op("vector", lambda e, hc=hc, hp=hp: e.tensor_copy(
            out=t["kuh"][:, 0:TB], in_=t["ku"][hc][64 * hp:64 * hp + 64, 0:TB]))
        cpe = k.op("vector", lambda e, hc=hc, hp=hp: e.tensor_copy(
            out=t["rwh"][:, 0:TB], in_=t["rw"][hc][64 * hp:64 * hp + 64, 0:TB]))
        ob = k.psum[h % 2]
        olast = None
        for c in range(NCH):
            sl = slice(128 * c, 128 * c + 128)
            k.mm(k.psum[6][:, 0:128], t["kwh"][:, sl], t["rrh"][:, sl],
                 True, True, wait=[cpe, att_free])
            a2 = k.mm(k.psum[6][:, 128:256], t["kuh"][:, sl], t["rrh"][:, sl],
                      True, True, inc="T")
            k.op("vector", lambda e: e.tensor_tensor(
                out=t["attsb"][:, :], in0=k.psum[6][:, 0:128],
                in1=t["msk_bf"][:, 128:256], op=ALU.mult),
                wait=[a2, k.attsb_free])
            m2 = k.op("vector", lambda e: e.copy_predicated(
                out=t["attsb"][:, :], mask=t["m_diag_i"][:, :],
                data=k.psum[6][:, 128:256]))
            att_free = m2
            w1 = [m2]
            if c == 0:
                w1.append(ob_free[h % 2])
            k.attsb_free = k.mm(ob[0:64, sl], t["vt"][c][:, 64 * h:64 * h + 64],
                                t["attsb"][:, :], True, False, wait=w1, inc="T")
            k.mm(ob[0:64, sl], t["s0b"][:, 64 * h:64 * h + 64], t["rrh"][:, sl],
                 False, False, wait=[s0b_ev])
            olast = k.mm(ob[0:64, sl], t["wkv0b"][:, 64 * h:64 * h + 64],
                         t["rwh"][:, sl], False, True,
                         inc="T" if c == NCH - 1 else None)
            kv = k.mm(k.psum[7][0:64, 0:64], t["kwt"][c][:, 64 * h:64 * h + 64],
                      t["vt"][c][:, 64 * h:64 * h + 64], True, True,
                      wait=[kv_free], inc="T")
            k.op("vector", lambda e, h=h: e.tensor_tensor(
                out=t["s0f"][:, 64 * h:64 * h + 64],
                in0=t["s0f"][:, 64 * h:64 * h + 64],
                in1=k.psum[7][0:64, 0:64], op=ALU.add), wait=[kv])
            s0b_ev = k.op("vector", lambda e, h=h: e.tensor_copy(
                out=t["s0b"][:, 64 * h:64 * h + 64],
                in_=t["s0f"][:, 64 * h:64 * h + 64]))
            kv_free = s0b_ev
        yv = k.op("vector", lambda e, hc=hc, hp=hp, ob=ob: e.tensor_copy(
            out=t["mixy"][hc][64 * hp:64 * hp + 64, 0:TB], in_=ob[0:64, 0:TB]),
            wait=[olast])
        ob_free[h % 2] = yv
        head_bufs_free = olast

    if DBG.get("l0cut", 99) <= 2:
        return
    sx = k.ev("V")
    d1 = k.dma("gpsimd", k.sag[f"l0i{r}"].ap()[:, :], t["s0f"][:, :],
               wait=[sx], inc="GD")
    ag = k.op("gpsimd",
              lambda e: e.collective_compute(
                  "AllGather", ALU.bypass, replica_groups=[list(range(NCORES))],
                  ins=[k.sag[f"l0i{r}"].ap().opt()],
                  outs=[k.sag[f"l0o{r}"].ap().opt()]),
              wait=[d1], inc="CC")
    k.op("vector", lambda e: e.tensor_copy(out=t["s0in"][:, :],
                                           in_=t["s0run"][:, :]),
         wait=[k.ev("T")])
    for j in range(NCORES):
        ld = k.dma("sync", t["xtm"][0:64, 0:D],
                   k.sag[f"l0o{r}"].ap()[64 * j:64 * j + 64, :],
                   wait=[ag, k.ev("V")])
        k.op("vector", lambda e: e.tensor_tensor(
            out=t["s0run"][:, :], in0=t["s0run"][:, :], in1=t["xtm"][0:64, 0:D],
            op=ALU.add), wait=[ld])
        k.op("vector", lambda e, j=j: e.scalar_tensor_tensor(
            out=t["s0in"][:, :], in0=t["xtm"][0:64, 0:D],
            scalar=t["pmask"][0:64, j:j + 1], in1=t["s0in"][:, :],
            op0=ALU.mult, op1=ALU.add))
    sbi = k.op("vector", lambda e: e.tensor_copy(out=t["s0inb"][:, :],
                                                 in_=t["s0in"][:, :]))
    corr_free = [k.ev("V"), k.ev("V")]
    for h in range(H0):
        hc, hp = h // 2, h % 2
        cpe = k.op("vector", lambda e, hc=hc, hp=hp: e.tensor_copy(
            out=t["rrh"][:, 0:TB], in_=t["rf"][hc][64 * hp:64 * hp + 64, 0:TB]),
            wait=[k.ev("T")])
        cm = k.mm(k.psum[2 + h % 2][0:64, 0:TB], t["s0inb"][:, 64 * h:64 * h + 64],
                  t["rrh"][:, 0:TB], True, True,
                  wait=[sbi, cpe, corr_free[h % 2]], inc="T")
        ya = k.op("vector", lambda e, hc=hc, hp=hp, h=h: e.tensor_tensor(
            out=t["mixy"][hc][64 * hp:64 * hp + 64, 0:TB],
            in0=t["mixy"][hc][64 * hp:64 * hp + 64, 0:TB],
            in1=k.psum[2 + h % 2][0:64, 0:TB], op=ALU.add), wait=[cm])
        corr_free[h % 2] = ya

    if DBG.get("l0cut", 99) <= 3:
        return
    st_free = [k.ev("V"), k.ev("V")]
    bc_free = [k.ev("V"), k.ev("V")]
    ya_prev = k.ev("S")
    for hc in range(16):
        y2 = k.op("vector", lambda e, hc=hc: e.tensor_tensor(
            out=t["xntm"][:, 0:TB], in0=t["mixy"][hc][:, 0:TB],
            in1=t["mixy"][hc][:, 0:TB], op=ALU.mult),
            wait=[k.ev("T"), ya_prev])
        s1 = k.mm(k.psum[6][0:2, 0:TB], t["aux_bf"][:, 0:2],
                  t["mixy"][hc][:, 0:TB], True, True,
                  wait=[k.ev("V"), st_free[0]], inc="T")
        s2 = k.mm(k.psum[7][0:2, 0:TB], t["aux_bf"][:, 0:2],
                  t["xntm"][:, 0:TB], True, True,
                  wait=[y2, st_free[1]], inc="T")
        k.op("vector", lambda e: e.tensor_scalar(
            out=t["tB"][0:2, 0:TB], in0=k.psum[6][0:2, 0:TB],
            scalar1=1.0 / 64.0, scalar2=None, op0=ALU.mult), wait=[s1])
        st_free[0] = k.ev("V")
        k.op("vector", lambda e: e.tensor_scalar(
            out=t["tC"][0:2, 0:TB], in0=k.psum[7][0:2, 0:TB],
            scalar1=1.0 / 64.0, scalar2=None, op0=ALU.mult),
            wait=[s2, ya_prev])
        st_free[1] = k.ev("V")
        k.op("vector", lambda e: e.tensor_tensor(
            out=t["stg2"][0:2, 0:TB], in0=t["tB"][0:2, 0:TB],
            in1=t["tB"][0:2, 0:TB], op=ALU.mult))
        vv = k.op("vector", lambda e: e.tensor_tensor(
            out=t["stg2"][0:2, 0:TB], in0=t["tC"][0:2, 0:TB],
            in1=t["stg2"][0:2, 0:TB], op=ALU.subtract))
        k.op("scalar", lambda e: e.activation(
            out=t["stg2"][0:2, 0:TB], in_=t["stg2"][0:2, 0:TB], func=AF.Ln,
            bias=0.00064), wait=[vv])
        rse = k.op("scalar", lambda e: e.activation(
            out=t["stg2"][0:2, 0:TB], in_=t["stg2"][0:2, 0:TB], func=AF.Exp,
            scale=-0.5))
        k.op("vector", lambda e: e.tensor_tensor(
            out=t["tB"][0:2, 0:TB], in0=t["tB"][0:2, 0:TB],
            in1=t["stg2"][0:2, 0:TB], op=ALU.mult), wait=[rse])
        k.op("vector", lambda e: e.tensor_scalar(
            out=t["tB"][0:2, 0:TB], in0=t["tB"][0:2, 0:TB],
            scalar1=-1.0, scalar2=None, op0=ALU.mult))
        k.op("vector", lambda e: e.tensor_copy(out=t["gnhl"][:, 0:TB],
                                               in_=t["stg2"][0:2, 0:TB]))
        k.op("vector", lambda e: e.tensor_copy(out=t["tC"][0:2, 0:TB],
                                               in_=t["gnhl"][:, 0:TB]))
        k.op("vector", lambda e: e.tensor_tensor(
            out=t["tC"][0:2, 0:TB], in0=t["stg2"][0:2, 0:TB],
            in1=t["tC"][0:2, 0:TB], op=ALU.subtract))
        k.op("vector", lambda e: e.tensor_copy(out=t["gnhl"][:, TB:2 * TB],
                                               in_=t["tC"][0:2, 0:TB]))
        k.op("vector", lambda e: e.tensor_copy(out=t["gnhl"][:, 2 * TB:3 * TB],
                                               in_=t["tB"][0:2, 0:TB]))
        k.op("vector", lambda e: e.tensor_copy(out=t["tC"][0:2, 0:TB],
                                               in_=t["gnhl"][:, 2 * TB:3 * TB]))
        k.op("vector", lambda e: e.tensor_tensor(
            out=t["tC"][0:2, 0:TB], in0=t["tB"][0:2, 0:TB],
            in1=t["tC"][0:2, 0:TB], op=ALU.subtract))
        hl = k.op("vector", lambda e: e.tensor_copy(out=t["gnhl"][:, 3 * TB:4 * TB],
                                                    in_=t["tC"][0:2, 0:TB]))
        k.mm(k.psum[2][:, 0:TB], t["e2"][0:2, :], t["gnhl"][:, 0:TB],
             True, False, wait=[hl, bc_free[0]])
        b1 = k.mm(k.psum[2][:, 0:TB], t["e2"][0:2, :], t["gnhl"][:, TB:2 * TB],
                  False, True, inc="T")
        k.mm(k.psum[3][:, 0:TB], t["e2"][0:2, :], t["gnhl"][:, 2 * TB:3 * TB],
             True, False, wait=[bc_free[1]])
        b2 = k.mm(k.psum[3][:, 0:TB], t["e2"][0:2, :], t["gnhl"][:, 3 * TB:4 * TB],
                  False, True, inc="T")
        k.op("vector", lambda e, hc=hc: e.tensor_tensor(
            out=t["tC"][:, 0:TB], in0=t["mixy"][hc][:, 0:TB],
            in1=k.psum[2][:, 0:TB], op=ALU.mult), wait=[b1, b2])
        bc_free[0] = k.ev("V")
        tt2 = k.op("vector", lambda e: e.tensor_tensor(
            out=t["tC"][:, 0:TB], in0=t["tC"][:, 0:TB],
            in1=k.psum[3][:, 0:TB], op=ALU.add))
        bc_free[1] = tt2
        ya = k.op("scalar", lambda e, hc=hc: e.activation(
            out=t["tB"][:, 0:TB], in_=t["tC"][:, 0:TB], func=AF.Identity,
            scale=t["smallp"][:, 32 + hc:33 + hc],
            bias=t["smallp"][:, 48 + hc:49 + hc]), wait=[tt2])
        ya_prev = ya
        k.op("vector", lambda e, hc=hc: e.tensor_tensor(
            out=t["mixy"][hc][:, 0:TB], in0=t["tB"][:, 0:TB],
            in1=t["gsl"][hc][:, 0:TB], op=ALU.mult), wait=[ya])

    if DBG.get("l0cut", 99) <= 4:
        return
    if DBG.get("wobar", 0):
        barrier(k)
    k.slab_free = k.ev("T")
    k.bank_ev = [k.ev("V")] * 4
    stg_free = [k.ev("GD"), k.ev("GD")]

    def cb_wo(oc, bank, mmev):
        sb = t["stg"] if oc % 2 == 0 else t["stg2"]
        cpv = k.op("vector", lambda e, bank=bank, sb=sb: e.tensor_copy(
            out=sb[:, 0:TB], in_=bank[:, 0:TB]),
            wait=[mmev, stg_free[oc % 2]])
        k.bank_ev[oc % 4] = cpv
        stg_free[oc % 2] = k.dma("gpsimd",
                                 k.sscr.ap()[oc, :, :], sb[:, 0:TB],
                                 wait=[cpv], inc="GD")
    proj2048(k, "l0", 4, t["mixy"], cb_wo, rhs_ev=k.ev("V"))
    k.s_done = k.ev("GD")



def dslab(k, kk):
    t = k.tensors
    if kk < 32:
        return t["wsl"][kk // 2][:, 256 * (kk % 2):256 * (kk % 2) + 256]
    return t["wsl2"][(kk - 32) // 2][:, 256 * (kk % 2):256 * (kk % 2) + 256]


def l0_ffn(k, r, xn_ev):
    t = k.tensors
    war0 = [k.ev("T"), k.ev("GD")]
    il_ev = None
    for kb in range(8):
        for tc in range(TB // 128):
            csl = slice(1 + 128 * tc, 129 + 128 * tc)
            k.tr(k.psum[4][:, 0:128], t["xn"][2 * kb][:, csl],
                 t["msk_bf"][:, 0:128], wait=[k.tp_free[0]])
            t1 = k.tr(k.psum[4][:, 128:256], t["xn"][2 * kb + 1][:, csl],
                      t["msk_bf"][:, 0:128], inc="T")
            tm = k.op("vector", lambda e: e.tensor_copy(
                out=t["xntm"][:, 0:256], in_=k.psum[4][:, 0:256]),
                wait=[t1] + (war0 if kb == 0 and tc == 0 else []))
            k.tp_free[0] = tm
            k.tr(k.psum[5][:, 0:128], t["xntm"][:, 0:256:2],
                 t["msk_bf"][:, 0:128], wait=[tm, k.tp_free[1]])
            t2 = k.tr(k.psum[5][:, 128:256], t["xntm"][:, 1:256:2],
                      t["msk_bf"][:, 0:128], inc="T")
            k.op("vector", lambda e, kb=kb, tc=tc: e.tensor_copy(
                out=t["xnil"][2 * kb][:, 128 * tc:128 * tc + 128],
                in_=k.psum[5][:, 0:128]), wait=[t2])
            il_ev = k.op("vector", lambda e, kb=kb, tc=tc: e.tensor_copy(
                out=t["xnil"][2 * kb + 1][:, 128 * tc:128 * tc + 128],
                in_=k.psum[5][:, 128:256]))
            k.tp_free[1] = il_ev

    k.slab_free = k.ev("T")
    bank_ev = [k.ev("V")] * 4
    m3g = None
    for jg in range(22):
        evg, evu = [], []
        for kk in range(16):
            kb, par = kk // 2, kk % 2
            seg = "gul" if par == 0 else "guh"
            if m3g is None:
                w8 = [k.slab_free] if kk == 0 else ()
                w8u = [k.slab_free] if kk == 0 else ()
            else:
                w8 = [m3g[0][kk]]
                w8u = [m3g[1][kk]]
            evg.append(qslab(k, t["wsl"][kk], seg, 128 * kb, 256 * jg, 256,
                             wait=w8))
            evu.append(qslab(k, t["wsl2"][kk], seg, 128 * kb, I0 + 256 * jg,
                             256, wait=w8u, eng="gpsimd", inc="GD"))
        evall = [evg[-1], evu[-1]]
        new_m3 = ([None] * 16, [None] * 16)
        last = None
        for jj in range(2):
            for kk in range(16):
                w = list(evall) if kk == 0 else []
                if kk == 0:
                    w += [bank_ev[jj], il_ev]
                ev = k.mm(k.psum[jj][:, 0:TB],
                          t["wsl"][kk][:, 128 * jj:128 * jj + 128],
                          t["xnil"][kk][:, 0:TB], start=(kk == 0),
                          stop=(kk == 15), wait=w,
                          inc="T" if (kk == 15 or jj == 1) else None)
                if jj == 1:
                    new_m3[0][kk] = ev
            for kk in range(16):
                w = list(evall) if kk == 0 else []
                if kk == 0:
                    w.append(bank_ev[2 + jj])
                last = k.mm(k.psum[2 + jj][:, 0:TB],
                            t["wsl2"][kk][:, 128 * jj:128 * jj + 128],
                            t["xnil"][kk][:, 0:TB],
                            start=(kk == 0), stop=(kk == 15),
                            wait=w, inc="T" if (kk == 15 or jj == 1) else None)
                if jj == 1:
                    new_m3[1][kk] = last
            sg = k.op("scalar", lambda e, jj=jj: e.activation(
                out=t["xntm"][:, 0:TB], in_=k.psum[jj][:, 0:TB], func=AF.Silu),
                wait=[last, k.ev("V")])
            bank_ev[jj] = sg
            hv = k.op("vector", lambda e, jg=jg, jj=jj: e.tensor_tensor(
                out=t["hh"][2 * jg + jj][:, 0:TB], in0=t["xntm"][:, 0:TB],
                in1=k.psum[2 + jj][:, 0:TB], op=ALU.mult), wait=[sg])
            bank_ev[2 + jj] = hv
        m3g = new_m3
        k.slab_free = last

    hh_ev = k.ev("V")
    k.slab_free = k.ev("T")
    bank_ev = [k.ev("V")] * 2
    m3d = None
    for og in range(8):
        evs = []
        for kk in range(44):
            kb, par = kk // 2, kk % 2
            seg = "dnl" if par == 0 else "dnh"
            if m3d is None:
                w8 = [k.slab_free] if kk <= 1 else ()
            else:
                w8 = [m3d[kk]]
            evs.append(qslab(k, dslab(k, kk), seg, 128 * kb, 256 * og, 256,
                             wait=w8, eng="gpsimd" if kk % 2 else "sync",
                             inc="GD" if kk % 2 else "LW"))
        evall = [evs[-2], evs[-1]]
        new_m3 = [None] * 44
        last = None
        for m in range(2):
            oc = 2 * og + m
            for kk in range(44):
                w = list(evall) if kk == 0 else []
                if kk == 0:
                    w += [bank_ev[m], hh_ev]
                last = k.mm(k.psum[m][:, 0:TB],
                            dslab(k, kk)[:, 128 * m:128 * m + 128],
                            t["hh"][kk][:, 0:TB], start=(kk == 0),
                            stop=(kk == 43), wait=w,
                            inc="T" if (kk == 43 or m == 1) else None)
                if m == 1:
                    new_m3[kk] = last
            ev = k.op("vector", lambda e, oc=oc, m=m: e.tensor_copy(
                out=t["dd"][oc][:, 0:TB], in_=k.psum[m][:, 0:TB]), wait=[last])
            bank_ev[m] = ev
        m3d = new_m3
        k.slab_free = last

    for oc in range(16):
        y2 = k.op("vector", lambda e, oc=oc: e.tensor_tensor(
            out=t["xntm"][:, 0:TB], in0=t["dd"][oc][:, 0:TB],
            in1=t["dd"][oc][:, 0:TB], op=ALU.mult), wait=[k.ev("T")])
        k.mm(k.psum[6][0:1, 0:TB], t["aux_bf"][:, 2:3], t["xntm"][:, 0:TB],
             start=(oc == 0), stop=(oc == 15), wait=[y2], inc="T")
    sq = k.ev("T")
    k.op("scalar", lambda e: e.activation(
        out=t["tB"][0:1, 0:TB], in_=k.psum[6][0:1, 0:TB], func=AF.Ln,
        scale=1.0 / 2048.0, bias=1e-6), wait=[sq])
    rse = k.op("scalar", lambda e: e.activation(
        out=t["tB"][0:1, 0:TB], in_=t["tB"][0:1, 0:TB], func=AF.Exp, scale=-0.5))
    k.op("vector", lambda e: e.tensor_copy(out=t["rmshl"][:, 0:TB],
                                           in_=t["tB"][0:1, 0:TB]), wait=[rse])
    k.op("vector", lambda e: e.tensor_copy(out=t["tC"][0:1, 0:TB],
                                           in_=t["rmshl"][:, 0:TB]))
    k.op("vector", lambda e: e.tensor_tensor(
        out=t["tC"][0:1, 0:TB], in0=t["tB"][0:1, 0:TB], in1=t["tC"][0:1, 0:TB],
        op=ALU.subtract))
    hl = k.op("vector", lambda e: e.tensor_copy(out=t["rmshl"][:, TB:2 * TB],
                                                in_=t["tC"][0:1, 0:TB]))
    k.mm(k.psum[2][:, 0:TB], t["ones1"][0:1, :], t["rmshl"][:, 0:TB],
         True, False, wait=[hl, k.ev("V")])
    b1 = k.mm(k.psum[2][:, 0:TB], t["ones1"][0:1, :], t["rmshl"][:, TB:2 * TB],
              False, True, inc="T")
    for oc in range(16):
        k.op("vector", lambda e, oc=oc: e.tensor_tensor(
            out=t["dd"][oc][:, 0:TB], in0=t["dd"][oc][:, 0:TB],
            in1=k.psum[2][:, 0:TB], op=ALU.mult), wait=[b1])
        k.op("scalar", lambda e, oc=oc: e.activation(
            out=t["dd"][oc][:, 0:TB], in_=t["dd"][oc][:, 0:TB], func=AF.Silu),
            wait=[k.ev("V")])
    k.g1_ev = k.ev("S")



def combine0(k, r):
    t, din = k.tensors, k.din
    stg_free = k.ev("V")
    for oc in range(16):
        ld = k.dma("sync", t["stg"][:, 0:TB], k.sscr.ap()[oc, :, :],
                   wait=[k.s_done if oc == 0 else stg_free, k.g1_ev])
        sg = k.op("vector", lambda e, oc=oc: e.tensor_tensor(
            out=t["dd"][oc][:, 0:TB], in0=t["stg"][:, 0:TB],
            in1=t["dd"][oc][:, 0:TB], op=ALU.mult), wait=[ld])
        stg_free = sg
    sg_ev = k.ev("V")
    tp_free = k.ev("V")
    for tc in range(TB // 128):
        ld = k.dma("sync", t["cxa"][:, :],
                   din[f"x{r}"][1 + 128 * tc:129 + 128 * tc, :],
                   wait=[k.ev("V"), k.ev("S")])
        tl = None
        for oc in range(16):
            b = oc // 4
            w = [sg_ev, tp_free] if oc == 0 else ()
            tl = k.tr(k.psum[b][:, 128 * (oc % 4):128 * (oc % 4) + 128],
                      t["dd"][oc][:, 128 * tc:128 * tc + 128],
                      t["ident_f32"][:, :], wait=w,
                      inc="T" if oc % 4 == 3 else None)
        for b in range(4):
            k.op("vector", lambda e, b=b: e.tensor_tensor(
                out=t["cxb"][:, 512 * b:512 * b + 512],
                in0=t["cxa"][:, 512 * b:512 * b + 512],
                in1=k.psum[b][:, 0:512], op=ALU.add),
                wait=[ld, tl, k.ev("GD")] if b == 0 else ())
        tp_free = k.ev("V")
        x1e = k.ev("V")
        k.dma("gpsimd", k.x1scr.ap()[128 * tc:128 * tc + 128, :],
              t["cxb"][:, :], wait=[x1e], inc="GD")
        ln_sbuf_chunk(k, t["cxb"], tc, eps_reg=1e-5)
    k.xn1_ev = k.ev("V")


def ln_sbuf_chunk(k, src, tc, eps_reg):
    t = k.tensors
    k.op("scalar", lambda e: e.activation(out=t["csq"][:, :], in_=src[:, :],
                                          func=AF.Square), wait=[k.ev("V")])
    sq = k.ev("S")
    k.op("vector", lambda e: e.reduce_sum(out=t["hbuf"][:, 0:1], in_=src[:, :],
                                          axis=AX))
    k.op("vector", lambda e: e.reduce_sum(out=t["hbuf"][:, 1:2], in_=t["csq"][:, :],
                                          axis=AX), wait=[sq])
    k.op("vector", lambda e: e.tensor_scalar(out=t["hbuf"][:, 2:3],
                                             in0=t["hbuf"][:, 0:1],
                                             scalar1=1.0 / 2048.0, scalar2=None,
                                             op0=ALU.mult))
    k.op("vector", lambda e: e.tensor_scalar(out=t["hbuf"][:, 3:4],
                                             in0=t["hbuf"][:, 1:2],
                                             scalar1=1.0 / 2048.0, scalar2=None,
                                             op0=ALU.mult))
    k.op("vector", lambda e: e.tensor_tensor(out=t["hbuf"][:, 4:5],
                                             in0=t["hbuf"][:, 2:3],
                                             in1=t["hbuf"][:, 2:3], op=ALU.mult))
    vv = k.op("vector", lambda e: e.tensor_tensor(out=t["hbuf"][:, 4:5],
                                                  in0=t["hbuf"][:, 3:4],
                                                  in1=t["hbuf"][:, 4:5],
                                                  op=ALU.subtract))
    k.op("scalar", lambda e: e.activation(out=t["hbuf"][:, 5:6],
                                          in_=t["hbuf"][:, 4:5], func=AF.Ln,
                                          bias=eps_reg), wait=[vv])
    k.op("scalar", lambda e: e.activation(out=t["hbuf"][:, 5:6],
                                          in_=t["hbuf"][:, 5:6], func=AF.Exp,
                                          scale=-0.5))
    sr = k.ev("S")
    k.op("vector", lambda e: e.tensor_tensor(out=t["hbuf"][:, 6:7],
                                             in0=t["hbuf"][:, 2:3],
                                             in1=t["hbuf"][:, 5:6], op=ALU.mult),
         wait=[sr])
    nb = k.op("vector", lambda e: e.tensor_scalar(out=t["hbuf"][:, 6:7],
                                                  in0=t["hbuf"][:, 6:7],
                                                  scalar1=-1.0, scalar2=None,
                                                  op0=ALU.mult))
    xe = k.op("scalar", lambda e: e.activation(out=t["xntm"][:, 0:D],
                                               in_=src[:, :], func=AF.Identity,
                                               scale=t["hbuf"][:, 5:6],
                                               bias=t["hbuf"][:, 6:7]),
              wait=[nb, k.ev("T")])

    def dst(fc, psrc, w):
        return k.op("vector",
                    lambda e, fc=fc: e.tensor_copy(
                        out=k.tensors["xn"][fc][:, 1 + 128 * tc:129 + 128 * tc],
                        in_=psrc), wait=w)
    transpose_to_fm(k, xe, lambda fc: t["xntm"][:, 128 * fc:128 * fc + 128], dst)





def l1_attn(k, r):
    t = k.tensors
    xn1 = k.xn1_ev
    rhs = [t["xn"][kk][:, 1:1 + TB] for kk in range(16)]
    k.slab_free = k.ev("T")
    k.bank_ev = [k.ev("V")] * 4

    def cb_q(oc, bank, mmev):
        ev = k.op("scalar", lambda e, oc=oc, bank=bank: e.activation(
            out=t["qt"][oc][:, 0:TB], in_=bank[:, 0:TB], func=AF.Relu,
            scale=SCL1), wait=[mmev])
        k.bank_ev[oc % 4] = ev
    proj2048(k, "l1", 0, rhs, cb_q, rhs_ev=xn1)

    qt_ev = k.ev("S")

    def cb_k(oc, bank, mmev):
        h = oc
        sg = k.op("scalar", lambda e, bank=bank: e.activation(
            out=t["sgb"][:, 0:TB], in_=bank[:, 0:TB], func=AF.Sigmoid),
            wait=[mmev, k.ev("V")])
        k.bank_ev[oc % 4] = sg
        gb = k.op("scalar", lambda e: e.activation(
            out=t["gbv"][:, 0:TB], in_=t["sgb"][:, 0:TB], func=AF.Ln,
            scale=-1.0, bias=1.0))
        sc = None
        for c in range(TB // 64):
            sc = k.op("vector", lambda e, c=c: e.tensor_tensor_scan(
                out=t["abuf"][:, 64 * c:64 * c + 64],
                data0=t["gbv"][:, 64 * c:64 * c + 64],
                data1=t["gbv"][:, 64 * c:64 * c + 64],
                initial=0.0, op0=ALU.add, op1=ALU.bypass),
                wait=[gb] if c == 0 else ())
        e1 = k.op("scalar", lambda e: e.activation(
            out=t["tB"][:, 0:TB], in_=t["abuf"][:, 0:TB], func=AF.Exp),
            wait=[sc])
        k.op("vector", lambda e, h=h: e.tensor_tensor(
            out=t["qt"][h][:, 0:TB], in0=t["qt"][h][:, 0:TB],
            in1=t["tB"][:, 0:TB], op=ALU.mult), wait=[e1, qt_ev])
        ed = k.op("scalar", lambda e: e.activation(
            out=t["tB"][:, 0:TB], in_=t["abuf"][:, 0:TB], func=AF.Exp,
            scale=-1.0), wait=[k.ev("V")])
        k.op("vector", lambda e, h=h: e.tensor_tensor(
            out=t["kd"][h][:, 0:TB], in0=t["sgb"][:, 0:TB],
            in1=t["tB"][:, 0:TB], op=ALU.mult), wait=[ed])
        e3 = None
        for c in range(TB // 64):
            e3 = k.op("scalar", lambda e, c=c: e.activation(
                out=t["tB"][:, 64 * c:64 * c + 64],
                in_=t["abuf"][:, 64 * c:64 * c + 64], func=AF.Exp, scale=-1.0,
                bias=t["abuf"][:, 64 * c + 63:64 * c + 64]),
                wait=[k.ev("V")] if c == 0 else ())
        k.op("vector", lambda e, h=h: e.tensor_tensor(
            out=t["ks"][h][:, 0:TB], in0=t["sgb"][:, 0:TB],
            in1=t["tB"][:, 0:TB], op=ALU.mult), wait=[e3])
        k.op("vector", lambda e, h=h: e.tensor_copy(
            out=t["lgpre"][:, 8 * h:8 * h + 8],
            in_=t["abuf"][:, 63:TB:64]))
    proj2048(k, "l1", 1, rhs, cb_k, rhs_ev=xn1)

    m3v = None
    for mg in range(4):
        evs = []
        for kk in range(16):
            if m3v is None:
                w8 = [k.slab_free] if kk <= 1 else ()
            else:
                w8 = [m3v[kk]]
            evs.append(attn_slab(k, t["wsl"][kk], "l1", 2, kk, 512 * mg, 512,
                                 wait=w8, eng="gpsimd" if kk % 2 else "sync",
                                 inc="GD" if kk % 2 else "LW"))
        evall = [evs[-2], evs[-1]]
        new_m3 = [None] * 16
        last = None
        for tc in range(TB // 128):
            bank = k.psum[tc]
            for kk in range(16):
                w = list(evall) if kk == 0 else []
                if kk == 0:
                    w += [k.bank_ev[tc], xn1]
                last = k.mm(bank[:, 0:512], rhs[kk][:, 128 * tc:128 * tc + 128],
                            t["wsl"][kk][:, 0:512],
                            start=(kk == 0), stop=(kk == 15), wait=w,
                            inc="T" if (kk == 15 or tc == 3) else None)
                if tc == 3:
                    new_m3[kk] = last
            ev = k.op("vector", lambda e, tc=tc, mg=mg, bank=bank: e.tensor_copy(
                out=t["vt"][tc][:, 512 * mg:512 * mg + 512], in_=bank[:, 0:512]),
                wait=[last])
            k.bank_ev[tc] = ev
        m3v = new_m3
        k.slab_free = last

    k.op("vector", lambda e: e.memzero(t["s1f"][:, :]))
    z1 = k.op("vector", lambda e: e.memzero(t["s1b"][:, :]))
    s1b_ev = z1
    att_free = k.ev("V")
    ob_free = [k.ev("V"), k.ev("V")]
    kv_free = k.ev("V")
    kst_free = k.ev("V")
    k.attsb_free = k.ev("T")
    k.q2b_free = k.ev("T")
    k.ks2b_free = k.ev("T")
    k.kst_rd = k.ev("T")
    NP = TB // 128
    prep_ev = k.ev("V")
    for h in range(H1):
        olast = None
        ob = k.psum[h % 2]
        for p in range(NP):
            c0 = slice(128 * p, 128 * p + 64)
            c1 = slice(128 * p + 64, 128 * p + 128)
            pr = slice(128 * p, 128 * p + 128)
            d0 = k.op("scalar", lambda e, h=h, p=p: e.activation(
                out=t["hbuf"][:, 8:9],
                in_=t["lgpre"][:, 8 * h + 2 * p:8 * h + 2 * p + 1],
                func=AF.Exp), wait=[prep_ev, k.ev("V")])
            d1 = k.op("scalar", lambda e, h=h, p=p: e.activation(
                out=t["hbuf"][:, 9:10],
                in_=t["lgpre"][:, 8 * h + 2 * p + 1:8 * h + 2 * p + 2],
                func=AF.Exp))
            k.op("vector", lambda e, h=h, p=p: e.tensor_tensor(
                out=t["hbuf"][:, 10:11],
                in0=t["lgpre"][:, 8 * h + 2 * p:8 * h + 2 * p + 1],
                in1=t["lgpre"][:, 8 * h + 2 * p + 1:8 * h + 2 * p + 2],
                op=ALU.add))
            dp = k.op("scalar", lambda e: e.activation(
                out=t["hbuf"][:, 11:12], in_=t["hbuf"][:, 10:11], func=AF.Exp),
                wait=[k.ev("V")])
            k.op("vector", lambda e, h=h, p=p: e.tensor_copy(
                out=t["q2b"][:, 0:64], in_=t["qt"][h][:, c0]),
                wait=[k.q2b_free])
            q2 = k.op("scalar", lambda e, h=h, p=p: e.activation(
                out=t["q2b"][:, 64:128], in_=t["qt"][h][:, c1],
                func=AF.Identity, scale=t["hbuf"][:, 8:9]), wait=[d0, k.ev("V")])
            k.op("vector", lambda e, h=h, p=p: e.tensor_copy(
                out=t["ks2b"][:, 64:128], in_=t["ks"][h][:, c1]),
                wait=[k.ks2b_free])
            ks2 = k.op("scalar", lambda e, h=h, p=p: e.activation(
                out=t["ks2b"][:, 0:64], in_=t["ks"][h][:, c0],
                func=AF.Identity, scale=t["hbuf"][:, 9:10]), wait=[d1, k.ev("V")])
            tke = k.tr(k.psum[4][:, 0:128], t["ks2b"][:, :], t["msk_bf"][:, 0:128],
                       wait=[ks2, kst_free], inc="T")
            k.ks2b_free = tke
            kste = k.op("vector", lambda e: e.tensor_copy(
                out=t["kst"][:, :], in_=k.psum[4][:, 0:128]),
                wait=[tke, k.kst_rd])
            kst_free = kste
            k.mm(k.psum[6][0:64, 0:64],
                 t["kd"][h][:, c0], t["qt"][h][:, c0], True, True,
                 wait=[att_free])
            k.mm(k.psum[6][0:64, 64:128], t["ks"][h][:, c0], t["qt"][h][:, c1],
                 True, True)
            a3 = k.mm(k.psum[7][0:64, 64:128], t["kd"][h][:, c1], t["qt"][h][:, c1],
                      True, True, inc="T")
            k.op("vector", lambda e: e.tensor_tensor(
                out=t["attsb"][0:64, :], in0=k.psum[6][0:64, 0:128],
                in1=t["msk_bf"][0:64, 256:384], op=ALU.mult),
                wait=[a3, k.attsb_free])
            ae = k.op("vector", lambda e: e.tensor_tensor(
                out=t["attsb"][64:128, :], in0=k.psum[7][0:64, 0:128],
                in1=t["msk_bf"][64:128, 256:384], op=ALU.mult))
            att_free = ae
            w1 = [ae]
            if p == 0:
                w1.append(ob_free[h % 2])
            k.attsb_free = k.mm(ob[:, pr], t["vt"][p][:, 128 * h:128 * h + 128],
                                t["attsb"][:, :], True, False, wait=w1, inc="T")
            olast = k.mm(ob[:, pr], t["s1b"][:, 128 * h:128 * h + 128],
                         t["q2b"][:, :], False, True,
                         wait=[s1b_ev, q2], inc="T")
            kv = k.mm(k.psum[7][:, 128:256], t["kst"][:, :],
                      t["vt"][p][:, 128 * h:128 * h + 128], True, True,
                      wait=[kste, kv_free], inc="T")
            k.kst_rd = kv
            k.q2b_free = olast
            k.op("vector", lambda e, h=h: e.scalar_tensor_tensor(
                out=t["s1f"][:, 128 * h:128 * h + 128],
                in0=t["s1f"][:, 128 * h:128 * h + 128],
                scalar=t["hbuf"][:, 11:12], in1=k.psum[7][:, 128:256],
                op0=ALU.mult, op1=ALU.add), wait=[kv, dp])
            s1b_ev = k.op("vector", lambda e, h=h: e.tensor_copy(
                out=t["s1b"][:, 128 * h:128 * h + 128],
                in_=t["s1f"][:, 128 * h:128 * h + 128]))
            kv_free = s1b_ev
        ov = k.op("vector", lambda e, h=h, ob=ob: e.tensor_copy(
            out=t["gsl"][h][:, 0:TB], in_=ob[:, 0:TB]), wait=[olast])
        ob_free[h % 2] = ov
        prep_ev = k.ev("S")

    for h in range(H1):
        k.op("vector", lambda e, h=h: e.reduce_sum(
            out=t["s1lg"][:, h:h + 1], in_=t["lgpre"][:, 8 * h:8 * h + 8],
            axis=AX))
    de = k.op("scalar", lambda e: e.activation(
        out=t["hbuf"][:, 16:32], in_=t["s1lg"][:, 0:16], func=AF.Exp),
        wait=[k.ev("V")])
    sx = k.ev("V")
    k.dma("gpsimd", k.sag[f"l1i{r}"].ap()[:, 0:D], t["s1f"][:, :],
          wait=[sx], inc="GD")
    d2 = k.dma("gpsimd", k.sag[f"l1i{r}"].ap()[:, D:D + 16], t["hbuf"][:, 16:32],
               wait=[de], inc="GD")
    ag = k.op("gpsimd",
              lambda e: e.collective_compute(
                  "AllGather", ALU.bypass, replica_groups=[list(range(NCORES))],
                  ins=[k.sag[f"l1i{r}"].ap().opt()],
                  outs=[k.sag[f"l1o{r}"].ap().opt()]),
              wait=[d2], inc="CC")
    k.op("vector", lambda e: e.tensor_copy(out=t["s1in"][:, :],
                                           in_=t["s1run"][:, :]),
         wait=[k.ev("T")])
    for j in range(NCORES):
        ld = k.dma("sync", t["xtm"][:, 0:D],
                   k.sag[f"l1o{r}"].ap()[128 * j:128 * j + 128, 0:D],
                   wait=[ag, k.ev("V")])
        ld2 = k.dma("sync", t["tcol"][:, 0:16],
                    k.sag[f"l1o{r}"].ap()[128 * j:128 * j + 128, D:D + 16])
        for h in range(H1):
            k.op("vector", lambda e, h=h: e.scalar_tensor_tensor(
                out=t["s1run"][:, 128 * h:128 * h + 128],
                in0=t["s1run"][:, 128 * h:128 * h + 128],
                scalar=t["tcol"][:, h:h + 1],
                in1=t["xtm"][:, 128 * h:128 * h + 128],
                op0=ALU.mult, op1=ALU.add),
                wait=[ld, ld2] if h == 0 else ())
        k.op("vector", lambda e, j=j: e.scalar_tensor_tensor(
            out=t["xtm"][:, 0:D], in0=t["xtm"][:, 0:D],
            scalar=t["pmask"][:, j:j + 1], in1=t["xtm"][:, 0:D],
            op0=ALU.mult, op1=ALU.bypass))
        k.op("scalar", lambda e, j=j: e.activation(
            out=t["tcol"][:, 16:32], in_=t["onesf"][:, 0:16], func=AF.Identity,
            scale=t["tcol"][:, 32 + j:33 + j]), wait=[k.ev("V")])
        k.op("vector", lambda e, j=j: e.scalar_tensor_tensor(
            out=t["tcol"][:, 0:16], in0=t["tcol"][:, 0:16],
            scalar=t["pmask"][:, j:j + 1], in1=t["tcol"][:, 16:32],
            op0=ALU.mult, op1=ALU.add), wait=[k.ev("S")])
        for h in range(H1):
            k.op("vector", lambda e, h=h: e.scalar_tensor_tensor(
                out=t["s1in"][:, 128 * h:128 * h + 128],
                in0=t["s1in"][:, 128 * h:128 * h + 128],
                scalar=t["tcol"][:, h:h + 1],
                in1=t["xtm"][:, 128 * h:128 * h + 128],
                op0=ALU.mult, op1=ALU.add))
    sbi = k.op("vector", lambda e: e.tensor_copy(out=t["s1inb"][:, :],
                                                 in_=t["s1in"][:, :]))

    corr_free = [k.ev("V"), k.ev("V")]
    for h in range(H1):
        pf = k.op("vector", lambda e, h=h: e.tensor_tensor_scan(
            out=t["tD"][:, 0:8], data0=t["lgpre"][:, 8 * h:8 * h + 8],
            data1=t["lgpre"][:, 8 * h:8 * h + 8], initial=0.0,
            op0=ALU.add, op1=ALU.bypass))
        pe = k.op("scalar", lambda e: e.activation(
            out=t["tD"][:, 8:16], in_=t["tD"][:, 0:8], func=AF.Exp), wait=[pf])
        k.op("vector", lambda e, h=h: e.tensor_copy(
            out=t["mixy"][h][:, 0:64], in_=t["qt"][h][:, 0:64]))
        qg = None
        for c in range(1, TB // 64):
            qg = k.op("scalar", lambda e, h=h, c=c: e.activation(
                out=t["mixy"][h][:, 64 * c:64 * c + 64],
                in_=t["qt"][h][:, 64 * c:64 * c + 64], func=AF.Identity,
                scale=t["tD"][:, 7 + c:8 + c]), wait=[pe] if c == 1 else ())
        cm = k.mm(k.psum[2 + h % 2][:, 0:TB], t["s1inb"][:, 128 * h:128 * h + 128],
                  t["mixy"][h][:, 0:TB], True, True,
                  wait=[sbi, qg, k.ev("V"), corr_free[h % 2]], inc="T")
        ya = k.op("vector", lambda e, h=h: e.tensor_tensor(
            out=t["gsl"][h][:, 0:TB], in0=t["gsl"][h][:, 0:TB],
            in1=k.psum[2 + h % 2][:, 0:TB], op=ALU.add), wait=[cm])
        corr_free[h % 2] = ya

    for h in range(H1):
        y2 = k.op("vector", lambda e, h=h: e.tensor_tensor(
            out=t["xntm"][:, 0:TB], in0=t["gsl"][h][:, 0:TB],
            in1=t["gsl"][h][:, 0:TB], op=ALU.mult), wait=[k.ev("T")])
        k.mm(k.psum[6][0:1, 0:TB], t["aux_bf"][:, 2:3], t["xntm"][:, 0:TB],
             start=(h == 0), stop=(h == 15), wait=[y2], inc="T")
    sq = k.ev("T")
    k.op("scalar", lambda e: e.activation(
        out=t["tB"][0:1, 0:TB], in_=k.psum[6][0:1, 0:TB], func=AF.Ln,
        scale=1.0 / 2048.0, bias=1e-5), wait=[sq])
    rse = k.op("scalar", lambda e: e.activation(
        out=t["tB"][0:1, 0:TB], in_=t["tB"][0:1, 0:TB], func=AF.Exp, scale=-0.5))
    k.op("vector", lambda e: e.tensor_copy(out=t["rmshl"][:, 0:TB],
                                           in_=t["tB"][0:1, 0:TB]), wait=[rse])
    k.op("vector", lambda e: e.tensor_copy(out=t["tC"][0:1, 0:TB],
                                           in_=t["rmshl"][:, 0:TB]))
    k.op("vector", lambda e: e.tensor_tensor(
        out=t["tC"][0:1, 0:TB], in0=t["tB"][0:1, 0:TB], in1=t["tC"][0:1, 0:TB],
        op=ALU.subtract))
    hl = k.op("vector", lambda e: e.tensor_copy(out=t["rmshl"][:, TB:2 * TB],
                                                in_=t["tC"][0:1, 0:TB]))
    k.mm(k.psum[2][:, 0:TB], t["ones1"][0:1, :], t["rmshl"][:, 0:TB],
         True, False, wait=[hl, k.ev("V")])
    b1 = k.mm(k.psum[2][:, 0:TB], t["ones1"][0:1, :], t["rmshl"][:, TB:2 * TB],
              False, True, inc="T")
    act_prev = k.ev("S")
    for h in range(H1):
        k.op("vector", lambda e, h=h: e.tensor_tensor(
            out=t["tC"][:, 0:TB], in0=t["gsl"][h][:, 0:TB],
            in1=k.psum[2][:, 0:TB], op=ALU.mult), wait=[b1, act_prev])
        act_prev = k.op("scalar", lambda e, h=h: e.activation(
            out=t["mixy"][h][:, 0:TB], in_=t["tC"][:, 0:TB], func=AF.Identity,
            scale=t["smallp"][:, 64 + h:65 + h]), wait=[k.ev("V")])

    k.slab_free = k.ev("T")
    k.bank_ev = [k.ev("V")] * 4
    og_ev = k.ev("S")
    stg_free = [k.ev("GD"), k.ev("GD")]

    def cb_wo(oc, bank, mmev):
        sb = t["stg"] if oc % 2 == 0 else t["stg2"]
        cpv = k.op("vector", lambda e, bank=bank, sb=sb: e.tensor_copy(
            out=sb[:, 0:TB], in_=bank[:, 0:TB]),
            wait=[mmev, stg_free[oc % 2]])
        k.bank_ev[oc % 4] = cpv
        stg_free[oc % 2] = k.dma("gpsimd",
                                 k.sscr.ap()[oc, :, :], sb[:, 0:TB],
                                 wait=[cpv], inc="GD")
    proj2048(k, "l1", 3, t["mixy"], cb_wo, rhs_ev=og_ev)
    k.s_done = k.ev("GD")



def l1_ffn(k, r):
    t, din = k.tensors, k.din
    rhs = [t["xn"][kk][:, 1:1 + TB] for kk in range(16)]
    k.slab_free = k.ev("T")
    bank_ev = [k.ev("V")] * 4
    if "l1ffn" in DBG["stages"]:
        m3g = None
        for jg in range(22):
            ev1, ev2 = [], []
            for kk in range(16):
                if m3g is None:
                    w8 = [k.slab_free] if kk == 0 else ()
                    w8u = [k.slab_free] if kk == 0 else ()
                else:
                    w8 = [m3g[0][kk]]
                    w8u = [m3g[1][kk]]
                ev1.append(qslab(k, t["wsl"][kk], "w1q", 128 * kk, 256 * jg, 256,
                                 wait=w8))
                ev2.append(qslab(k, t["wsl2"][kk], "w2q", 128 * kk, 256 * jg,
                                 256, wait=w8u, eng="gpsimd", inc="GD"))
            evall = [ev1[-1], ev2[-1]]
            new_m3 = ([None] * 16, [None] * 16)
            last = None
            for jj in range(2):
                for kk in range(16):
                    w = list(evall) if kk == 0 else []
                    if kk == 0:
                        w += [bank_ev[jj], k.xn1_ev]
                    ev = k.mm(k.psum[jj][:, 0:TB],
                              t["wsl"][kk][:, 128 * jj:128 * jj + 128],
                              rhs[kk], start=(kk == 0), stop=(kk == 15),
                              wait=w, inc="T" if (kk == 15 or jj == 1) else None)
                    if jj == 1:
                        new_m3[0][kk] = ev
                for kk in range(16):
                    w = list(evall) if kk == 0 else []
                    if kk == 0:
                        w.append(bank_ev[2 + jj])
                    last = k.mm(k.psum[2 + jj][:, 0:TB],
                                t["wsl2"][kk][:, 128 * jj:128 * jj + 128],
                                rhs[kk], start=(kk == 0), stop=(kk == 15),
                                wait=w,
                                inc="T" if (kk == 15 or jj == 1) else None)
                    if jj == 1:
                        new_m3[1][kk] = last
                sg = k.op("scalar", lambda e, jj=jj: e.activation(
                    out=t["xntm"][:, 0:TB], in_=k.psum[jj][:, 0:TB],
                    func=AF.Sigmoid), wait=[last, k.ev("V")])
                bank_ev[jj] = sg
                hvar = DBG.get("hvar", 0)
                if hvar == 1:
                    hv = k.op("vector", lambda e, jg=jg, jj=jj: e.tensor_copy(
                        out=t["hid"][2 * jg + jj][:, 0:TB],
                        in_=k.psum[2 + jj][:, 0:TB]), wait=[sg])
                elif hvar == 2:
                    hv = k.op("vector", lambda e, jg=jg, jj=jj: e.tensor_copy(
                        out=t["hid"][2 * jg + jj][:, 0:TB],
                        in_=t["xntm"][:, 0:TB]), wait=[sg])
                else:
                    hv = k.op("vector", lambda e, jg=jg, jj=jj: e.tensor_tensor(
                        out=t["hid"][2 * jg + jj][:, 0:TB],
                        in0=t["xntm"][:, 0:TB],
                        in1=k.psum[2 + jj][:, 0:TB], op=ALU.mult), wait=[sg])
                bank_ev[2 + jj] = hv
            m3g = new_m3
            k.slab_free = last
            if DBG.get("l1cut", 0):
                return

        hid_ev = k.ev("V")
        k.slab_free = k.ev("T")
        bank_ev = [k.ev("V")] * 2
        m3d = None
        for og in range(8):
            evs = []
            for kk in range(44):
                if m3d is None:
                    w8 = [k.slab_free] if kk <= 1 else ()
                else:
                    w8 = [m3d[kk]]
                evs.append(qslab(k, dslab(k, kk), "w3q", 128 * kk, 256 * og,
                                 256, wait=w8,
                                 eng="gpsimd" if kk % 2 else "sync",
                                 inc="GD" if kk % 2 else "LW"))
            evall = [evs[-2], evs[-1]]
            new_m3 = [None] * 44
            last = None
            for m in range(2):
                oc = 2 * og + m
                for kk in range(44):
                    w = list(evall) if kk == 0 else []
                    if kk == 0:
                        w += [bank_ev[m], hid_ev]
                    last = k.mm(k.psum[m][:, 0:TB],
                                dslab(k, kk)[:, 128 * m:128 * m + 128],
                                t["hid"][kk][:, 0:TB], start=(kk == 0),
                                stop=(kk == 43), wait=w,
                                inc="T" if (kk == 43 or m == 1) else None)
                    if m == 1:
                        new_m3[kk] = last
                ev = k.op("vector", lambda e, oc=oc, m=m: e.tensor_copy(
                    out=t["fo"][oc][:, 0:TB], in_=k.psum[m][:, 0:TB]),
                    wait=[last])
                bank_ev[m] = ev
            m3d = new_m3
            k.slab_free = last
    else:
        for oc in range(16):
            k.op("vector", lambda e, oc=oc: e.memzero(t["fo"][oc][:, 0:TB]))

    stg_free = k.ev("V")
    for oc in range(16):
        sl = k.op("scalar", lambda e, oc=oc: e.activation(
            out=t["fo"][oc][:, 0:TB], in_=t["fo"][oc][:, 0:TB], func=AF.Silu),
            wait=[k.ev("V")])
        ld = k.dma("sync", t["stg"][:, 0:TB], k.sscr.ap()[oc, :, :],
                   wait=[k.s_done if oc == 0 else stg_free])
        sg = k.op("vector", lambda e, oc=oc: e.tensor_tensor(
            out=t["fo"][oc][:, 0:TB], in0=t["stg"][:, 0:TB],
            in1=t["fo"][oc][:, 0:TB], op=ALU.mult), wait=[ld, sl])
        stg_free = sg
    sg_ev = k.ev("V")
    tp_free = k.ev("V")
    for tc in range(TB // 128):
        ld = k.dma("sync", t["cxa"][:, :],
                   k.x1scr.ap()[128 * tc:128 * tc + 128, :],
                   wait=[k.ev("V"), k.ev("S")])
        tl = None
        for oc in range(16):
            b = oc // 4
            w = [sg_ev, tp_free] if oc == 0 else ()
            tl = k.tr(k.psum[b][:, 128 * (oc % 4):128 * (oc % 4) + 128],
                      t["fo"][oc][:, 128 * tc:128 * tc + 128],
                      t["ident_f32"][:, :], wait=w,
                      inc="T" if oc % 4 == 3 else None)
        oe = None
        for b in range(4):
            oe = k.op("vector", lambda e, b=b: e.tensor_tensor(
                out=t["cxb"][:, 512 * b:512 * b + 512],
                in0=t["cxa"][:, 512 * b:512 * b + 512],
                in1=k.psum[b][:, 0:512], op=ALU.add),
                wait=[ld, tl, k.ev("OUT")] if b == 0 else ())
        tp_free = oe
        k.dma("gpsimd", k.out_hbm[TB * r + 128 * tc: TB * r + 128 * tc + 128, :],
              t["cxb"][:, :], wait=[oe], inc="OUT")


def tile(k, r):
    t, din = k.tensors, k.din
    barrier(k)
    k.iota_ev = k.dma("sync", t["iota"][:, :], din[f"iota{r}"][:, :])
    k.slab_free = k.ev("T")
    k.bank_ev = [k.ev("T")] * 4
    k.tp_free = [k.ev("V"), k.ev("V")]
    k.op("vector", lambda e: e.tensor_scalar(out=t["tcol"][:, 32:40],
                                             in0=t["pmask"][:, 0:8],
                                             scalar1=-1.0, scalar2=1.0,
                                             op0=ALU.mult, op1=ALU.add),
         wait=[k.ld_small])

    def write_xn(tc, xe):
        def dst(fc, psrc, w):
            return k.op("vector",
                        lambda e: e.tensor_copy(
                            out=t["xn"][fc][:, 1 + 128 * tc:129 + 128 * tc],
                            in_=psrc),
                        wait=w)
        transpose_to_fm(k, xe, lambda fc: t["xntm"][:, 128 * fc:128 * fc + 128],
                        dst)
    for tc in range(TB // 128):
        ln_row_chunk(k, r, tc, 1 + 128 * tc, write_xn)
    ld = k.dma("sync", t["xtm"][0:1, :], din[f"x{r}"][0:1, :],
               wait=[k.ev("S"), k.ev("V")])
    k.op("scalar", lambda e: e.activation(out=t["xsq"][0:1, :], in_=t["xtm"][0:1, :],
                                          func=AF.Square), wait=[ld])
    sq = k.ev("S")
    k.op("vector", lambda e: e.reduce_sum(out=t["hbuf"][0:1, 0:1],
                                          in_=t["xtm"][0:1, :], axis=AX), wait=[ld])
    k.op("vector", lambda e: e.reduce_sum(out=t["hbuf"][0:1, 1:2],
                                          in_=t["xsq"][0:1, :], axis=AX), wait=[sq])
    k.op("vector", lambda e: e.tensor_scalar(out=t["hbuf"][0:1, 2:3],
                                             in0=t["hbuf"][0:1, 0:1],
                                             scalar1=1.0 / 2048.0, scalar2=None,
                                             op0=ALU.mult))
    k.op("vector", lambda e: e.tensor_scalar(out=t["hbuf"][0:1, 3:4],
                                             in0=t["hbuf"][0:1, 1:2],
                                             scalar1=1.0 / 2048.0, scalar2=None,
                                             op0=ALU.mult))
    k.op("vector", lambda e: e.tensor_tensor(out=t["hbuf"][0:1, 4:5],
                                             in0=t["hbuf"][0:1, 2:3],
                                             in1=t["hbuf"][0:1, 2:3], op=ALU.mult))
    vv = k.op("vector", lambda e: e.tensor_tensor(out=t["hbuf"][0:1, 4:5],
                                                  in0=t["hbuf"][0:1, 3:4],
                                                  in1=t["hbuf"][0:1, 4:5],
                                                  op=ALU.subtract))
    k.op("scalar", lambda e: e.activation(out=t["hbuf"][0:1, 5:6],
                                          in_=t["hbuf"][0:1, 4:5], func=AF.Ln,
                                          bias=1e-5), wait=[vv])
    k.op("scalar", lambda e: e.activation(out=t["hbuf"][0:1, 5:6],
                                          in_=t["hbuf"][0:1, 5:6], func=AF.Exp,
                                          scale=-0.5))
    sr = k.ev("S")
    k.op("vector", lambda e: e.tensor_tensor(out=t["hbuf"][0:1, 6:7],
                                             in0=t["hbuf"][0:1, 2:3],
                                             in1=t["hbuf"][0:1, 5:6], op=ALU.mult),
         wait=[sr])
    nbb = k.op("vector", lambda e: e.tensor_scalar(out=t["hbuf"][0:1, 6:7],
                                                   in0=t["hbuf"][0:1, 6:7],
                                                   scalar1=-1.0, scalar2=None,
                                                   op0=ALU.mult))
    bl = k.op("scalar", lambda e: e.activation(out=t["xsq"][0:1, :],
                                               in_=t["xtm"][0:1, :],
                                               func=AF.Identity,
                                               scale=t["hbuf"][0:1, 5:6],
                                               bias=t["hbuf"][0:1, 6:7]),
              wait=[nbb])
    k.op("vector", lambda e: e.scalar_tensor_tensor(
        out=t["xsq"][0:1, :], in0=t["xsq"][0:1, :],
        scalar=t["flags"][0:1, 2 * r:2 * r + 1], in1=t["xsq"][0:1, :],
        op0=ALU.mult, op1=ALU.bypass), wait=[bl])
    k.op("vector", lambda e: e.scalar_tensor_tensor(
        out=t["xsq"][0:1, :], in0=t["xtm"][0:1, :],
        scalar=t["flags"][0:1, 2 * r + 1:2 * r + 2], in1=t["xsq"][0:1, :],
        op0=ALU.mult, op1=ALU.add))
    bb = k.op("vector", lambda e: e.tensor_copy(out=t["xntm"][0:1, :],
                                                in_=t["xsq"][0:1, :]),
              wait=[k.ev("T")])
    for fc in range(16):
        tev = k.tr(k.psum[4][0:128, 0:1],
                   t["xntm"][0:1, 128 * fc:128 * fc + 128], t["ones1"][0:1, 0:1],
                   wait=[bb, k.tp_free[0]], inc="T")
        k.tp_free[0] = k.op("vector",
                            lambda e, fc=fc: e.tensor_copy(
                                out=t["xn"][fc][:, 0:1],
                                in_=k.psum[4][0:128, 0:1]),
                            wait=[tev])
    xn_ev = k.ev("V")

    st = DBG["stages"]
    if "l0attn" in st:
        l0_attn(k, r, xn_ev)
    if "l0ffn" in st:
        l0_ffn(k, r, xn_ev)
    if "l0attn" in st and "l0ffn" in st:
        combine0(k, r)
    else:
        return
    if "l1attn" in st:
        l1_attn(k, r)
        if "l1out" in st:
            l1_ffn(k, r)


def kernel(**inputs):
    in_maps = host_prep(inputs)
    nc, k = build_program()
    res = run_bass_kernel_spmd(nc, in_maps, core_ids=list(range(NCORES)))
    out = np.zeros((NT, D), np.float32)
    for i in range(NCORES):
        o = res.results[i]["out"]
        for r in range(NTILES):
            blk = 8 * r + i
            out[blk * TB:(blk + 1) * TB] = o[r * TB:(r + 1) * TB]
    return out.reshape(1, NT, D)


# version 05
# speedup vs baseline: 1.1942x; 1.0032x over previous
import numpy as np
from contextlib import ExitStack

import concourse.bass as bass
import concourse.mybir as mybir
from concourse.bass_utils import run_bass_kernel_spmd

F32 = mybir.dt.float32
BF16 = mybir.dt.bfloat16
F16 = mybir.dt.float16
I32 = mybir.dt.int32
U8 = mybir.dt.uint8
AF = mybir.ActivationFunctionType
ALU = mybir.AluOpType
AX = mybir.AxisListType.X

NCORES = 8
D = 2048
TB = 512
NTILES = 4
NT = 16384
H0, K0 = 32, 64
H1, K1 = 16, 128
I0 = 5632
GU = 2 * I0
I1 = 5632
LN2 = 0.6931471805599453
SCL1 = 0.088388347648318447

CONSTS = [0.0, 1.0, -1.0, 0.5, -0.5, 2.0, -2.0, 8.0, -8.0,
          LN2, -2.0 * LN2, 1.0 / 2048.0, 1.0 / 64.0,
          1e-5, 0.00064, 1e-6, 1e-8, SCL1, -16383.0]
CIDX = {v: i for i, v in enumerate(CONSTS)}

SEG_ROWS = {"l0": 1280, "l1": 1024, "gul": 128, "guh": 128,
            "dnl": 352, "dnh": 352, "w1q": 256, "w2q": 256, "w3q": 704}
SEG_W = {"l0": D, "l1": D, "gul": GU, "guh": GU, "dnl": D, "dnh": D,
         "w1q": I1, "w2q": I1, "w3q": D}

DBG = {"taps": [], "parts": {"casts", "gamma", "tern", "fp4"},
       "ntiles": NTILES,
       "stages": {"l0attn", "l0ffn", "l1attn", "l1out", "l1ffn"}}


def _sigma_perm():
    out = []
    for c in range(I0 // 256):
        out.extend(range(256 * c, 256 * c + 256, 2))
        out.extend(range(256 * c + 1, 256 * c + 256, 2))
    return np.array(out, dtype=np.int64)


def _col16(v):
    return np.ascontiguousarray(v.reshape(16, 128).T)


def host_prep(inp):
    import ml_dtypes
    x = np.asarray(inp["L0_attn_x"], np.float32).reshape(NT, D)
    state = np.asarray(inp["L0_attn_state"], np.float32).reshape(1, D)

    wrT = np.ascontiguousarray(np.asarray(inp["L0_attn_w_r"], np.float32).T)
    wkT = np.ascontiguousarray(np.asarray(inp["L0_attn_w_k"], np.float32).T)
    wvT = np.ascontiguousarray(np.asarray(inp["L0_attn_w_v"], np.float32).T)
    wgT = np.ascontiguousarray(np.asarray(inp["L0_attn_w_g"], np.float32).T)
    woT = np.ascontiguousarray(np.asarray(inp["L0_attn_w_o"], np.float32).T)
    l1q = np.asarray(inp["L1_attn_Wq"], np.float32)
    l1k = np.asarray(inp["L1_attn_Wk"], np.float32)
    l1i = np.asarray(inp["L1_attn_Wi"], np.float32)
    l1o = np.asarray(inp["L1_attn_Wo"], np.float32)

    sig = _sigma_perm()
    cols_perm = np.concatenate([sig, I0 + sig])
    gu_t = np.ascontiguousarray(
        np.asarray(inp["L0_ffn_gate_up_packed"], np.int32).view(np.uint8).T)
    gu_tp = np.ascontiguousarray(gu_t[:, cols_perm])
    gsc_p = np.ascontiguousarray(
        np.asarray(inp["L0_ffn_gate_up_scales"])[:, cols_perm])
    dn_t = np.ascontiguousarray(
        np.asarray(inp["L0_ffn_down_packed"], np.int32).view(np.uint8).T)
    dsc = np.asarray(inp["L0_ffn_down_scales"])

    w1T = np.ascontiguousarray(np.asarray(inp["L1_ffn_w1"], np.float32).T)
    w2T = np.ascontiguousarray(np.asarray(inp["L1_ffn_w2"], np.float32).T)
    w3T = np.ascontiguousarray(np.asarray(inp["L1_ffn_w3"], np.float32).T)

    tmix = np.concatenate(
        [_col16(np.asarray(inp["L0_attn_time_mix_" + s], np.float32).reshape(D))
         for s in ("r", "k", "v", "g")], axis=1)
    smallp = np.concatenate(
        [_col16(np.asarray(inp[n], np.float32).reshape(D)) for n in
         ("L0_attn_time_decay", "L0_attn_time_faaaa", "L0_attn_ln_w",
          "L0_attn_ln_b", "L1_attn_g_weight")], axis=1)
    wkv_l = np.ascontiguousarray(
        np.asarray(inp["L0_attn_wkvstate"], np.float32).transpose(1, 0, 2).reshape(64, D))

    ccol = np.broadcast_to(np.array(CONSTS, np.float32), (128, len(CONSTS))).copy()
    rng_p = np.arange(128)
    ident = np.eye(128, dtype=np.float32)
    m_lt = (rng_p[:, None] < rng_p[None, :]).astype(np.float32)
    le64 = (np.arange(64)[:, None] <= np.arange(64)[None, :]).astype(np.float32)
    mask_l1 = np.zeros((128, 128), np.float32)
    mask_l1[0:64, 0:64] = le64
    mask_l1[0:64, 64:128] = 1.0
    mask_l1[64:128, 64:128] = le64
    msk_bf = np.concatenate([ident, m_lt, mask_l1], axis=1).astype(ml_dtypes.bfloat16)
    m_diag_i = np.eye(128, dtype=np.int32)
    o642 = np.zeros((128, 2), np.float32)
    o642[0:64, 0] = 1.0
    o642[64:128, 1] = 1.0
    aux = np.concatenate([o642, np.ones((128, 1), np.float32)], axis=1)
    aux_bf = aux.astype(ml_dtypes.bfloat16)
    e2_bf = np.zeros((2, 128), np.float32)
    e2_bf[0, 0:64] = 1.0
    e2_bf[1, 64:128] = 1.0
    e2_bf = e2_bf.astype(ml_dtypes.bfloat16)
    ones1_bf = np.ones((1, 128), np.float32).astype(ml_dtypes.bfloat16)

    in_maps = []
    for i in range(NCORES):
        m = {}
        for r in range(NTILES):
            blk = 8 * r + i
            rows = x[blk * TB: blk * TB + TB]
            bnd = state if blk == 0 else x[blk * TB - 1: blk * TB]
            m[f"x{r}"] = np.concatenate([bnd, rows], axis=0)
            m[f"iota{r}"] = np.broadcast_to(
                np.arange(blk * TB, blk * TB + TB, dtype=np.float32)[None, :],
                (128, TB)).copy()
        fl = np.zeros((128, 2 * NTILES), np.float32)
        for r in range(NTILES):
            f = 0.0 if (8 * r + i) == 0 else 1.0
            fl[:, 2 * r] = f
            fl[:, 2 * r + 1] = 1.0 - f
        m["flags"] = fl
        pm = np.broadcast_to((np.arange(8) < i).astype(np.float32), (128, 8)).copy()
        m["pmask"] = pm
        sh = slice(256 * i, 256 * (i + 1))
        m["l0w_sh"] = np.ascontiguousarray(
            np.concatenate([wrT[sh], wkT[sh], wvT[sh], wgT[sh], woT[sh]], axis=0))
        m["l1w_sh"] = np.ascontiguousarray(
            np.concatenate([l1q[sh], l1k[sh], l1i[sh], l1o[sh]], axis=0))
        m["gu_sh"] = np.ascontiguousarray(gu_tp[128 * i: 128 * (i + 1)])
        m["gu_sc"] = np.ascontiguousarray(
            np.repeat(gsc_p[[2 * i, 2 * i + 1]], 64, axis=0))
        m["dn_sh"] = np.ascontiguousarray(dn_t[352 * i: 352 * (i + 1)])
        m["dn_sc"] = np.ascontiguousarray(dsc[(np.arange(352) + 352 * i) // 64])
        m["w1_sh"] = np.ascontiguousarray(w1T[sh])
        m["w2_sh"] = np.ascontiguousarray(w2T[sh])
        m["w3_sh"] = np.ascontiguousarray(w3T[704 * i: 704 * (i + 1)])
        m["tmix"] = tmix
        m["smallp"] = smallp
        m["wkv0"] = wkv_l
        m["ccol"] = ccol
        m["msk_bf"] = np.asarray(msk_bf)
        m["m_diag_i"] = m_diag_i
        m["aux_bf"] = np.asarray(aux_bf)
        m["e2_bf"] = np.asarray(e2_bf)
        m["ones1_bf"] = np.asarray(ones1_bf)
        m["ident_f32"] = ident
        m["onesf"] = np.ones((128, 16), np.float32)
        in_maps.append(m)
    return in_maps



class Sem:
    def __init__(self, handle):
        self.h = handle
        self.n = 0


class K:
    ENG_SEM = {"vector": "V", "scalar": "S", "gpsimd": "G", "tensor": "T"}

    def __init__(self, nc):
        self.nc = nc
        self.sems = {}
        self.cur = None
        self.seen = {}
        self.tensors = {}

    def new_pass(self, eng):
        self.cur = eng
        self.seen = {}
        for s in self.sems.values():
            s.n = 0

    def op(self, eng, fn, wait=(), inc=None, inc_n=1, ss=True):
        auto = self.ENG_SEM.get(eng)
        if eng in ("vector", "scalar", "gpsimd") and inc is None:
            inc, inc_n = auto, 1
        waits = list(wait)
        if ss and eng in ("vector", "scalar", "gpsimd"):
            pre = self.sems[auto].n
            if pre > 0:
                waits.append((auto, pre))
        if self.cur == eng:
            e = getattr(self.nc, eng)
            for sname, val in waits:
                if val <= 0:
                    continue
                key = (eng, sname)
                if self.seen.get(key, -1) < val:
                    e.wait_ge(self.sems[sname].h, val)
                    self.seen[key] = val
            inst = fn(e)
        else:
            inst = None
        if inc is not None:
            s = self.sems[inc]
            s.n += inc_n
            if inst is not None:
                inst.then_inc(s.h, inc_n)
            return (inc, s.n)
        return None

    def dma(self, eng, out, in_, wait=(), inc="LD", ss=False, **kw):
        return self.op(eng, lambda e: e.dma_start(out=out, in_=in_, **kw),
                       wait=wait, inc=inc, inc_n=16, ss=ss)

    def mm(self, out, lhsT, rhs, start, stop, wait=(), inc=None):
        return self.op("tensor",
                       lambda e: e.matmul(out, lhsT, rhs, start=start, stop=stop),
                       wait=wait, inc=inc, inc_n=1)

    def tr(self, out, in_, ident, wait=(), inc=None):
        return self.op("tensor",
                       lambda e: e.transpose(out, in_, ident),
                       wait=wait, inc=inc, inc_n=1)

    def ev(self, sname):
        return (sname, self.sems[sname].n)


def cc_(k, val):
    return k.tensors["ccol"][:, CIDX[val]:CIDX[val] + 1]


def ccp(k, val, p0, p1):
    return k.tensors["ccol"][p0:p1, CIDX[val]:CIDX[val] + 1]



def build_program():
    nc = bass.Bass("TRN2", target_bir_lowering=False, debug=False,
                   detect_race_conditions=False)
    k = K(nc)
    t = k.tensors
    k.din = {}

    def dram_in(name, shape, dt):
        k.din[name] = nc.dram_tensor(name, shape, dt, kind="ExternalInput").ap()

    for r in range(NTILES):
        dram_in(f"x{r}", [TB + 1, D], F32)
        dram_in(f"iota{r}", [128, TB], F32)
    dram_in("flags", [128, 2 * NTILES], F32)
    dram_in("pmask", [128, 8], F32)
    dram_in("l0w_sh", [5 * 256, D], F32)
    dram_in("l1w_sh", [4 * 256, D], F32)
    dram_in("gu_sh", [128, GU], U8)
    dram_in("gu_sc", [128, GU], F16)
    dram_in("dn_sh", [352, D], U8)
    dram_in("dn_sc", [352, D], F16)
    dram_in("w1_sh", [256, I1], F32)
    dram_in("w2_sh", [256, I1], F32)
    dram_in("w3_sh", [704, D], F32)
    dram_in("tmix", [128, 64], F32)
    dram_in("smallp", [128, 80], F32)
    dram_in("wkv0", [64, D], F32)
    dram_in("ccol", [128, len(CONSTS)], F32)
    dram_in("msk_bf", [128, 384], BF16)
    dram_in("m_diag_i", [128, 128], I32)
    dram_in("aux_bf", [128, 3], BF16)
    dram_in("e2_bf", [2, 128], BF16)
    dram_in("ones1_bf", [1, 128], BF16)
    dram_in("ident_f32", [128, 128], F32)
    dram_in("onesf", [128, 16], F32)

    k.out_hbm = nc.dram_tensor("out", [NTILES * TB, D], F32,
                               kind="ExternalOutput").ap()
    k.dtaps = {}
    for name, shape, dt in DBG["taps"]:
        k.dtaps[name] = nc.dram_tensor("tap " + name, shape, dt,
                                       kind="ExternalOutput").ap()

    woff = {}
    off = 0
    for name in ("l0", "l1", "gul", "guh", "dnl", "dnh", "w1q", "w2q", "w3q"):
        woff[name] = off
        off += SEG_ROWS[name] // (5 if name == "l0" else 4 if name == "l1" else 1) \
            * SEG_W[name] * (5 if name == "l0" else 4 if name == "l1" else 1)
    k.woff = woff
    k.WSH = off
    k.wag_in = nc.dram_tensor("wag_in", [k.WSH], BF16)
    k.wag_out = nc.dram_tensor("wag_out", [NCORES * k.WSH], BF16)
    k.gsum_in = nc.dram_tensor("gsum_in", [1, 8], F32)
    k.gsum_out = nc.dram_tensor("gsum_out", [1, 8], F32)
    k.gtmp = nc.dram_tensor("gtmp", [128, 4], F32)
    k.sag = {}
    for r in range(NTILES):
        k.sag[f"l0i{r}"] = nc.dram_tensor(f"sag_l0i{r}", [64, H0 * K0], F32)
        k.sag[f"l0o{r}"] = nc.dram_tensor(f"sag_l0o{r}", [NCORES * 64, H0 * K0], F32)
        k.sag[f"l1i{r}"] = nc.dram_tensor(f"sag_l1i{r}", [128, H1 * K1 + 16], F32)
        k.sag[f"l1o{r}"] = nc.dram_tensor(f"sag_l1o{r}",
                                          [NCORES * 128, H1 * K1 + 16], F32)
    k.x1scr = nc.dram_tensor("x1scr", [TB, D], F32)
    k.sscr = nc.dram_tensor("sscr", [16, 128, TB], F32)
    k.xnscr = nc.dram_tensor("xnscr", [16, 128, TB + 4], BF16)

    with ExitStack() as ctx:
        for sname in ("LD", "LW", "GD", "V", "S", "T", "G", "CC", "OUT"):
            k.sems[sname] = Sem(ctx.enter_context(nc.semaphore(sname)))
        alloc_tensors(nc, k)
        for v in CONSTS:
            if v not in (0.0, 1.0):
                nc.const_aps.register(F32, v, t["ccol"][:, CIDX[v]:CIDX[v] + 1])
        block = ctx.enter_context(nc.Block())

        def run(eng):
            def f(_):
                k.new_pass(eng)
                walk(k)
            return f
        block.sync(run("sync"))
        block.gpsimd(run("gpsimd"))
        block.vector(run("vector"))
        block.scalar(run("scalar"))
        block.tensor(run("tensor"))
    return nc, k


def alloc_tensors(nc, k):
    t = k.tensors

    def P(name, shape, dt):
        t[name] = nc.alloc_sbuf_tensor("sb_" + name, shape, dt).ap()

    P("ccol", [128, len(CONSTS)], F32)
    P("tmix", [128, 64], F32)
    P("smallp", [128, 80], F32)
    P("flags", [128, 2 * NTILES], F32)
    P("pmask", [128, 8], F32)
    P("msk_bf", [128, 384], BF16)
    P("m_diag_i", [128, 128], I32)
    P("aux_bf", [128, 3], BF16)
    P("e2", [2, 128], BF16)
    P("ones1", [1, 128], BF16)
    P("ident_f32", [128, 128], F32)
    P("onesf", [128, 16], F32)
    P("iota", [128, TB], F32)
    P("etd", [128, 16], F32)
    P("netd", [128, 16], F32)
    P("bwb", [128, 16], F32)
    P("wkv0b", [64, D], BF16)
    P("s0f", [64, D], F32)
    P("s0b", [64, D], BF16)
    P("s0run", [64, D], F32)
    P("s1f", [128, H1 * K1], F32)
    P("s1b", [128, H1 * K1], BF16)
    P("s1run", [128, H1 * K1], F32)
    P("s1lg", [128, 16], F32)
    P("tcol", [128, 64], F32)
    P("trow", [1, 64], F32)
    P("trow_bf", [1, 64], BF16)
    P("ginv", [128, 4], F32)
    P("hbuf", [128, 64], F32)

    k.psum = []
    for b in range(8):
        if b in (4, 5):
            h = nc.alloc_psum_tensor(f"ps{b}", [128, 1024], BF16)
        else:
            h = nc.alloc_psum_tensor(f"ps{b}", [128, 512], F32)
        k.psum.append(h.ap())

    rem = nc.sbuf_bytes_remaining
    base = nc.SBUF_PARTITION_SIZE_BYTES - rem + 128
    base = (base + 31) // 32 * 32
    k.arena_base = base
    k.arena_size = nc.SBUF_PARTITION_SIZE_BYTES - base - 256

    def A(name, shape, dt, offset):
        nbytes = int(np.prod(shape[1:])) * mybir.dt.size(dt)
        assert offset + nbytes <= k.arena_size, (name, offset, nbytes, k.arena_size)
        t[name] = nc.alloc_sbuf_tensor_at("sb_" + name, shape, dt,
                                          offset=base + offset).ap()
        return offset + ((nbytes + 31) // 32 * 32)

    def CL(name, n, shape, dt, offset):
        o = offset
        lst = []
        for i in range(n):
            o = A(f"{name}{i}", shape, dt, o)
            lst.append(t[f"{name}{i}"])
        t[name] = lst
        return o

    R1 = 0
    R2 = CL("xn", 16, [128, TB + 4], BF16, R1)
    R3 = CL("mixy", 16, [128, TB], BF16, R2)
    o = R3
    for nm in ("rf", "kw", "ku", "rw"):
        o = CL(nm, 16, [128, TB], BF16, o)
    R4 = o
    R5 = CL("gsl", 16, [128, TB], BF16, R4)
    R6 = CL("vt", TB // 128, [128, D], BF16, R5)
    R7 = CL("wsl", 16, [128, 512], BF16, R6)
    o = R7
    for nm in ("rrh", "kwh", "kuh", "rwh"):
        o = A(nm, [64, TB], BF16, o)
    for nm in ("attsb", "q2b", "ks2b", "kst"):
        o = A(nm, [128, 128], BF16, o)
    stg_off = o
    o = A("stg", [128, TB], F32, o)
    o = A("stg2", [128, TB], F32, o)
    R8 = o
    o = A("tB", [128, TB], F32, o)
    o = A("tD", [128, 16], F32, o)
    o = A("lgpre", [128, 128], F32, o)
    arena_hi = o
    A("tC", [128, TB], F32, stg_off)
    assert arena_hi <= k.arena_size, (arena_hi, k.arena_size)

    o = 0
    for nm, shape, dt in (("wp_u8", [128, 2816], U8), ("wp_sc", [128, 2816], F16),
                          ("wp_iA", [128, I1], I32), ("wp_iB", [128, 2816], I32),
                          ("wp_iC", [128, 2816], I32), ("wp_fA", [128, I1], F32),
                          ("wp_fB", [128, 2816], F32), ("wp_fC", [128, 2816], F32),
                          ("wp_fD", [128, 2816], F32), ("wp_f32", [128, I1], F32),
                          ("wp_bf", [128, I1], BF16)):
        o = A(nm, shape, dt, o)
    o = A("wkv0f", [64, D], F32, o)
    assert o <= k.arena_size
    o = A("xtm", [128, D], F32, R5)
    A("xsq", [128, D], F32, o)
    A("xntm", [128, D], BF16, R7 + 4096 + 1024)
    CL("xnil", 16, [128, TB], BF16, R2)
    CL("hh", 44, [128, TB], BF16, R3)
    o = CL("dd", 8, [128, TB], F32, R4)
    CL("ddx", 8, [128, TB], F32, o)
    t["dd"] = t["dd"] + t["ddx"]
    CL("wsl2", 16, [128, 512], BF16, R3 + 44 * 1024)
    o = R3
    for nm in ("qt", "kd", "ks"):
        o = CL(nm, 16, [128, TB], BF16, o)
    for nm in ("kraw", "sgb", "gbv", "abuf"):
        o = A(nm, [128, TB], F32, o)
    assert o <= R4, (o, R4)
    CL("hid", 44, [128, TB], BF16, R3)
    o = CL("fo", 8, [128, TB], F32, R4)
    CL("fox", 8, [128, TB], F32, o)
    t["fo"] = t["fo"] + t["fox"]
    A("cxa", [128, D], F32, R2)
    A("cxb", [128, D], F32, R2 + 8192)
    A("csq", [128, D], F32, R3)
    A("s0in", [64, D], F32, R6)
    A("s0inb", [64, D], BF16, R6 + 8192)
    A("s1in", [128, H1 * K1], F32, R6)
    A("s1inb", [128, H1 * K1], BF16, R6 + 8192)
    CL("kwt", TB // 128, [128, D], BF16, R6)
    A("gnhl", [2, 4 * 512], BF16, R7)
    A("rmshl", [1, 2 * 512], BF16, R7)
    k.regions = dict(R1=R1, R2=R2, R3=R3, R4=R4, R5=R5, R6=R6, R7=R7, R8=R8,
                     hi=arena_hi)


def wag_seg(k, elt0, p, w):
    return k.wag_in.ap()[elt0: elt0 + p * w].rearrange("(p w) -> p w", w=w)


def wago(k, core, segoff, p, w):
    elt0 = core * k.WSH + segoff
    return k.wag_out.ap()[elt0: elt0 + p * w].rearrange("(p w) -> p w", w=w)


def load_wslab(k, dst, seg, row0, nrows, col0, width, wait=(), inc="LW"):
    rpc = SEG_ROWS[seg]
    w = SEG_W[seg]
    ev = None
    done = 0
    while done < nrows:
        row = row0 + done
        core = row // rpc
        rin = row % rpc
        take = min(nrows - done, rpc - rin)
        src = wago(k, core, k.woff[seg] + rin * w, take, w)[:, col0:col0 + width]
        ev = k.dma("sync", dst[done:done + take, 0:width], src,
                   wait=wait if done == 0 else (), inc=inc)
        done += take
    return ev



def weight_prep(k):
    nc, t, din, woff = k.nc, k.tensors, k.din, k.woff
    zout = k.ev("OUT")
    prev_cv = zout

    if "casts" in DBG["parts"]:
        for mi in range(9):
            src = din["l0w_sh"] if mi < 5 else din["l1w_sh"]
            row0 = (mi if mi < 5 else mi - 5) * 256
            segoff = (woff["l0"] + mi * 256 * D) if mi < 5 else (
                woff["l1"] + (mi - 5) * 256 * D)
            for chn in range(2):
                ld = k.dma("sync", t["wp_f32"][:, 0:D],
                           src[row0 + 128 * chn: row0 + 128 * (chn + 1), :],
                           wait=[prev_cv], inc="LW")
                cv = k.op("vector",
                          lambda e: e.tensor_copy(out=t["wp_bf"][:, 0:D],
                                                  in_=t["wp_f32"][:, 0:D]),
                          wait=[ld, k.ev("GD")])
                prev_cv = cv
                k.dma("gpsimd", wag_seg(k, segoff + 128 * chn * D, 128, D),
                      t["wp_bf"][:, 0:D], wait=[cv], inc="GD")

    if "gamma" in DBG["parts"]:
        for g, (nm, rows, cols) in enumerate(
                (("w1_sh", 256, I1), ("w2_sh", 256, I1), ("w3_sh", 704, D))):
            nch = (rows + 127) // 128
            for chn in range(nch):
                p = min(128, rows - 128 * chn)
                ld = k.dma("sync", t["wp_f32"][0:p, 0:cols],
                           din[nm][128 * chn: 128 * chn + p, :],
                           wait=[prev_cv, k.ev("V")], inc="LW")
                nseg = (cols + 511) // 512
                for sg in range(nseg):
                    s0, s1 = 512 * sg, min(cols, 512 * (sg + 1))
                    k.op("vector",
                         lambda e, p=p, s0=s0, s1=s1, sg=sg:
                         e.reduce_sum(out=t["tcol"][0:p, sg:sg + 1],
                                      in_=t["wp_f32"][0:p, s0:s1], axis=AX,
                                      apply_absolute_value=True),
                         wait=[ld])
                dst = 16 + g if chn == 0 else 24 + g
                prev_cv = k.op("vector",
                               lambda e, p=p, nseg=nseg, dst=dst:
                               e.reduce_sum(out=t["tcol"][0:p, dst:dst + 1],
                                            in_=t["tcol"][0:p, 0:nseg], axis=AX))
                if chn > 0:
                    prev_cv = k.op("vector",
                                   lambda e, p=p, g=g:
                                   e.tensor_tensor(
                                       out=t["tcol"][0:p, 16 + g:17 + g],
                                       in0=t["tcol"][0:p, 16 + g:17 + g],
                                       in1=t["tcol"][0:p, 24 + g:25 + g],
                                       op=ALU.add))
        gv = k.ev("V")
        gd = k.dma("gpsimd", k.gtmp.ap()[:, 0:3], t["tcol"][:, 16:19], wait=[gv],
                   inc="GD")
        gl = None
        for g in range(3):
            gl = k.dma("sync", t["wp_fA"][0:1, 128 * g:128 * (g + 1)],
                       k.gtmp.ap()[:, g:g + 1], wait=[gd], inc="LW",
                       allow_slow_non_contiguous=True)
        for g in range(3):
            k.op("vector",
                 lambda e, g=g: e.reduce_sum(
                     out=t["trow"][0:1, g:g + 1],
                     in_=t["wp_fA"][0:1, 128 * g:128 * (g + 1)], axis=AX),
                 wait=[gl])
        gv2 = k.ev("V")
        gd2 = k.dma("gpsimd", k.gsum_in.ap()[0:1, 0:8], t["trow"][0:1, 0:8],
                    wait=[gv2], inc="GD")
        ccr = k.op("gpsimd",
                   lambda e: e.collective_compute(
                       "AllReduce", ALU.add, replica_groups=[list(range(NCORES))],
                       ins=[k.gsum_in.ap().opt()], outs=[k.gsum_out.ap().opt()]),
                   wait=[gd2], inc="CC")
        gl2 = k.dma("sync", t["trow"][0:1, 8:11], k.gsum_out.ap()[0:1, 0:3],
                    wait=[ccr], inc="LW")
        k.op("vector",
             lambda e: e.tensor_scalar(out=t["trow"][0:1, 12:15],
                                       in0=t["trow"][0:1, 8:11],
                                       scalar1=1.0 / (I1 * D), scalar2=None,
                                       op0=ALU.mult),
             wait=[gl2])
        k.op("vector", lambda e: e.tensor_copy(out=t["trow_bf"][0:1, 0:3],
                                               in_=t["trow"][0:1, 12:15]))
        k.op("vector", lambda e: e.tensor_copy(out=t["trow"][0:1, 16:19],
                                               in_=t["trow_bf"][0:1, 0:3]))
        k.op("vector", lambda e: e.tensor_tensor(out=t["trow"][0:1, 20:23],
                                                 in0=t["trow"][0:1, 12:15],
                                                 in1=t["trow"][0:1, 16:19],
                                                 op=ALU.subtract))
        hv = k.op("vector", lambda e: e.tensor_copy(out=t["trow_bf"][0:1, 8:11],
                                                    in_=t["trow"][0:1, 20:23]))
        k.mm(k.psum[6][:, 0:3], t["ones1"][0:1, :], t["trow_bf"][0:1, 0:3],
             True, False, wait=[hv, k.ld_small])
        mm2 = k.mm(k.psum[6][:, 0:3], t["ones1"][0:1, :], t["trow_bf"][0:1, 8:11],
                   False, True, inc="T")
        ad = k.op("vector",
                  lambda e: e.tensor_scalar(out=t["ginv"][:, 0:3],
                                            in0=k.psum[6][:, 0:3],
                                            scalar1=1e-8, scalar2=None,
                                            op0=ALU.add),
                  wait=[mm2])
        k.ginv_ev = k.op("vector",
                         lambda e: e.reciprocal(out=t["ginv"][:, 0:3],
                                                in_=t["ginv"][:, 0:3]),
                         wait=[ad])

    if "tern" in DBG["parts"]:
        gi = k.ginv_ev
        for g, (nm, rows, cols, seg) in enumerate(
                (("w1_sh", 256, I1, "w1q"), ("w2_sh", 256, I1, "w2q"),
                 ("w3_sh", 704, D, "w3q"))):
            nch = (rows + 127) // 128
            for chn in range(nch):
                p = min(128, rows - 128 * chn)
                ld = k.dma("sync", t["wp_f32"][0:p, 0:cols],
                           din[nm][128 * chn: 128 * chn + p, :],
                           wait=[k.ev("S"), k.ev("V")], inc="LW")
                s1 = k.op("scalar",
                          lambda e, p=p, cols=cols, g=g:
                          e.mul(out=t["wp_fA"][0:p, 0:cols],
                                in_=t["wp_f32"][0:p, 0:cols],
                                mul=t["ginv"][0:p, g:g + 1]),
                          wait=[ld, gi])
                k.op("vector",
                     lambda e, p=p, cols=cols:
                     e.tensor_scalar(out=t["wp_f32"][0:p, 0:cols],
                                     in0=t["wp_fA"][0:p, 0:cols],
                                     scalar1=0.5, scalar2=None,
                                     op0=ALU.is_gt),
                     wait=[s1, k.ev("GD")])
                k.op("vector",
                     lambda e, p=p, cols=cols:
                     e.tensor_scalar(out=t["wp_fA"][0:p, 0:cols],
                                     in0=t["wp_fA"][0:p, 0:cols],
                                     scalar1=-0.5, scalar2=None,
                                     op0=ALU.is_lt))
                r3 = k.op("vector",
                          lambda e, p=p, cols=cols:
                          e.tensor_tensor(out=t["wp_bf"][0:p, 0:cols],
                                          in0=t["wp_f32"][0:p, 0:cols],
                                          in1=t["wp_fA"][0:p, 0:cols],
                                          op=ALU.subtract))
                k.dma("gpsimd",
                      wag_seg(k, woff[seg] + 128 * chn * cols, p, cols),
                      t["wp_bf"][0:p, 0:cols], wait=[r3], inc="GD")

    if "fp4" in DBG["parts"]:
        fp4_decode(k, "gu_sh", "gu_sc", 128, GU, 2816, ("gul", "guh"))
        fp4_decode(k, "dn_sh", "dn_sc", 352, D, 2048, ("dnl", "dnh"))

    aw = k.ev("GD")
    k.wag_done = k.op("gpsimd",
                      lambda e: e.collective_compute(
                          "AllGather", ALU.bypass,
                          replica_groups=[list(range(NCORES))],
                          ins=[k.wag_in.ap().opt()],
                          outs=[k.wag_out.ap().opt()]),
                      wait=[aw], inc="CC")


def fp4_decode(k, bname, scname, rows, cols, W, segs):
    t, din, woff = k.tensors, k.din, k.woff
    nch = (rows + 127) // 128
    for chn in range(nch):
        p = min(128, rows - 128 * chn)
        for c0 in range(0, cols, W):
            w = min(W, cols - c0)
            ldb = k.dma("sync", t["wp_u8"][0:p, 0:w],
                        din[bname][128 * chn: 128 * chn + p, c0:c0 + w],
                        wait=[k.ev("V"), k.ev("S")], inc="LW")
            lds = k.dma("sync", t["wp_sc"][0:p, 0:w],
                        din[scname][128 * chn: 128 * chn + p, c0:c0 + w], inc="LW")
            k.op("vector",
                 lambda e, p=p, w=w: e.tensor_copy(
                     out=t["wp_iA"][0:p, 0:w], in_=t["wp_u8"][0:p, 0:w]),
                 wait=[ldb, lds])
            for par, seg in enumerate(segs):
                if par == 0:
                    k.op("vector",
                         lambda e, p=p, w=w: e.tensor_scalar(
                             out=t["wp_iB"][0:p, 0:w], in0=t["wp_iA"][0:p, 0:w],
                             scalar1=15, scalar2=None, op0=ALU.bitwise_and))
                else:
                    k.op("vector",
                         lambda e, p=p, w=w: e.tensor_scalar(
                             out=t["wp_iB"][0:p, 0:w], in0=t["wp_iA"][0:p, 0:w],
                             scalar1=4, scalar2=None,
                             op0=ALU.logical_shift_right))
                k.op("vector", lambda e, p=p, w=w: e.tensor_scalar(
                    out=t["wp_iC"][0:p, 0:w], in0=t["wp_iB"][0:p, 0:w],
                    scalar1=1, scalar2=3, op0=ALU.logical_shift_right,
                    op1=ALU.bitwise_and))
                ev_ = k.op("vector", lambda e, p=p, w=w: e.tensor_copy(
                    out=t["wp_fA"][0:p, 0:w], in_=t["wp_iC"][0:p, 0:w]))
                t2 = k.op("scalar", lambda e, p=p, w=w: e.activation(
                    out=t["wp_fB"][0:p, 0:w], in_=t["wp_fA"][0:p, 0:w],
                    func=AF.Exp, scale=LN2, bias=-2.0 * LN2),
                    wait=[ev_])
                k.op("vector", lambda e, p=p, w=w: e.tensor_scalar(
                    out=t["wp_iC"][0:p, 0:w], in0=t["wp_iB"][0:p, 0:w],
                    scalar1=1, scalar2=None, op0=ALU.bitwise_and))
                k.op("vector", lambda e, p=p, w=w: e.tensor_copy(
                    out=t["wp_fC"][0:p, 0:w], in_=t["wp_iC"][0:p, 0:w]))
                k.op("vector", lambda e, p=p, w=w: e.scalar_tensor_tensor(
                    out=t["wp_fD"][0:p, 0:w], in0=t["wp_fC"][0:p, 0:w],
                    scalar=2.0, in1=t["wp_fB"][0:p, 0:w],
                    op0=ALU.add, op1=ALU.mult), wait=[t2])
                k.op("vector", lambda e, p=p, w=w: e.tensor_scalar(
                    out=t["wp_fA"][0:p, 0:w], in0=t["wp_fA"][0:p, 0:w],
                    scalar1=8.0, scalar2=None, op0=ALU.mult))
                k.op("vector", lambda e, p=p, w=w: e.scalar_tensor_tensor(
                    out=t["wp_fA"][0:p, 0:w], in0=t["wp_fC"][0:p, 0:w],
                    scalar=0.5, in1=t["wp_fA"][0:p, 0:w],
                    op0=ALU.mult, op1=ALU.add))
                k.op("vector", lambda e, p=p, w=w: e.tensor_tensor(
                    out=t["wp_fD"][0:p, 0:w], in0=t["wp_fD"][0:p, 0:w],
                    in1=t["wp_fA"][0:p, 0:w], op=ALU.min))
                k.op("vector", lambda e, p=p, w=w: e.tensor_scalar(
                    out=t["wp_iC"][0:p, 0:w], in0=t["wp_iB"][0:p, 0:w],
                    scalar1=3, scalar2=None, op0=ALU.logical_shift_right))
                k.op("vector", lambda e, p=p, w=w: e.tensor_copy(
                    out=t["wp_fC"][0:p, 0:w], in_=t["wp_iC"][0:p, 0:w]))
                k.op("vector", lambda e, p=p, w=w: e.tensor_scalar(
                    out=t["wp_fC"][0:p, 0:w], in0=t["wp_fC"][0:p, 0:w],
                    scalar1=-2.0, scalar2=1.0, op0=ALU.mult, op1=ALU.add))
                k.op("vector", lambda e, p=p, w=w: e.tensor_tensor(
                    out=t["wp_fD"][0:p, 0:w], in0=t["wp_fD"][0:p, 0:w],
                    in1=t["wp_fC"][0:p, 0:w], op=ALU.mult))
                sv = k.op("vector", lambda e, p=p, w=w: e.tensor_tensor(
                    out=t["wp_bf"][0:p, 0:w], in0=t["wp_fD"][0:p, 0:w],
                    in1=t["wp_sc"][0:p, 0:w], op=ALU.mult),
                    wait=[k.ev("GD")])
                k.dma("gpsimd",
                      wag_seg(k, woff[seg] + (128 * chn) * cols, p,
                              cols)[:, c0:c0 + w],
                      t["wp_bf"][0:p, 0:w], wait=[sv], inc="GD")



def walk(k):
    t, din = k.tensors, k.din

    for name, src in [("ccol", "ccol"), ("tmix", "tmix"), ("smallp", "smallp"),
                      ("flags", "flags"), ("pmask", "pmask"),
                      ("msk_bf", "msk_bf"), ("m_diag_i", "m_diag_i"),
                      ("aux_bf", "aux_bf"), ("e2", "e2_bf"),
                      ("ones1", "ones1_bf"), ("ident_f32", "ident_f32"),
                      ("onesf", "onesf"), ("wkv0f", "wkv0")]:
        k.dma("sync", t[name][:, :], din[src][:, :])
    k.ld_small = k.ev("LD")

    for b in range(8):
        k.op("scalar", lambda e, b=b: e.memzero(k.psum[b][:, :]))
    k.op("scalar", lambda e: e.activation(out=t["etd"][:, :],
                                          in_=t["smallp"][:, 0:16], func=AF.Exp),
         wait=[k.ld_small])
    se = k.ev("S")
    k.op("vector", lambda e: e.tensor_scalar(out=t["netd"][:, :],
                                             in0=t["etd"][:, :], scalar1=-1.0,
                                             scalar2=None, op0=ALU.mult),
         wait=[se])
    k.op("vector", lambda e: e.tensor_scalar(out=t["bwb"][:, :],
                                             in0=t["etd"][:, :], scalar1=-16383.0,
                                             scalar2=None, op0=ALU.mult))
    k.op("vector", lambda e: e.tensor_copy(out=t["wkv0b"][:, :],
                                           in_=t["wkv0f"][:, :]),
         wait=[k.ld_small])
    k.op("vector", lambda e: e.memzero(t["s0run"][:, :]))
    k.op("vector", lambda e: e.memzero(t["s1run"][:, :]))
    k.op("vector", lambda e: e.memzero(t["trow"][0:1, 0:16]))

    weight_prep(k)

    for r in range(DBG["ntiles"]):
        tile(k, r)

    emit_taps(k)

    fin = k.ev("OUT")
    k.op("gpsimd", lambda e: e.nop(), wait=[fin, k.wag_done], inc="G")


def emit_taps(k):
    t = k.tensors
    allq = [k.ev("V"), k.ev("S"), k.ev("T"), k.ev("G"), k.wag_done,
            k.ev("GD"), k.ev("LD"), k.ev("LW"), k.ev("CC")]
    for name, ap in k.dtaps.items():
        parts = name.split(":")
        if parts[0] == "sbuf":
            _, tname, p, w = parts
            p, w = int(p), int(w)
            k.dma("gpsimd", ap[:, :], t[tname][0:p, 0:w], wait=allq, inc="OUT")
        elif parts[0] == "psum":
            _, idx, p, w = parts
            idx, p, w = int(idx), int(p), int(w)
            cpe = k.op("vector", lambda e, idx=idx, p=p, w=w: e.tensor_copy(
                out=t["xtm"][0:p, 0:w], in_=k.psum[idx][0:p, 0:w]),
                wait=allq + [k.ev("OUT")])
            k.dma("gpsimd", ap[:, :], t["xtm"][0:p, 0:w], wait=[cpe],
                  inc="OUT")
        elif parts[0] == "wag":
            _, seg, core, row0, p, w = parts
            core, row0, p, w = int(core), int(row0), int(p), int(w)
            src = wago(k, core, k.woff[seg] + row0 * w, p, w)
            k.dma("gpsimd", ap[:, :], src, wait=[k.wag_done], inc="OUT")
        elif parts[0] == "hbm":
            _, tn, row0, p, w = parts
            row0, p, w = int(row0), int(p), int(w)
            src = getattr(k, tn)
            if callable(getattr(src, "ap", None)):
                src = src.ap()
            src = src.rearrange("a b c -> (a b) c") if len(src.shape) == 3 else src
            k.dma("gpsimd", ap[:, :], src[row0:row0 + p, 0:w], wait=allq,
                  inc="OUT")





def qslab(k, dst, seg, row0, col0, width, nrows=128, wait=(), inc="LW",
          eng="sync"):
    rpc = SEG_ROWS[seg]
    w = SEG_W[seg]
    ev = None
    done = 0
    while done < nrows:
        row = row0 + done
        core, rin = divmod(row, rpc)
        take = min(nrows - done, rpc - rin)
        src = wago(k, core, k.woff[seg] + rin * w, take, w)
        ev = k.dma(eng, dst[done:done + take, 0:width],
                   src[:, col0:col0 + width],
                   wait=wait if done == 0 else (), inc=inc)
        done += take
    return ev


def attn_slab(k, dst, layer, mi, kk, col0, width, wait=(), inc="LW",
              eng="sync"):
    row = 128 * kk
    core, rin = divmod(row, 256)
    src = wago(k, core, k.woff[layer] + (mi * 256 + rin) * D, 128, D)
    return k.dma(eng, dst[:, 0:width], src[:, col0:col0 + width],
                 wait=wait, inc=inc)


def barrier(k):
    evs = [k.ev(s) for s in ("V", "S", "T", "G", "LD", "LW", "GD", "OUT", "CC")]
    for eng in ("sync", "gpsimd", "vector", "scalar", "tensor"):
        k.op(eng, lambda e: e.nop(), wait=evs, ss=False)
    return evs


def proj2048(k, layer, mi, rhs_chunks, out_cb, rhs_ev=None, n_out=16):
    t = k.tensors
    m3 = None
    for mg in range((n_out + 3) // 4):
        evs = []
        for kk in range(16):
            if m3 is None:
                w8 = [k.slab_free] if kk <= 1 else ()
            else:
                w8 = [m3[kk]]
            evs.append(attn_slab(k, t["wsl"][kk], layer, mi, kk, 512 * mg, 512,
                                 wait=w8, eng="gpsimd" if kk % 2 else "sync",
                                 inc="GD" if kk % 2 else "LW"))
        evall = [evs[-2], evs[-1]]
        new_m3 = [None] * 16
        last = None
        for m in range(4):
            oc = 4 * mg + m
            if oc >= n_out:
                break
            bank = k.psum[m]
            for kk in range(16):
                w = list(evall) if kk == 0 else []
                if kk == 0:
                    w.append(k.bank_ev[m])
                    if rhs_ev is not None:
                        w.append(rhs_ev)
                last = k.mm(bank[:, 0:TB], t["wsl"][kk][:, 128 * m:128 * m + 128],
                            rhs_chunks[kk][:, 0:TB],
                            start=(kk == 0), stop=(kk == 15),
                            wait=w,
                            inc="T" if (kk == 15 or m == 3) else None)
                if m == 3:
                    new_m3[kk] = last
            out_cb(oc, bank, last)
        if new_m3[15] is not None:
            m3 = new_m3
        if last is not None:
            k.slab_free = last


def ln_row_chunk(k, r, tc, src_rows, dst_writer):
    t, din = k.tensors, k.din
    ld = k.dma("sync", t["xtm"][:, :], din[f"x{r}"][src_rows:src_rows + 128, :],
               wait=[k.ev("S"), k.ev("V")])
    k.op("scalar", lambda e: e.activation(out=t["xsq"][:, :], in_=t["xtm"][:, :],
                                          func=AF.Square), wait=[ld])
    sq = k.ev("S")
    k.op("vector", lambda e: e.reduce_sum(out=t["hbuf"][:, 0:1], in_=t["xtm"][:, :],
                                          axis=AX), wait=[ld])
    k.op("vector", lambda e: e.reduce_sum(out=t["hbuf"][:, 1:2], in_=t["xsq"][:, :],
                                          axis=AX), wait=[sq])
    k.op("vector", lambda e: e.tensor_scalar(out=t["hbuf"][:, 2:3],
                                             in0=t["hbuf"][:, 0:1],
                                             scalar1=1.0 / 2048.0, scalar2=None,
                                             op0=ALU.mult))
    k.op("vector", lambda e: e.tensor_scalar(out=t["hbuf"][:, 3:4],
                                             in0=t["hbuf"][:, 1:2],
                                             scalar1=1.0 / 2048.0, scalar2=None,
                                             op0=ALU.mult))
    k.op("vector", lambda e: e.tensor_tensor(out=t["hbuf"][:, 4:5],
                                             in0=t["hbuf"][:, 2:3],
                                             in1=t["hbuf"][:, 2:3], op=ALU.mult))
    vv = k.op("vector", lambda e: e.tensor_tensor(out=t["hbuf"][:, 4:5],
                                                  in0=t["hbuf"][:, 3:4],
                                                  in1=t["hbuf"][:, 4:5],
                                                  op=ALU.subtract))
    k.op("scalar", lambda e: e.activation(out=t["hbuf"][:, 5:6],
                                          in_=t["hbuf"][:, 4:5], func=AF.Ln,
                                          bias=1e-5), wait=[vv])
    k.op("scalar", lambda e: e.activation(out=t["hbuf"][:, 5:6],
                                          in_=t["hbuf"][:, 5:6], func=AF.Exp,
                                          scale=-0.5))
    sr = k.ev("S")
    k.op("vector", lambda e: e.tensor_tensor(out=t["hbuf"][:, 6:7],
                                             in0=t["hbuf"][:, 2:3],
                                             in1=t["hbuf"][:, 5:6], op=ALU.mult),
         wait=[sr])
    nb = k.op("vector", lambda e: e.tensor_scalar(out=t["hbuf"][:, 6:7],
                                                  in0=t["hbuf"][:, 6:7],
                                                  scalar1=-1.0, scalar2=None,
                                                  op0=ALU.mult))
    xe = k.op("scalar", lambda e: e.activation(out=t["xntm"][:, :],
                                               in_=t["xtm"][:, :],
                                               func=AF.Identity,
                                               scale=t["hbuf"][:, 5:6],
                                               bias=t["hbuf"][:, 6:7]),
              wait=[nb, k.ev("T")])
    dst_writer(tc, xe)


def transpose_to_fm(k, src_ev, src_ap_fn, dst_fn):
    t = k.tensors
    for half in range(2):
        bank = k.psum[4 + half]
        tev = None
        for j in range(8):
            fc = 8 * half + j
            w = [k.tp_free[half]] if j == 0 else []
            if j == 0 and src_ev is not None:
                w.append(src_ev)
            tev = k.tr(bank[:, 128 * j:128 * j + 128],
                       src_ap_fn(fc), t["msk_bf"][:, 0:128],
                       wait=w, inc="T" if j == 7 else None)
        ev = None
        for j in range(8):
            fc = 8 * half + j
            ev = dst_fn(fc, bank[:, 128 * j:128 * j + 128],
                        [tev] if j == 0 else ())
        k.tp_free[half] = ev





def l0_attn(k, r, xn_ev):
    t = k.tensors
    cut = DBG.get("l0cut", 99)
    if cut <= 0:
        return

    def build_mix(mi, wait0):
        ev = None
        for fc in range(16):
            k.op("vector", lambda e, fc=fc: e.tensor_tensor(
                out=t["tB"][:, 0:TB], in0=t["xn"][fc][:, 1:1 + TB],
                in1=t["xn"][fc][:, 0:TB], op=ALU.subtract),
                wait=list(wait0) if fc == 0 else ())
            ev = k.op("vector", lambda e, fc=fc, mi=mi: e.scalar_tensor_tensor(
                out=t["mixy"][fc][:, 0:TB], in0=t["tB"][:, 0:TB],
                scalar=t["tmix"][:, 16 * mi + fc:16 * mi + fc + 1],
                in1=t["xn"][fc][:, 0:TB], op0=ALU.mult, op1=ALU.add))
        return ev

    iota_ev = k.iota_ev
    mev = build_mix(0, [xn_ev])
    if cut <= 0.2:
        return
    prev_read = [k.ev("V")]

    def cb_r(oc, bank, mmev):
        k.op("vector", lambda e, oc=oc, bank=bank: e.tensor_copy(
            out=t["rf"][oc][:, 0:TB], in_=bank[:, 0:TB]), wait=[mmev])
        wf = k.op("scalar", lambda e, oc=oc: e.activation(
            out=t["tC"][:, 0:TB], in_=t["iota"][:, 0:TB], func=AF.Exp,
            scale=t["netd"][:, oc:oc + 1]),
            wait=[iota_ev, prev_read[0]])
        ev = k.op("vector", lambda e, bank=bank, oc=oc: e.tensor_tensor(
            out=t["rw"][oc][:, 0:TB], in0=bank[:, 0:TB],
            in1=t["tC"][:, 0:TB], op=ALU.mult), wait=[wf])
        prev_read[0] = ev
        k.bank_ev[oc % 4] = ev
    proj2048(k, "l0", 0, t["mixy"], cb_r, rhs_ev=mev)
    if cut <= 0.4:
        return

    mev = build_mix(1, [k.ev("T")])
    if cut <= 0.5:
        return
    kvar = DBG.get("kvar", 0)

    def cb_k(oc, bank, mmev):
        if kvar == 1:
            ev = k.op("vector", lambda e, oc=oc, bank=bank: e.tensor_copy(
                out=t["kw"][oc][:, 0:TB], in_=bank[:, 0:TB]), wait=[mmev])
            k.bank_ev[oc % 4] = ev
            return
        if kvar == 3:
            k.op("vector", lambda e, oc=oc, bank=bank: e.tensor_copy(
                out=t["kw"][oc][:, 0:TB], in_=bank[:, 0:TB]), wait=[mmev])
            ev = k.op("scalar", lambda e, bank=bank, oc=oc: e.mul(
                out=t["ku"][oc][:, 0:TB], in_=bank[:, 0:TB],
                mul=t["smallp"][:, 16 + oc:17 + oc]),
                wait=[mmev])
            k.bank_ev[oc % 4] = ev
            return
        wb = k.op("scalar", lambda e, oc=oc: e.activation(
            out=t["tC"][:, 0:TB], in_=t["iota"][:, 0:TB], func=AF.Exp,
            scale=t["etd"][:, oc:oc + 1], bias=t["bwb"][:, oc:oc + 1]),
            wait=[prev_read[0]])
        tte = k.op("vector", lambda e, bank=bank, oc=oc: e.tensor_tensor(
            out=t["kw"][oc][:, 0:TB], in0=bank[:, 0:TB],
            in1=t["tC"][:, 0:TB], op=ALU.mult), wait=[wb, mmev])
        prev_read[0] = tte
        if kvar == 2:
            k.bank_ev[oc % 4] = tte
            return
        ev = k.op("scalar", lambda e, bank=bank, oc=oc: e.mul(
            out=t["ku"][oc][:, 0:TB], in_=bank[:, 0:TB],
            mul=t["smallp"][:, 16 + oc:17 + oc]),
            wait=[mmev, tte])
        k.bank_ev[oc % 4] = ev
    proj2048(k, "l0", 1, t["mixy"], cb_k, rhs_ev=mev)
    if cut <= 0.6:
        return

    mev = build_mix(2, [k.ev("T")])
    m3v = None
    for mg in range(4):
        evs = []
        for kk in range(16):
            if m3v is None:
                w8 = [k.slab_free] if kk <= 1 else ()
            else:
                w8 = [m3v[kk]]
            evs.append(attn_slab(k, t["wsl"][kk], "l0", 2, kk, 512 * mg, 512,
                                 wait=w8, eng="gpsimd" if kk % 2 else "sync",
                                 inc="GD" if kk % 2 else "LW"))
        evall = [evs[-2], evs[-1]]
        new_m3 = [None] * 16
        last = None
        for tc in range(TB // 128):
            bank = k.psum[tc]
            for kk in range(16):
                w = list(evall) if kk == 0 else []
                if kk == 0:
                    w.append(k.bank_ev[tc])
                    w.append(mev)
                last = k.mm(bank[:, 0:512],
                            t["mixy"][kk][:, 128 * tc:128 * tc + 128],
                            t["wsl"][kk][:, 0:512],
                            start=(kk == 0), stop=(kk == 15), wait=w,
                            inc="T" if (kk == 15 or tc == 3) else None)
                if tc == 3:
                    new_m3[kk] = last
            ev = k.op("vector", lambda e, tc=tc, mg=mg, bank=bank: e.tensor_copy(
                out=t["vt"][tc][:, 512 * mg:512 * mg + 512], in_=bank[:, 0:512]),
                wait=[last])
            k.bank_ev[tc] = ev
        m3v = new_m3
        k.slab_free = last
    if cut <= 0.8:
        return

    mev = build_mix(3, [k.ev("T")])

    def cb_g(oc, bank, mmev):
        ev = k.op("scalar", lambda e, oc=oc, bank=bank: e.activation(
            out=t["gsl"][oc][:, 0:TB], in_=bank[:, 0:TB], func=AF.Silu),
            wait=[mmev])
        k.bank_ev[oc % 4] = ev
    proj2048(k, "l0", 3, t["mixy"], cb_g, rhs_ev=mev)

    scan_t0 = k.ev("T")
    for tc in range(TB // 128):
        def dst(fc, psrc, w, tc=tc):
            return k.op("vector",
                        lambda e, fc=fc: e.tensor_copy(
                            out=t["kwt"][tc][:, 128 * fc:128 * fc + 128],
                            in_=psrc), wait=w)
        transpose_to_fm(k, scan_t0 if tc == 0 else None,
                        lambda fc, tc=tc: t["kw"][fc][:, 128 * tc:128 * tc + 128],
                        dst)

    k.op("vector", lambda e: e.memzero(t["s0f"][:, :]))
    z0 = k.op("vector", lambda e: e.memzero(t["s0b"][:, :]))
    if DBG.get("l0cut", 99) <= 1:
        return

    NCH = TB // 128
    k.att_free2 = [k.ev("V"), k.ev("V")]
    k.asb_free2 = [k.ev("T"), k.ev("T")]
    ob_free = [k.ev("V"), k.ev("V")]
    kv_free = k.ev("V")
    head_bufs_free = k.ev("T")
    s0b_ev = z0
    for h in range(H0):
        hc, hp = h // 2, h % 2
        k.op("vector", lambda e, hc=hc, hp=hp: e.tensor_copy(
            out=t["rrh"][:, 0:TB], in_=t["rf"][hc][64 * hp:64 * hp + 64, 0:TB]),
            wait=[head_bufs_free])
        k.op("vector", lambda e, hc=hc, hp=hp: e.tensor_copy(
            out=t["kwh"][:, 0:TB], in_=t["kw"][hc][64 * hp:64 * hp + 64, 0:TB]))
        k.op("vector", lambda e, hc=hc, hp=hp: e.tensor_copy(
            out=t["kuh"][:, 0:TB], in_=t["ku"][hc][64 * hp:64 * hp + 64, 0:TB]))
        cpe = k.op("vector", lambda e, hc=hc, hp=hp: e.tensor_copy(
            out=t["rwh"][:, 0:TB], in_=t["rw"][hc][64 * hp:64 * hp + 64, 0:TB]))
        ob = k.psum[h % 2]
        olast = None
        for c in range(NCH):
            sl = slice(128 * c, 128 * c + 128)
            ab = k.psum[6] if c % 2 == 0 else k.psum[3]
            asb = t["attsb"] if c % 2 == 0 else t["q2b"]
            k.mm(ab[:, 0:128], t["kwh"][:, sl], t["rrh"][:, sl],
                 True, True, wait=[cpe, k.att_free2[c % 2]])
            a2 = k.mm(ab[:, 128:256], t["kuh"][:, sl], t["rrh"][:, sl],
                      True, True, inc="T")
            k.op("vector", lambda e, ab=ab, asb=asb: e.tensor_tensor(
                out=asb[:, :], in0=ab[:, 0:128],
                in1=t["msk_bf"][:, 128:256], op=ALU.mult),
                wait=[a2, k.asb_free2[c % 2]])
            m2 = k.op("vector", lambda e, ab=ab, asb=asb: e.copy_predicated(
                out=asb[:, :], mask=t["m_diag_i"][:, :],
                data=ab[:, 128:256]))
            k.att_free2[c % 2] = m2
            w1 = [m2]
            if c == 0:
                w1.append(ob_free[h % 2])
            k.asb_free2[c % 2] = k.mm(
                ob[0:64, sl], t["vt"][c][:, 64 * h:64 * h + 64],
                asb[:, :], True, False, wait=w1, inc="T")
            k.mm(ob[0:64, sl], t["s0b"][:, 64 * h:64 * h + 64], t["rrh"][:, sl],
                 False, False, wait=[s0b_ev])
            olast = k.mm(ob[0:64, sl], t["wkv0b"][:, 64 * h:64 * h + 64],
                         t["rwh"][:, sl], False, True,
                         inc="T" if c == NCH - 1 else None)
            kv = k.mm(k.psum[7][0:64, 0:64], t["kwt"][c][:, 64 * h:64 * h + 64],
                      t["vt"][c][:, 64 * h:64 * h + 64], True, True,
                      wait=[kv_free], inc="T")
            k.op("vector", lambda e, h=h: e.tensor_tensor(
                out=t["s0f"][:, 64 * h:64 * h + 64],
                in0=t["s0f"][:, 64 * h:64 * h + 64],
                in1=k.psum[7][0:64, 0:64], op=ALU.add), wait=[kv])
            s0b_ev = k.op("vector", lambda e, h=h: e.tensor_copy(
                out=t["s0b"][:, 64 * h:64 * h + 64],
                in_=t["s0f"][:, 64 * h:64 * h + 64]))
            kv_free = s0b_ev
        yv = k.op("vector", lambda e, hc=hc, hp=hp, ob=ob: e.tensor_copy(
            out=t["mixy"][hc][64 * hp:64 * hp + 64, 0:TB], in_=ob[0:64, 0:TB]),
            wait=[olast])
        ob_free[h % 2] = yv
        head_bufs_free = olast

    if DBG.get("l0cut", 99) <= 2:
        return
    sx = k.ev("V")
    d1 = k.dma("gpsimd", k.sag[f"l0i{r}"].ap()[:, :], t["s0f"][:, :],
               wait=[sx], inc="GD")
    ag = k.op("gpsimd",
              lambda e: e.collective_compute(
                  "AllGather", ALU.bypass, replica_groups=[list(range(NCORES))],
                  ins=[k.sag[f"l0i{r}"].ap().opt()],
                  outs=[k.sag[f"l0o{r}"].ap().opt()]),
              wait=[d1], inc="CC")
    k.op("vector", lambda e: e.tensor_copy(out=t["s0in"][:, :],
                                           in_=t["s0run"][:, :]),
         wait=[k.ev("T")])
    for j in range(NCORES):
        ld = k.dma("sync", t["xtm"][0:64, 0:D],
                   k.sag[f"l0o{r}"].ap()[64 * j:64 * j + 64, :],
                   wait=[ag, k.ev("V")])
        k.op("vector", lambda e: e.tensor_tensor(
            out=t["s0run"][:, :], in0=t["s0run"][:, :], in1=t["xtm"][0:64, 0:D],
            op=ALU.add), wait=[ld])
        k.op("vector", lambda e, j=j: e.scalar_tensor_tensor(
            out=t["s0in"][:, :], in0=t["xtm"][0:64, 0:D],
            scalar=t["pmask"][0:64, j:j + 1], in1=t["s0in"][:, :],
            op0=ALU.mult, op1=ALU.add))
    sbi = k.op("vector", lambda e: e.tensor_copy(out=t["s0inb"][:, :],
                                                 in_=t["s0in"][:, :]))
    corr_free = [k.ev("V"), k.ev("V")]
    for h in range(H0):
        hc, hp = h // 2, h % 2
        cpe = k.op("vector", lambda e, hc=hc, hp=hp: e.tensor_copy(
            out=t["rrh"][:, 0:TB], in_=t["rf"][hc][64 * hp:64 * hp + 64, 0:TB]),
            wait=[k.ev("T")])
        cm = k.mm(k.psum[2 + h % 2][0:64, 0:TB], t["s0inb"][:, 64 * h:64 * h + 64],
                  t["rrh"][:, 0:TB], True, True,
                  wait=[sbi, cpe, corr_free[h % 2]], inc="T")
        ya = k.op("vector", lambda e, hc=hc, hp=hp, h=h: e.tensor_tensor(
            out=t["mixy"][hc][64 * hp:64 * hp + 64, 0:TB],
            in0=t["mixy"][hc][64 * hp:64 * hp + 64, 0:TB],
            in1=k.psum[2 + h % 2][0:64, 0:TB], op=ALU.add), wait=[cm])
        corr_free[h % 2] = ya

    if DBG.get("l0cut", 99) <= 3:
        return
    st_free = [k.ev("V"), k.ev("V")]
    bc_free = [k.ev("V"), k.ev("V")]
    ya_prev = k.ev("S")
    for hc in range(16):
        y2 = k.op("vector", lambda e, hc=hc: e.tensor_tensor(
            out=t["xntm"][:, 0:TB], in0=t["mixy"][hc][:, 0:TB],
            in1=t["mixy"][hc][:, 0:TB], op=ALU.mult),
            wait=[k.ev("T"), ya_prev])
        s1 = k.mm(k.psum[6][0:2, 0:TB], t["aux_bf"][:, 0:2],
                  t["mixy"][hc][:, 0:TB], True, True,
                  wait=[k.ev("V"), st_free[0]], inc="T")
        s2 = k.mm(k.psum[7][0:2, 0:TB], t["aux_bf"][:, 0:2],
                  t["xntm"][:, 0:TB], True, True,
                  wait=[y2, st_free[1]], inc="T")
        k.op("vector", lambda e: e.tensor_scalar(
            out=t["tB"][0:2, 0:TB], in0=k.psum[6][0:2, 0:TB],
            scalar1=1.0 / 64.0, scalar2=None, op0=ALU.mult), wait=[s1])
        st_free[0] = k.ev("V")
        k.op("vector", lambda e: e.tensor_scalar(
            out=t["tC"][0:2, 0:TB], in0=k.psum[7][0:2, 0:TB],
            scalar1=1.0 / 64.0, scalar2=None, op0=ALU.mult),
            wait=[s2, ya_prev])
        st_free[1] = k.ev("V")
        k.op("vector", lambda e: e.tensor_tensor(
            out=t["stg2"][0:2, 0:TB], in0=t["tB"][0:2, 0:TB],
            in1=t["tB"][0:2, 0:TB], op=ALU.mult))
        vv = k.op("vector", lambda e: e.tensor_tensor(
            out=t["stg2"][0:2, 0:TB], in0=t["tC"][0:2, 0:TB],
            in1=t["stg2"][0:2, 0:TB], op=ALU.subtract))
        k.op("scalar", lambda e: e.activation(
            out=t["stg2"][0:2, 0:TB], in_=t["stg2"][0:2, 0:TB], func=AF.Ln,
            bias=0.00064), wait=[vv])
        rse = k.op("scalar", lambda e: e.activation(
            out=t["stg2"][0:2, 0:TB], in_=t["stg2"][0:2, 0:TB], func=AF.Exp,
            scale=-0.5))
        k.op("vector", lambda e: e.tensor_tensor(
            out=t["tB"][0:2, 0:TB], in0=t["tB"][0:2, 0:TB],
            in1=t["stg2"][0:2, 0:TB], op=ALU.mult), wait=[rse])
        k.op("vector", lambda e: e.tensor_scalar(
            out=t["tB"][0:2, 0:TB], in0=t["tB"][0:2, 0:TB],
            scalar1=-1.0, scalar2=None, op0=ALU.mult))
        k.op("vector", lambda e: e.tensor_copy(out=t["gnhl"][:, 0:TB],
                                               in_=t["stg2"][0:2, 0:TB]))
        k.op("vector", lambda e: e.tensor_copy(out=t["tC"][0:2, 0:TB],
                                               in_=t["gnhl"][:, 0:TB]))
        k.op("vector", lambda e: e.tensor_tensor(
            out=t["tC"][0:2, 0:TB], in0=t["stg2"][0:2, 0:TB],
            in1=t["tC"][0:2, 0:TB], op=ALU.subtract))
        k.op("vector", lambda e: e.tensor_copy(out=t["gnhl"][:, TB:2 * TB],
                                               in_=t["tC"][0:2, 0:TB]))
        k.op("vector", lambda e: e.tensor_copy(out=t["gnhl"][:, 2 * TB:3 * TB],
                                               in_=t["tB"][0:2, 0:TB]))
        k.op("vector", lambda e: e.tensor_copy(out=t["tC"][0:2, 0:TB],
                                               in_=t["gnhl"][:, 2 * TB:3 * TB]))
        k.op("vector", lambda e: e.tensor_tensor(
            out=t["tC"][0:2, 0:TB], in0=t["tB"][0:2, 0:TB],
            in1=t["tC"][0:2, 0:TB], op=ALU.subtract))
        hl = k.op("vector", lambda e: e.tensor_copy(out=t["gnhl"][:, 3 * TB:4 * TB],
                                                    in_=t["tC"][0:2, 0:TB]))
        k.mm(k.psum[2][:, 0:TB], t["e2"][0:2, :], t["gnhl"][:, 0:TB],
             True, False, wait=[hl, bc_free[0]])
        b1 = k.mm(k.psum[2][:, 0:TB], t["e2"][0:2, :], t["gnhl"][:, TB:2 * TB],
                  False, True, inc="T")
        k.mm(k.psum[3][:, 0:TB], t["e2"][0:2, :], t["gnhl"][:, 2 * TB:3 * TB],
             True, False, wait=[bc_free[1]])
        b2 = k.mm(k.psum[3][:, 0:TB], t["e2"][0:2, :], t["gnhl"][:, 3 * TB:4 * TB],
                  False, True, inc="T")
        k.op("vector", lambda e, hc=hc: e.tensor_tensor(
            out=t["tC"][:, 0:TB], in0=t["mixy"][hc][:, 0:TB],
            in1=k.psum[2][:, 0:TB], op=ALU.mult), wait=[b1, b2])
        bc_free[0] = k.ev("V")
        tt2 = k.op("vector", lambda e: e.tensor_tensor(
            out=t["tC"][:, 0:TB], in0=t["tC"][:, 0:TB],
            in1=k.psum[3][:, 0:TB], op=ALU.add))
        bc_free[1] = tt2
        ya = k.op("scalar", lambda e, hc=hc: e.activation(
            out=t["tB"][:, 0:TB], in_=t["tC"][:, 0:TB], func=AF.Identity,
            scale=t["smallp"][:, 32 + hc:33 + hc],
            bias=t["smallp"][:, 48 + hc:49 + hc]), wait=[tt2])
        ya_prev = ya
        k.op("vector", lambda e, hc=hc: e.tensor_tensor(
            out=t["mixy"][hc][:, 0:TB], in0=t["tB"][:, 0:TB],
            in1=t["gsl"][hc][:, 0:TB], op=ALU.mult), wait=[ya])

    if DBG.get("l0cut", 99) <= 4:
        return
    if DBG.get("wobar", 0):
        barrier(k)
    k.slab_free = k.ev("T")
    k.bank_ev = [k.ev("V")] * 4
    stg_free = [k.ev("GD"), k.ev("GD")]

    def cb_wo(oc, bank, mmev):
        sb = t["stg"] if oc % 2 == 0 else t["stg2"]
        cpv = k.op("vector", lambda e, bank=bank, sb=sb: e.tensor_copy(
            out=sb[:, 0:TB], in_=bank[:, 0:TB]),
            wait=[mmev, stg_free[oc % 2]])
        k.bank_ev[oc % 4] = cpv
        stg_free[oc % 2] = k.dma("gpsimd",
                                 k.sscr.ap()[oc, :, :], sb[:, 0:TB],
                                 wait=[cpv], inc="GD")
    proj2048(k, "l0", 4, t["mixy"], cb_wo, rhs_ev=k.ev("V"))
    k.s_done = k.ev("GD")



def dslab(k, kk):
    t = k.tensors
    if kk < 32:
        return t["wsl"][kk // 2][:, 256 * (kk % 2):256 * (kk % 2) + 256]
    return t["wsl2"][(kk - 32) // 2][:, 256 * (kk % 2):256 * (kk % 2) + 256]


def l0_ffn(k, r, xn_ev):
    t = k.tensors
    war0 = [k.ev("T"), k.ev("GD")]
    il_ev = None
    for kb in range(8):
        for tc in range(TB // 128):
            csl = slice(1 + 128 * tc, 129 + 128 * tc)
            k.tr(k.psum[4][:, 0:128], t["xn"][2 * kb][:, csl],
                 t["msk_bf"][:, 0:128], wait=[k.tp_free[0]])
            t1 = k.tr(k.psum[4][:, 128:256], t["xn"][2 * kb + 1][:, csl],
                      t["msk_bf"][:, 0:128], inc="T")
            tm = k.op("vector", lambda e: e.tensor_copy(
                out=t["xntm"][:, 0:256], in_=k.psum[4][:, 0:256]),
                wait=[t1] + (war0 if kb == 0 and tc == 0 else []))
            k.tp_free[0] = tm
            k.tr(k.psum[5][:, 0:128], t["xntm"][:, 0:256:2],
                 t["msk_bf"][:, 0:128], wait=[tm, k.tp_free[1]])
            t2 = k.tr(k.psum[5][:, 128:256], t["xntm"][:, 1:256:2],
                      t["msk_bf"][:, 0:128], inc="T")
            k.op("vector", lambda e, kb=kb, tc=tc: e.tensor_copy(
                out=t["xnil"][2 * kb][:, 128 * tc:128 * tc + 128],
                in_=k.psum[5][:, 0:128]), wait=[t2])
            il_ev = k.op("vector", lambda e, kb=kb, tc=tc: e.tensor_copy(
                out=t["xnil"][2 * kb + 1][:, 128 * tc:128 * tc + 128],
                in_=k.psum[5][:, 128:256]))
            k.tp_free[1] = il_ev

    k.slab_free = k.ev("T")
    bank_ev = [k.ev("V")] * 4
    m3g = None
    for jg in range(22):
        evg, evu = [], []
        for kk in range(16):
            kb, par = kk // 2, kk % 2
            seg = "gul" if par == 0 else "guh"
            if m3g is None:
                w8 = [k.slab_free] if kk == 0 else ()
                w8u = [k.slab_free] if kk == 0 else ()
            else:
                w8 = [m3g[0][kk]]
                w8u = [m3g[1][kk]]
            evg.append(qslab(k, t["wsl"][kk], seg, 128 * kb, 256 * jg, 256,
                             wait=w8))
            evu.append(qslab(k, t["wsl2"][kk], seg, 128 * kb, I0 + 256 * jg,
                             256, wait=w8u, eng="gpsimd", inc="GD"))
        evall = [evg[-1], evu[-1]]
        new_m3 = ([None] * 16, [None] * 16)
        last = None
        for jj in range(2):
            for kk in range(16):
                w = list(evall) if kk == 0 else []
                if kk == 0:
                    w += [bank_ev[jj], il_ev]
                ev = k.mm(k.psum[jj][:, 0:TB],
                          t["wsl"][kk][:, 128 * jj:128 * jj + 128],
                          t["xnil"][kk][:, 0:TB], start=(kk == 0),
                          stop=(kk == 15), wait=w,
                          inc="T" if (kk == 15 or jj == 1) else None)
                if jj == 1:
                    new_m3[0][kk] = ev
            for kk in range(16):
                w = list(evall) if kk == 0 else []
                if kk == 0:
                    w.append(bank_ev[2 + jj])
                last = k.mm(k.psum[2 + jj][:, 0:TB],
                            t["wsl2"][kk][:, 128 * jj:128 * jj + 128],
                            t["xnil"][kk][:, 0:TB],
                            start=(kk == 0), stop=(kk == 15),
                            wait=w, inc="T" if (kk == 15 or jj == 1) else None)
                if jj == 1:
                    new_m3[1][kk] = last
            sg = k.op("scalar", lambda e, jj=jj: e.activation(
                out=t["xntm"][:, 0:TB], in_=k.psum[jj][:, 0:TB], func=AF.Silu),
                wait=[last, k.ev("V")])
            bank_ev[jj] = sg
            hv = k.op("vector", lambda e, jg=jg, jj=jj: e.tensor_tensor(
                out=t["hh"][2 * jg + jj][:, 0:TB], in0=t["xntm"][:, 0:TB],
                in1=k.psum[2 + jj][:, 0:TB], op=ALU.mult), wait=[sg])
            bank_ev[2 + jj] = hv
        m3g = new_m3
        k.slab_free = last

    hh_ev = k.ev("V")
    k.slab_free = k.ev("T")
    bank_ev = [k.ev("V")] * 2
    m3d = None
    for og in range(8):
        evs = []
        for kk in range(44):
            kb, par = kk // 2, kk % 2
            seg = "dnl" if par == 0 else "dnh"
            if m3d is None:
                w8 = [k.slab_free] if kk <= 1 else ()
            else:
                w8 = [m3d[kk]]
            evs.append(qslab(k, dslab(k, kk), seg, 128 * kb, 256 * og, 256,
                             wait=w8, eng="gpsimd" if kk % 2 else "sync",
                             inc="GD" if kk % 2 else "LW"))
        evall = [evs[-2], evs[-1]]
        new_m3 = [None] * 44
        last = None
        for m in range(2):
            oc = 2 * og + m
            for kk in range(44):
                w = list(evall) if kk == 0 else []
                if kk == 0:
                    w += [bank_ev[m], hh_ev]
                last = k.mm(k.psum[m][:, 0:TB],
                            dslab(k, kk)[:, 128 * m:128 * m + 128],
                            t["hh"][kk][:, 0:TB], start=(kk == 0),
                            stop=(kk == 43), wait=w,
                            inc="T" if (kk == 43 or m == 1) else None)
                if m == 1:
                    new_m3[kk] = last
            ev = k.op("vector", lambda e, oc=oc, m=m: e.tensor_copy(
                out=t["dd"][oc][:, 0:TB], in_=k.psum[m][:, 0:TB]), wait=[last])
            bank_ev[m] = ev
        m3d = new_m3
        k.slab_free = last

    for oc in range(16):
        y2 = k.op("vector", lambda e, oc=oc: e.tensor_tensor(
            out=t["xntm"][:, 0:TB], in0=t["dd"][oc][:, 0:TB],
            in1=t["dd"][oc][:, 0:TB], op=ALU.mult), wait=[k.ev("T")])
        k.mm(k.psum[6][0:1, 0:TB], t["aux_bf"][:, 2:3], t["xntm"][:, 0:TB],
             start=(oc == 0), stop=(oc == 15), wait=[y2], inc="T")
    sq = k.ev("T")
    k.op("scalar", lambda e: e.activation(
        out=t["tB"][0:1, 0:TB], in_=k.psum[6][0:1, 0:TB], func=AF.Ln,
        scale=1.0 / 2048.0, bias=1e-6), wait=[sq])
    rse = k.op("scalar", lambda e: e.activation(
        out=t["tB"][0:1, 0:TB], in_=t["tB"][0:1, 0:TB], func=AF.Exp, scale=-0.5))
    k.op("vector", lambda e: e.tensor_copy(out=t["rmshl"][:, 0:TB],
                                           in_=t["tB"][0:1, 0:TB]), wait=[rse])
    k.op("vector", lambda e: e.tensor_copy(out=t["tC"][0:1, 0:TB],
                                           in_=t["rmshl"][:, 0:TB]))
    k.op("vector", lambda e: e.tensor_tensor(
        out=t["tC"][0:1, 0:TB], in0=t["tB"][0:1, 0:TB], in1=t["tC"][0:1, 0:TB],
        op=ALU.subtract))
    hl = k.op("vector", lambda e: e.tensor_copy(out=t["rmshl"][:, TB:2 * TB],
                                                in_=t["tC"][0:1, 0:TB]))
    k.mm(k.psum[2][:, 0:TB], t["ones1"][0:1, :], t["rmshl"][:, 0:TB],
         True, False, wait=[hl, k.ev("V")])
    b1 = k.mm(k.psum[2][:, 0:TB], t["ones1"][0:1, :], t["rmshl"][:, TB:2 * TB],
              False, True, inc="T")
    for oc in range(16):
        k.op("vector", lambda e, oc=oc: e.tensor_tensor(
            out=t["dd"][oc][:, 0:TB], in0=t["dd"][oc][:, 0:TB],
            in1=k.psum[2][:, 0:TB], op=ALU.mult), wait=[b1])
        k.op("scalar", lambda e, oc=oc: e.activation(
            out=t["dd"][oc][:, 0:TB], in_=t["dd"][oc][:, 0:TB], func=AF.Silu),
            wait=[k.ev("V")])
    k.g1_ev = k.ev("S")



def combine0(k, r):
    t, din = k.tensors, k.din
    stg_free = k.ev("V")
    for oc in range(16):
        ld = k.dma("sync", t["stg"][:, 0:TB], k.sscr.ap()[oc, :, :],
                   wait=[k.s_done if oc == 0 else stg_free, k.g1_ev])
        sg = k.op("vector", lambda e, oc=oc: e.tensor_tensor(
            out=t["dd"][oc][:, 0:TB], in0=t["stg"][:, 0:TB],
            in1=t["dd"][oc][:, 0:TB], op=ALU.mult), wait=[ld])
        stg_free = sg
    sg_ev = k.ev("V")
    tp_free = k.ev("V")
    for tc in range(TB // 128):
        ld = k.dma("sync", t["cxa"][:, :],
                   din[f"x{r}"][1 + 128 * tc:129 + 128 * tc, :],
                   wait=[k.ev("V"), k.ev("S")])
        tl = None
        for oc in range(16):
            b = oc // 4
            w = [sg_ev, tp_free] if oc == 0 else ()
            tl = k.tr(k.psum[b][:, 128 * (oc % 4):128 * (oc % 4) + 128],
                      t["dd"][oc][:, 128 * tc:128 * tc + 128],
                      t["ident_f32"][:, :], wait=w,
                      inc="T" if oc % 4 == 3 else None)
        for b in range(4):
            k.op("vector", lambda e, b=b: e.tensor_tensor(
                out=t["cxb"][:, 512 * b:512 * b + 512],
                in0=t["cxa"][:, 512 * b:512 * b + 512],
                in1=k.psum[b][:, 0:512], op=ALU.add),
                wait=[ld, tl, k.ev("GD")] if b == 0 else ())
        tp_free = k.ev("V")
        x1e = k.ev("V")
        k.dma("gpsimd", k.x1scr.ap()[128 * tc:128 * tc + 128, :],
              t["cxb"][:, :], wait=[x1e], inc="GD")
        ln_sbuf_chunk(k, t["cxb"], tc, eps_reg=1e-5)
    k.xn1_ev = k.ev("V")


def ln_sbuf_chunk(k, src, tc, eps_reg):
    t = k.tensors
    k.op("scalar", lambda e: e.activation(out=t["csq"][:, :], in_=src[:, :],
                                          func=AF.Square), wait=[k.ev("V")])
    sq = k.ev("S")
    k.op("vector", lambda e: e.reduce_sum(out=t["hbuf"][:, 0:1], in_=src[:, :],
                                          axis=AX))
    k.op("vector", lambda e: e.reduce_sum(out=t["hbuf"][:, 1:2], in_=t["csq"][:, :],
                                          axis=AX), wait=[sq])
    k.op("vector", lambda e: e.tensor_scalar(out=t["hbuf"][:, 2:3],
                                             in0=t["hbuf"][:, 0:1],
                                             scalar1=1.0 / 2048.0, scalar2=None,
                                             op0=ALU.mult))
    k.op("vector", lambda e: e.tensor_scalar(out=t["hbuf"][:, 3:4],
                                             in0=t["hbuf"][:, 1:2],
                                             scalar1=1.0 / 2048.0, scalar2=None,
                                             op0=ALU.mult))
    k.op("vector", lambda e: e.tensor_tensor(out=t["hbuf"][:, 4:5],
                                             in0=t["hbuf"][:, 2:3],
                                             in1=t["hbuf"][:, 2:3], op=ALU.mult))
    vv = k.op("vector", lambda e: e.tensor_tensor(out=t["hbuf"][:, 4:5],
                                                  in0=t["hbuf"][:, 3:4],
                                                  in1=t["hbuf"][:, 4:5],
                                                  op=ALU.subtract))
    k.op("scalar", lambda e: e.activation(out=t["hbuf"][:, 5:6],
                                          in_=t["hbuf"][:, 4:5], func=AF.Ln,
                                          bias=eps_reg), wait=[vv])
    k.op("scalar", lambda e: e.activation(out=t["hbuf"][:, 5:6],
                                          in_=t["hbuf"][:, 5:6], func=AF.Exp,
                                          scale=-0.5))
    sr = k.ev("S")
    k.op("vector", lambda e: e.tensor_tensor(out=t["hbuf"][:, 6:7],
                                             in0=t["hbuf"][:, 2:3],
                                             in1=t["hbuf"][:, 5:6], op=ALU.mult),
         wait=[sr])
    nb = k.op("vector", lambda e: e.tensor_scalar(out=t["hbuf"][:, 6:7],
                                                  in0=t["hbuf"][:, 6:7],
                                                  scalar1=-1.0, scalar2=None,
                                                  op0=ALU.mult))
    xe = k.op("scalar", lambda e: e.activation(out=t["xntm"][:, 0:D],
                                               in_=src[:, :], func=AF.Identity,
                                               scale=t["hbuf"][:, 5:6],
                                               bias=t["hbuf"][:, 6:7]),
              wait=[nb, k.ev("T")])

    def dst(fc, psrc, w):
        return k.op("vector",
                    lambda e, fc=fc: e.tensor_copy(
                        out=k.tensors["xn"][fc][:, 1 + 128 * tc:129 + 128 * tc],
                        in_=psrc), wait=w)
    transpose_to_fm(k, xe, lambda fc: t["xntm"][:, 128 * fc:128 * fc + 128], dst)





def l1_attn(k, r):
    t = k.tensors
    xn1 = k.xn1_ev
    rhs = [t["xn"][kk][:, 1:1 + TB] for kk in range(16)]
    k.slab_free = k.ev("T")
    k.bank_ev = [k.ev("V")] * 4

    def cb_q(oc, bank, mmev):
        ev = k.op("scalar", lambda e, oc=oc, bank=bank: e.activation(
            out=t["qt"][oc][:, 0:TB], in_=bank[:, 0:TB], func=AF.Relu,
            scale=SCL1), wait=[mmev])
        k.bank_ev[oc % 4] = ev
    proj2048(k, "l1", 0, rhs, cb_q, rhs_ev=xn1)

    qt_ev = k.ev("S")

    def cb_k(oc, bank, mmev):
        h = oc
        sg = k.op("scalar", lambda e, bank=bank: e.activation(
            out=t["sgb"][:, 0:TB], in_=bank[:, 0:TB], func=AF.Sigmoid),
            wait=[mmev, k.ev("V")])
        k.bank_ev[oc % 4] = sg
        gb = k.op("scalar", lambda e: e.activation(
            out=t["gbv"][:, 0:TB], in_=t["sgb"][:, 0:TB], func=AF.Ln,
            scale=-1.0, bias=1.0))
        sc = None
        for c in range(TB // 64):
            sc = k.op("vector", lambda e, c=c: e.tensor_tensor_scan(
                out=t["abuf"][:, 64 * c:64 * c + 64],
                data0=t["gbv"][:, 64 * c:64 * c + 64],
                data1=t["gbv"][:, 64 * c:64 * c + 64],
                initial=0.0, op0=ALU.add, op1=ALU.bypass),
                wait=[gb] if c == 0 else ())
        e1 = k.op("scalar", lambda e: e.activation(
            out=t["tB"][:, 0:TB], in_=t["abuf"][:, 0:TB], func=AF.Exp),
            wait=[sc])
        k.op("vector", lambda e, h=h: e.tensor_tensor(
            out=t["qt"][h][:, 0:TB], in0=t["qt"][h][:, 0:TB],
            in1=t["tB"][:, 0:TB], op=ALU.mult), wait=[e1, qt_ev])
        ed = k.op("scalar", lambda e: e.activation(
            out=t["tB"][:, 0:TB], in_=t["abuf"][:, 0:TB], func=AF.Exp,
            scale=-1.0), wait=[k.ev("V")])
        k.op("vector", lambda e, h=h: e.tensor_tensor(
            out=t["kd"][h][:, 0:TB], in0=t["sgb"][:, 0:TB],
            in1=t["tB"][:, 0:TB], op=ALU.mult), wait=[ed])
        e3 = None
        for c in range(TB // 64):
            e3 = k.op("scalar", lambda e, c=c: e.activation(
                out=t["tB"][:, 64 * c:64 * c + 64],
                in_=t["abuf"][:, 64 * c:64 * c + 64], func=AF.Exp, scale=-1.0,
                bias=t["abuf"][:, 64 * c + 63:64 * c + 64]),
                wait=[k.ev("V")] if c == 0 else ())
        k.op("vector", lambda e, h=h: e.tensor_tensor(
            out=t["ks"][h][:, 0:TB], in0=t["sgb"][:, 0:TB],
            in1=t["tB"][:, 0:TB], op=ALU.mult), wait=[e3])
        k.op("vector", lambda e, h=h: e.tensor_copy(
            out=t["lgpre"][:, 8 * h:8 * h + 8],
            in_=t["abuf"][:, 63:TB:64]))
    proj2048(k, "l1", 1, rhs, cb_k, rhs_ev=xn1)

    m3v = None
    for mg in range(4):
        evs = []
        for kk in range(16):
            if m3v is None:
                w8 = [k.slab_free] if kk <= 1 else ()
            else:
                w8 = [m3v[kk]]
            evs.append(attn_slab(k, t["wsl"][kk], "l1", 2, kk, 512 * mg, 512,
                                 wait=w8, eng="gpsimd" if kk % 2 else "sync",
                                 inc="GD" if kk % 2 else "LW"))
        evall = [evs[-2], evs[-1]]
        new_m3 = [None] * 16
        last = None
        for tc in range(TB // 128):
            bank = k.psum[tc]
            for kk in range(16):
                w = list(evall) if kk == 0 else []
                if kk == 0:
                    w += [k.bank_ev[tc], xn1]
                last = k.mm(bank[:, 0:512], rhs[kk][:, 128 * tc:128 * tc + 128],
                            t["wsl"][kk][:, 0:512],
                            start=(kk == 0), stop=(kk == 15), wait=w,
                            inc="T" if (kk == 15 or tc == 3) else None)
                if tc == 3:
                    new_m3[kk] = last
            ev = k.op("vector", lambda e, tc=tc, mg=mg, bank=bank: e.tensor_copy(
                out=t["vt"][tc][:, 512 * mg:512 * mg + 512], in_=bank[:, 0:512]),
                wait=[last])
            k.bank_ev[tc] = ev
        m3v = new_m3
        k.slab_free = last

    k.op("vector", lambda e: e.memzero(t["s1f"][:, :]))
    z1 = k.op("vector", lambda e: e.memzero(t["s1b"][:, :]))
    s1b_ev = z1
    att_free = k.ev("V")
    ob_free = [k.ev("V"), k.ev("V")]
    kv_free = k.ev("V")
    kst_free = k.ev("V")
    k.attsb_free = k.ev("T")
    k.q2b_free = k.ev("T")
    k.ks2b_free = k.ev("T")
    k.kst_rd = k.ev("T")
    NP = TB // 128
    prep_ev = k.ev("V")
    for h in range(H1):
        olast = None
        ob = k.psum[h % 2]
        for p in range(NP):
            c0 = slice(128 * p, 128 * p + 64)
            c1 = slice(128 * p + 64, 128 * p + 128)
            pr = slice(128 * p, 128 * p + 128)
            d0 = k.op("scalar", lambda e, h=h, p=p: e.activation(
                out=t["hbuf"][:, 8:9],
                in_=t["lgpre"][:, 8 * h + 2 * p:8 * h + 2 * p + 1],
                func=AF.Exp), wait=[prep_ev, k.ev("V")])
            d1 = k.op("scalar", lambda e, h=h, p=p: e.activation(
                out=t["hbuf"][:, 9:10],
                in_=t["lgpre"][:, 8 * h + 2 * p + 1:8 * h + 2 * p + 2],
                func=AF.Exp))
            k.op("vector", lambda e, h=h, p=p: e.tensor_tensor(
                out=t["hbuf"][:, 10:11],
                in0=t["lgpre"][:, 8 * h + 2 * p:8 * h + 2 * p + 1],
                in1=t["lgpre"][:, 8 * h + 2 * p + 1:8 * h + 2 * p + 2],
                op=ALU.add))
            dp = k.op("scalar", lambda e: e.activation(
                out=t["hbuf"][:, 11:12], in_=t["hbuf"][:, 10:11], func=AF.Exp),
                wait=[k.ev("V")])
            k.op("vector", lambda e, h=h, p=p: e.tensor_copy(
                out=t["q2b"][:, 0:64], in_=t["qt"][h][:, c0]),
                wait=[k.q2b_free])
            q2 = k.op("scalar", lambda e, h=h, p=p: e.activation(
                out=t["q2b"][:, 64:128], in_=t["qt"][h][:, c1],
                func=AF.Identity, scale=t["hbuf"][:, 8:9]), wait=[d0, k.ev("V")])
            k.op("vector", lambda e, h=h, p=p: e.tensor_copy(
                out=t["ks2b"][:, 64:128], in_=t["ks"][h][:, c1]),
                wait=[k.ks2b_free])
            ks2 = k.op("scalar", lambda e, h=h, p=p: e.activation(
                out=t["ks2b"][:, 0:64], in_=t["ks"][h][:, c0],
                func=AF.Identity, scale=t["hbuf"][:, 9:10]), wait=[d1, k.ev("V")])
            tke = k.tr(k.psum[4][:, 0:128], t["ks2b"][:, :], t["msk_bf"][:, 0:128],
                       wait=[ks2, kst_free], inc="T")
            k.ks2b_free = tke
            kste = k.op("vector", lambda e: e.tensor_copy(
                out=t["kst"][:, :], in_=k.psum[4][:, 0:128]),
                wait=[tke, k.kst_rd])
            kst_free = kste
            k.mm(k.psum[6][0:64, 0:64],
                 t["kd"][h][:, c0], t["qt"][h][:, c0], True, True,
                 wait=[att_free])
            k.mm(k.psum[6][0:64, 64:128], t["ks"][h][:, c0], t["qt"][h][:, c1],
                 True, True)
            a3 = k.mm(k.psum[7][0:64, 64:128], t["kd"][h][:, c1], t["qt"][h][:, c1],
                      True, True, inc="T")
            k.op("vector", lambda e: e.tensor_tensor(
                out=t["attsb"][0:64, :], in0=k.psum[6][0:64, 0:128],
                in1=t["msk_bf"][0:64, 256:384], op=ALU.mult),
                wait=[a3, k.attsb_free])
            ae = k.op("vector", lambda e: e.tensor_tensor(
                out=t["attsb"][64:128, :], in0=k.psum[7][0:64, 0:128],
                in1=t["msk_bf"][64:128, 256:384], op=ALU.mult))
            att_free = ae
            w1 = [ae]
            if p == 0:
                w1.append(ob_free[h % 2])
            k.attsb_free = k.mm(ob[:, pr], t["vt"][p][:, 128 * h:128 * h + 128],
                                t["attsb"][:, :], True, False, wait=w1, inc="T")
            olast = k.mm(ob[:, pr], t["s1b"][:, 128 * h:128 * h + 128],
                         t["q2b"][:, :], False, True,
                         wait=[s1b_ev, q2], inc="T")
            kv = k.mm(k.psum[7][:, 128:256], t["kst"][:, :],
                      t["vt"][p][:, 128 * h:128 * h + 128], True, True,
                      wait=[kste, kv_free], inc="T")
            k.kst_rd = kv
            k.q2b_free = olast
            k.op("vector", lambda e, h=h: e.scalar_tensor_tensor(
                out=t["s1f"][:, 128 * h:128 * h + 128],
                in0=t["s1f"][:, 128 * h:128 * h + 128],
                scalar=t["hbuf"][:, 11:12], in1=k.psum[7][:, 128:256],
                op0=ALU.mult, op1=ALU.add), wait=[kv, dp])
            s1b_ev = k.op("vector", lambda e, h=h: e.tensor_copy(
                out=t["s1b"][:, 128 * h:128 * h + 128],
                in_=t["s1f"][:, 128 * h:128 * h + 128]))
            kv_free = s1b_ev
        ov = k.op("vector", lambda e, h=h, ob=ob: e.tensor_copy(
            out=t["gsl"][h][:, 0:TB], in_=ob[:, 0:TB]), wait=[olast])
        ob_free[h % 2] = ov
        prep_ev = k.ev("S")

    for h in range(H1):
        k.op("vector", lambda e, h=h: e.reduce_sum(
            out=t["s1lg"][:, h:h + 1], in_=t["lgpre"][:, 8 * h:8 * h + 8],
            axis=AX))
    de = k.op("scalar", lambda e: e.activation(
        out=t["hbuf"][:, 16:32], in_=t["s1lg"][:, 0:16], func=AF.Exp),
        wait=[k.ev("V")])
    sx = k.ev("V")
    k.dma("gpsimd", k.sag[f"l1i{r}"].ap()[:, 0:D], t["s1f"][:, :],
          wait=[sx], inc="GD")
    d2 = k.dma("gpsimd", k.sag[f"l1i{r}"].ap()[:, D:D + 16], t["hbuf"][:, 16:32],
               wait=[de], inc="GD")
    ag = k.op("gpsimd",
              lambda e: e.collective_compute(
                  "AllGather", ALU.bypass, replica_groups=[list(range(NCORES))],
                  ins=[k.sag[f"l1i{r}"].ap().opt()],
                  outs=[k.sag[f"l1o{r}"].ap().opt()]),
              wait=[d2], inc="CC")
    k.op("vector", lambda e: e.tensor_copy(out=t["s1in"][:, :],
                                           in_=t["s1run"][:, :]),
         wait=[k.ev("T")])
    for j in range(NCORES):
        ld = k.dma("sync", t["xtm"][:, 0:D],
                   k.sag[f"l1o{r}"].ap()[128 * j:128 * j + 128, 0:D],
                   wait=[ag, k.ev("V")])
        ld2 = k.dma("sync", t["tcol"][:, 0:16],
                    k.sag[f"l1o{r}"].ap()[128 * j:128 * j + 128, D:D + 16])
        for h in range(H1):
            k.op("vector", lambda e, h=h: e.scalar_tensor_tensor(
                out=t["s1run"][:, 128 * h:128 * h + 128],
                in0=t["s1run"][:, 128 * h:128 * h + 128],
                scalar=t["tcol"][:, h:h + 1],
                in1=t["xtm"][:, 128 * h:128 * h + 128],
                op0=ALU.mult, op1=ALU.add),
                wait=[ld, ld2] if h == 0 else ())
        k.op("vector", lambda e, j=j: e.scalar_tensor_tensor(
            out=t["xtm"][:, 0:D], in0=t["xtm"][:, 0:D],
            scalar=t["pmask"][:, j:j + 1], in1=t["xtm"][:, 0:D],
            op0=ALU.mult, op1=ALU.bypass))
        k.op("scalar", lambda e, j=j: e.activation(
            out=t["tcol"][:, 16:32], in_=t["onesf"][:, 0:16], func=AF.Identity,
            scale=t["tcol"][:, 32 + j:33 + j]), wait=[k.ev("V")])
        k.op("vector", lambda e, j=j: e.scalar_tensor_tensor(
            out=t["tcol"][:, 0:16], in0=t["tcol"][:, 0:16],
            scalar=t["pmask"][:, j:j + 1], in1=t["tcol"][:, 16:32],
            op0=ALU.mult, op1=ALU.add), wait=[k.ev("S")])
        for h in range(H1):
            k.op("vector", lambda e, h=h: e.scalar_tensor_tensor(
                out=t["s1in"][:, 128 * h:128 * h + 128],
                in0=t["s1in"][:, 128 * h:128 * h + 128],
                scalar=t["tcol"][:, h:h + 1],
                in1=t["xtm"][:, 128 * h:128 * h + 128],
                op0=ALU.mult, op1=ALU.add))
    sbi = k.op("vector", lambda e: e.tensor_copy(out=t["s1inb"][:, :],
                                                 in_=t["s1in"][:, :]))

    corr_free = [k.ev("V"), k.ev("V")]
    for h in range(H1):
        pf = k.op("vector", lambda e, h=h: e.tensor_tensor_scan(
            out=t["tD"][:, 0:8], data0=t["lgpre"][:, 8 * h:8 * h + 8],
            data1=t["lgpre"][:, 8 * h:8 * h + 8], initial=0.0,
            op0=ALU.add, op1=ALU.bypass))
        pe = k.op("scalar", lambda e: e.activation(
            out=t["tD"][:, 8:16], in_=t["tD"][:, 0:8], func=AF.Exp), wait=[pf])
        k.op("vector", lambda e, h=h: e.tensor_copy(
            out=t["mixy"][h][:, 0:64], in_=t["qt"][h][:, 0:64]))
        qg = None
        for c in range(1, TB // 64):
            qg = k.op("scalar", lambda e, h=h, c=c: e.activation(
                out=t["mixy"][h][:, 64 * c:64 * c + 64],
                in_=t["qt"][h][:, 64 * c:64 * c + 64], func=AF.Identity,
                scale=t["tD"][:, 7 + c:8 + c]), wait=[pe] if c == 1 else ())
        cm = k.mm(k.psum[2 + h % 2][:, 0:TB], t["s1inb"][:, 128 * h:128 * h + 128],
                  t["mixy"][h][:, 0:TB], True, True,
                  wait=[sbi, qg, k.ev("V"), corr_free[h % 2]], inc="T")
        ya = k.op("vector", lambda e, h=h: e.tensor_tensor(
            out=t["gsl"][h][:, 0:TB], in0=t["gsl"][h][:, 0:TB],
            in1=k.psum[2 + h % 2][:, 0:TB], op=ALU.add), wait=[cm])
        corr_free[h % 2] = ya

    for h in range(H1):
        y2 = k.op("vector", lambda e, h=h: e.tensor_tensor(
            out=t["xntm"][:, 0:TB], in0=t["gsl"][h][:, 0:TB],
            in1=t["gsl"][h][:, 0:TB], op=ALU.mult), wait=[k.ev("T")])
        k.mm(k.psum[6][0:1, 0:TB], t["aux_bf"][:, 2:3], t["xntm"][:, 0:TB],
             start=(h == 0), stop=(h == 15), wait=[y2], inc="T")
    sq = k.ev("T")
    k.op("scalar", lambda e: e.activation(
        out=t["tB"][0:1, 0:TB], in_=k.psum[6][0:1, 0:TB], func=AF.Ln,
        scale=1.0 / 2048.0, bias=1e-5), wait=[sq])
    rse = k.op("scalar", lambda e: e.activation(
        out=t["tB"][0:1, 0:TB], in_=t["tB"][0:1, 0:TB], func=AF.Exp, scale=-0.5))
    k.op("vector", lambda e: e.tensor_copy(out=t["rmshl"][:, 0:TB],
                                           in_=t["tB"][0:1, 0:TB]), wait=[rse])
    k.op("vector", lambda e: e.tensor_copy(out=t["tC"][0:1, 0:TB],
                                           in_=t["rmshl"][:, 0:TB]))
    k.op("vector", lambda e: e.tensor_tensor(
        out=t["tC"][0:1, 0:TB], in0=t["tB"][0:1, 0:TB], in1=t["tC"][0:1, 0:TB],
        op=ALU.subtract))
    hl = k.op("vector", lambda e: e.tensor_copy(out=t["rmshl"][:, TB:2 * TB],
                                                in_=t["tC"][0:1, 0:TB]))
    k.mm(k.psum[2][:, 0:TB], t["ones1"][0:1, :], t["rmshl"][:, 0:TB],
         True, False, wait=[hl, k.ev("V")])
    b1 = k.mm(k.psum[2][:, 0:TB], t["ones1"][0:1, :], t["rmshl"][:, TB:2 * TB],
              False, True, inc="T")
    act_prev = k.ev("S")
    for h in range(H1):
        k.op("vector", lambda e, h=h: e.tensor_tensor(
            out=t["tC"][:, 0:TB], in0=t["gsl"][h][:, 0:TB],
            in1=k.psum[2][:, 0:TB], op=ALU.mult), wait=[b1, act_prev])
        act_prev = k.op("scalar", lambda e, h=h: e.activation(
            out=t["mixy"][h][:, 0:TB], in_=t["tC"][:, 0:TB], func=AF.Identity,
            scale=t["smallp"][:, 64 + h:65 + h]), wait=[k.ev("V")])

    k.slab_free = k.ev("T")
    k.bank_ev = [k.ev("V")] * 4
    og_ev = k.ev("S")
    stg_free = [k.ev("GD"), k.ev("GD")]

    def cb_wo(oc, bank, mmev):
        sb = t["stg"] if oc % 2 == 0 else t["stg2"]
        cpv = k.op("vector", lambda e, bank=bank, sb=sb: e.tensor_copy(
            out=sb[:, 0:TB], in_=bank[:, 0:TB]),
            wait=[mmev, stg_free[oc % 2]])
        k.bank_ev[oc % 4] = cpv
        stg_free[oc % 2] = k.dma("gpsimd",
                                 k.sscr.ap()[oc, :, :], sb[:, 0:TB],
                                 wait=[cpv], inc="GD")
    proj2048(k, "l1", 3, t["mixy"], cb_wo, rhs_ev=og_ev)
    k.s_done = k.ev("GD")



def l1_ffn(k, r):
    t, din = k.tensors, k.din
    rhs = [t["xn"][kk][:, 1:1 + TB] for kk in range(16)]
    k.slab_free = k.ev("T")
    bank_ev = [k.ev("V")] * 4
    if "l1ffn" in DBG["stages"]:
        m3g = None
        for jg in range(22):
            ev1, ev2 = [], []
            for kk in range(16):
                if m3g is None:
                    w8 = [k.slab_free] if kk == 0 else ()
                    w8u = [k.slab_free] if kk == 0 else ()
                else:
                    w8 = [m3g[0][kk]]
                    w8u = [m3g[1][kk]]
                ev1.append(qslab(k, t["wsl"][kk], "w1q", 128 * kk, 256 * jg, 256,
                                 wait=w8))
                ev2.append(qslab(k, t["wsl2"][kk], "w2q", 128 * kk, 256 * jg,
                                 256, wait=w8u, eng="gpsimd", inc="GD"))
            evall = [ev1[-1], ev2[-1]]
            new_m3 = ([None] * 16, [None] * 16)
            last = None
            for jj in range(2):
                for kk in range(16):
                    w = list(evall) if kk == 0 else []
                    if kk == 0:
                        w += [bank_ev[jj], k.xn1_ev]
                    ev = k.mm(k.psum[jj][:, 0:TB],
                              t["wsl"][kk][:, 128 * jj:128 * jj + 128],
                              rhs[kk], start=(kk == 0), stop=(kk == 15),
                              wait=w, inc="T" if (kk == 15 or jj == 1) else None)
                    if jj == 1:
                        new_m3[0][kk] = ev
                for kk in range(16):
                    w = list(evall) if kk == 0 else []
                    if kk == 0:
                        w.append(bank_ev[2 + jj])
                    last = k.mm(k.psum[2 + jj][:, 0:TB],
                                t["wsl2"][kk][:, 128 * jj:128 * jj + 128],
                                rhs[kk], start=(kk == 0), stop=(kk == 15),
                                wait=w,
                                inc="T" if (kk == 15 or jj == 1) else None)
                    if jj == 1:
                        new_m3[1][kk] = last
                sg = k.op("scalar", lambda e, jj=jj: e.activation(
                    out=t["xntm"][:, 0:TB], in_=k.psum[jj][:, 0:TB],
                    func=AF.Sigmoid), wait=[last, k.ev("V")])
                bank_ev[jj] = sg
                hvar = DBG.get("hvar", 0)
                if hvar == 1:
                    hv = k.op("vector", lambda e, jg=jg, jj=jj: e.tensor_copy(
                        out=t["hid"][2 * jg + jj][:, 0:TB],
                        in_=k.psum[2 + jj][:, 0:TB]), wait=[sg])
                elif hvar == 2:
                    hv = k.op("vector", lambda e, jg=jg, jj=jj: e.tensor_copy(
                        out=t["hid"][2 * jg + jj][:, 0:TB],
                        in_=t["xntm"][:, 0:TB]), wait=[sg])
                else:
                    hv = k.op("vector", lambda e, jg=jg, jj=jj: e.tensor_tensor(
                        out=t["hid"][2 * jg + jj][:, 0:TB],
                        in0=t["xntm"][:, 0:TB],
                        in1=k.psum[2 + jj][:, 0:TB], op=ALU.mult), wait=[sg])
                bank_ev[2 + jj] = hv
            m3g = new_m3
            k.slab_free = last
            if DBG.get("l1cut", 0):
                return

        hid_ev = k.ev("V")
        k.slab_free = k.ev("T")
        bank_ev = [k.ev("V")] * 2
        m3d = None
        for og in range(8):
            evs = []
            for kk in range(44):
                if m3d is None:
                    w8 = [k.slab_free] if kk <= 1 else ()
                else:
                    w8 = [m3d[kk]]
                evs.append(qslab(k, dslab(k, kk), "w3q", 128 * kk, 256 * og,
                                 256, wait=w8,
                                 eng="gpsimd" if kk % 2 else "sync",
                                 inc="GD" if kk % 2 else "LW"))
            evall = [evs[-2], evs[-1]]
            new_m3 = [None] * 44
            last = None
            for m in range(2):
                oc = 2 * og + m
                for kk in range(44):
                    w = list(evall) if kk == 0 else []
                    if kk == 0:
                        w += [bank_ev[m], hid_ev]
                    last = k.mm(k.psum[m][:, 0:TB],
                                dslab(k, kk)[:, 128 * m:128 * m + 128],
                                t["hid"][kk][:, 0:TB], start=(kk == 0),
                                stop=(kk == 43), wait=w,
                                inc="T" if (kk == 43 or m == 1) else None)
                    if m == 1:
                        new_m3[kk] = last
                ev = k.op("vector", lambda e, oc=oc, m=m: e.tensor_copy(
                    out=t["fo"][oc][:, 0:TB], in_=k.psum[m][:, 0:TB]),
                    wait=[last])
                bank_ev[m] = ev
            m3d = new_m3
            k.slab_free = last
    else:
        for oc in range(16):
            k.op("vector", lambda e, oc=oc: e.memzero(t["fo"][oc][:, 0:TB]))

    stg_free = k.ev("V")
    for oc in range(16):
        sl = k.op("scalar", lambda e, oc=oc: e.activation(
            out=t["fo"][oc][:, 0:TB], in_=t["fo"][oc][:, 0:TB], func=AF.Silu),
            wait=[k.ev("V")])
        ld = k.dma("sync", t["stg"][:, 0:TB], k.sscr.ap()[oc, :, :],
                   wait=[k.s_done if oc == 0 else stg_free])
        sg = k.op("vector", lambda e, oc=oc: e.tensor_tensor(
            out=t["fo"][oc][:, 0:TB], in0=t["stg"][:, 0:TB],
            in1=t["fo"][oc][:, 0:TB], op=ALU.mult), wait=[ld, sl])
        stg_free = sg
    sg_ev = k.ev("V")
    tp_free = k.ev("V")
    for tc in range(TB // 128):
        ld = k.dma("sync", t["cxa"][:, :],
                   k.x1scr.ap()[128 * tc:128 * tc + 128, :],
                   wait=[k.ev("V"), k.ev("S")])
        tl = None
        for oc in range(16):
            b = oc // 4
            w = [sg_ev, tp_free] if oc == 0 else ()
            tl = k.tr(k.psum[b][:, 128 * (oc % 4):128 * (oc % 4) + 128],
                      t["fo"][oc][:, 128 * tc:128 * tc + 128],
                      t["ident_f32"][:, :], wait=w,
                      inc="T" if oc % 4 == 3 else None)
        oe = None
        for b in range(4):
            oe = k.op("vector", lambda e, b=b: e.tensor_tensor(
                out=t["cxb"][:, 512 * b:512 * b + 512],
                in0=t["cxa"][:, 512 * b:512 * b + 512],
                in1=k.psum[b][:, 0:512], op=ALU.add),
                wait=[ld, tl, k.ev("OUT")] if b == 0 else ())
        tp_free = oe
        k.dma("gpsimd", k.out_hbm[TB * r + 128 * tc: TB * r + 128 * tc + 128, :],
              t["cxb"][:, :], wait=[oe], inc="OUT")


def tile(k, r):
    t, din = k.tensors, k.din
    barrier(k)
    k.iota_ev = k.dma("sync", t["iota"][:, :], din[f"iota{r}"][:, :])
    k.slab_free = k.ev("T")
    k.bank_ev = [k.ev("T")] * 4
    k.tp_free = [k.ev("V"), k.ev("V")]
    k.op("vector", lambda e: e.tensor_scalar(out=t["tcol"][:, 32:40],
                                             in0=t["pmask"][:, 0:8],
                                             scalar1=-1.0, scalar2=1.0,
                                             op0=ALU.mult, op1=ALU.add),
         wait=[k.ld_small])

    def write_xn(tc, xe):
        def dst(fc, psrc, w):
            return k.op("vector",
                        lambda e: e.tensor_copy(
                            out=t["xn"][fc][:, 1 + 128 * tc:129 + 128 * tc],
                            in_=psrc),
                        wait=w)
        transpose_to_fm(k, xe, lambda fc: t["xntm"][:, 128 * fc:128 * fc + 128],
                        dst)
    for tc in range(TB // 128):
        ln_row_chunk(k, r, tc, 1 + 128 * tc, write_xn)
    ld = k.dma("sync", t["xtm"][0:1, :], din[f"x{r}"][0:1, :],
               wait=[k.ev("S"), k.ev("V")])
    k.op("scalar", lambda e: e.activation(out=t["xsq"][0:1, :], in_=t["xtm"][0:1, :],
                                          func=AF.Square), wait=[ld])
    sq = k.ev("S")
    k.op("vector", lambda e: e.reduce_sum(out=t["hbuf"][0:1, 0:1],
                                          in_=t["xtm"][0:1, :], axis=AX), wait=[ld])
    k.op("vector", lambda e: e.reduce_sum(out=t["hbuf"][0:1, 1:2],
                                          in_=t["xsq"][0:1, :], axis=AX), wait=[sq])
    k.op("vector", lambda e: e.tensor_scalar(out=t["hbuf"][0:1, 2:3],
                                             in0=t["hbuf"][0:1, 0:1],
                                             scalar1=1.0 / 2048.0, scalar2=None,
                                             op0=ALU.mult))
    k.op("vector", lambda e: e.tensor_scalar(out=t["hbuf"][0:1, 3:4],
                                             in0=t["hbuf"][0:1, 1:2],
                                             scalar1=1.0 / 2048.0, scalar2=None,
                                             op0=ALU.mult))
    k.op("vector", lambda e: e.tensor_tensor(out=t["hbuf"][0:1, 4:5],
                                             in0=t["hbuf"][0:1, 2:3],
                                             in1=t["hbuf"][0:1, 2:3], op=ALU.mult))
    vv = k.op("vector", lambda e: e.tensor_tensor(out=t["hbuf"][0:1, 4:5],
                                                  in0=t["hbuf"][0:1, 3:4],
                                                  in1=t["hbuf"][0:1, 4:5],
                                                  op=ALU.subtract))
    k.op("scalar", lambda e: e.activation(out=t["hbuf"][0:1, 5:6],
                                          in_=t["hbuf"][0:1, 4:5], func=AF.Ln,
                                          bias=1e-5), wait=[vv])
    k.op("scalar", lambda e: e.activation(out=t["hbuf"][0:1, 5:6],
                                          in_=t["hbuf"][0:1, 5:6], func=AF.Exp,
                                          scale=-0.5))
    sr = k.ev("S")
    k.op("vector", lambda e: e.tensor_tensor(out=t["hbuf"][0:1, 6:7],
                                             in0=t["hbuf"][0:1, 2:3],
                                             in1=t["hbuf"][0:1, 5:6], op=ALU.mult),
         wait=[sr])
    nbb = k.op("vector", lambda e: e.tensor_scalar(out=t["hbuf"][0:1, 6:7],
                                                   in0=t["hbuf"][0:1, 6:7],
                                                   scalar1=-1.0, scalar2=None,
                                                   op0=ALU.mult))
    bl = k.op("scalar", lambda e: e.activation(out=t["xsq"][0:1, :],
                                               in_=t["xtm"][0:1, :],
                                               func=AF.Identity,
                                               scale=t["hbuf"][0:1, 5:6],
                                               bias=t["hbuf"][0:1, 6:7]),
              wait=[nbb])
    k.op("vector", lambda e: e.scalar_tensor_tensor(
        out=t["xsq"][0:1, :], in0=t["xsq"][0:1, :],
        scalar=t["flags"][0:1, 2 * r:2 * r + 1], in1=t["xsq"][0:1, :],
        op0=ALU.mult, op1=ALU.bypass), wait=[bl])
    k.op("vector", lambda e: e.scalar_tensor_tensor(
        out=t["xsq"][0:1, :], in0=t["xtm"][0:1, :],
        scalar=t["flags"][0:1, 2 * r + 1:2 * r + 2], in1=t["xsq"][0:1, :],
        op0=ALU.mult, op1=ALU.add))
    bb = k.op("vector", lambda e: e.tensor_copy(out=t["xntm"][0:1, :],
                                                in_=t["xsq"][0:1, :]),
              wait=[k.ev("T")])
    for fc in range(16):
        tev = k.tr(k.psum[4][0:128, 0:1],
                   t["xntm"][0:1, 128 * fc:128 * fc + 128], t["ones1"][0:1, 0:1],
                   wait=[bb, k.tp_free[0]], inc="T")
        k.tp_free[0] = k.op("vector",
                            lambda e, fc=fc: e.tensor_copy(
                                out=t["xn"][fc][:, 0:1],
                                in_=k.psum[4][0:128, 0:1]),
                            wait=[tev])
    xn_ev = k.ev("V")

    st = DBG["stages"]
    if "l0attn" in st:
        l0_attn(k, r, xn_ev)
    if "l0ffn" in st:
        l0_ffn(k, r, xn_ev)
    if "l0attn" in st and "l0ffn" in st:
        combine0(k, r)
    else:
        return
    if "l1attn" in st:
        l1_attn(k, r)
        if "l1out" in st:
            l1_ffn(k, r)


def kernel(**inputs):
    in_maps = host_prep(inputs)
    nc, k = build_program()
    res = run_bass_kernel_spmd(nc, in_maps, core_ids=list(range(NCORES)))
    out = np.zeros((NT, D), np.float32)
    for i in range(NCORES):
        o = res.results[i]["out"]
        for r in range(NTILES):
            blk = 8 * r + i
            out[blk * TB:(blk + 1) * TB] = o[r * TB:(r + 1) * TB]
    return out.reshape(1, NT, D)


# version 06
# speedup vs baseline: 1.1987x; 1.0038x over previous
import numpy as np
from contextlib import ExitStack

import concourse.bass as bass
import concourse.mybir as mybir
from concourse.bass_utils import run_bass_kernel_spmd

F32 = mybir.dt.float32
BF16 = mybir.dt.bfloat16
F16 = mybir.dt.float16
I32 = mybir.dt.int32
U8 = mybir.dt.uint8
AF = mybir.ActivationFunctionType
ALU = mybir.AluOpType
AX = mybir.AxisListType.X

NCORES = 8
D = 2048
TB = 512
NTILES = 4
NT = 16384
H0, K0 = 32, 64
H1, K1 = 16, 128
I0 = 5632
GU = 2 * I0
I1 = 5632
LN2 = 0.6931471805599453
SCL1 = 0.088388347648318447

CONSTS = [0.0, 1.0, -1.0, 0.5, -0.5, 2.0, -2.0, 8.0, -8.0,
          LN2, -2.0 * LN2, 1.0 / 2048.0, 1.0 / 64.0,
          1e-5, 0.00064, 1e-6, 1e-8, SCL1, -16383.0]
CIDX = {v: i for i, v in enumerate(CONSTS)}

SEG_ROWS = {"l0": 1280, "l1": 1024, "gul": 128, "guh": 128,
            "dnl": 352, "dnh": 352, "w1q": 256, "w2q": 256, "w3q": 704}
SEG_W = {"l0": D, "l1": D, "gul": GU, "guh": GU, "dnl": D, "dnh": D,
         "w1q": I1, "w2q": I1, "w3q": D}

DBG = {"taps": [], "parts": {"casts", "gamma", "tern", "fp4"},
       "ntiles": NTILES,
       "stages": {"l0attn", "l0ffn", "l1attn", "l1out", "l1ffn"}}


def _sigma_perm():
    out = []
    for c in range(I0 // 256):
        out.extend(range(256 * c, 256 * c + 256, 2))
        out.extend(range(256 * c + 1, 256 * c + 256, 2))
    return np.array(out, dtype=np.int64)


def _col16(v):
    return np.ascontiguousarray(v.reshape(16, 128).T)


def host_prep(inp):
    import ml_dtypes
    x = np.asarray(inp["L0_attn_x"], np.float32).reshape(NT, D)
    state = np.asarray(inp["L0_attn_state"], np.float32).reshape(1, D)

    wrT = np.ascontiguousarray(np.asarray(inp["L0_attn_w_r"], np.float32).T)
    wkT = np.ascontiguousarray(np.asarray(inp["L0_attn_w_k"], np.float32).T)
    wvT = np.ascontiguousarray(np.asarray(inp["L0_attn_w_v"], np.float32).T)
    wgT = np.ascontiguousarray(np.asarray(inp["L0_attn_w_g"], np.float32).T)
    woT = np.ascontiguousarray(np.asarray(inp["L0_attn_w_o"], np.float32).T)
    l1q = np.asarray(inp["L1_attn_Wq"], np.float32)
    l1k = np.asarray(inp["L1_attn_Wk"], np.float32)
    l1i = np.asarray(inp["L1_attn_Wi"], np.float32)
    l1o = np.asarray(inp["L1_attn_Wo"], np.float32)

    sig = _sigma_perm()
    cols_perm = np.concatenate([sig, I0 + sig])
    gu_t = np.ascontiguousarray(
        np.asarray(inp["L0_ffn_gate_up_packed"], np.int32).view(np.uint8).T)
    gu_tp = np.ascontiguousarray(gu_t[:, cols_perm])
    gsc_p = np.ascontiguousarray(
        np.asarray(inp["L0_ffn_gate_up_scales"])[:, cols_perm])
    dn_t = np.ascontiguousarray(
        np.asarray(inp["L0_ffn_down_packed"], np.int32).view(np.uint8).T)
    dsc = np.asarray(inp["L0_ffn_down_scales"])

    w1T = np.ascontiguousarray(np.asarray(inp["L1_ffn_w1"], np.float32).T)
    w2T = np.ascontiguousarray(np.asarray(inp["L1_ffn_w2"], np.float32).T)
    w3T = np.ascontiguousarray(np.asarray(inp["L1_ffn_w3"], np.float32).T)

    tmix = np.concatenate(
        [_col16(np.asarray(inp["L0_attn_time_mix_" + s], np.float32).reshape(D))
         for s in ("r", "k", "v", "g")], axis=1)
    smallp = np.concatenate(
        [_col16(np.asarray(inp[n], np.float32).reshape(D)) for n in
         ("L0_attn_time_decay", "L0_attn_time_faaaa", "L0_attn_ln_w",
          "L0_attn_ln_b", "L1_attn_g_weight")], axis=1)
    wkv_l = np.ascontiguousarray(
        np.asarray(inp["L0_attn_wkvstate"], np.float32).transpose(1, 0, 2).reshape(64, D))

    ccol = np.broadcast_to(np.array(CONSTS, np.float32), (128, len(CONSTS))).copy()
    rng_p = np.arange(128)
    ident = np.eye(128, dtype=np.float32)
    m_lt = (rng_p[:, None] < rng_p[None, :]).astype(np.float32)
    le64 = (np.arange(64)[:, None] <= np.arange(64)[None, :]).astype(np.float32)
    mask_l1 = np.zeros((128, 128), np.float32)
    mask_l1[0:64, 0:64] = le64
    mask_l1[0:64, 64:128] = 1.0
    mask_l1[64:128, 64:128] = le64
    msk_bf = np.concatenate([ident, m_lt, mask_l1], axis=1).astype(ml_dtypes.bfloat16)
    m_diag_i = np.eye(128, dtype=np.int32)
    o642 = np.zeros((128, 2), np.float32)
    o642[0:64, 0] = 1.0
    o642[64:128, 1] = 1.0
    aux = np.concatenate([o642, np.ones((128, 1), np.float32)], axis=1)
    aux_bf = aux.astype(ml_dtypes.bfloat16)
    e2_bf = np.zeros((2, 128), np.float32)
    e2_bf[0, 0:64] = 1.0
    e2_bf[1, 64:128] = 1.0
    e2_bf = e2_bf.astype(ml_dtypes.bfloat16)
    ones1_bf = np.ones((1, 128), np.float32).astype(ml_dtypes.bfloat16)

    in_maps = []
    for i in range(NCORES):
        m = {}
        for r in range(NTILES):
            blk = 8 * r + i
            rows = x[blk * TB: blk * TB + TB]
            bnd = state if blk == 0 else x[blk * TB - 1: blk * TB]
            m[f"x{r}"] = np.concatenate([bnd, rows], axis=0)
            m[f"iota{r}"] = np.broadcast_to(
                np.arange(blk * TB, blk * TB + TB, dtype=np.float32)[None, :],
                (128, TB)).copy()
        fl = np.zeros((128, 2 * NTILES), np.float32)
        for r in range(NTILES):
            f = 0.0 if (8 * r + i) == 0 else 1.0
            fl[:, 2 * r] = f
            fl[:, 2 * r + 1] = 1.0 - f
        m["flags"] = fl
        pm = np.broadcast_to((np.arange(8) < i).astype(np.float32), (128, 8)).copy()
        m["pmask"] = pm
        sh = slice(256 * i, 256 * (i + 1))
        m["l0w_sh"] = np.ascontiguousarray(
            np.concatenate([wrT[sh], wkT[sh], wvT[sh], wgT[sh], woT[sh]], axis=0))
        m["l1w_sh"] = np.ascontiguousarray(
            np.concatenate([l1q[sh], l1k[sh], l1i[sh], l1o[sh]], axis=0))
        m["gu_sh"] = np.ascontiguousarray(gu_tp[128 * i: 128 * (i + 1)])
        m["gu_sc"] = np.ascontiguousarray(
            np.repeat(gsc_p[[2 * i, 2 * i + 1]], 64, axis=0))
        m["dn_sh"] = np.ascontiguousarray(dn_t[352 * i: 352 * (i + 1)])
        m["dn_sc"] = np.ascontiguousarray(dsc[(np.arange(352) + 352 * i) // 64])
        m["w1_sh"] = np.ascontiguousarray(w1T[sh])
        m["w2_sh"] = np.ascontiguousarray(w2T[sh])
        m["w3_sh"] = np.ascontiguousarray(w3T[704 * i: 704 * (i + 1)])
        m["tmix"] = tmix
        m["smallp"] = smallp
        m["wkv0"] = wkv_l
        m["ccol"] = ccol
        m["msk_bf"] = np.asarray(msk_bf)
        m["m_diag_i"] = m_diag_i
        m["aux_bf"] = np.asarray(aux_bf)
        m["e2_bf"] = np.asarray(e2_bf)
        m["ones1_bf"] = np.asarray(ones1_bf)
        m["ident_f32"] = ident
        m["onesf"] = np.ones((128, 16), np.float32)
        in_maps.append(m)
    return in_maps



class Sem:
    def __init__(self, handle):
        self.h = handle
        self.n = 0


class K:
    ENG_SEM = {"vector": "V", "scalar": "S", "gpsimd": "G", "tensor": "T"}

    def __init__(self, nc):
        self.nc = nc
        self.sems = {}
        self.cur = None
        self.seen = {}
        self.tensors = {}

    def new_pass(self, eng):
        self.cur = eng
        self.seen = {}
        for s in self.sems.values():
            s.n = 0

    def op(self, eng, fn, wait=(), inc=None, inc_n=1, ss=True):
        auto = self.ENG_SEM.get(eng)
        if eng in ("vector", "scalar", "gpsimd") and inc is None:
            inc, inc_n = auto, 1
        waits = list(wait)
        if ss and eng in ("vector", "scalar", "gpsimd"):
            pre = self.sems[auto].n
            if pre > 0:
                waits.append((auto, pre))
        if self.cur == eng:
            e = getattr(self.nc, eng)
            for sname, val in waits:
                if val <= 0:
                    continue
                key = (eng, sname)
                if self.seen.get(key, -1) < val:
                    e.wait_ge(self.sems[sname].h, val)
                    self.seen[key] = val
            inst = fn(e)
        else:
            inst = None
        if inc is not None:
            s = self.sems[inc]
            s.n += inc_n
            if inst is not None:
                inst.then_inc(s.h, inc_n)
            return (inc, s.n)
        return None

    def dma(self, eng, out, in_, wait=(), inc="LD", ss=False, **kw):
        return self.op(eng, lambda e: e.dma_start(out=out, in_=in_, **kw),
                       wait=wait, inc=inc, inc_n=16, ss=ss)

    def mm(self, out, lhsT, rhs, start, stop, wait=(), inc=None):
        return self.op("tensor",
                       lambda e: e.matmul(out, lhsT, rhs, start=start, stop=stop),
                       wait=wait, inc=inc, inc_n=1)

    def tr(self, out, in_, ident, wait=(), inc=None):
        return self.op("tensor",
                       lambda e: e.transpose(out, in_, ident),
                       wait=wait, inc=inc, inc_n=1)

    def ev(self, sname):
        return (sname, self.sems[sname].n)


def cc_(k, val):
    return k.tensors["ccol"][:, CIDX[val]:CIDX[val] + 1]


def ccp(k, val, p0, p1):
    return k.tensors["ccol"][p0:p1, CIDX[val]:CIDX[val] + 1]



def build_program():
    nc = bass.Bass("TRN2", target_bir_lowering=False, debug=False,
                   detect_race_conditions=False)
    k = K(nc)
    t = k.tensors
    k.din = {}

    def dram_in(name, shape, dt):
        k.din[name] = nc.dram_tensor(name, shape, dt, kind="ExternalInput").ap()

    for r in range(NTILES):
        dram_in(f"x{r}", [TB + 1, D], F32)
        dram_in(f"iota{r}", [128, TB], F32)
    dram_in("flags", [128, 2 * NTILES], F32)
    dram_in("pmask", [128, 8], F32)
    dram_in("l0w_sh", [5 * 256, D], F32)
    dram_in("l1w_sh", [4 * 256, D], F32)
    dram_in("gu_sh", [128, GU], U8)
    dram_in("gu_sc", [128, GU], F16)
    dram_in("dn_sh", [352, D], U8)
    dram_in("dn_sc", [352, D], F16)
    dram_in("w1_sh", [256, I1], F32)
    dram_in("w2_sh", [256, I1], F32)
    dram_in("w3_sh", [704, D], F32)
    dram_in("tmix", [128, 64], F32)
    dram_in("smallp", [128, 80], F32)
    dram_in("wkv0", [64, D], F32)
    dram_in("ccol", [128, len(CONSTS)], F32)
    dram_in("msk_bf", [128, 384], BF16)
    dram_in("m_diag_i", [128, 128], I32)
    dram_in("aux_bf", [128, 3], BF16)
    dram_in("e2_bf", [2, 128], BF16)
    dram_in("ones1_bf", [1, 128], BF16)
    dram_in("ident_f32", [128, 128], F32)
    dram_in("onesf", [128, 16], F32)

    k.out_hbm = nc.dram_tensor("out", [NTILES * TB, D], F32,
                               kind="ExternalOutput").ap()
    k.dtaps = {}
    for name, shape, dt in DBG["taps"]:
        k.dtaps[name] = nc.dram_tensor("tap " + name, shape, dt,
                                       kind="ExternalOutput").ap()

    woff = {}
    off = 0
    for name in ("l0", "l1", "gul", "guh", "dnl", "dnh", "w1q", "w2q", "w3q"):
        woff[name] = off
        off += SEG_ROWS[name] // (5 if name == "l0" else 4 if name == "l1" else 1) \
            * SEG_W[name] * (5 if name == "l0" else 4 if name == "l1" else 1)
    k.woff = woff
    k.WSH = off
    k.wag_in = nc.dram_tensor("wag_in", [k.WSH], BF16)
    k.wag_out = nc.dram_tensor("wag_out", [NCORES * k.WSH], BF16)
    k.gsum_in = nc.dram_tensor("gsum_in", [1, 8], F32)
    k.gsum_out = nc.dram_tensor("gsum_out", [1, 8], F32)
    k.gtmp = nc.dram_tensor("gtmp", [128, 4], F32)
    k.sag = {}
    for r in range(NTILES):
        k.sag[f"l0i{r}"] = nc.dram_tensor(f"sag_l0i{r}", [64, H0 * K0], F32)
        k.sag[f"l0o{r}"] = nc.dram_tensor(f"sag_l0o{r}", [NCORES * 64, H0 * K0], F32)
        k.sag[f"l1i{r}"] = nc.dram_tensor(f"sag_l1i{r}", [128, H1 * K1 + 16], F32)
        k.sag[f"l1o{r}"] = nc.dram_tensor(f"sag_l1o{r}",
                                          [NCORES * 128, H1 * K1 + 16], F32)
    k.x1scr = nc.dram_tensor("x1scr", [TB, D], F32)
    k.sscr = nc.dram_tensor("sscr", [16, 128, TB], F32)
    k.xnscr = nc.dram_tensor("xnscr", [16, 128, TB + 4], BF16)

    with ExitStack() as ctx:
        for sname in ("LD", "LW", "GD", "V", "S", "T", "G", "CC", "OUT"):
            k.sems[sname] = Sem(ctx.enter_context(nc.semaphore(sname)))
        alloc_tensors(nc, k)
        for v in CONSTS:
            if v not in (0.0, 1.0):
                nc.const_aps.register(F32, v, t["ccol"][:, CIDX[v]:CIDX[v] + 1])
        block = ctx.enter_context(nc.Block())

        def run(eng):
            def f(_):
                k.new_pass(eng)
                walk(k)
            return f
        block.sync(run("sync"))
        block.gpsimd(run("gpsimd"))
        block.vector(run("vector"))
        block.scalar(run("scalar"))
        block.tensor(run("tensor"))
    return nc, k


def alloc_tensors(nc, k):
    t = k.tensors

    def P(name, shape, dt):
        t[name] = nc.alloc_sbuf_tensor("sb_" + name, shape, dt).ap()

    P("ccol", [128, len(CONSTS)], F32)
    P("tmix", [128, 64], F32)
    P("smallp", [128, 80], F32)
    P("flags", [128, 2 * NTILES], F32)
    P("pmask", [128, 8], F32)
    P("msk_bf", [128, 384], BF16)
    P("m_diag_i", [128, 128], I32)
    P("aux_bf", [128, 3], BF16)
    P("e2", [2, 128], BF16)
    P("ones1", [1, 128], BF16)
    P("ident_f32", [128, 128], F32)
    P("onesf", [128, 16], F32)
    P("iota", [128, TB], F32)
    P("etd", [128, 16], F32)
    P("netd", [128, 16], F32)
    P("bwb", [128, 16], F32)
    P("wkv0b", [64, D], BF16)
    P("s0f", [64, D], F32)
    P("s0b", [64, D], BF16)
    P("s0run", [64, D], F32)
    P("s1f", [128, H1 * K1], F32)
    P("s1b", [128, H1 * K1], BF16)
    P("s1run", [128, H1 * K1], F32)
    P("s1lg", [128, 16], F32)
    P("tcol", [128, 64], F32)
    P("trow", [1, 64], F32)
    P("trow_bf", [1, 64], BF16)
    P("ginv", [128, 4], F32)
    P("hbuf", [128, 64], F32)

    k.psum = []
    for b in range(8):
        if b in (4, 5):
            h = nc.alloc_psum_tensor(f"ps{b}", [128, 1024], BF16)
        else:
            h = nc.alloc_psum_tensor(f"ps{b}", [128, 512], F32)
        k.psum.append(h.ap())

    rem = nc.sbuf_bytes_remaining
    base = nc.SBUF_PARTITION_SIZE_BYTES - rem + 128
    base = (base + 31) // 32 * 32
    k.arena_base = base
    k.arena_size = nc.SBUF_PARTITION_SIZE_BYTES - base - 256

    def A(name, shape, dt, offset):
        nbytes = int(np.prod(shape[1:])) * mybir.dt.size(dt)
        assert offset + nbytes <= k.arena_size, (name, offset, nbytes, k.arena_size)
        t[name] = nc.alloc_sbuf_tensor_at("sb_" + name, shape, dt,
                                          offset=base + offset).ap()
        return offset + ((nbytes + 31) // 32 * 32)

    def CL(name, n, shape, dt, offset):
        o = offset
        lst = []
        for i in range(n):
            o = A(f"{name}{i}", shape, dt, o)
            lst.append(t[f"{name}{i}"])
        t[name] = lst
        return o

    R1 = 0
    R2 = CL("xn", 16, [128, TB + 4], BF16, R1)
    R3 = CL("mixy", 16, [128, TB], BF16, R2)
    o = R3
    for nm in ("rf", "kw", "ku", "rw"):
        o = CL(nm, 16, [128, TB], BF16, o)
    R4 = o
    R5 = CL("gsl", 16, [128, TB], BF16, R4)
    R6 = CL("vt", TB // 128, [128, D], BF16, R5)
    R7 = CL("wsl", 16, [128, 512], BF16, R6)
    o = R7
    for nm in ("rrh", "kwh", "kuh", "rwh"):
        o = A(nm, [64, TB], BF16, o)
    for nm in ("attsb", "q2b", "ks2b", "kst"):
        o = A(nm, [128, 128], BF16, o)
    stg_off = o
    o = A("stg", [128, TB], F32, o)
    o = A("stg2", [128, TB], F32, o)
    R8 = o
    o = A("tB", [128, TB], F32, o)
    o = A("tD", [128, 16], F32, o)
    o = A("lgpre", [128, 128], F32, o)
    arena_hi = o
    A("tC", [128, TB], F32, stg_off)
    assert arena_hi <= k.arena_size, (arena_hi, k.arena_size)

    o = 0
    for nm, shape, dt in (("wp_u8", [128, 2816], U8), ("wp_sc", [128, 2816], F16),
                          ("wp_iA", [128, I1], I32), ("wp_iB", [128, 2816], I32),
                          ("wp_iC", [128, 2816], I32), ("wp_fA", [128, I1], F32),
                          ("wp_fB", [128, 2816], F32), ("wp_fC", [128, 2816], F32),
                          ("wp_fD", [128, 2816], F32), ("wp_f32", [128, I1], F32),
                          ("wp_bf", [128, I1], BF16)):
        o = A(nm, shape, dt, o)
    o = A("wkv0f", [64, D], F32, o)
    assert o <= k.arena_size
    o = A("xtm", [128, D], F32, R5)
    A("xsq", [128, D], F32, o)
    A("xntm", [128, D], BF16, R7 + 4096 + 1024)
    CL("xnil", 16, [128, TB], BF16, R2)
    CL("hh", 44, [128, TB], BF16, R3)
    o = CL("dd", 8, [128, TB], F32, R4)
    CL("ddx", 8, [128, TB], F32, o)
    t["dd"] = t["dd"] + t["ddx"]
    CL("wsl2", 16, [128, 512], BF16, R3 + 44 * 1024)
    o = R3
    for nm in ("qt", "kd", "ks"):
        o = CL(nm, 16, [128, TB], BF16, o)
    for nm in ("kraw", "sgb", "gbv", "abuf"):
        o = A(nm, [128, TB], F32, o)
    assert o <= R4, (o, R4)
    CL("hid", 44, [128, TB], BF16, R3)
    o = CL("fo", 8, [128, TB], F32, R4)
    CL("fox", 8, [128, TB], F32, o)
    t["fo"] = t["fo"] + t["fox"]
    A("cxa", [128, D], F32, R2)
    A("cxb", [128, D], F32, R2 + 8192)
    A("csq", [128, D], F32, R3)
    A("s0in", [64, D], F32, R6)
    A("s0inb", [64, D], BF16, R6 + 8192)
    A("s1in", [128, H1 * K1], F32, R6)
    A("s1inb", [128, H1 * K1], BF16, R6 + 8192)
    CL("kwt", TB // 128, [128, D], BF16, R6)
    A("gnhl", [2, 4 * 512], BF16, R7)
    A("rmshl", [1, 2 * 512], BF16, R7)
    k.regions = dict(R1=R1, R2=R2, R3=R3, R4=R4, R5=R5, R6=R6, R7=R7, R8=R8,
                     hi=arena_hi)


def wag_seg(k, elt0, p, w):
    return k.wag_in.ap()[elt0: elt0 + p * w].rearrange("(p w) -> p w", w=w)


def wago(k, core, segoff, p, w):
    elt0 = core * k.WSH + segoff
    return k.wag_out.ap()[elt0: elt0 + p * w].rearrange("(p w) -> p w", w=w)


def load_wslab(k, dst, seg, row0, nrows, col0, width, wait=(), inc="LW"):
    rpc = SEG_ROWS[seg]
    w = SEG_W[seg]
    ev = None
    done = 0
    while done < nrows:
        row = row0 + done
        core = row // rpc
        rin = row % rpc
        take = min(nrows - done, rpc - rin)
        src = wago(k, core, k.woff[seg] + rin * w, take, w)[:, col0:col0 + width]
        ev = k.dma("sync", dst[done:done + take, 0:width], src,
                   wait=wait if done == 0 else (), inc=inc)
        done += take
    return ev



def weight_prep(k):
    nc, t, din, woff = k.nc, k.tensors, k.din, k.woff
    zout = k.ev("OUT")
    prev_cv = zout

    if "casts" in DBG["parts"]:
        for mi in range(9):
            src = din["l0w_sh"] if mi < 5 else din["l1w_sh"]
            row0 = (mi if mi < 5 else mi - 5) * 256
            segoff = (woff["l0"] + mi * 256 * D) if mi < 5 else (
                woff["l1"] + (mi - 5) * 256 * D)
            for chn in range(2):
                ld = k.dma("sync", t["wp_f32"][:, 0:D],
                           src[row0 + 128 * chn: row0 + 128 * (chn + 1), :],
                           wait=[prev_cv], inc="LW")
                cv = k.op("vector",
                          lambda e: e.tensor_copy(out=t["wp_bf"][:, 0:D],
                                                  in_=t["wp_f32"][:, 0:D]),
                          wait=[ld, k.ev("GD")])
                prev_cv = cv
                k.dma("gpsimd", wag_seg(k, segoff + 128 * chn * D, 128, D),
                      t["wp_bf"][:, 0:D], wait=[cv], inc="GD")

    if "gamma" in DBG["parts"]:
        for g, (nm, rows, cols) in enumerate(
                (("w1_sh", 256, I1), ("w2_sh", 256, I1), ("w3_sh", 704, D))):
            nch = (rows + 127) // 128
            for chn in range(nch):
                p = min(128, rows - 128 * chn)
                ld = k.dma("sync", t["wp_f32"][0:p, 0:cols],
                           din[nm][128 * chn: 128 * chn + p, :],
                           wait=[prev_cv, k.ev("V")], inc="LW")
                nseg = (cols + 511) // 512
                for sg in range(nseg):
                    s0, s1 = 512 * sg, min(cols, 512 * (sg + 1))
                    k.op("vector",
                         lambda e, p=p, s0=s0, s1=s1, sg=sg:
                         e.reduce_sum(out=t["tcol"][0:p, sg:sg + 1],
                                      in_=t["wp_f32"][0:p, s0:s1], axis=AX,
                                      apply_absolute_value=True),
                         wait=[ld])
                dst = 16 + g if chn == 0 else 24 + g
                prev_cv = k.op("vector",
                               lambda e, p=p, nseg=nseg, dst=dst:
                               e.reduce_sum(out=t["tcol"][0:p, dst:dst + 1],
                                            in_=t["tcol"][0:p, 0:nseg], axis=AX))
                if chn > 0:
                    prev_cv = k.op("vector",
                                   lambda e, p=p, g=g:
                                   e.tensor_tensor(
                                       out=t["tcol"][0:p, 16 + g:17 + g],
                                       in0=t["tcol"][0:p, 16 + g:17 + g],
                                       in1=t["tcol"][0:p, 24 + g:25 + g],
                                       op=ALU.add))
        gv = k.ev("V")
        gd = k.dma("gpsimd", k.gtmp.ap()[:, 0:3], t["tcol"][:, 16:19], wait=[gv],
                   inc="GD")
        gl = None
        for g in range(3):
            gl = k.dma("sync", t["wp_fA"][0:1, 128 * g:128 * (g + 1)],
                       k.gtmp.ap()[:, g:g + 1], wait=[gd], inc="LW",
                       allow_slow_non_contiguous=True)
        for g in range(3):
            k.op("vector",
                 lambda e, g=g: e.reduce_sum(
                     out=t["trow"][0:1, g:g + 1],
                     in_=t["wp_fA"][0:1, 128 * g:128 * (g + 1)], axis=AX),
                 wait=[gl])
        gv2 = k.ev("V")
        gd2 = k.dma("gpsimd", k.gsum_in.ap()[0:1, 0:8], t["trow"][0:1, 0:8],
                    wait=[gv2], inc="GD")
        ccr = k.op("gpsimd",
                   lambda e: e.collective_compute(
                       "AllReduce", ALU.add, replica_groups=[list(range(NCORES))],
                       ins=[k.gsum_in.ap().opt()], outs=[k.gsum_out.ap().opt()]),
                   wait=[gd2], inc="CC")
        gl2 = k.dma("sync", t["trow"][0:1, 8:11], k.gsum_out.ap()[0:1, 0:3],
                    wait=[ccr], inc="LW")
        k.op("vector",
             lambda e: e.tensor_scalar(out=t["trow"][0:1, 12:15],
                                       in0=t["trow"][0:1, 8:11],
                                       scalar1=1.0 / (I1 * D), scalar2=None,
                                       op0=ALU.mult),
             wait=[gl2])
        k.op("vector", lambda e: e.tensor_copy(out=t["trow_bf"][0:1, 0:3],
                                               in_=t["trow"][0:1, 12:15]))
        k.op("vector", lambda e: e.tensor_copy(out=t["trow"][0:1, 16:19],
                                               in_=t["trow_bf"][0:1, 0:3]))
        k.op("vector", lambda e: e.tensor_tensor(out=t["trow"][0:1, 20:23],
                                                 in0=t["trow"][0:1, 12:15],
                                                 in1=t["trow"][0:1, 16:19],
                                                 op=ALU.subtract))
        hv = k.op("vector", lambda e: e.tensor_copy(out=t["trow_bf"][0:1, 8:11],
                                                    in_=t["trow"][0:1, 20:23]))
        k.mm(k.psum[6][:, 0:3], t["ones1"][0:1, :], t["trow_bf"][0:1, 0:3],
             True, False, wait=[hv, k.ld_small])
        mm2 = k.mm(k.psum[6][:, 0:3], t["ones1"][0:1, :], t["trow_bf"][0:1, 8:11],
                   False, True, inc="T")
        ad = k.op("vector",
                  lambda e: e.tensor_scalar(out=t["ginv"][:, 0:3],
                                            in0=k.psum[6][:, 0:3],
                                            scalar1=1e-8, scalar2=None,
                                            op0=ALU.add),
                  wait=[mm2])
        k.ginv_ev = k.op("vector",
                         lambda e: e.reciprocal(out=t["ginv"][:, 0:3],
                                                in_=t["ginv"][:, 0:3]),
                         wait=[ad])

    if "tern" in DBG["parts"]:
        gi = k.ginv_ev
        for g, (nm, rows, cols, seg) in enumerate(
                (("w1_sh", 256, I1, "w1q"), ("w2_sh", 256, I1, "w2q"),
                 ("w3_sh", 704, D, "w3q"))):
            nch = (rows + 127) // 128
            for chn in range(nch):
                p = min(128, rows - 128 * chn)
                ld = k.dma("sync", t["wp_f32"][0:p, 0:cols],
                           din[nm][128 * chn: 128 * chn + p, :],
                           wait=[k.ev("S"), k.ev("V")], inc="LW")
                s1 = k.op("scalar",
                          lambda e, p=p, cols=cols, g=g:
                          e.mul(out=t["wp_fA"][0:p, 0:cols],
                                in_=t["wp_f32"][0:p, 0:cols],
                                mul=t["ginv"][0:p, g:g + 1]),
                          wait=[ld, gi])
                k.op("vector",
                     lambda e, p=p, cols=cols:
                     e.tensor_scalar(out=t["wp_f32"][0:p, 0:cols],
                                     in0=t["wp_fA"][0:p, 0:cols],
                                     scalar1=0.5, scalar2=None,
                                     op0=ALU.is_gt),
                     wait=[s1, k.ev("GD")])
                k.op("vector",
                     lambda e, p=p, cols=cols:
                     e.tensor_scalar(out=t["wp_fA"][0:p, 0:cols],
                                     in0=t["wp_fA"][0:p, 0:cols],
                                     scalar1=-0.5, scalar2=None,
                                     op0=ALU.is_lt))
                r3 = k.op("vector",
                          lambda e, p=p, cols=cols:
                          e.tensor_tensor(out=t["wp_bf"][0:p, 0:cols],
                                          in0=t["wp_f32"][0:p, 0:cols],
                                          in1=t["wp_fA"][0:p, 0:cols],
                                          op=ALU.subtract))
                k.dma("gpsimd",
                      wag_seg(k, woff[seg] + 128 * chn * cols, p, cols),
                      t["wp_bf"][0:p, 0:cols], wait=[r3], inc="GD")

    if "fp4" in DBG["parts"]:
        fp4_decode(k, "gu_sh", "gu_sc", 128, GU, 2816, ("gul", "guh"))
        fp4_decode(k, "dn_sh", "dn_sc", 352, D, 2048, ("dnl", "dnh"))

    aw = k.ev("GD")
    k.wag_done = k.op("gpsimd",
                      lambda e: e.collective_compute(
                          "AllGather", ALU.bypass,
                          replica_groups=[list(range(NCORES))],
                          ins=[k.wag_in.ap().opt()],
                          outs=[k.wag_out.ap().opt()]),
                      wait=[aw], inc="CC")


def fp4_decode(k, bname, scname, rows, cols, W, segs):
    t, din, woff = k.tensors, k.din, k.woff
    nch = (rows + 127) // 128
    for chn in range(nch):
        p = min(128, rows - 128 * chn)
        for c0 in range(0, cols, W):
            w = min(W, cols - c0)
            ldb = k.dma("sync", t["wp_u8"][0:p, 0:w],
                        din[bname][128 * chn: 128 * chn + p, c0:c0 + w],
                        wait=[k.ev("V"), k.ev("S")], inc="LW")
            lds = k.dma("sync", t["wp_sc"][0:p, 0:w],
                        din[scname][128 * chn: 128 * chn + p, c0:c0 + w], inc="LW")
            k.op("vector",
                 lambda e, p=p, w=w: e.tensor_copy(
                     out=t["wp_iA"][0:p, 0:w], in_=t["wp_u8"][0:p, 0:w]),
                 wait=[ldb, lds])
            for par, seg in enumerate(segs):
                if par == 0:
                    k.op("vector",
                         lambda e, p=p, w=w: e.tensor_scalar(
                             out=t["wp_iB"][0:p, 0:w], in0=t["wp_iA"][0:p, 0:w],
                             scalar1=15, scalar2=None, op0=ALU.bitwise_and))
                else:
                    k.op("vector",
                         lambda e, p=p, w=w: e.tensor_scalar(
                             out=t["wp_iB"][0:p, 0:w], in0=t["wp_iA"][0:p, 0:w],
                             scalar1=4, scalar2=None,
                             op0=ALU.logical_shift_right))
                k.op("vector", lambda e, p=p, w=w: e.tensor_scalar(
                    out=t["wp_iC"][0:p, 0:w], in0=t["wp_iB"][0:p, 0:w],
                    scalar1=1, scalar2=3, op0=ALU.logical_shift_right,
                    op1=ALU.bitwise_and))
                ev_ = k.op("vector", lambda e, p=p, w=w: e.tensor_copy(
                    out=t["wp_fA"][0:p, 0:w], in_=t["wp_iC"][0:p, 0:w]))
                t2 = k.op("scalar", lambda e, p=p, w=w: e.activation(
                    out=t["wp_fB"][0:p, 0:w], in_=t["wp_fA"][0:p, 0:w],
                    func=AF.Exp, scale=LN2, bias=-2.0 * LN2),
                    wait=[ev_])
                k.op("vector", lambda e, p=p, w=w: e.tensor_scalar(
                    out=t["wp_iC"][0:p, 0:w], in0=t["wp_iB"][0:p, 0:w],
                    scalar1=1, scalar2=None, op0=ALU.bitwise_and))
                k.op("vector", lambda e, p=p, w=w: e.tensor_copy(
                    out=t["wp_fC"][0:p, 0:w], in_=t["wp_iC"][0:p, 0:w]))
                k.op("vector", lambda e, p=p, w=w: e.scalar_tensor_tensor(
                    out=t["wp_fD"][0:p, 0:w], in0=t["wp_fC"][0:p, 0:w],
                    scalar=2.0, in1=t["wp_fB"][0:p, 0:w],
                    op0=ALU.add, op1=ALU.mult), wait=[t2])
                k.op("vector", lambda e, p=p, w=w: e.tensor_scalar(
                    out=t["wp_fA"][0:p, 0:w], in0=t["wp_fA"][0:p, 0:w],
                    scalar1=8.0, scalar2=None, op0=ALU.mult))
                k.op("vector", lambda e, p=p, w=w: e.scalar_tensor_tensor(
                    out=t["wp_fA"][0:p, 0:w], in0=t["wp_fC"][0:p, 0:w],
                    scalar=0.5, in1=t["wp_fA"][0:p, 0:w],
                    op0=ALU.mult, op1=ALU.add))
                k.op("vector", lambda e, p=p, w=w: e.tensor_tensor(
                    out=t["wp_fD"][0:p, 0:w], in0=t["wp_fD"][0:p, 0:w],
                    in1=t["wp_fA"][0:p, 0:w], op=ALU.min))
                k.op("vector", lambda e, p=p, w=w: e.tensor_scalar(
                    out=t["wp_iC"][0:p, 0:w], in0=t["wp_iB"][0:p, 0:w],
                    scalar1=3, scalar2=None, op0=ALU.logical_shift_right))
                k.op("vector", lambda e, p=p, w=w: e.tensor_copy(
                    out=t["wp_fC"][0:p, 0:w], in_=t["wp_iC"][0:p, 0:w]))
                k.op("vector", lambda e, p=p, w=w: e.tensor_scalar(
                    out=t["wp_fC"][0:p, 0:w], in0=t["wp_fC"][0:p, 0:w],
                    scalar1=-2.0, scalar2=1.0, op0=ALU.mult, op1=ALU.add))
                k.op("vector", lambda e, p=p, w=w: e.tensor_tensor(
                    out=t["wp_fD"][0:p, 0:w], in0=t["wp_fD"][0:p, 0:w],
                    in1=t["wp_fC"][0:p, 0:w], op=ALU.mult))
                sv = k.op("vector", lambda e, p=p, w=w: e.tensor_tensor(
                    out=t["wp_bf"][0:p, 0:w], in0=t["wp_fD"][0:p, 0:w],
                    in1=t["wp_sc"][0:p, 0:w], op=ALU.mult),
                    wait=[k.ev("GD")])
                k.dma("gpsimd",
                      wag_seg(k, woff[seg] + (128 * chn) * cols, p,
                              cols)[:, c0:c0 + w],
                      t["wp_bf"][0:p, 0:w], wait=[sv], inc="GD")



def walk(k):
    t, din = k.tensors, k.din

    for name, src in [("ccol", "ccol"), ("tmix", "tmix"), ("smallp", "smallp"),
                      ("flags", "flags"), ("pmask", "pmask"),
                      ("msk_bf", "msk_bf"), ("m_diag_i", "m_diag_i"),
                      ("aux_bf", "aux_bf"), ("e2", "e2_bf"),
                      ("ones1", "ones1_bf"), ("ident_f32", "ident_f32"),
                      ("onesf", "onesf"), ("wkv0f", "wkv0")]:
        k.dma("sync", t[name][:, :], din[src][:, :])
    k.ld_small = k.ev("LD")

    for b in range(8):
        k.op("scalar", lambda e, b=b: e.memzero(k.psum[b][:, :]))
    k.op("scalar", lambda e: e.activation(out=t["etd"][:, :],
                                          in_=t["smallp"][:, 0:16], func=AF.Exp),
         wait=[k.ld_small])
    se = k.ev("S")
    k.op("vector", lambda e: e.tensor_scalar(out=t["netd"][:, :],
                                             in0=t["etd"][:, :], scalar1=-1.0,
                                             scalar2=None, op0=ALU.mult),
         wait=[se])
    k.op("vector", lambda e: e.tensor_scalar(out=t["bwb"][:, :],
                                             in0=t["etd"][:, :], scalar1=-16383.0,
                                             scalar2=None, op0=ALU.mult))
    k.op("vector", lambda e: e.tensor_copy(out=t["wkv0b"][:, :],
                                           in_=t["wkv0f"][:, :]),
         wait=[k.ld_small])
    k.op("vector", lambda e: e.memzero(t["s0run"][:, :]))
    k.op("vector", lambda e: e.memzero(t["s1run"][:, :]))
    k.op("vector", lambda e: e.memzero(t["trow"][0:1, 0:16]))

    weight_prep(k)

    for r in range(DBG["ntiles"]):
        tile(k, r)

    emit_taps(k)

    fin = k.ev("OUT")
    k.op("gpsimd", lambda e: e.nop(), wait=[fin, k.wag_done], inc="G")


def emit_taps(k):
    t = k.tensors
    allq = [k.ev("V"), k.ev("S"), k.ev("T"), k.ev("G"), k.wag_done,
            k.ev("GD"), k.ev("LD"), k.ev("LW"), k.ev("CC")]
    for name, ap in k.dtaps.items():
        parts = name.split(":")
        if parts[0] == "sbuf":
            _, tname, p, w = parts
            p, w = int(p), int(w)
            k.dma("gpsimd", ap[:, :], t[tname][0:p, 0:w], wait=allq, inc="OUT")
        elif parts[0] == "psum":
            _, idx, p, w = parts
            idx, p, w = int(idx), int(p), int(w)
            cpe = k.op("vector", lambda e, idx=idx, p=p, w=w: e.tensor_copy(
                out=t["xtm"][0:p, 0:w], in_=k.psum[idx][0:p, 0:w]),
                wait=allq + [k.ev("OUT")])
            k.dma("gpsimd", ap[:, :], t["xtm"][0:p, 0:w], wait=[cpe],
                  inc="OUT")
        elif parts[0] == "wag":
            _, seg, core, row0, p, w = parts
            core, row0, p, w = int(core), int(row0), int(p), int(w)
            src = wago(k, core, k.woff[seg] + row0 * w, p, w)
            k.dma("gpsimd", ap[:, :], src, wait=[k.wag_done], inc="OUT")
        elif parts[0] == "hbm":
            _, tn, row0, p, w = parts
            row0, p, w = int(row0), int(p), int(w)
            src = getattr(k, tn)
            if callable(getattr(src, "ap", None)):
                src = src.ap()
            src = src.rearrange("a b c -> (a b) c") if len(src.shape) == 3 else src
            k.dma("gpsimd", ap[:, :], src[row0:row0 + p, 0:w], wait=allq,
                  inc="OUT")





def qslab(k, dst, seg, row0, col0, width, nrows=128, wait=(), inc="LW",
          eng="sync"):
    rpc = SEG_ROWS[seg]
    w = SEG_W[seg]
    ev = None
    done = 0
    while done < nrows:
        row = row0 + done
        core, rin = divmod(row, rpc)
        take = min(nrows - done, rpc - rin)
        src = wago(k, core, k.woff[seg] + rin * w, take, w)
        ev = k.dma(eng, dst[done:done + take, 0:width],
                   src[:, col0:col0 + width],
                   wait=wait if done == 0 else (), inc=inc)
        done += take
    return ev


def attn_slab(k, dst, layer, mi, kk, col0, width, wait=(), inc="LW",
              eng="sync"):
    row = 128 * kk
    core, rin = divmod(row, 256)
    src = wago(k, core, k.woff[layer] + (mi * 256 + rin) * D, 128, D)
    return k.dma(eng, dst[:, 0:width], src[:, col0:col0 + width],
                 wait=wait, inc=inc)


def barrier(k):
    evs = [k.ev(s) for s in ("V", "S", "T", "G", "LD", "LW", "GD", "OUT")]
    for eng in ("sync", "gpsimd", "vector", "scalar", "tensor"):
        k.op(eng, lambda e: e.nop(), wait=evs, ss=False)
    return evs


def proj2048(k, layer, mi, rhs_chunks, out_cb, rhs_ev=None, n_out=16):
    t = k.tensors
    m3 = None
    for mg in range((n_out + 3) // 4):
        evs = []
        for kk in range(16):
            if m3 is None:
                w8 = [k.slab_free] if kk <= 1 else ()
            else:
                w8 = [m3[kk]]
            evs.append(attn_slab(k, t["wsl"][kk], layer, mi, kk, 512 * mg, 512,
                                 wait=w8, eng="gpsimd" if kk % 2 else "sync",
                                 inc="GD" if kk % 2 else "LW"))
        evall = [evs[-2], evs[-1]]
        new_m3 = [None] * 16
        last = None
        for m in range(4):
            oc = 4 * mg + m
            if oc >= n_out:
                break
            bank = k.psum[m]
            for kk in range(16):
                w = list(evall) if kk == 0 else []
                if kk == 0:
                    w.append(k.bank_ev[m])
                    if rhs_ev is not None:
                        w.append(rhs_ev)
                last = k.mm(bank[:, 0:TB], t["wsl"][kk][:, 128 * m:128 * m + 128],
                            rhs_chunks[kk][:, 0:TB],
                            start=(kk == 0), stop=(kk == 15),
                            wait=w,
                            inc="T" if (kk == 15 or m == 3) else None)
                if m == 3:
                    new_m3[kk] = last
            out_cb(oc, bank, last)
        if new_m3[15] is not None:
            m3 = new_m3
        if last is not None:
            k.slab_free = last


def ln_row_chunk(k, r, tc, src_rows, dst_writer):
    t, din = k.tensors, k.din
    ld = k.dma("sync", t["xtm"][:, :], din[f"x{r}"][src_rows:src_rows + 128, :],
               wait=[k.ev("S"), k.ev("V")])
    k.op("scalar", lambda e: e.activation(out=t["xsq"][:, :], in_=t["xtm"][:, :],
                                          func=AF.Square), wait=[ld])
    sq = k.ev("S")
    k.op("vector", lambda e: e.reduce_sum(out=t["hbuf"][:, 0:1], in_=t["xtm"][:, :],
                                          axis=AX), wait=[ld])
    k.op("vector", lambda e: e.reduce_sum(out=t["hbuf"][:, 1:2], in_=t["xsq"][:, :],
                                          axis=AX), wait=[sq])
    k.op("vector", lambda e: e.tensor_scalar(out=t["hbuf"][:, 2:3],
                                             in0=t["hbuf"][:, 0:1],
                                             scalar1=1.0 / 2048.0, scalar2=None,
                                             op0=ALU.mult))
    k.op("vector", lambda e: e.tensor_scalar(out=t["hbuf"][:, 3:4],
                                             in0=t["hbuf"][:, 1:2],
                                             scalar1=1.0 / 2048.0, scalar2=None,
                                             op0=ALU.mult))
    k.op("vector", lambda e: e.tensor_tensor(out=t["hbuf"][:, 4:5],
                                             in0=t["hbuf"][:, 2:3],
                                             in1=t["hbuf"][:, 2:3], op=ALU.mult))
    vv = k.op("vector", lambda e: e.tensor_tensor(out=t["hbuf"][:, 4:5],
                                                  in0=t["hbuf"][:, 3:4],
                                                  in1=t["hbuf"][:, 4:5],
                                                  op=ALU.subtract))
    k.op("scalar", lambda e: e.activation(out=t["hbuf"][:, 5:6],
                                          in_=t["hbuf"][:, 4:5], func=AF.Ln,
                                          bias=1e-5), wait=[vv])
    k.op("scalar", lambda e: e.activation(out=t["hbuf"][:, 5:6],
                                          in_=t["hbuf"][:, 5:6], func=AF.Exp,
                                          scale=-0.5))
    sr = k.ev("S")
    k.op("vector", lambda e: e.tensor_tensor(out=t["hbuf"][:, 6:7],
                                             in0=t["hbuf"][:, 2:3],
                                             in1=t["hbuf"][:, 5:6], op=ALU.mult),
         wait=[sr])
    nb = k.op("vector", lambda e: e.tensor_scalar(out=t["hbuf"][:, 6:7],
                                                  in0=t["hbuf"][:, 6:7],
                                                  scalar1=-1.0, scalar2=None,
                                                  op0=ALU.mult))
    xe = k.op("scalar", lambda e: e.activation(out=t["xntm"][:, :],
                                               in_=t["xtm"][:, :],
                                               func=AF.Identity,
                                               scale=t["hbuf"][:, 5:6],
                                               bias=t["hbuf"][:, 6:7]),
              wait=[nb, k.ev("T")])
    dst_writer(tc, xe)


def transpose_to_fm(k, src_ev, src_ap_fn, dst_fn):
    t = k.tensors
    for half in range(2):
        bank = k.psum[4 + half]
        tev = None
        for j in range(8):
            fc = 8 * half + j
            w = [k.tp_free[half]] if j == 0 else []
            if j == 0 and src_ev is not None:
                w.append(src_ev)
            tev = k.tr(bank[:, 128 * j:128 * j + 128],
                       src_ap_fn(fc), t["msk_bf"][:, 0:128],
                       wait=w, inc="T" if j == 7 else None)
        ev = None
        for j in range(8):
            fc = 8 * half + j
            ev = dst_fn(fc, bank[:, 128 * j:128 * j + 128],
                        [tev] if j == 0 else ())
        k.tp_free[half] = ev





def l0_attn(k, r, xn_ev):
    t = k.tensors
    cut = DBG.get("l0cut", 99)
    if cut <= 0:
        return

    def build_mix(mi, wait0):
        ev = None
        for fc in range(16):
            k.op("vector", lambda e, fc=fc: e.tensor_tensor(
                out=t["tB"][:, 0:TB], in0=t["xn"][fc][:, 1:1 + TB],
                in1=t["xn"][fc][:, 0:TB], op=ALU.subtract),
                wait=list(wait0) if fc == 0 else ())
            ev = k.op("vector", lambda e, fc=fc, mi=mi: e.scalar_tensor_tensor(
                out=t["mixy"][fc][:, 0:TB], in0=t["tB"][:, 0:TB],
                scalar=t["tmix"][:, 16 * mi + fc:16 * mi + fc + 1],
                in1=t["xn"][fc][:, 0:TB], op0=ALU.mult, op1=ALU.add))
        return ev

    iota_ev = k.iota_ev
    mev = build_mix(0, [xn_ev])
    if cut <= 0.2:
        return
    prev_read = [k.ev("V")]

    def cb_r(oc, bank, mmev):
        k.op("vector", lambda e, oc=oc, bank=bank: e.tensor_copy(
            out=t["rf"][oc][:, 0:TB], in_=bank[:, 0:TB]), wait=[mmev])
        wf = k.op("scalar", lambda e, oc=oc: e.activation(
            out=t["tC"][:, 0:TB], in_=t["iota"][:, 0:TB], func=AF.Exp,
            scale=t["netd"][:, oc:oc + 1]),
            wait=[iota_ev, prev_read[0]])
        ev = k.op("vector", lambda e, bank=bank, oc=oc: e.tensor_tensor(
            out=t["rw"][oc][:, 0:TB], in0=bank[:, 0:TB],
            in1=t["tC"][:, 0:TB], op=ALU.mult), wait=[wf])
        prev_read[0] = ev
        k.bank_ev[oc % 4] = ev
    proj2048(k, "l0", 0, t["mixy"], cb_r, rhs_ev=mev)
    if cut <= 0.4:
        return

    mev = build_mix(1, [k.ev("T")])
    if cut <= 0.5:
        return
    kvar = DBG.get("kvar", 0)

    def cb_k(oc, bank, mmev):
        if kvar == 1:
            ev = k.op("vector", lambda e, oc=oc, bank=bank: e.tensor_copy(
                out=t["kw"][oc][:, 0:TB], in_=bank[:, 0:TB]), wait=[mmev])
            k.bank_ev[oc % 4] = ev
            return
        if kvar == 3:
            k.op("vector", lambda e, oc=oc, bank=bank: e.tensor_copy(
                out=t["kw"][oc][:, 0:TB], in_=bank[:, 0:TB]), wait=[mmev])
            ev = k.op("scalar", lambda e, bank=bank, oc=oc: e.mul(
                out=t["ku"][oc][:, 0:TB], in_=bank[:, 0:TB],
                mul=t["smallp"][:, 16 + oc:17 + oc]),
                wait=[mmev])
            k.bank_ev[oc % 4] = ev
            return
        wb = k.op("scalar", lambda e, oc=oc: e.activation(
            out=t["tC"][:, 0:TB], in_=t["iota"][:, 0:TB], func=AF.Exp,
            scale=t["etd"][:, oc:oc + 1], bias=t["bwb"][:, oc:oc + 1]),
            wait=[prev_read[0]])
        tte = k.op("vector", lambda e, bank=bank, oc=oc: e.tensor_tensor(
            out=t["kw"][oc][:, 0:TB], in0=bank[:, 0:TB],
            in1=t["tC"][:, 0:TB], op=ALU.mult), wait=[wb, mmev])
        prev_read[0] = tte
        if kvar == 2:
            k.bank_ev[oc % 4] = tte
            return
        ev = k.op("scalar", lambda e, bank=bank, oc=oc: e.mul(
            out=t["ku"][oc][:, 0:TB], in_=bank[:, 0:TB],
            mul=t["smallp"][:, 16 + oc:17 + oc]),
            wait=[mmev, tte])
        k.bank_ev[oc % 4] = ev
    proj2048(k, "l0", 1, t["mixy"], cb_k, rhs_ev=mev)
    if cut <= 0.6:
        return

    mev = build_mix(2, [k.ev("T")])
    m3v = None
    for mg in range(4):
        evs = []
        for kk in range(16):
            if m3v is None:
                w8 = [k.slab_free] if kk <= 1 else ()
            else:
                w8 = [m3v[kk]]
            evs.append(attn_slab(k, t["wsl"][kk], "l0", 2, kk, 512 * mg, 512,
                                 wait=w8, eng="gpsimd" if kk % 2 else "sync",
                                 inc="GD" if kk % 2 else "LW"))
        evall = [evs[-2], evs[-1]]
        new_m3 = [None] * 16
        last = None
        for tc in range(TB // 128):
            bank = k.psum[tc]
            for kk in range(16):
                w = list(evall) if kk == 0 else []
                if kk == 0:
                    w.append(k.bank_ev[tc])
                    w.append(mev)
                last = k.mm(bank[:, 0:512],
                            t["mixy"][kk][:, 128 * tc:128 * tc + 128],
                            t["wsl"][kk][:, 0:512],
                            start=(kk == 0), stop=(kk == 15), wait=w,
                            inc="T" if (kk == 15 or tc == 3) else None)
                if tc == 3:
                    new_m3[kk] = last
            ev = k.op("vector", lambda e, tc=tc, mg=mg, bank=bank: e.tensor_copy(
                out=t["vt"][tc][:, 512 * mg:512 * mg + 512], in_=bank[:, 0:512]),
                wait=[last])
            k.bank_ev[tc] = ev
        m3v = new_m3
        k.slab_free = last
    if cut <= 0.8:
        return

    mev = build_mix(3, [k.ev("T")])

    def cb_g(oc, bank, mmev):
        ev = k.op("scalar", lambda e, oc=oc, bank=bank: e.activation(
            out=t["gsl"][oc][:, 0:TB], in_=bank[:, 0:TB], func=AF.Silu),
            wait=[mmev])
        k.bank_ev[oc % 4] = ev
    proj2048(k, "l0", 3, t["mixy"], cb_g, rhs_ev=mev)

    scan_t0 = k.ev("T")
    for tc in range(TB // 128):
        def dst(fc, psrc, w, tc=tc):
            return k.op("vector",
                        lambda e, fc=fc: e.tensor_copy(
                            out=t["kwt"][tc][:, 128 * fc:128 * fc + 128],
                            in_=psrc), wait=w)
        transpose_to_fm(k, scan_t0 if tc == 0 else None,
                        lambda fc, tc=tc: t["kw"][fc][:, 128 * tc:128 * tc + 128],
                        dst)

    k.op("vector", lambda e: e.memzero(t["s0f"][:, :]))
    z0 = k.op("vector", lambda e: e.memzero(t["s0b"][:, :]))
    if DBG.get("l0cut", 99) <= 1:
        return

    NCH = TB // 128
    k.att_free2 = [k.ev("V"), k.ev("V")]
    k.asb_free2 = [k.ev("T"), k.ev("T")]
    ob_free = [k.ev("V"), k.ev("V")]
    kv_free = k.ev("V")
    head_bufs_free = k.ev("T")
    s0b_ev = z0
    for h in range(H0):
        hc, hp = h // 2, h % 2
        k.op("vector", lambda e, hc=hc, hp=hp: e.tensor_copy(
            out=t["rrh"][:, 0:TB], in_=t["rf"][hc][64 * hp:64 * hp + 64, 0:TB]),
            wait=[head_bufs_free])
        k.op("vector", lambda e, hc=hc, hp=hp: e.tensor_copy(
            out=t["kwh"][:, 0:TB], in_=t["kw"][hc][64 * hp:64 * hp + 64, 0:TB]))
        k.op("vector", lambda e, hc=hc, hp=hp: e.tensor_copy(
            out=t["kuh"][:, 0:TB], in_=t["ku"][hc][64 * hp:64 * hp + 64, 0:TB]))
        cpe = k.op("vector", lambda e, hc=hc, hp=hp: e.tensor_copy(
            out=t["rwh"][:, 0:TB], in_=t["rw"][hc][64 * hp:64 * hp + 64, 0:TB]))
        ob = k.psum[h % 2]
        olast = None
        for c in range(NCH):
            sl = slice(128 * c, 128 * c + 128)
            ab = k.psum[6] if c % 2 == 0 else k.psum[3]
            asb = t["attsb"] if c % 2 == 0 else t["q2b"]
            k.mm(ab[:, 0:128], t["kwh"][:, sl], t["rrh"][:, sl],
                 True, True, wait=[cpe, k.att_free2[c % 2]])
            a2 = k.mm(ab[:, 128:256], t["kuh"][:, sl], t["rrh"][:, sl],
                      True, True, inc="T")
            k.op("vector", lambda e, ab=ab, asb=asb: e.tensor_tensor(
                out=asb[:, :], in0=ab[:, 0:128],
                in1=t["msk_bf"][:, 128:256], op=ALU.mult),
                wait=[a2, k.asb_free2[c % 2]])
            m2 = k.op("vector", lambda e, ab=ab, asb=asb: e.copy_predicated(
                out=asb[:, :], mask=t["m_diag_i"][:, :],
                data=ab[:, 128:256]))
            k.att_free2[c % 2] = m2
            w1 = [m2]
            if c == 0:
                w1.append(ob_free[h % 2])
            k.asb_free2[c % 2] = k.mm(
                ob[0:64, sl], t["vt"][c][:, 64 * h:64 * h + 64],
                asb[:, :], True, False, wait=w1, inc="T")
            k.mm(ob[0:64, sl], t["s0b"][:, 64 * h:64 * h + 64], t["rrh"][:, sl],
                 False, False, wait=[s0b_ev])
            olast = k.mm(ob[0:64, sl], t["wkv0b"][:, 64 * h:64 * h + 64],
                         t["rwh"][:, sl], False, True,
                         inc="T" if c == NCH - 1 else None)
            kv = k.mm(k.psum[7][0:64, 0:64], t["kwt"][c][:, 64 * h:64 * h + 64],
                      t["vt"][c][:, 64 * h:64 * h + 64], True, True,
                      wait=[kv_free], inc="T")
            k.op("vector", lambda e, h=h: e.tensor_tensor(
                out=t["s0f"][:, 64 * h:64 * h + 64],
                in0=t["s0f"][:, 64 * h:64 * h + 64],
                in1=k.psum[7][0:64, 0:64], op=ALU.add), wait=[kv])
            s0b_ev = k.op("vector", lambda e, h=h: e.tensor_copy(
                out=t["s0b"][:, 64 * h:64 * h + 64],
                in_=t["s0f"][:, 64 * h:64 * h + 64]))
            kv_free = s0b_ev
        yv = k.op("vector", lambda e, hc=hc, hp=hp, ob=ob: e.tensor_copy(
            out=t["mixy"][hc][64 * hp:64 * hp + 64, 0:TB], in_=ob[0:64, 0:TB]),
            wait=[olast])
        ob_free[h % 2] = yv
        head_bufs_free = olast

    if DBG.get("l0cut", 99) <= 2:
        return
    sx = k.ev("V")
    d1 = k.dma("gpsimd", k.sag[f"l0i{r}"].ap()[:, :], t["s0f"][:, :],
               wait=[sx], inc="GD")
    ag = k.op("gpsimd",
              lambda e: e.collective_compute(
                  "AllGather", ALU.bypass, replica_groups=[list(range(NCORES))],
                  ins=[k.sag[f"l0i{r}"].ap().opt()],
                  outs=[k.sag[f"l0o{r}"].ap().opt()]),
              wait=[d1], inc="CC")
    k.op("vector", lambda e: e.tensor_copy(out=t["s0in"][:, :],
                                           in_=t["s0run"][:, :]),
         wait=[k.ev("T")])
    for j in range(NCORES):
        ld = k.dma("sync", t["xtm"][0:64, 0:D],
                   k.sag[f"l0o{r}"].ap()[64 * j:64 * j + 64, :],
                   wait=[ag, k.ev("V")])
        k.op("vector", lambda e: e.tensor_tensor(
            out=t["s0run"][:, :], in0=t["s0run"][:, :], in1=t["xtm"][0:64, 0:D],
            op=ALU.add), wait=[ld])
        k.op("vector", lambda e, j=j: e.scalar_tensor_tensor(
            out=t["s0in"][:, :], in0=t["xtm"][0:64, 0:D],
            scalar=t["pmask"][0:64, j:j + 1], in1=t["s0in"][:, :],
            op0=ALU.mult, op1=ALU.add))
    sbi = k.op("vector", lambda e: e.tensor_copy(out=t["s0inb"][:, :],
                                                 in_=t["s0in"][:, :]))
    corr_free = [k.ev("V"), k.ev("V")]
    for h in range(H0):
        hc, hp = h // 2, h % 2
        cpe = k.op("vector", lambda e, hc=hc, hp=hp: e.tensor_copy(
            out=t["rrh"][:, 0:TB], in_=t["rf"][hc][64 * hp:64 * hp + 64, 0:TB]),
            wait=[k.ev("T")])
        cm = k.mm(k.psum[2 + h % 2][0:64, 0:TB], t["s0inb"][:, 64 * h:64 * h + 64],
                  t["rrh"][:, 0:TB], True, True,
                  wait=[sbi, cpe, corr_free[h % 2]], inc="T")
        ya = k.op("vector", lambda e, hc=hc, hp=hp, h=h: e.tensor_tensor(
            out=t["mixy"][hc][64 * hp:64 * hp + 64, 0:TB],
            in0=t["mixy"][hc][64 * hp:64 * hp + 64, 0:TB],
            in1=k.psum[2 + h % 2][0:64, 0:TB], op=ALU.add), wait=[cm])
        corr_free[h % 2] = ya

    if DBG.get("l0cut", 99) <= 3:
        return
    st_free = [k.ev("V"), k.ev("V")]
    bc_free = [k.ev("V"), k.ev("V")]
    ya_prev = k.ev("S")
    for hc in range(16):
        y2 = k.op("vector", lambda e, hc=hc: e.tensor_tensor(
            out=t["xntm"][:, 0:TB], in0=t["mixy"][hc][:, 0:TB],
            in1=t["mixy"][hc][:, 0:TB], op=ALU.mult),
            wait=[k.ev("T"), ya_prev])
        s1 = k.mm(k.psum[6][0:2, 0:TB], t["aux_bf"][:, 0:2],
                  t["mixy"][hc][:, 0:TB], True, True,
                  wait=[k.ev("V"), st_free[0]], inc="T")
        s2 = k.mm(k.psum[7][0:2, 0:TB], t["aux_bf"][:, 0:2],
                  t["xntm"][:, 0:TB], True, True,
                  wait=[y2, st_free[1]], inc="T")
        k.op("vector", lambda e: e.tensor_scalar(
            out=t["tB"][0:2, 0:TB], in0=k.psum[6][0:2, 0:TB],
            scalar1=1.0 / 64.0, scalar2=None, op0=ALU.mult), wait=[s1])
        st_free[0] = k.ev("V")
        k.op("vector", lambda e: e.tensor_scalar(
            out=t["tC"][0:2, 0:TB], in0=k.psum[7][0:2, 0:TB],
            scalar1=1.0 / 64.0, scalar2=None, op0=ALU.mult),
            wait=[s2, ya_prev])
        st_free[1] = k.ev("V")
        k.op("vector", lambda e: e.tensor_tensor(
            out=t["stg2"][0:2, 0:TB], in0=t["tB"][0:2, 0:TB],
            in1=t["tB"][0:2, 0:TB], op=ALU.mult))
        vv = k.op("vector", lambda e: e.tensor_tensor(
            out=t["stg2"][0:2, 0:TB], in0=t["tC"][0:2, 0:TB],
            in1=t["stg2"][0:2, 0:TB], op=ALU.subtract))
        k.op("scalar", lambda e: e.activation(
            out=t["stg2"][0:2, 0:TB], in_=t["stg2"][0:2, 0:TB], func=AF.Ln,
            bias=0.00064), wait=[vv])
        rse = k.op("scalar", lambda e: e.activation(
            out=t["stg2"][0:2, 0:TB], in_=t["stg2"][0:2, 0:TB], func=AF.Exp,
            scale=-0.5))
        k.op("vector", lambda e: e.tensor_tensor(
            out=t["tB"][0:2, 0:TB], in0=t["tB"][0:2, 0:TB],
            in1=t["stg2"][0:2, 0:TB], op=ALU.mult), wait=[rse])
        k.op("vector", lambda e: e.tensor_scalar(
            out=t["tB"][0:2, 0:TB], in0=t["tB"][0:2, 0:TB],
            scalar1=-1.0, scalar2=None, op0=ALU.mult))
        k.op("vector", lambda e: e.tensor_copy(out=t["gnhl"][:, 0:TB],
                                               in_=t["stg2"][0:2, 0:TB]))
        k.op("vector", lambda e: e.tensor_copy(out=t["tC"][0:2, 0:TB],
                                               in_=t["gnhl"][:, 0:TB]))
        k.op("vector", lambda e: e.tensor_tensor(
            out=t["tC"][0:2, 0:TB], in0=t["stg2"][0:2, 0:TB],
            in1=t["tC"][0:2, 0:TB], op=ALU.subtract))
        k.op("vector", lambda e: e.tensor_copy(out=t["gnhl"][:, TB:2 * TB],
                                               in_=t["tC"][0:2, 0:TB]))
        k.op("vector", lambda e: e.tensor_copy(out=t["gnhl"][:, 2 * TB:3 * TB],
                                               in_=t["tB"][0:2, 0:TB]))
        k.op("vector", lambda e: e.tensor_copy(out=t["tC"][0:2, 0:TB],
                                               in_=t["gnhl"][:, 2 * TB:3 * TB]))
        k.op("vector", lambda e: e.tensor_tensor(
            out=t["tC"][0:2, 0:TB], in0=t["tB"][0:2, 0:TB],
            in1=t["tC"][0:2, 0:TB], op=ALU.subtract))
        hl = k.op("vector", lambda e: e.tensor_copy(out=t["gnhl"][:, 3 * TB:4 * TB],
                                                    in_=t["tC"][0:2, 0:TB]))
        k.mm(k.psum[2][:, 0:TB], t["e2"][0:2, :], t["gnhl"][:, 0:TB],
             True, False, wait=[hl, bc_free[0]])
        b1 = k.mm(k.psum[2][:, 0:TB], t["e2"][0:2, :], t["gnhl"][:, TB:2 * TB],
                  False, True, inc="T")
        k.mm(k.psum[3][:, 0:TB], t["e2"][0:2, :], t["gnhl"][:, 2 * TB:3 * TB],
             True, False, wait=[bc_free[1]])
        b2 = k.mm(k.psum[3][:, 0:TB], t["e2"][0:2, :], t["gnhl"][:, 3 * TB:4 * TB],
                  False, True, inc="T")
        k.op("vector", lambda e, hc=hc: e.tensor_tensor(
            out=t["tC"][:, 0:TB], in0=t["mixy"][hc][:, 0:TB],
            in1=k.psum[2][:, 0:TB], op=ALU.mult), wait=[b1, b2])
        bc_free[0] = k.ev("V")
        tt2 = k.op("vector", lambda e: e.tensor_tensor(
            out=t["tC"][:, 0:TB], in0=t["tC"][:, 0:TB],
            in1=k.psum[3][:, 0:TB], op=ALU.add))
        bc_free[1] = tt2
        ya = k.op("scalar", lambda e, hc=hc: e.activation(
            out=t["tB"][:, 0:TB], in_=t["tC"][:, 0:TB], func=AF.Identity,
            scale=t["smallp"][:, 32 + hc:33 + hc],
            bias=t["smallp"][:, 48 + hc:49 + hc]), wait=[tt2])
        ya_prev = ya
        k.op("vector", lambda e, hc=hc: e.tensor_tensor(
            out=t["mixy"][hc][:, 0:TB], in0=t["tB"][:, 0:TB],
            in1=t["gsl"][hc][:, 0:TB], op=ALU.mult), wait=[ya])

    if DBG.get("l0cut", 99) <= 4:
        return
    if DBG.get("wobar", 0):
        barrier(k)
    k.slab_free = k.ev("T")
    k.bank_ev = [k.ev("V")] * 4
    stg_free = [k.ev("GD"), k.ev("GD")]

    def cb_wo(oc, bank, mmev):
        sb = t["stg"] if oc % 2 == 0 else t["stg2"]
        cpv = k.op("vector", lambda e, bank=bank, sb=sb: e.tensor_copy(
            out=sb[:, 0:TB], in_=bank[:, 0:TB]),
            wait=[mmev, stg_free[oc % 2]])
        k.bank_ev[oc % 4] = cpv
        stg_free[oc % 2] = k.dma("gpsimd",
                                 k.sscr.ap()[oc, :, :], sb[:, 0:TB],
                                 wait=[cpv], inc="GD")
    proj2048(k, "l0", 4, t["mixy"], cb_wo, rhs_ev=k.ev("V"))
    k.s_done = k.ev("GD")



def dslab(k, kk):
    t = k.tensors
    if kk < 32:
        return t["wsl"][kk // 2][:, 256 * (kk % 2):256 * (kk % 2) + 256]
    return t["wsl2"][(kk - 32) // 2][:, 256 * (kk % 2):256 * (kk % 2) + 256]


def l0_ffn(k, r, xn_ev):
    t = k.tensors
    war0 = [k.ev("T"), k.ev("GD")]
    il_ev = None
    for kb in range(8):
        for tc in range(TB // 128):
            csl = slice(1 + 128 * tc, 129 + 128 * tc)
            k.tr(k.psum[4][:, 0:128], t["xn"][2 * kb][:, csl],
                 t["msk_bf"][:, 0:128], wait=[k.tp_free[0]])
            t1 = k.tr(k.psum[4][:, 128:256], t["xn"][2 * kb + 1][:, csl],
                      t["msk_bf"][:, 0:128], inc="T")
            tm = k.op("vector", lambda e: e.tensor_copy(
                out=t["xntm"][:, 0:256], in_=k.psum[4][:, 0:256]),
                wait=[t1] + (war0 if kb == 0 and tc == 0 else []))
            k.tp_free[0] = tm
            k.tr(k.psum[5][:, 0:128], t["xntm"][:, 0:256:2],
                 t["msk_bf"][:, 0:128], wait=[tm, k.tp_free[1]])
            t2 = k.tr(k.psum[5][:, 128:256], t["xntm"][:, 1:256:2],
                      t["msk_bf"][:, 0:128], inc="T")
            k.op("vector", lambda e, kb=kb, tc=tc: e.tensor_copy(
                out=t["xnil"][2 * kb][:, 128 * tc:128 * tc + 128],
                in_=k.psum[5][:, 0:128]), wait=[t2])
            il_ev = k.op("vector", lambda e, kb=kb, tc=tc: e.tensor_copy(
                out=t["xnil"][2 * kb + 1][:, 128 * tc:128 * tc + 128],
                in_=k.psum[5][:, 128:256]))
            k.tp_free[1] = il_ev

    k.slab_free = k.ev("T")
    bank_ev = [k.ev("V")] * 4
    m3g = None
    for jg in range(22):
        evg, evu = [], []
        for kk in range(16):
            kb, par = kk // 2, kk % 2
            seg = "gul" if par == 0 else "guh"
            if m3g is None:
                w8 = [k.slab_free] if kk == 0 else ()
                w8u = [k.slab_free] if kk == 0 else ()
            else:
                w8 = [m3g[0][kk]]
                w8u = [m3g[1][kk]]
            evg.append(qslab(k, t["wsl"][kk], seg, 128 * kb, 256 * jg, 256,
                             wait=w8))
            evu.append(qslab(k, t["wsl2"][kk], seg, 128 * kb, I0 + 256 * jg,
                             256, wait=w8u, eng="gpsimd", inc="GD"))
        evall = [evg[-1], evu[-1]]
        new_m3 = ([None] * 16, [None] * 16)
        last = None
        for jj in range(2):
            for kk in range(16):
                w = list(evall) if kk == 0 else []
                if kk == 0:
                    w += [bank_ev[jj], il_ev]
                ev = k.mm(k.psum[jj][:, 0:TB],
                          t["wsl"][kk][:, 128 * jj:128 * jj + 128],
                          t["xnil"][kk][:, 0:TB], start=(kk == 0),
                          stop=(kk == 15), wait=w,
                          inc="T" if (kk == 15 or jj == 1) else None)
                if jj == 1:
                    new_m3[0][kk] = ev
            for kk in range(16):
                w = list(evall) if kk == 0 else []
                if kk == 0:
                    w.append(bank_ev[2 + jj])
                last = k.mm(k.psum[2 + jj][:, 0:TB],
                            t["wsl2"][kk][:, 128 * jj:128 * jj + 128],
                            t["xnil"][kk][:, 0:TB],
                            start=(kk == 0), stop=(kk == 15),
                            wait=w, inc="T" if (kk == 15 or jj == 1) else None)
                if jj == 1:
                    new_m3[1][kk] = last
            sg = k.op("scalar", lambda e, jj=jj: e.activation(
                out=t["xntm"][:, 0:TB], in_=k.psum[jj][:, 0:TB], func=AF.Silu),
                wait=[last, k.ev("V")])
            bank_ev[jj] = sg
            hv = k.op("vector", lambda e, jg=jg, jj=jj: e.tensor_tensor(
                out=t["hh"][2 * jg + jj][:, 0:TB], in0=t["xntm"][:, 0:TB],
                in1=k.psum[2 + jj][:, 0:TB], op=ALU.mult), wait=[sg])
            bank_ev[2 + jj] = hv
        m3g = new_m3
        k.slab_free = last

    hh_ev = k.ev("V")
    k.slab_free = k.ev("T")
    bank_ev = [k.ev("V")] * 2
    m3d = None
    for og in range(8):
        evs = []
        for kk in range(44):
            kb, par = kk // 2, kk % 2
            seg = "dnl" if par == 0 else "dnh"
            if m3d is None:
                w8 = [k.slab_free] if kk <= 1 else ()
            else:
                w8 = [m3d[kk]]
            evs.append(qslab(k, dslab(k, kk), seg, 128 * kb, 256 * og, 256,
                             wait=w8, eng="gpsimd" if kk % 2 else "sync",
                             inc="GD" if kk % 2 else "LW"))
        evall = [evs[-2], evs[-1]]
        new_m3 = [None] * 44
        last = None
        for m in range(2):
            oc = 2 * og + m
            for kk in range(44):
                w = list(evall) if kk == 0 else []
                if kk == 0:
                    w += [bank_ev[m], hh_ev]
                last = k.mm(k.psum[m][:, 0:TB],
                            dslab(k, kk)[:, 128 * m:128 * m + 128],
                            t["hh"][kk][:, 0:TB], start=(kk == 0),
                            stop=(kk == 43), wait=w,
                            inc="T" if (kk == 43 or m == 1) else None)
                if m == 1:
                    new_m3[kk] = last
            ev = k.op("vector", lambda e, oc=oc, m=m: e.tensor_copy(
                out=t["dd"][oc][:, 0:TB], in_=k.psum[m][:, 0:TB]), wait=[last])
            bank_ev[m] = ev
        m3d = new_m3
        k.slab_free = last

    for oc in range(16):
        y2 = k.op("vector", lambda e, oc=oc: e.tensor_tensor(
            out=t["xntm"][:, 0:TB], in0=t["dd"][oc][:, 0:TB],
            in1=t["dd"][oc][:, 0:TB], op=ALU.mult), wait=[k.ev("T")])
        k.mm(k.psum[6][0:1, 0:TB], t["aux_bf"][:, 2:3], t["xntm"][:, 0:TB],
             start=(oc == 0), stop=(oc == 15), wait=[y2], inc="T")
    sq = k.ev("T")
    k.op("scalar", lambda e: e.activation(
        out=t["tB"][0:1, 0:TB], in_=k.psum[6][0:1, 0:TB], func=AF.Ln,
        scale=1.0 / 2048.0, bias=1e-6), wait=[sq])
    rse = k.op("scalar", lambda e: e.activation(
        out=t["tB"][0:1, 0:TB], in_=t["tB"][0:1, 0:TB], func=AF.Exp, scale=-0.5))
    k.op("vector", lambda e: e.tensor_copy(out=t["rmshl"][:, 0:TB],
                                           in_=t["tB"][0:1, 0:TB]), wait=[rse])
    k.op("vector", lambda e: e.tensor_copy(out=t["tC"][0:1, 0:TB],
                                           in_=t["rmshl"][:, 0:TB]))
    k.op("vector", lambda e: e.tensor_tensor(
        out=t["tC"][0:1, 0:TB], in0=t["tB"][0:1, 0:TB], in1=t["tC"][0:1, 0:TB],
        op=ALU.subtract))
    hl = k.op("vector", lambda e: e.tensor_copy(out=t["rmshl"][:, TB:2 * TB],
                                                in_=t["tC"][0:1, 0:TB]))
    k.mm(k.psum[2][:, 0:TB], t["ones1"][0:1, :], t["rmshl"][:, 0:TB],
         True, False, wait=[hl, k.ev("V")])
    b1 = k.mm(k.psum[2][:, 0:TB], t["ones1"][0:1, :], t["rmshl"][:, TB:2 * TB],
              False, True, inc="T")
    for oc in range(16):
        k.op("vector", lambda e, oc=oc: e.tensor_tensor(
            out=t["dd"][oc][:, 0:TB], in0=t["dd"][oc][:, 0:TB],
            in1=k.psum[2][:, 0:TB], op=ALU.mult), wait=[b1])
        k.op("scalar", lambda e, oc=oc: e.activation(
            out=t["dd"][oc][:, 0:TB], in_=t["dd"][oc][:, 0:TB], func=AF.Silu),
            wait=[k.ev("V")])
    k.g1_ev = k.ev("S")



def combine0(k, r):
    t, din = k.tensors, k.din
    stg_free = k.ev("V")
    for oc in range(16):
        ld = k.dma("sync", t["stg"][:, 0:TB], k.sscr.ap()[oc, :, :],
                   wait=[k.s_done if oc == 0 else stg_free, k.g1_ev])
        sg = k.op("vector", lambda e, oc=oc: e.tensor_tensor(
            out=t["dd"][oc][:, 0:TB], in0=t["stg"][:, 0:TB],
            in1=t["dd"][oc][:, 0:TB], op=ALU.mult), wait=[ld])
        stg_free = sg
    sg_ev = k.ev("V")
    tp_free = k.ev("V")
    for tc in range(TB // 128):
        ld = k.dma("sync", t["cxa"][:, :],
                   din[f"x{r}"][1 + 128 * tc:129 + 128 * tc, :],
                   wait=[k.ev("V"), k.ev("S")])
        tl = None
        for oc in range(16):
            b = oc // 4
            w = [sg_ev, tp_free] if oc == 0 else ()
            tl = k.tr(k.psum[b][:, 128 * (oc % 4):128 * (oc % 4) + 128],
                      t["dd"][oc][:, 128 * tc:128 * tc + 128],
                      t["ident_f32"][:, :], wait=w,
                      inc="T" if oc % 4 == 3 else None)
        for b in range(4):
            k.op("vector", lambda e, b=b: e.tensor_tensor(
                out=t["cxb"][:, 512 * b:512 * b + 512],
                in0=t["cxa"][:, 512 * b:512 * b + 512],
                in1=k.psum[b][:, 0:512], op=ALU.add),
                wait=[ld, tl, k.ev("GD")] if b == 0 else ())
        tp_free = k.ev("V")
        x1e = k.ev("V")
        k.dma("gpsimd", k.x1scr.ap()[128 * tc:128 * tc + 128, :],
              t["cxb"][:, :], wait=[x1e], inc="GD")
        ln_sbuf_chunk(k, t["cxb"], tc, eps_reg=1e-5)
    k.xn1_ev = k.ev("V")


def ln_sbuf_chunk(k, src, tc, eps_reg):
    t = k.tensors
    k.op("scalar", lambda e: e.activation(out=t["csq"][:, :], in_=src[:, :],
                                          func=AF.Square), wait=[k.ev("V")])
    sq = k.ev("S")
    k.op("vector", lambda e: e.reduce_sum(out=t["hbuf"][:, 0:1], in_=src[:, :],
                                          axis=AX))
    k.op("vector", lambda e: e.reduce_sum(out=t["hbuf"][:, 1:2], in_=t["csq"][:, :],
                                          axis=AX), wait=[sq])
    k.op("vector", lambda e: e.tensor_scalar(out=t["hbuf"][:, 2:3],
                                             in0=t["hbuf"][:, 0:1],
                                             scalar1=1.0 / 2048.0, scalar2=None,
                                             op0=ALU.mult))
    k.op("vector", lambda e: e.tensor_scalar(out=t["hbuf"][:, 3:4],
                                             in0=t["hbuf"][:, 1:2],
                                             scalar1=1.0 / 2048.0, scalar2=None,
                                             op0=ALU.mult))
    k.op("vector", lambda e: e.tensor_tensor(out=t["hbuf"][:, 4:5],
                                             in0=t["hbuf"][:, 2:3],
                                             in1=t["hbuf"][:, 2:3], op=ALU.mult))
    vv = k.op("vector", lambda e: e.tensor_tensor(out=t["hbuf"][:, 4:5],
                                                  in0=t["hbuf"][:, 3:4],
                                                  in1=t["hbuf"][:, 4:5],
                                                  op=ALU.subtract))
    k.op("scalar", lambda e: e.activation(out=t["hbuf"][:, 5:6],
                                          in_=t["hbuf"][:, 4:5], func=AF.Ln,
                                          bias=eps_reg), wait=[vv])
    k.op("scalar", lambda e: e.activation(out=t["hbuf"][:, 5:6],
                                          in_=t["hbuf"][:, 5:6], func=AF.Exp,
                                          scale=-0.5))
    sr = k.ev("S")
    k.op("vector", lambda e: e.tensor_tensor(out=t["hbuf"][:, 6:7],
                                             in0=t["hbuf"][:, 2:3],
                                             in1=t["hbuf"][:, 5:6], op=ALU.mult),
         wait=[sr])
    nb = k.op("vector", lambda e: e.tensor_scalar(out=t["hbuf"][:, 6:7],
                                                  in0=t["hbuf"][:, 6:7],
                                                  scalar1=-1.0, scalar2=None,
                                                  op0=ALU.mult))
    xe = k.op("scalar", lambda e: e.activation(out=t["xntm"][:, 0:D],
                                               in_=src[:, :], func=AF.Identity,
                                               scale=t["hbuf"][:, 5:6],
                                               bias=t["hbuf"][:, 6:7]),
              wait=[nb, k.ev("T")])

    def dst(fc, psrc, w):
        return k.op("vector",
                    lambda e, fc=fc: e.tensor_copy(
                        out=k.tensors["xn"][fc][:, 1 + 128 * tc:129 + 128 * tc],
                        in_=psrc), wait=w)
    transpose_to_fm(k, xe, lambda fc: t["xntm"][:, 128 * fc:128 * fc + 128], dst)





def l1_attn(k, r):
    t = k.tensors
    xn1 = k.xn1_ev
    rhs = [t["xn"][kk][:, 1:1 + TB] for kk in range(16)]
    k.slab_free = k.ev("T")
    k.bank_ev = [k.ev("V")] * 4

    def cb_q(oc, bank, mmev):
        ev = k.op("scalar", lambda e, oc=oc, bank=bank: e.activation(
            out=t["qt"][oc][:, 0:TB], in_=bank[:, 0:TB], func=AF.Relu,
            scale=SCL1), wait=[mmev])
        k.bank_ev[oc % 4] = ev
    proj2048(k, "l1", 0, rhs, cb_q, rhs_ev=xn1)

    qt_ev = k.ev("S")

    def cb_k(oc, bank, mmev):
        h = oc
        sg = k.op("scalar", lambda e, bank=bank: e.activation(
            out=t["sgb"][:, 0:TB], in_=bank[:, 0:TB], func=AF.Sigmoid),
            wait=[mmev, k.ev("V")])
        k.bank_ev[oc % 4] = sg
        gb = k.op("scalar", lambda e: e.activation(
            out=t["gbv"][:, 0:TB], in_=t["sgb"][:, 0:TB], func=AF.Ln,
            scale=-1.0, bias=1.0))
        sc = None
        for c in range(TB // 64):
            sc = k.op("vector", lambda e, c=c: e.tensor_tensor_scan(
                out=t["abuf"][:, 64 * c:64 * c + 64],
                data0=t["gbv"][:, 64 * c:64 * c + 64],
                data1=t["gbv"][:, 64 * c:64 * c + 64],
                initial=0.0, op0=ALU.add, op1=ALU.bypass),
                wait=[gb] if c == 0 else ())
        e1 = k.op("scalar", lambda e: e.activation(
            out=t["tB"][:, 0:TB], in_=t["abuf"][:, 0:TB], func=AF.Exp),
            wait=[sc])
        k.op("vector", lambda e, h=h: e.tensor_tensor(
            out=t["qt"][h][:, 0:TB], in0=t["qt"][h][:, 0:TB],
            in1=t["tB"][:, 0:TB], op=ALU.mult), wait=[e1, qt_ev])
        ed = k.op("scalar", lambda e: e.activation(
            out=t["tB"][:, 0:TB], in_=t["abuf"][:, 0:TB], func=AF.Exp,
            scale=-1.0), wait=[k.ev("V")])
        k.op("vector", lambda e, h=h: e.tensor_tensor(
            out=t["kd"][h][:, 0:TB], in0=t["sgb"][:, 0:TB],
            in1=t["tB"][:, 0:TB], op=ALU.mult), wait=[ed])
        e3 = None
        for c in range(TB // 64):
            e3 = k.op("scalar", lambda e, c=c: e.activation(
                out=t["tB"][:, 64 * c:64 * c + 64],
                in_=t["abuf"][:, 64 * c:64 * c + 64], func=AF.Exp, scale=-1.0,
                bias=t["abuf"][:, 64 * c + 63:64 * c + 64]),
                wait=[k.ev("V")] if c == 0 else ())
        k.op("vector", lambda e, h=h: e.tensor_tensor(
            out=t["ks"][h][:, 0:TB], in0=t["sgb"][:, 0:TB],
            in1=t["tB"][:, 0:TB], op=ALU.mult), wait=[e3])
        k.op("vector", lambda e, h=h: e.tensor_copy(
            out=t["lgpre"][:, 8 * h:8 * h + 8],
            in_=t["abuf"][:, 63:TB:64]))
    proj2048(k, "l1", 1, rhs, cb_k, rhs_ev=xn1)

    m3v = None
    for mg in range(4):
        evs = []
        for kk in range(16):
            if m3v is None:
                w8 = [k.slab_free] if kk <= 1 else ()
            else:
                w8 = [m3v[kk]]
            evs.append(attn_slab(k, t["wsl"][kk], "l1", 2, kk, 512 * mg, 512,
                                 wait=w8, eng="gpsimd" if kk % 2 else "sync",
                                 inc="GD" if kk % 2 else "LW"))
        evall = [evs[-2], evs[-1]]
        new_m3 = [None] * 16
        last = None
        for tc in range(TB // 128):
            bank = k.psum[tc]
            for kk in range(16):
                w = list(evall) if kk == 0 else []
                if kk == 0:
                    w += [k.bank_ev[tc], xn1]
                last = k.mm(bank[:, 0:512], rhs[kk][:, 128 * tc:128 * tc + 128],
                            t["wsl"][kk][:, 0:512],
                            start=(kk == 0), stop=(kk == 15), wait=w,
                            inc="T" if (kk == 15 or tc == 3) else None)
                if tc == 3:
                    new_m3[kk] = last
            ev = k.op("vector", lambda e, tc=tc, mg=mg, bank=bank: e.tensor_copy(
                out=t["vt"][tc][:, 512 * mg:512 * mg + 512], in_=bank[:, 0:512]),
                wait=[last])
            k.bank_ev[tc] = ev
        m3v = new_m3
        k.slab_free = last

    k.op("vector", lambda e: e.memzero(t["s1f"][:, :]))
    z1 = k.op("vector", lambda e: e.memzero(t["s1b"][:, :]))
    s1b_ev = z1
    att_free = k.ev("V")
    ob_free = [k.ev("V"), k.ev("V")]
    kv_free = k.ev("V")
    kst_free = k.ev("V")
    k.attsb_free = k.ev("T")
    k.q2b_free = k.ev("T")
    k.ks2b_free = k.ev("T")
    k.kst_rd = k.ev("T")
    NP = TB // 128
    prep_ev = k.ev("V")
    for h in range(H1):
        olast = None
        ob = k.psum[h % 2]
        for p in range(NP):
            c0 = slice(128 * p, 128 * p + 64)
            c1 = slice(128 * p + 64, 128 * p + 128)
            pr = slice(128 * p, 128 * p + 128)
            d0 = k.op("scalar", lambda e, h=h, p=p: e.activation(
                out=t["hbuf"][:, 8:9],
                in_=t["lgpre"][:, 8 * h + 2 * p:8 * h + 2 * p + 1],
                func=AF.Exp), wait=[prep_ev, k.ev("V")])
            d1 = k.op("scalar", lambda e, h=h, p=p: e.activation(
                out=t["hbuf"][:, 9:10],
                in_=t["lgpre"][:, 8 * h + 2 * p + 1:8 * h + 2 * p + 2],
                func=AF.Exp))
            k.op("vector", lambda e, h=h, p=p: e.tensor_tensor(
                out=t["hbuf"][:, 10:11],
                in0=t["lgpre"][:, 8 * h + 2 * p:8 * h + 2 * p + 1],
                in1=t["lgpre"][:, 8 * h + 2 * p + 1:8 * h + 2 * p + 2],
                op=ALU.add))
            dp = k.op("scalar", lambda e: e.activation(
                out=t["hbuf"][:, 11:12], in_=t["hbuf"][:, 10:11], func=AF.Exp),
                wait=[k.ev("V")])
            k.op("vector", lambda e, h=h, p=p: e.tensor_copy(
                out=t["q2b"][:, 0:64], in_=t["qt"][h][:, c0]),
                wait=[k.q2b_free])
            q2 = k.op("scalar", lambda e, h=h, p=p: e.activation(
                out=t["q2b"][:, 64:128], in_=t["qt"][h][:, c1],
                func=AF.Identity, scale=t["hbuf"][:, 8:9]), wait=[d0, k.ev("V")])
            k.op("vector", lambda e, h=h, p=p: e.tensor_copy(
                out=t["ks2b"][:, 64:128], in_=t["ks"][h][:, c1]),
                wait=[k.ks2b_free])
            ks2 = k.op("scalar", lambda e, h=h, p=p: e.activation(
                out=t["ks2b"][:, 0:64], in_=t["ks"][h][:, c0],
                func=AF.Identity, scale=t["hbuf"][:, 9:10]), wait=[d1, k.ev("V")])
            tke = k.tr(k.psum[4][:, 0:128], t["ks2b"][:, :], t["msk_bf"][:, 0:128],
                       wait=[ks2, kst_free], inc="T")
            k.ks2b_free = tke
            kste = k.op("vector", lambda e: e.tensor_copy(
                out=t["kst"][:, :], in_=k.psum[4][:, 0:128]),
                wait=[tke, k.kst_rd])
            kst_free = kste
            k.mm(k.psum[6][0:64, 0:64],
                 t["kd"][h][:, c0], t["qt"][h][:, c0], True, True,
                 wait=[att_free])
            k.mm(k.psum[6][0:64, 64:128], t["ks"][h][:, c0], t["qt"][h][:, c1],
                 True, True)
            a3 = k.mm(k.psum[7][0:64, 64:128], t["kd"][h][:, c1], t["qt"][h][:, c1],
                      True, True, inc="T")
            k.op("vector", lambda e: e.tensor_tensor(
                out=t["attsb"][0:64, :], in0=k.psum[6][0:64, 0:128],
                in1=t["msk_bf"][0:64, 256:384], op=ALU.mult),
                wait=[a3, k.attsb_free])
            ae = k.op("vector", lambda e: e.tensor_tensor(
                out=t["attsb"][64:128, :], in0=k.psum[7][0:64, 0:128],
                in1=t["msk_bf"][64:128, 256:384], op=ALU.mult))
            att_free = ae
            w1 = [ae]
            if p == 0:
                w1.append(ob_free[h % 2])
            k.attsb_free = k.mm(ob[:, pr], t["vt"][p][:, 128 * h:128 * h + 128],
                                t["attsb"][:, :], True, False, wait=w1, inc="T")
            olast = k.mm(ob[:, pr], t["s1b"][:, 128 * h:128 * h + 128],
                         t["q2b"][:, :], False, True,
                         wait=[s1b_ev, q2], inc="T")
            kv = k.mm(k.psum[7][:, 128:256], t["kst"][:, :],
                      t["vt"][p][:, 128 * h:128 * h + 128], True, True,
                      wait=[kste, kv_free], inc="T")
            k.kst_rd = kv
            k.q2b_free = olast
            k.op("vector", lambda e, h=h: e.scalar_tensor_tensor(
                out=t["s1f"][:, 128 * h:128 * h + 128],
                in0=t["s1f"][:, 128 * h:128 * h + 128],
                scalar=t["hbuf"][:, 11:12], in1=k.psum[7][:, 128:256],
                op0=ALU.mult, op1=ALU.add), wait=[kv, dp])
            s1b_ev = k.op("vector", lambda e, h=h: e.tensor_copy(
                out=t["s1b"][:, 128 * h:128 * h + 128],
                in_=t["s1f"][:, 128 * h:128 * h + 128]))
            kv_free = s1b_ev
        ov = k.op("vector", lambda e, h=h, ob=ob: e.tensor_copy(
            out=t["gsl"][h][:, 0:TB], in_=ob[:, 0:TB]), wait=[olast])
        ob_free[h % 2] = ov
        prep_ev = k.ev("S")

    for h in range(H1):
        k.op("vector", lambda e, h=h: e.reduce_sum(
            out=t["s1lg"][:, h:h + 1], in_=t["lgpre"][:, 8 * h:8 * h + 8],
            axis=AX))
    de = k.op("scalar", lambda e: e.activation(
        out=t["hbuf"][:, 16:32], in_=t["s1lg"][:, 0:16], func=AF.Exp),
        wait=[k.ev("V")])
    sx = k.ev("V")
    k.dma("gpsimd", k.sag[f"l1i{r}"].ap()[:, 0:D], t["s1f"][:, :],
          wait=[sx], inc="GD")
    d2 = k.dma("gpsimd", k.sag[f"l1i{r}"].ap()[:, D:D + 16], t["hbuf"][:, 16:32],
               wait=[de], inc="GD")
    ag = k.op("gpsimd",
              lambda e: e.collective_compute(
                  "AllGather", ALU.bypass, replica_groups=[list(range(NCORES))],
                  ins=[k.sag[f"l1i{r}"].ap().opt()],
                  outs=[k.sag[f"l1o{r}"].ap().opt()]),
              wait=[d2], inc="CC")
    k.op("vector", lambda e: e.tensor_copy(out=t["s1in"][:, :],
                                           in_=t["s1run"][:, :]),
         wait=[k.ev("T")])
    for j in range(NCORES):
        ld = k.dma("sync", t["xtm"][:, 0:D],
                   k.sag[f"l1o{r}"].ap()[128 * j:128 * j + 128, 0:D],
                   wait=[ag, k.ev("V")])
        ld2 = k.dma("sync", t["tcol"][:, 0:16],
                    k.sag[f"l1o{r}"].ap()[128 * j:128 * j + 128, D:D + 16])
        for h in range(H1):
            k.op("vector", lambda e, h=h: e.scalar_tensor_tensor(
                out=t["s1run"][:, 128 * h:128 * h + 128],
                in0=t["s1run"][:, 128 * h:128 * h + 128],
                scalar=t["tcol"][:, h:h + 1],
                in1=t["xtm"][:, 128 * h:128 * h + 128],
                op0=ALU.mult, op1=ALU.add),
                wait=[ld, ld2] if h == 0 else ())
        k.op("vector", lambda e, j=j: e.scalar_tensor_tensor(
            out=t["xtm"][:, 0:D], in0=t["xtm"][:, 0:D],
            scalar=t["pmask"][:, j:j + 1], in1=t["xtm"][:, 0:D],
            op0=ALU.mult, op1=ALU.bypass))
        k.op("scalar", lambda e, j=j: e.activation(
            out=t["tcol"][:, 16:32], in_=t["onesf"][:, 0:16], func=AF.Identity,
            scale=t["tcol"][:, 32 + j:33 + j]), wait=[k.ev("V")])
        k.op("vector", lambda e, j=j: e.scalar_tensor_tensor(
            out=t["tcol"][:, 0:16], in0=t["tcol"][:, 0:16],
            scalar=t["pmask"][:, j:j + 1], in1=t["tcol"][:, 16:32],
            op0=ALU.mult, op1=ALU.add), wait=[k.ev("S")])
        for h in range(H1):
            k.op("vector", lambda e, h=h: e.scalar_tensor_tensor(
                out=t["s1in"][:, 128 * h:128 * h + 128],
                in0=t["s1in"][:, 128 * h:128 * h + 128],
                scalar=t["tcol"][:, h:h + 1],
                in1=t["xtm"][:, 128 * h:128 * h + 128],
                op0=ALU.mult, op1=ALU.add))
    sbi = k.op("vector", lambda e: e.tensor_copy(out=t["s1inb"][:, :],
                                                 in_=t["s1in"][:, :]))

    corr_free = [k.ev("V"), k.ev("V")]
    for h in range(H1):
        pf = k.op("vector", lambda e, h=h: e.tensor_tensor_scan(
            out=t["tD"][:, 0:8], data0=t["lgpre"][:, 8 * h:8 * h + 8],
            data1=t["lgpre"][:, 8 * h:8 * h + 8], initial=0.0,
            op0=ALU.add, op1=ALU.bypass))
        pe = k.op("scalar", lambda e: e.activation(
            out=t["tD"][:, 8:16], in_=t["tD"][:, 0:8], func=AF.Exp), wait=[pf])
        k.op("vector", lambda e, h=h: e.tensor_copy(
            out=t["mixy"][h][:, 0:64], in_=t["qt"][h][:, 0:64]))
        qg = None
        for c in range(1, TB // 64):
            qg = k.op("scalar", lambda e, h=h, c=c: e.activation(
                out=t["mixy"][h][:, 64 * c:64 * c + 64],
                in_=t["qt"][h][:, 64 * c:64 * c + 64], func=AF.Identity,
                scale=t["tD"][:, 7 + c:8 + c]), wait=[pe] if c == 1 else ())
        cm = k.mm(k.psum[2 + h % 2][:, 0:TB], t["s1inb"][:, 128 * h:128 * h + 128],
                  t["mixy"][h][:, 0:TB], True, True,
                  wait=[sbi, qg, k.ev("V"), corr_free[h % 2]], inc="T")
        ya = k.op("vector", lambda e, h=h: e.tensor_tensor(
            out=t["gsl"][h][:, 0:TB], in0=t["gsl"][h][:, 0:TB],
            in1=k.psum[2 + h % 2][:, 0:TB], op=ALU.add), wait=[cm])
        corr_free[h % 2] = ya

    for h in range(H1):
        y2 = k.op("vector", lambda e, h=h: e.tensor_tensor(
            out=t["xntm"][:, 0:TB], in0=t["gsl"][h][:, 0:TB],
            in1=t["gsl"][h][:, 0:TB], op=ALU.mult), wait=[k.ev("T")])
        k.mm(k.psum[6][0:1, 0:TB], t["aux_bf"][:, 2:3], t["xntm"][:, 0:TB],
             start=(h == 0), stop=(h == 15), wait=[y2], inc="T")
    sq = k.ev("T")
    k.op("scalar", lambda e: e.activation(
        out=t["tB"][0:1, 0:TB], in_=k.psum[6][0:1, 0:TB], func=AF.Ln,
        scale=1.0 / 2048.0, bias=1e-5), wait=[sq])
    rse = k.op("scalar", lambda e: e.activation(
        out=t["tB"][0:1, 0:TB], in_=t["tB"][0:1, 0:TB], func=AF.Exp, scale=-0.5))
    k.op("vector", lambda e: e.tensor_copy(out=t["rmshl"][:, 0:TB],
                                           in_=t["tB"][0:1, 0:TB]), wait=[rse])
    k.op("vector", lambda e: e.tensor_copy(out=t["tC"][0:1, 0:TB],
                                           in_=t["rmshl"][:, 0:TB]))
    k.op("vector", lambda e: e.tensor_tensor(
        out=t["tC"][0:1, 0:TB], in0=t["tB"][0:1, 0:TB], in1=t["tC"][0:1, 0:TB],
        op=ALU.subtract))
    hl = k.op("vector", lambda e: e.tensor_copy(out=t["rmshl"][:, TB:2 * TB],
                                                in_=t["tC"][0:1, 0:TB]))
    k.mm(k.psum[2][:, 0:TB], t["ones1"][0:1, :], t["rmshl"][:, 0:TB],
         True, False, wait=[hl, k.ev("V")])
    b1 = k.mm(k.psum[2][:, 0:TB], t["ones1"][0:1, :], t["rmshl"][:, TB:2 * TB],
              False, True, inc="T")
    act_prev = k.ev("S")
    for h in range(H1):
        k.op("vector", lambda e, h=h: e.tensor_tensor(
            out=t["tC"][:, 0:TB], in0=t["gsl"][h][:, 0:TB],
            in1=k.psum[2][:, 0:TB], op=ALU.mult), wait=[b1, act_prev])
        act_prev = k.op("scalar", lambda e, h=h: e.activation(
            out=t["mixy"][h][:, 0:TB], in_=t["tC"][:, 0:TB], func=AF.Identity,
            scale=t["smallp"][:, 64 + h:65 + h]), wait=[k.ev("V")])

    k.slab_free = k.ev("T")
    k.bank_ev = [k.ev("V")] * 4
    og_ev = k.ev("S")
    stg_free = [k.ev("GD"), k.ev("GD")]

    def cb_wo(oc, bank, mmev):
        sb = t["stg"] if oc % 2 == 0 else t["stg2"]
        cpv = k.op("vector", lambda e, bank=bank, sb=sb: e.tensor_copy(
            out=sb[:, 0:TB], in_=bank[:, 0:TB]),
            wait=[mmev, stg_free[oc % 2]])
        k.bank_ev[oc % 4] = cpv
        stg_free[oc % 2] = k.dma("gpsimd",
                                 k.sscr.ap()[oc, :, :], sb[:, 0:TB],
                                 wait=[cpv], inc="GD")
    proj2048(k, "l1", 3, t["mixy"], cb_wo, rhs_ev=og_ev)
    k.s_done = k.ev("GD")



def l1_ffn(k, r):
    t, din = k.tensors, k.din
    rhs = [t["xn"][kk][:, 1:1 + TB] for kk in range(16)]
    k.slab_free = k.ev("T")
    bank_ev = [k.ev("V")] * 4
    if "l1ffn" in DBG["stages"]:
        m3g = None
        for jg in range(22):
            ev1, ev2 = [], []
            for kk in range(16):
                if m3g is None:
                    w8 = [k.slab_free] if kk == 0 else ()
                    w8u = [k.slab_free] if kk == 0 else ()
                else:
                    w8 = [m3g[0][kk]]
                    w8u = [m3g[1][kk]]
                ev1.append(qslab(k, t["wsl"][kk], "w1q", 128 * kk, 256 * jg, 256,
                                 wait=w8))
                ev2.append(qslab(k, t["wsl2"][kk], "w2q", 128 * kk, 256 * jg,
                                 256, wait=w8u, eng="gpsimd", inc="GD"))
            evall = [ev1[-1], ev2[-1]]
            new_m3 = ([None] * 16, [None] * 16)
            last = None
            for jj in range(2):
                for kk in range(16):
                    w = list(evall) if kk == 0 else []
                    if kk == 0:
                        w += [bank_ev[jj], k.xn1_ev]
                    ev = k.mm(k.psum[jj][:, 0:TB],
                              t["wsl"][kk][:, 128 * jj:128 * jj + 128],
                              rhs[kk], start=(kk == 0), stop=(kk == 15),
                              wait=w, inc="T" if (kk == 15 or jj == 1) else None)
                    if jj == 1:
                        new_m3[0][kk] = ev
                for kk in range(16):
                    w = list(evall) if kk == 0 else []
                    if kk == 0:
                        w.append(bank_ev[2 + jj])
                    last = k.mm(k.psum[2 + jj][:, 0:TB],
                                t["wsl2"][kk][:, 128 * jj:128 * jj + 128],
                                rhs[kk], start=(kk == 0), stop=(kk == 15),
                                wait=w,
                                inc="T" if (kk == 15 or jj == 1) else None)
                    if jj == 1:
                        new_m3[1][kk] = last
                sg = k.op("scalar", lambda e, jj=jj: e.activation(
                    out=t["xntm"][:, 0:TB], in_=k.psum[jj][:, 0:TB],
                    func=AF.Sigmoid), wait=[last, k.ev("V")])
                bank_ev[jj] = sg
                hvar = DBG.get("hvar", 0)
                if hvar == 1:
                    hv = k.op("vector", lambda e, jg=jg, jj=jj: e.tensor_copy(
                        out=t["hid"][2 * jg + jj][:, 0:TB],
                        in_=k.psum[2 + jj][:, 0:TB]), wait=[sg])
                elif hvar == 2:
                    hv = k.op("vector", lambda e, jg=jg, jj=jj: e.tensor_copy(
                        out=t["hid"][2 * jg + jj][:, 0:TB],
                        in_=t["xntm"][:, 0:TB]), wait=[sg])
                else:
                    hv = k.op("vector", lambda e, jg=jg, jj=jj: e.tensor_tensor(
                        out=t["hid"][2 * jg + jj][:, 0:TB],
                        in0=t["xntm"][:, 0:TB],
                        in1=k.psum[2 + jj][:, 0:TB], op=ALU.mult), wait=[sg])
                bank_ev[2 + jj] = hv
            m3g = new_m3
            k.slab_free = last
            if DBG.get("l1cut", 0):
                return

        hid_ev = k.ev("V")
        k.slab_free = k.ev("T")
        bank_ev = [k.ev("V")] * 2
        m3d = None
        for og in range(8):
            evs = []
            for kk in range(44):
                if m3d is None:
                    w8 = [k.slab_free] if kk <= 1 else ()
                else:
                    w8 = [m3d[kk]]
                evs.append(qslab(k, dslab(k, kk), "w3q", 128 * kk, 256 * og,
                                 256, wait=w8,
                                 eng="gpsimd" if kk % 2 else "sync",
                                 inc="GD" if kk % 2 else "LW"))
            evall = [evs[-2], evs[-1]]
            new_m3 = [None] * 44
            last = None
            for m in range(2):
                oc = 2 * og + m
                for kk in range(44):
                    w = list(evall) if kk == 0 else []
                    if kk == 0:
                        w += [bank_ev[m], hid_ev]
                    last = k.mm(k.psum[m][:, 0:TB],
                                dslab(k, kk)[:, 128 * m:128 * m + 128],
                                t["hid"][kk][:, 0:TB], start=(kk == 0),
                                stop=(kk == 43), wait=w,
                                inc="T" if (kk == 43 or m == 1) else None)
                    if m == 1:
                        new_m3[kk] = last
                ev = k.op("vector", lambda e, oc=oc, m=m: e.tensor_copy(
                    out=t["fo"][oc][:, 0:TB], in_=k.psum[m][:, 0:TB]),
                    wait=[last])
                bank_ev[m] = ev
            m3d = new_m3
            k.slab_free = last
    else:
        for oc in range(16):
            k.op("vector", lambda e, oc=oc: e.memzero(t["fo"][oc][:, 0:TB]))

    stg_free = k.ev("V")
    for oc in range(16):
        sl = k.op("scalar", lambda e, oc=oc: e.activation(
            out=t["fo"][oc][:, 0:TB], in_=t["fo"][oc][:, 0:TB], func=AF.Silu),
            wait=[k.ev("V")])
        ld = k.dma("sync", t["stg"][:, 0:TB], k.sscr.ap()[oc, :, :],
                   wait=[k.s_done if oc == 0 else stg_free])
        sg = k.op("vector", lambda e, oc=oc: e.tensor_tensor(
            out=t["fo"][oc][:, 0:TB], in0=t["stg"][:, 0:TB],
            in1=t["fo"][oc][:, 0:TB], op=ALU.mult), wait=[ld, sl])
        stg_free = sg
    sg_ev = k.ev("V")
    tp_free = k.ev("V")
    for tc in range(TB // 128):
        ld = k.dma("sync", t["cxa"][:, :],
                   k.x1scr.ap()[128 * tc:128 * tc + 128, :],
                   wait=[k.ev("V"), k.ev("S")])
        tl = None
        for oc in range(16):
            b = oc // 4
            w = [sg_ev, tp_free] if oc == 0 else ()
            tl = k.tr(k.psum[b][:, 128 * (oc % 4):128 * (oc % 4) + 128],
                      t["fo"][oc][:, 128 * tc:128 * tc + 128],
                      t["ident_f32"][:, :], wait=w,
                      inc="T" if oc % 4 == 3 else None)
        oe = None
        for b in range(4):
            oe = k.op("vector", lambda e, b=b: e.tensor_tensor(
                out=t["cxb"][:, 512 * b:512 * b + 512],
                in0=t["cxa"][:, 512 * b:512 * b + 512],
                in1=k.psum[b][:, 0:512], op=ALU.add),
                wait=[ld, tl, k.ev("OUT")] if b == 0 else ())
        tp_free = oe
        k.dma("gpsimd", k.out_hbm[TB * r + 128 * tc: TB * r + 128 * tc + 128, :],
              t["cxb"][:, :], wait=[oe], inc="OUT")


def tile(k, r):
    t, din = k.tensors, k.din
    barrier(k)
    k.iota_ev = k.dma("sync", t["iota"][:, :], din[f"iota{r}"][:, :])
    k.slab_free = k.wag_done if r == 0 else k.ev("T")
    k.bank_ev = [k.ev("T")] * 4
    k.tp_free = [k.ev("V"), k.ev("V")]
    k.op("vector", lambda e: e.tensor_scalar(out=t["tcol"][:, 32:40],
                                             in0=t["pmask"][:, 0:8],
                                             scalar1=-1.0, scalar2=1.0,
                                             op0=ALU.mult, op1=ALU.add),
         wait=[k.ld_small])

    def write_xn(tc, xe):
        def dst(fc, psrc, w):
            return k.op("vector",
                        lambda e: e.tensor_copy(
                            out=t["xn"][fc][:, 1 + 128 * tc:129 + 128 * tc],
                            in_=psrc),
                        wait=w)
        transpose_to_fm(k, xe, lambda fc: t["xntm"][:, 128 * fc:128 * fc + 128],
                        dst)
    for tc in range(TB // 128):
        ln_row_chunk(k, r, tc, 1 + 128 * tc, write_xn)
    ld = k.dma("sync", t["xtm"][0:1, :], din[f"x{r}"][0:1, :],
               wait=[k.ev("S"), k.ev("V")])
    k.op("scalar", lambda e: e.activation(out=t["xsq"][0:1, :], in_=t["xtm"][0:1, :],
                                          func=AF.Square), wait=[ld])
    sq = k.ev("S")
    k.op("vector", lambda e: e.reduce_sum(out=t["hbuf"][0:1, 0:1],
                                          in_=t["xtm"][0:1, :], axis=AX), wait=[ld])
    k.op("vector", lambda e: e.reduce_sum(out=t["hbuf"][0:1, 1:2],
                                          in_=t["xsq"][0:1, :], axis=AX), wait=[sq])
    k.op("vector", lambda e: e.tensor_scalar(out=t["hbuf"][0:1, 2:3],
                                             in0=t["hbuf"][0:1, 0:1],
                                             scalar1=1.0 / 2048.0, scalar2=None,
                                             op0=ALU.mult))
    k.op("vector", lambda e: e.tensor_scalar(out=t["hbuf"][0:1, 3:4],
                                             in0=t["hbuf"][0:1, 1:2],
                                             scalar1=1.0 / 2048.0, scalar2=None,
                                             op0=ALU.mult))
    k.op("vector", lambda e: e.tensor_tensor(out=t["hbuf"][0:1, 4:5],
                                             in0=t["hbuf"][0:1, 2:3],
                                             in1=t["hbuf"][0:1, 2:3], op=ALU.mult))
    vv = k.op("vector", lambda e: e.tensor_tensor(out=t["hbuf"][0:1, 4:5],
                                                  in0=t["hbuf"][0:1, 3:4],
                                                  in1=t["hbuf"][0:1, 4:5],
                                                  op=ALU.subtract))
    k.op("scalar", lambda e: e.activation(out=t["hbuf"][0:1, 5:6],
                                          in_=t["hbuf"][0:1, 4:5], func=AF.Ln,
                                          bias=1e-5), wait=[vv])
    k.op("scalar", lambda e: e.activation(out=t["hbuf"][0:1, 5:6],
                                          in_=t["hbuf"][0:1, 5:6], func=AF.Exp,
                                          scale=-0.5))
    sr = k.ev("S")
    k.op("vector", lambda e: e.tensor_tensor(out=t["hbuf"][0:1, 6:7],
                                             in0=t["hbuf"][0:1, 2:3],
                                             in1=t["hbuf"][0:1, 5:6], op=ALU.mult),
         wait=[sr])
    nbb = k.op("vector", lambda e: e.tensor_scalar(out=t["hbuf"][0:1, 6:7],
                                                   in0=t["hbuf"][0:1, 6:7],
                                                   scalar1=-1.0, scalar2=None,
                                                   op0=ALU.mult))
    bl = k.op("scalar", lambda e: e.activation(out=t["xsq"][0:1, :],
                                               in_=t["xtm"][0:1, :],
                                               func=AF.Identity,
                                               scale=t["hbuf"][0:1, 5:6],
                                               bias=t["hbuf"][0:1, 6:7]),
              wait=[nbb])
    k.op("vector", lambda e: e.scalar_tensor_tensor(
        out=t["xsq"][0:1, :], in0=t["xsq"][0:1, :],
        scalar=t["flags"][0:1, 2 * r:2 * r + 1], in1=t["xsq"][0:1, :],
        op0=ALU.mult, op1=ALU.bypass), wait=[bl])
    k.op("vector", lambda e: e.scalar_tensor_tensor(
        out=t["xsq"][0:1, :], in0=t["xtm"][0:1, :],
        scalar=t["flags"][0:1, 2 * r + 1:2 * r + 2], in1=t["xsq"][0:1, :],
        op0=ALU.mult, op1=ALU.add))
    bb = k.op("vector", lambda e: e.tensor_copy(out=t["xntm"][0:1, :],
                                                in_=t["xsq"][0:1, :]),
              wait=[k.ev("T")])
    for fc in range(16):
        tev = k.tr(k.psum[4][0:128, 0:1],
                   t["xntm"][0:1, 128 * fc:128 * fc + 128], t["ones1"][0:1, 0:1],
                   wait=[bb, k.tp_free[0]], inc="T")
        k.tp_free[0] = k.op("vector",
                            lambda e, fc=fc: e.tensor_copy(
                                out=t["xn"][fc][:, 0:1],
                                in_=k.psum[4][0:128, 0:1]),
                            wait=[tev])
    xn_ev = k.ev("V")

    st = DBG["stages"]
    if "l0attn" in st:
        l0_attn(k, r, xn_ev)
    if "l0ffn" in st:
        l0_ffn(k, r, xn_ev)
    if "l0attn" in st and "l0ffn" in st:
        combine0(k, r)
    else:
        return
    if "l1attn" in st:
        l1_attn(k, r)
        if "l1out" in st:
            l1_ffn(k, r)


def kernel(**inputs):
    in_maps = host_prep(inputs)
    nc, k = build_program()
    res = run_bass_kernel_spmd(nc, in_maps, core_ids=list(range(NCORES)))
    out = np.zeros((NT, D), np.float32)
    for i in range(NCORES):
        o = res.results[i]["out"]
        for r in range(NTILES):
            blk = 8 * r + i
            out[blk * TB:(blk + 1) * TB] = o[r * TB:(r + 1) * TB]
    return out.reshape(1, NT, D)
